# version 05
# baseline (speedup 1.0000x reference)
import numpy as np
import ml_dtypes
import concourse.bass as bass
import concourse.mybir as mybir
from concourse.bass_utils import run_bass_kernel_spmd
from contextlib import ExitStack

F32 = mybir.dt.float32
BF16 = mybir.dt.bfloat16
I32 = mybir.dt.int32
AF = mybir.ActivationFunctionType
ALU = mybir.AluOpType
AX = mybir.AxisListType
bf16 = ml_dtypes.bfloat16

B, S, D = 2, 8192, 2048
TOK = 2048
NT = TOK // 128
DC = D // 128
HD = 128
WIN = 512
FF0 = 8192
EPS_RMS = 1.1920928955078125e-07
EPS_LN = 1e-5
MAGIC = float(np.float32(1.5 * 2 ** 23))
NEG = -1.0e30

ENGINES = ("sync", "gpsimd", "vector", "scalar", "tensor")


class Buf:
    __slots__ = ("w", "rs")

    def __init__(self):
        self.w = None
        self.rs = []


class Prog:

    def __init__(self):
        self.ops = []

    def emit(self, eng, fn, reads=(), writes=(), dma=False, deps=(), coll=False):
        i = len(self.ops)
        dep = set(d for d in deps if d is not None)
        for b in reads:
            if b.w is not None:
                dep.add(b.w)
        for b in writes:
            if b.w is not None:
                dep.add(b.w)
            dep.update(b.rs)
        dep.discard(i)
        self.ops.append({"eng": eng, "fn": fn, "deps": dep, "dma": dma, "i": i,
                         "coll": coll})
        for b in reads:
            b.rs.append(i)
        for b in writes:
            b.w = i
            b.rs = []
        return i

    N_DMA_SEMS = 12

    def barrier(self):
        deps = set()
        last = {}
        ndma = {}
        for op in reversed(self.ops):
            e = op["eng"]
            if op["dma"]:
                if ndma.get(e, 0) < self.N_DMA_SEMS:
                    deps.add(op["i"])
                    ndma[e] = ndma.get(e, 0) + 1
            elif e not in last:
                last[e] = op["i"]
                deps.add(op["i"])
        deps = sorted(deps)
        for e in ENGINES:
            self.emit(e, lambda eng: eng.nop(), deps=deps)

    def finalize(self, nc, es):
        by_eng = {e: [] for e in ENGINES}
        for op in self.ops:
            by_eng[op["eng"]].append(op)

        consumers = {}
        for op in self.ops:
            for d in op["deps"]:
                consumers.setdefault(d, []).append(op["i"])

        rr = {}
        last_on_sem = {}
        ncoll = [0]
        for op in self.ops:
            if op.get("coll"):
                op["skey"] = f"coll{ncoll[0]}"
                ncoll[0] += 1
            elif op["dma"]:
                j = rr.get(op["eng"], 0)
                rr[op["eng"]] = j + 1
                k = f"{op['eng']}_dma{j % self.N_DMA_SEMS}"
                prev = last_on_sem.get(k)
                if prev is not None:
                    extra = [c for c in consumers.get(prev, [prev])
                             if c < op["i"]]
                    if not extra:
                        extra = [prev]
                    op["deps"].update(extra)
                last_on_sem[k] = op["i"]
                op["skey"] = k
            else:
                op["skey"] = op["eng"]

        need_sig = set()
        for op in self.ops:
            need_sig.update(op["deps"])
        semval = {}
        for op in self.ops:
            if op["i"] in need_sig:
                op["inc"] = 16 if op["dma"] else 1
                k = op["skey"]
                semval[k] = semval.get(k, 0) + op["inc"]
                op["sig"] = semval[k]
            else:
                op["sig"] = None

        sem_keys = sorted({op["skey"] for op in self.ops if op["sig"] is not None})
        sems = {k: es.enter_context(nc.semaphore(f"s_{k}")) for k in sem_keys}

        for op in self.ops:
            waits = {}
            for d in op["deps"]:
                dop = self.ops[d]
                k, v = dop["skey"], dop["sig"]
                if waits.get(k, 0) < v:
                    waits[k] = v
            op["waits"] = waits

        block = es.enter_context(nc.Block())
        for eng_name in ENGINES:
            deco = getattr(block, eng_name)

            def _mk(eng_name):
                ops = by_eng[eng_name]

                def _body(eng):
                    water = {}
                    for op in ops:
                        for k, v in sorted(op["waits"].items()):
                            if water.get(k, 0) < v:
                                eng.wait_ge(sems[k], v)
                                water[k] = v
                        inst = op["fn"](eng)
                        if op["sig"] is not None:
                            inst.then_inc(sems[op["skey"]], op["inc"])
                return _body

            deco(_mk(eng_name))
        return sems


def dram(io, nc, name, shape, dt_, kind):
    if io is not None and name in io:
        return io[name]
    return nc.dram_tensor(name, shape, dt_, kind=kind).ap()


DT_SIZE = {F32: 4, BF16: 2, I32: 4}


class LaunchCtx:
    def __init__(self, nc, es, shared=None):
        self.nc = nc
        self.es = es
        if shared is None:
            self.P = Prog()
            self.arena = None
            self.ps = [es.enter_context(nc.psum_tensor(f"psf{i}", [128, 512], F32))
                       for i in range(6)]
            self.psb = [es.enter_context(nc.psum_tensor(f"psb{i}", [128, 1024], BF16))
                        for i in range(2)]
            self.ps_buf = [Buf() for _ in self.ps]
            self.psb_buf = [Buf() for _ in self.psb]
            self.consts = es.enter_context(nc.sbuf_tensor("consts", [128, 8], F32))
            self.consts_b = Buf()
            self.P.emit("vector", lambda e: e.memset(self.consts[:, 0:1], EPS_LN),
                        writes=(self.consts_b,))
            self.P.emit("vector", lambda e: e.memset(self.consts[:, 1:2], EPS_RMS),
                        writes=(self.consts_b,))
        else:
            self.P = shared.P
            self.arena = shared.arena
            self.arena_off = [0]
            self.ps = shared.ps
            self.psb = shared.psb
            self.ps_buf = shared.ps_buf
            self.psb_buf = shared.psb_buf
            self.consts = shared.consts
            self.consts_b = shared.consts_b

    def sb(self, name, shape, dtype):
        if self.arena is None:
            t = self.es.enter_context(self.nc.sbuf_tensor(name, shape, dtype))
            return t, Buf()
        nb = int(np.prod(shape[1:])) * DT_SIZE[dtype]
        off = (self.arena_off[0] + 63) & ~63
        self.arena_off[0] = off + nb
        assert self.arena_off[0] <= self.arena.shape[1], \
            f"arena overflow at {name}: {self.arena_off[0]}"
        ap = self.arena[:, off:off + nb].bitcast(dtype)
        if shape[0] != 128:
            ap = ap[0:shape[0], :]
        return ap, Buf()


def emit_ln_tile(L, dst, dst_b, src, src_b, stats, stats_b, width=2048,
                 eps_col=0, out_kind="normal", res=None, res_b=None):
    P = L.P
    n = width // 512
    for i in range(n):
        P.emit("vector", (lambda i: lambda e: e.bn_stats(
            stats[:, i * 6:(i + 1) * 6], src[:, i * 512:(i + 1) * 512]))(i),
            reads=(src_b,), writes=(stats_b,))
    P.emit("vector", lambda e: e.bn_aggr(stats[:, 24:26], stats[:, 0:6 * n]),
           reads=(stats_b,), writes=(stats_b,))
    P.emit("vector", lambda e: e.tensor_scalar(stats[:, 26:27], stats[:, 24:25],
                                               -1.0, None, ALU.mult),
           reads=(stats_b,), writes=(stats_b,))
    P.emit("scalar", lambda e: e.activation(stats[:, 27:28], stats[:, 25:26],
                                            AF.Sqrt, bias=L.consts[:, eps_col:eps_col + 1]),
           reads=(stats_b, L.consts_b), writes=(stats_b,))
    P.emit("vector", lambda e: e.reciprocal(stats[:, 28:29], stats[:, 27:28]),
           reads=(stats_b,), writes=(stats_b,))
    P.emit("vector", lambda e: e.tensor_scalar(
        dst[:], src[:], stats[:, 26:27], stats[:, 28:29], ALU.add, ALU.mult),
        reads=(src_b, stats_b), writes=(dst_b,))
    if out_kind == "resid":
        P.emit("vector", lambda e: e.tensor_tensor(dst[:], dst[:], res[:], ALU.add),
               reads=(dst_b, res_b), writes=(dst_b,))


def emit_transposes(L, dst, dst_b, src, src_b, ident, ident_b, chunk_map):
    P = L.P
    for base in range(0, len(chunk_map), 8):
        grp = chunk_map[base:base + 8]
        bi = (base // 8) % len(L.psb)
        bank, bank_b = L.psb[bi], L.psb_buf[bi]
        for k, (sc, _) in enumerate(grp):
            P.emit("tensor", (lambda k, sc, bank: lambda e: e.transpose(
                bank[:, k * 128:(k + 1) * 128], src[:, sc:sc + 128], ident[:]))(k, sc, bank),
                reads=(src_b, ident_b), writes=(bank_b,))
        for k, (_, dc_) in enumerate(grp):
            P.emit("vector", (lambda k, dc_, bank: lambda e: e.tensor_copy(
                dst[:, dc_:dc_ + 128], bank[:, k * 128:(k + 1) * 128]))(k, dc_, bank),
                reads=(bank_b,), writes=(dst_b,))



def build_l1_ffn(nc, es, L=None, io=None):
    fin = L is None
    if L is None:
        L = LaunchCtx(nc, es)
    P = L.P

    x3 = dram(io, nc, "x3", [TOK, D], F32, "ExternalInput")
    wg = dram(io, nc, "wg", [64, 128, 16 * 128], I32, "ExternalInput")
    wu = dram(io, nc, "wu", [64, 128, 16 * 128], I32, "ExternalInput")
    wd = dram(io, nc, "wd", [64, 128, D], I32, "ExternalInput")
    gsc = dram(io, nc, "gsc", [128, 64], F32, "ExternalInput")
    usc = dram(io, nc, "usc", [128, 64], F32, "ExternalInput")
    dscr = dram(io, nc, "dsc_row", [1, D], F32, "ExternalInput")
    identi = dram(io, nc, "ident_bf", [128, 128], BF16, "ExternalInput")
    identfi = dram(io, nc, "ident_f", [128, 128], F32, "ExternalInput")
    onesri = dram(io, nc, "ones_row", [1, 512], BF16, "ExternalInput")
    onesci = dram(io, nc, "ones_col", [128, 1], BF16, "ExternalInput")
    y = dram(io, nc, "y", [TOK, D], F32, "ExternalOutput")
    acc = dram(io, nc, "acc_scr", [TOK, D], F32, "Internal")

    ident, ident_b = L.sb("identt", [128, 128], BF16)
    identf, identf_b = L.sb("identtf", [128, 128], F32)
    onesr, onesr_b = L.sb("onesr", [1, 512], BF16)
    onesc, onesc_b = L.sb("onesc", [128, 1], BF16)
    onesrf, onesrf_b = L.sb("onesrf", [1, 128], F32)
    gscs, gscs_b = L.sb("gscs", [128, 64], F32)
    uscs, uscs_b = L.sb("uscs", [128, 64], F32)
    dsb, dsb_b = L.sb("ds_bc", [128, D], F32)
    dsrow, dsrow_b = L.sb("dsrow", [1, D], F32)
    xq = [L.sb(f"xq{i}", [128, D], F32) for i in range(2)]
    t3f, t3f_b = L.sb("t3f", [128, D], F32)
    t3bf, t3bf_b = L.sb("t3bf", [128, D], BF16)
    stats, stats_b = L.sb("stats", [128, 32], F32)
    t3T, t3T_b = L.sb("t3T", [128, DC * TOK], BF16)
    rsneg, rsneg_b = L.sb("rsneg", [1, 2 * TOK], BF16)
    rrow, rrow_b = L.sb("rrow", [1, 128], F32)
    wgu = [L.sb(f"wgu{i}", [128, 16 * 128], BF16) for i in range(4)]
    wdt = [L.sb(f"wdt{i}", [128, 512], BF16) for i in range(2)]
    hbuf, hbuf_b = L.sb("hbuf", [128, 8 * TOK], BF16)
    gtmp, gtmp_b = L.sb("gtmp", [128, 512], F32)
    utmp, utmp_b = L.sb("utmp", [128, 512], F32)
    sgt, sgt_b = L.sb("sgt", [128, 512], F32)
    hs, hs_b = L.sb("hs", [1, TOK], F32)
    hstmp, hstmp_b = L.sb("hstmp", [1, 512], F32)
    hcol, hcol_b = L.sb("hcol", [128, 1], F32)
    dtile, dtile_b = L.sb("dtile", [128, 512], F32)
    outt, outt_b = L.sb("outt", [128, D], F32)
    yt, yt_b = L.sb("yt", [128, D], F32)

    acc_bufs = [[Buf() for _ in range(4)] for _ in range(NT)]

    P.emit("sync", lambda e: e.dma_start(out=ident[:], in_=identi[:]), writes=(ident_b,), dma=True)
    P.emit("sync", lambda e: e.dma_start(out=identf[:], in_=identfi[:]), writes=(identf_b,), dma=True)
    P.emit("sync", lambda e: e.dma_start(out=onesr[:], in_=onesri[:]), writes=(onesr_b,), dma=True)
    P.emit("sync", lambda e: e.dma_start(out=onesc[:], in_=onesci[:]), writes=(onesc_b,), dma=True)
    P.emit("sync", lambda e: e.dma_start(out=gscs[:], in_=gsc[:]), writes=(gscs_b,), dma=True)
    P.emit("sync", lambda e: e.dma_start(out=uscs[:], in_=usc[:]), writes=(uscs_b,), dma=True)
    P.emit("sync", lambda e: e.dma_start(out=dsrow[:], in_=dscr[:]), writes=(dsrow_b,), dma=True)
    P.emit("vector", lambda e: e.memset(onesrf[:], 1.0), writes=(onesrf_b,))
    P.emit("vector", lambda e: e.memset(hs[:], 0.0), writes=(hs_b,))

    for i in range(4):
        bank, bank_b = L.ps[4], L.ps_buf[4]
        P.emit("tensor", (lambda i, bank: lambda e: e.matmul(
            bank[:], onesrf[:], dsrow[:, i * 512:(i + 1) * 512], start=True, stop=True))(i, bank),
            reads=(onesrf_b, dsrow_b), writes=(bank_b,))
        P.emit("vector", (lambda i, bank: lambda e: e.tensor_copy(
            dsb[:, i * 512:(i + 1) * 512], bank[:]))(i, bank),
            reads=(bank_b,), writes=(dsb_b,))

    for qt in range(NT):
        xi, xi_b = xq[qt % 2]
        P.emit("sync", (lambda qt, xi: lambda e: e.dma_start(
            out=xi[:], in_=x3[qt * 128:(qt + 1) * 128, :]))(qt, xi),
            writes=(xi_b,), dma=True)
        emit_ln_tile(L, t3f, t3f_b, xi, xi_b, stats, stats_b)
        P.emit("vector", lambda e: e.tensor_copy(t3bf[:], t3f[:]),
               reads=(t3f_b,), writes=(t3bf_b,))
        P.emit("vector", lambda e: e.tensor_reduce(
            stats[:, 29:30], t3bf[:], AX.X, ALU.add),
            reads=(t3bf_b,), writes=(stats_b,))
        bank, bank_b = L.ps[5], L.ps_buf[5]
        P.emit("tensor", (lambda bank: lambda e: e.matmul(
            bank[0:1, 0:128], stats[:, 29:30], identf[:], start=True, stop=True))(bank),
            reads=(stats_b, identf_b), writes=(bank_b,))
        P.emit("vector", (lambda bank: lambda e: e.tensor_scalar(
            rrow[:], bank[0:1, 0:128], -128.0, None, ALU.mult))(bank),
            reads=(bank_b,), writes=(rrow_b,))
        P.emit("vector", (lambda qt: lambda e: e.tensor_copy(
            rsneg[0:1, qt * 128:(qt + 1) * 128], rrow[:]))(qt),
            reads=(rrow_b,), writes=(rsneg_b,))
        P.emit("vector", (lambda qt: lambda e: e.tensor_tensor(
            rsneg[0:1, TOK + qt * 128:TOK + (qt + 1) * 128], rrow[:],
            rsneg[0:1, qt * 128:(qt + 1) * 128], ALU.subtract))(qt),
            reads=(rrow_b, rsneg_b), writes=(rsneg_b,))
        emit_transposes(L, t3T, t3T_b, t3bf, t3bf_b, ident, ident_b,
                        [(dc_ * 128, (dc_ * NT + qt) * 128) for dc_ in range(DC)])

    for bb in range(8):
        for oc in range(8):
            ocg = bb * 8 + oc
            specs = ((wg, gscs, gscs_b, gtmp, gtmp_b), (wu, uscs, uscs_b, utmp, utmp_b))
            for gi, (w_dram, sc_s, sc_b, tmp, tmp_b) in enumerate(specs):
                wt, wt_b = wgu[(oc % 2) * 2 + gi]
                P.emit("gpsimd", (lambda ocg, w_dram, wt: lambda e: e.dma_start(
                    out=wt[:], in_=w_dram[ocg]))(ocg, w_dram, wt),
                    writes=(wt_b,), dma=True)
            for qg in range(4):
                for gi, (w_dram, sc_s, sc_b, tmp, tmp_b) in enumerate(specs):
                    wt, wt_b = wgu[(oc % 2) * 2 + gi]
                    bank, bank_b = L.ps[gi], L.ps_buf[gi]
                    P.emit("tensor", (lambda qg, bank: lambda e: e.matmul(
                        bank[:], onesr[0:1, 0:128],
                        rsneg[0:1, qg * 512:(qg + 1) * 512],
                        start=True, stop=False))(qg, bank),
                        reads=(onesr_b, rsneg_b), writes=(bank_b,))
                    P.emit("tensor", (lambda qg, bank: lambda e: e.matmul(
                        bank[:], onesr[0:1, 0:128],
                        rsneg[0:1, TOK + qg * 512:TOK + (qg + 1) * 512],
                        start=False, stop=False))(qg, bank),
                        reads=(onesr_b, rsneg_b), writes=(bank_b,))
                    for dc_ in range(DC):
                        P.emit("tensor", (lambda qg, dc_, wt, bank: lambda e: e.matmul(
                            bank[:], wt[:, dc_ * 128:(dc_ + 1) * 128],
                            t3T[:, (dc_ * 4 + qg) * 512:(dc_ * 4 + qg + 1) * 512],
                            start=False, stop=(dc_ == DC - 1)))(qg, dc_, wt, bank),
                            reads=(wt_b, t3T_b), writes=(bank_b,))
                    P.emit("vector", (lambda ocg, bank, tmp, sc_s, sc_b: lambda e: e.tensor_scalar(
                        tmp[:], bank[:], sc_s[:, ocg:ocg + 1], None, ALU.mult))(ocg, bank, tmp, sc_s, sc_b),
                        reads=(bank_b, sc_b), writes=(tmp_b,))
                P.emit("scalar", lambda e: e.activation(sgt[:], gtmp[:], AF.Silu),
                       reads=(gtmp_b,), writes=(sgt_b,))
                P.emit("vector", (lambda oc, qg: lambda e: e.tensor_tensor(
                    hbuf[:, (oc * 4 + qg) * 512:(oc * 4 + qg + 1) * 512],
                    sgt[:], utmp[:], ALU.mult))(oc, qg),
                    reads=(sgt_b, utmp_b), writes=(hbuf_b,))
            bank, bank_b = L.ps[4], L.ps_buf[4]
            for qg in range(4):
                P.emit("tensor", (lambda oc, qg, bank: lambda e: e.matmul(
                    bank[0:1, 0:512], onesc[:],
                    hbuf[:, (oc * 4 + qg) * 512:(oc * 4 + qg + 1) * 512],
                    start=True, stop=True))(oc, qg, bank),
                    reads=(onesc_b, hbuf_b), writes=(bank_b,))
                P.emit("vector", (lambda qg, bank: lambda e: e.tensor_copy(
                    hstmp[:], bank[0:1, 0:512]))(qg, bank),
                    reads=(bank_b,), writes=(hstmp_b,))
                P.emit("vector", (lambda qg: lambda e: e.tensor_tensor(
                    hs[0:1, qg * 512:(qg + 1) * 512],
                    hs[0:1, qg * 512:(qg + 1) * 512], hstmp[:], ALU.add))(qg),
                    reads=(hs_b, hstmp_b), writes=(hs_b,))
        for qgrp in range(4):
            for ds_ in range(4):
                for oc in range(8):
                    wt, wt_b = wdt[oc % 2]
                    ocg = bb * 8 + oc
                    P.emit("gpsimd", (lambda ocg, ds_, wt: lambda e: e.dma_start(
                        out=wt[:], in_=wd[ocg, :, ds_ * 512:(ds_ + 1) * 512]))(ocg, ds_, wt),
                        writes=(wt_b,), dma=True)
                    for jj in range(4):
                        qt = qgrp * 4 + jj
                        bank, bank_b = L.ps[jj], L.ps_buf[jj]
                        P.emit("tensor", (lambda oc, qt, jj, wt, bank: lambda e: e.matmul(
                            bank[:], hbuf[:, oc * TOK + qt * 128:oc * TOK + (qt + 1) * 128],
                            wt[:], start=(oc == 0), stop=(oc == 7)))(oc, qt, jj, wt, bank),
                            reads=(hbuf_b, wt_b), writes=(bank_b,))
                for jj in range(4):
                    qt = qgrp * 4 + jj
                    bank, bank_b = L.ps[jj], L.ps_buf[jj]
                    ab = acc_bufs[qt][ds_]
                    if bb == 0:
                        P.emit("vector", (lambda bank: lambda e: e.tensor_copy(
                            dtile[:], bank[:]))(bank),
                            reads=(bank_b,), writes=(dtile_b,))
                    else:
                        P.emit("sync", (lambda qt, ds_: lambda e: e.dma_start(
                            out=dtile[:], in_=acc[qt * 128:(qt + 1) * 128,
                                                  ds_ * 512:(ds_ + 1) * 512]))(qt, ds_),
                            reads=(ab,), writes=(dtile_b,), dma=True)
                        P.emit("vector", (lambda bank: lambda e: e.tensor_tensor(
                            dtile[:], dtile[:], bank[:], ALU.add))(bank),
                            reads=(dtile_b, bank_b), writes=(dtile_b,))
                    P.emit("sync", (lambda qt, ds_: lambda e: e.dma_start(
                        out=acc[qt * 128:(qt + 1) * 128, ds_ * 512:(ds_ + 1) * 512],
                        in_=dtile[:]))(qt, ds_),
                        reads=(dtile_b,), writes=(ab,), dma=True)

    store_ops = []
    for qt in range(NT):
        xi, xi_b = xq[qt % 2]
        P.emit("sync", (lambda qt, xi: lambda e: e.dma_start(
            out=xi[:], in_=x3[qt * 128:(qt + 1) * 128, :]))(qt, xi),
            writes=(xi_b,), dma=True)
        bank, bank_b = L.ps[5], L.ps_buf[5]
        P.emit("tensor", (lambda qt, bank: lambda e: e.matmul(
            bank[:, 0:128], hs[0:1, qt * 128:(qt + 1) * 128], onesrf[:],
            start=True, stop=True))(qt, bank),
            reads=(hs_b, onesrf_b), writes=(bank_b,))
        P.emit("vector", (lambda bank: lambda e: e.tensor_scalar(
            hcol[:], bank[:, 0:1], -128.0, None, ALU.mult))(bank),
            reads=(bank_b,), writes=(hcol_b,))
        for ds_ in range(4):
            ab = acc_bufs[qt][ds_]
            P.emit("sync", (lambda qt, ds_: lambda e: e.dma_start(
                out=dtile[:], in_=acc[qt * 128:(qt + 1) * 128,
                                      ds_ * 512:(ds_ + 1) * 512]))(qt, ds_),
                reads=(ab,), writes=(dtile_b,), dma=True)
            P.emit("vector", (lambda ds_: lambda e: e.tensor_scalar(
                outt[:, ds_ * 512:(ds_ + 1) * 512], dtile[:], hcol[:], None,
                ALU.add))(ds_),
                reads=(dtile_b, hcol_b), writes=(outt_b,))
        P.emit("vector", lambda e: e.tensor_tensor(outt[:], outt[:], dsb[:], ALU.mult),
               reads=(outt_b, dsb_b), writes=(outt_b,))
        emit_ln_tile(L, yt, yt_b, outt, outt_b, stats, stats_b,
                     out_kind="resid", res=xi, res_b=xi_b)
        store_ops.append(P.emit("sync", (lambda qt: lambda e: e.dma_start(
            out=y[qt * 128:(qt + 1) * 128, :], in_=yt[:]))(qt),
            reads=(yt_b,), dma=True))

    P.emit("sync", lambda e: e.nop(), deps=store_ops)
    if fin:
        P.finalize(nc, es)
    return nc


def _const_inputs():
    return dict(ident_bf=np.eye(128, dtype=bf16),
                ident_f=np.eye(128, dtype=np.float32),
                ones_row=np.ones((1, 512), dtype=bf16),
                ones_col=np.ones((128, 1), dtype=bf16))


def host_prep_l1_ffn(inputs):
    gw = inputs["L1_ffn_gate_w"]
    uw = inputs["L1_ffn_up_w"]
    dw = inputs["L1_ffn_down_w"]
    def _gu(w):
        wr = w.reshape(64, 128, 16, 128)
        return np.ascontiguousarray(wr.transpose(0, 3, 2, 1)).reshape(64, 128, 16 * 128)
    wgA = _gu(gw)
    wuA = _gu(uw)
    wdA = np.ascontiguousarray(dw.T).reshape(64, 128, D)
    gsc = np.ascontiguousarray(inputs["L1_ffn_gate_scale"].reshape(64, 128).T)
    usc = np.ascontiguousarray(inputs["L1_ffn_up_scale"].reshape(64, 128).T)
    dsc = np.ascontiguousarray(inputs["L1_ffn_down_scale"].reshape(1, D))
    d = dict(wg=wgA, wu=wuA, wd=wdA, gsc=gsc, usc=usc, dsc_row=dsc)
    d.update(_const_inputs())
    return d


def contiguous_rows(core):
    b, c = core // 4, core % 4
    r1 = np.arange(c * 1024, (c + 1) * 1024)
    r2 = np.arange((7 - c) * 1024, (8 - c) * 1024)
    return b, np.concatenate([r1, r2])


def interleaved_rows(core):
    b, c = core // 4, core % 4
    tiles = [63 - 4 * s - c for s in range(16)]
    rows = np.concatenate([np.arange(t * 128, (t + 1) * 128) for t in tiles])
    return b, rows


def kernel(**inputs):
    raise NotImplementedError("under construction")



def build_l1_qkv(nc, es, L=None, io=None):
    fin = L is None
    if L is None:
        L = LaunchCtx(nc, es)
    P = L.P

    x2 = dram(io, nc, "x2", [TOK, D], F32, "ExternalInput")
    wqA = dram(io, nc, "wqA", [16, 128, 16 * 128], F32, "ExternalInput")
    wkA = dram(io, nc, "wkA", [16, 128, 16 * 128], F32, "ExternalInput")
    wvB = dram(io, nc, "wvB", [16, 128, D], F32, "ExternalInput")
    bqP = dram(io, nc, "bqP", [128, 16], F32, "ExternalInput")
    bkP = dram(io, nc, "bkP", [128, 16], F32, "ExternalInput")
    bvR = dram(io, nc, "bv_row", [1, D], F32, "ExternalInput")
    rqc = dram(io, nc, "ropeq_c", [64, TOK], F32, "ExternalInput")
    rqs = dram(io, nc, "ropeq_s", [64, TOK], F32, "ExternalInput")
    rkc = dram(io, nc, "ropek_c", [64, TOK], F32, "ExternalInput")
    rks = dram(io, nc, "ropek_s", [64, TOK], F32, "ExternalInput")
    identi = dram(io, nc, "ident_bf", [128, 128], BF16, "ExternalInput")
    onesrfi = dram(io, nc, "ones_row_f", [1, 128], F32, "ExternalInput")
    qt_o = dram(io, nc, "qt_o", [D, TOK], BF16, "ExternalOutput")
    kt_o = dram(io, nc, "kt_o", [D, TOK], BF16, "ExternalOutput")
    v_o = dram(io, nc, "v_o", [TOK, D], BF16, "ExternalOutput")

    x2load = (io.get("x2_load") if io else None) or (
        lambda e, qt, dst: e.dma_start(out=dst, in_=x2[qt * 128:(qt + 1) * 128, :]))
    x2eng = (io.get("x2_eng") if io else None) or "sync"
    x2dep = io.get("x2_dep") if io else None
    ident, ident_b = L.sb("identt", [128, 128], BF16)
    onesrf, onesrf_b = L.sb("onesrf", [1, 128], F32)
    bvrow, bvrow_b = L.sb("bvrow", [1, D], F32)
    bvbc, bvbc_b = L.sb("bvbc", [128, D], F32)
    bqs, bqs_b = L.sb("bqs", [128, 16], F32)
    bks, bks_b = L.sb("bks", [128, 16], F32)
    rq = [L.sb(f"rq{i}", [128, TOK], F32) for i in range(2)]
    rk = [L.sb(f"rk{i}", [128, TOK], F32) for i in range(2)]
    xq = [L.sb(f"xq{i}", [128, D], F32) for i in range(2)]
    t2f, t2f_b = L.sb("t2f", [128, D], F32)
    t2bf, t2bf_b = L.sb("t2bf", [128, D], BF16)
    stats, stats_b = L.sb("stats", [128, 32], F32)
    t2T, t2T_b = L.sb("t2T", [128, DC * TOK], BF16)
    wqk = [L.sb(f"wqk{i}", [128, 16 * 128], BF16) for i in range(2)]
    wvt = [L.sb(f"wvt{i}", [128, 512], BF16) for i in range(2)]
    ptmp, ptmp_b = L.sb("ptmp", [128, 512], F32)
    r1, r1_b = L.sb("ropet", [64, 512], F32)
    r2, r2_b = L.sb("ropet2", [64, 512], F32)
    qko, qko_b = L.sb("qko", [128, 512], BF16)
    vsb, vsb_b = L.sb("vsb", [128, D], BF16)

    P.emit("sync", lambda e: e.dma_start(out=ident[:], in_=identi[:]), writes=(ident_b,), dma=True)
    P.emit("sync", lambda e: e.dma_start(out=onesrf[:], in_=onesrfi[:]), writes=(onesrf_b,), dma=True)
    P.emit("sync", lambda e: e.dma_start(out=bvrow[:], in_=bvR[:]), writes=(bvrow_b,), dma=True)
    P.emit("sync", lambda e: e.dma_start(out=bqs[:], in_=bqP[:]), writes=(bqs_b,), dma=True)
    P.emit("sync", lambda e: e.dma_start(out=bks[:], in_=bkP[:]), writes=(bks_b,), dma=True)
    for (tt_, ttb_), srct in ((rq[0], rqc), (rq[1], rqs), (rk[0], rkc), (rk[1], rks)):
        P.emit("sync", (lambda tt_, srct: lambda e: e.dma_start(
            out=tt_[0:64, :], in_=srct[:]))(tt_, srct), writes=(ttb_,), dma=True)
        P.emit("sync", (lambda tt_, srct: lambda e: e.dma_start(
            out=tt_[64:128, :], in_=srct[:]))(tt_, srct), writes=(ttb_,), dma=True)
    for i in range(4):
        bank, bank_b = L.ps[4], L.ps_buf[4]
        P.emit("tensor", (lambda i, bank: lambda e: e.matmul(
            bank[:], onesrf[:], bvrow[:, i * 512:(i + 1) * 512], start=True, stop=True))(i, bank),
            reads=(onesrf_b, bvrow_b), writes=(bank_b,))
        P.emit("vector", (lambda i, bank: lambda e: e.tensor_copy(
            bvbc[:, i * 512:(i + 1) * 512], bank[:]))(i, bank),
            reads=(bank_b,), writes=(bvbc_b,))

    for qt in range(NT):
        xi, xi_b = xq[qt % 2]
        P.emit(x2eng, (lambda qt, xi: lambda e: x2load(e, qt, xi[:]))(qt, xi),
               writes=(xi_b,), dma=True, deps=(x2dep,))
        emit_ln_tile(L, t2f, t2f_b, xi, xi_b, stats, stats_b)
        P.emit("vector", lambda e: e.tensor_copy(t2bf[:], t2f[:]),
               reads=(t2f_b,), writes=(t2bf_b,))
        emit_transposes(L, t2T, t2T_b, t2bf, t2bf_b, ident, ident_b,
                        [(dc_ * 128, (dc_ * NT + qt) * 128) for dc_ in range(DC)])

    store_ops = []
    for wi, (wA, bb, bb_b, rc, rs_, out_d) in enumerate(
            ((wqA, bqs, bqs_b, rq[0], rq[1], qt_o),
             (wkA, bks, bks_b, rk[0], rk[1], kt_o))):
        for oc in range(16):
            wt, wt_b = wqk[oc % 2]
            P.emit("gpsimd", (lambda oc, wA, wt: lambda e: e.dma_start(
                out=wt[:], in_=wA[oc]))(oc, wA, wt),
                writes=(wt_b,), dma=True)
            for qg in range(4):
                bank, bank_b = L.ps[wi], L.ps_buf[wi]
                for dc_ in range(DC):
                    P.emit("tensor", (lambda qg, dc_, wt, bank: lambda e: e.matmul(
                        bank[:], wt[:, dc_ * 128:(dc_ + 1) * 128],
                        t2T[:, (dc_ * 4 + qg) * 512:(dc_ * 4 + qg + 1) * 512],
                        start=(dc_ == 0), stop=(dc_ == DC - 1)))(qg, dc_, wt, bank),
                        reads=(wt_b, t2T_b), writes=(bank_b,))
                P.emit("vector", (lambda oc, bank, bb: lambda e: e.tensor_scalar(
                    ptmp[:], bank[:], bb[:, oc:oc + 1], None, ALU.add))(oc, bank, bb),
                    reads=(bank_b, bb_b), writes=(ptmp_b,))
                rcs, rcb = rc
                rss, rsb = rs_
                sl = slice(qg * 512, (qg + 1) * 512)
                P.emit("vector", (lambda sl, rcs: lambda e: e.tensor_tensor(
                    r1[:], ptmp[0:64, :], rcs[0:64, sl], ALU.mult))(sl, rcs),
                    reads=(ptmp_b, rcb), writes=(r1_b,))
                P.emit("vector", (lambda sl, rss: lambda e: e.tensor_tensor(
                    r2[:], ptmp[64:128, :], rss[64:128, sl], ALU.mult))(sl, rss),
                    reads=(ptmp_b, rsb), writes=(r2_b,))
                P.emit("vector", lambda e: e.tensor_tensor(
                    qko[0:64, :], r1[:], r2[:], ALU.subtract),
                    reads=(r1_b, r2_b), writes=(qko_b,))
                P.emit("vector", (lambda sl, rcs: lambda e: e.tensor_tensor(
                    r1[:], ptmp[64:128, :], rcs[64:128, sl], ALU.mult))(sl, rcs),
                    reads=(ptmp_b, rcb), writes=(r1_b,))
                P.emit("vector", (lambda sl, rss: lambda e: e.tensor_tensor(
                    r2[:], ptmp[0:64, :], rss[0:64, sl], ALU.mult))(sl, rss),
                    reads=(ptmp_b, rsb), writes=(r2_b,))
                P.emit("vector", lambda e: e.tensor_tensor(
                    qko[64:128, :], r1[:], r2[:], ALU.add),
                    reads=(r1_b, r2_b), writes=(qko_b,))
                store_ops.append(P.emit("sync", (lambda oc, qg, out_d: lambda e: e.dma_start(
                    out=out_d[oc * 128:(oc + 1) * 128, qg * 512:(qg + 1) * 512],
                    in_=qko[:]))(oc, qg, out_d),
                    reads=(qko_b,), dma=True))

    for oslab in range(4):
        for qg in range(4):
            for dc_ in range(DC):
                wt, wt_b = wvt[dc_ % 2]
                P.emit("gpsimd", (lambda oslab, dc_, wt: lambda e: e.dma_start(
                    out=wt[:], in_=wvB[dc_, :, oslab * 512:(oslab + 1) * 512]))(oslab, dc_, wt),
                    writes=(wt_b,), dma=True)
                for j in range(4):
                    qt = qg * 4 + j
                    bank, bank_b = L.ps[2 + j], L.ps_buf[2 + j]
                    P.emit("tensor", (lambda qt, dc_, j, wt, bank: lambda e: e.matmul(
                        bank[:], t2T[:, (dc_ * NT + qt) * 128:(dc_ * NT + qt) * 128 + 128],
                        wt[:], start=(dc_ == 0), stop=(dc_ == DC - 1)))(qt, dc_, j, wt, bank),
                        reads=(t2T_b, wt_b), writes=(bank_b,))
            for j in range(4):
                qt = qg * 4 + j
                bank, bank_b = L.ps[2 + j], L.ps_buf[2 + j]
                P.emit("vector", (lambda oslab, bank: lambda e: e.tensor_tensor(
                    vsb[:, oslab * 512:(oslab + 1) * 512], bank[:],
                    bvbc[:, oslab * 512:(oslab + 1) * 512], ALU.add))(oslab, bank),
                    reads=(bank_b, bvbc_b), writes=(vsb_b,))
                store_ops.append(P.emit("sync", (lambda qt, oslab: lambda e: e.dma_start(
                    out=v_o[qt * 128:(qt + 1) * 128, oslab * 512:(oslab + 1) * 512],
                    in_=vsb[:, oslab * 512:(oslab + 1) * 512]))(qt, oslab),
                    reads=(vsb_b,), dma=True))

    P.emit("sync", lambda e: e.nop(), deps=store_ops)
    if fin:
        P.finalize(nc, es)
    return nc


_ROPE_PERM = None


def _rope_perm():
    global _ROPE_PERM
    if _ROPE_PERM is None:
        p = []
        for h in range(16):
            p += [h * 128 + 2 * i for i in range(64)]
            p += [h * 128 + 2 * i + 1 for i in range(64)]
        _ROPE_PERM = np.array(p)
    return _ROPE_PERM


def host_prep_l1_qkv(inputs, core, rows=None):
    if rows is None:
        b, rows = interleaved_rows(core)
    perm = _rope_perm()
    Wq = inputs["L1_attn_Wq"][:, perm]
    Wk = inputs["L1_attn_Wk"][:, perm]
    Wv = inputs["L1_attn_Wv"]
    bq = inputs["L1_attn_bq"][perm]
    bk = inputs["L1_attn_bk"][perm]

    def _tileA(W):
        Wr = W.reshape(16, 128, 16, 128)
        return np.ascontiguousarray(Wr.transpose(2, 1, 0, 3)).reshape(16, 128, 16 * 128)

    wqA = _tileA(Wq)
    wkA = _tileA(Wk)
    wvB = np.ascontiguousarray(Wv.reshape(16, 128, D))
    bqP = np.ascontiguousarray(bq.reshape(16, 128).T)
    bkP = np.ascontiguousarray(bk.reshape(16, 128).T)
    bvR = inputs["L1_attn_bv"].reshape(1, D)
    pos = rows.astype(np.float32)
    i = np.arange(64, dtype=np.float32)
    theta = np.power(np.float32(10000.0), -2.0 * i / 128.0).astype(np.float32)
    ang = theta[:, None] * pos[None, :]
    c = np.cos(ang).astype(np.float32)
    s = np.sin(ang).astype(np.float32)
    sc = np.float32(1.0 / np.sqrt(128.0))
    return dict(wqA=wqA, wkA=wkA, wvB=wvB, bqP=bqP, bkP=bkP, bv_row=bvR,
                ropeq_c=c * sc, ropeq_s=s * sc, ropek_c=c, ropek_s=s,
                ident_bf=np.eye(128, dtype=bf16),
                ones_row_f=np.ones((1, 128), dtype=np.float32))



def build_l1_attn(nc, es, L=None, io=None):
    fin = L is None
    if L is None:
        L = LaunchCtx(nc, es)
    P = L.P

    qto = dram(io, nc, "qto", [D, TOK], BF16, "ExternalInput")
    ktg = dram(io, nc, "ktg", [4 * D, TOK], BF16, "ExternalInput")
    vg = dram(io, nc, "vg", [4 * TOK, D], BF16, "ExternalInput")
    kto = dram(io, nc, "kto", [D, TOK], BF16, "ExternalInput")
    vo = dram(io, nc, "vo", [TOK, D], BF16, "ExternalInput")
    x2 = dram(io, nc, "x2", [TOK, D], F32, "ExternalInput")
    woB = dram(io, nc, "woB", [16, 128, D], F32, "ExternalInput")
    boR = dram(io, nc, "bo_row", [1, D], F32, "ExternalInput")
    killR = dram(io, nc, "kill_slots", [1, 64 * 16], BF16, "ExternalInput")
    triA = dram(io, nc, "tri_add", [128, 128], F32, "ExternalInput")
    triM = dram(io, nc, "tri01T", [128, 128], BF16, "ExternalInput")
    identfi = dram(io, nc, "ident_f", [128, 128], F32, "ExternalInput")
    onesri = dram(io, nc, "ones_row", [1, 512], BF16, "ExternalInput")
    onesci = dram(io, nc, "ones_col", [128, 1], BF16, "ExternalInput")
    onesrfi = dram(io, nc, "ones_row_f", [1, 128], F32, "ExternalInput")
    x3 = dram(io, nc, "x3", [TOK, D], F32, "ExternalOutput")

    x2load = (io.get("x2_load") if io else None) or (
        lambda e, qt, dst: e.dma_start(out=dst, in_=x2[qt * 128:(qt + 1) * 128, :]))
    x2eng = (io.get("x2_eng") if io else None) or "sync"
    x2dep = io.get("x2_dep") if io else None
    ktg_v = ktg.rearrange("(r n) m -> r n m", r=4)
    vg_v = vg.rearrange("(r n) m -> r n m", r=4)

    identf, identf_b = L.sb("identtf", [128, 128], F32)
    onesr, onesr_b = L.sb("onesr", [1, 512], BF16)
    onesc, onesc_b = L.sb("onesc", [128, 1], BF16)
    onesrf, onesrf_b = L.sb("onesrf", [1, 128], F32)
    kills, kills_b = L.sb("kills", [1, 64 * 16], BF16)
    tria, tria_b = L.sb("tria", [128, 128], F32)
    trim, trim_b = L.sb("trim", [128, 128], BF16)
    borow, borow_b = L.sb("borow", [1, D], F32)
    bobc, bobc_b = L.sb("bobc", [128, D], F32)
    qh, qh_b = L.sb("qh", [128, TOK], BF16)
    ktoh, ktoh_b = L.sb("ktoh", [128, TOK], BF16)
    voh, voh_b = L.sb("voh", [128, TOK], BF16)
    kth2 = [L.sb(f"kth{i}", [128, 4 * TOK], BF16) for i in range(2)]
    _vh = L.sb("vh0", [128, 4 * TOK], BF16)
    vh2 = [_vh, _vh]
    msb, msb_b = L.sb("msb", [128, 16], F32)
    mtmp, mtmp_b = L.sb("mtmp", [128, 1], F32)
    mneg, mneg_b = L.sb("mneg", [1, TOK], BF16)
    mrow, mrow_b = L.sb("mrow", [1, 128], F32)
    krow, krow_b = L.sb("krow", [1, 512], BF16)
    sdt, sdt_b = L.sb("sdt", [128, 128], F32)
    pts2 = [L.sb(f"pts{i}", [128, 512], BF16) for i in range(2)]
    ptd, ptd_b = L.sb("ptd", [128, 128], BF16)
    lrow, lrow_b = L.sb("lrow", [1, 512], F32)
    linv, linv_b = L.sb("linv", [1, 512], F32)
    linvbc, linvbc_b = L.sb("linvbc", [128, 512], F32)
    otsb, otsb_b = L.sb("otsb", [128, 16 * TOK], BF16)
    wot = [L.sb(f"wot{i}", [128, 512], BF16) for i in range(2)]
    ao4, ao4_b = L.sb("ao4", [128, 2 * D], F32)
    xq = [L.sb("xq0", [128, D], F32)]
    xq = [xq[0], xq[0]]
    stats, stats_b = L.sb("stats", [128, 32], F32)
    yt, yt_b = L.sb("yt", [128, D], F32)

    for t, bb_, src in ((identf, identf_b, identfi), (onesr, onesr_b, onesri),
                        (onesc, onesc_b, onesci), (onesrf, onesrf_b, onesrfi),
                        (kills, kills_b, killR), (tria, tria_b, triA),
                        (trim, trim_b, triM), (borow, borow_b, boR)):
        P.emit("sync", (lambda t, src: lambda e: e.dma_start(out=t[:], in_=src[:]))(t, src),
               writes=(bb_,), dma=True)
    for i in range(4):
        bank, bank_b = L.ps[4], L.ps_buf[4]
        P.emit("tensor", (lambda i, bank: lambda e: e.matmul(
            bank[:], onesrf[:], borow[:, i * 512:(i + 1) * 512], start=True, stop=True))(i, bank),
            reads=(onesrf_b, borow_b), writes=(bank_b,))
        P.emit("vector", (lambda i, bank: lambda e: e.tensor_copy(
            bobc[:, i * 512:(i + 1) * 512], bank[:]))(i, bank),
            reads=(bank_b,), writes=(bobc_b,))

    for h in range(16):
        kth, kth_b = kth2[h % 2]
        vh, vh_b = vh2[h % 2]
        P.emit("sync", (lambda h: lambda e: e.dma_start(
            out=qh[:], in_=qto[h * 128:(h + 1) * 128, :]))(h),
            writes=(qh_b,), dma=True)
        P.emit("sync", (lambda h: lambda e: e.dma_start(
            out=ktoh[:], in_=kto[h * 128:(h + 1) * 128, :]))(h),
            writes=(ktoh_b,), dma=True)
        P.emit("sync", (lambda h: lambda e: e.dma_start(
            out=voh[:].rearrange("p (s d) -> p s d", s=16),
            in_=vo[:, h * 128:(h + 1) * 128]
            .rearrange("(s p) d -> p s d", p=128)))(h),
            writes=(voh_b,), dma=True)
        for ug in range(16):
            P.emit("sync", (lambda h, ug, kth: lambda e: e.dma_start(
                out=kth[:, ug * 512:(ug + 1) * 512].rearrange("n (r m) -> n r m", r=4),
                in_=ktg_v[:, h * 128:(h + 1) * 128, ug * 128:(ug + 1) * 128]
                .rearrange("r n m -> n r m")))(h, ug, kth),
                writes=(kth_b,), dma=True)
            P.emit("sync", (lambda h, ug, vh: lambda e: e.dma_start(
                out=vh[:, ug * 512:(ug + 1) * 512].rearrange("t (r d) -> t r d", r=4),
                in_=vg_v[:, ug * 128:(ug + 1) * 128, h * 128:(h + 1) * 128]
                .rearrange("r t d -> t r d")))(h, ug, vh),
                writes=(vh_b,), dma=True)

        for s_ in range(16):
            for ug in range(s_, 16):
                bank, bank_b = L.ps[ug % 2], L.ps_buf[ug % 2]
                P.emit("tensor", (lambda s_, ug, bank, kth: lambda e: e.matmul(
                    bank[:], qh[:, s_ * 128:(s_ + 1) * 128],
                    kth[:, ug * 512:(ug + 1) * 512], start=True, stop=True))(s_, ug, bank, kth),
                    reads=(qh_b, kth_b), writes=(bank_b,))
                if ug == s_:
                    P.emit("vector", (lambda s_, bank: lambda e: e.reduce_max(
                        msb[:, s_:s_ + 1], bank[:], AX.X))(s_, bank),
                        reads=(bank_b,), writes=(msb_b,))
                else:
                    P.emit("vector", (lambda bank: lambda e: e.reduce_max(
                        mtmp[:], bank[:], AX.X))(bank),
                        reads=(bank_b,), writes=(mtmp_b,))
                    P.emit("vector", (lambda s_: lambda e: e.tensor_max(
                        msb[:, s_:s_ + 1], msb[:, s_:s_ + 1], mtmp[:]))(s_),
                        reads=(msb_b, mtmp_b), writes=(msb_b,))
            bank, bank_b = L.ps[2], L.ps_buf[2]
            P.emit("tensor", (lambda s_, bank: lambda e: e.matmul(
                bank[:, 0:128], qh[:, s_ * 128:(s_ + 1) * 128],
                ktoh[:, s_ * 128:(s_ + 1) * 128], start=True, stop=True))(s_, bank),
                reads=(qh_b, ktoh_b), writes=(bank_b,))
            P.emit("vector", (lambda bank: lambda e: e.scalar_tensor_tensor(
                sdt[:], bank[:, 0:128], 1.0, tria[:], ALU.mult, ALU.add))(bank),
                reads=(bank_b, tria_b), writes=(sdt_b,))
            P.emit("vector", lambda e: e.reduce_max(mtmp[:], sdt[:], AX.X),
                   reads=(sdt_b,), writes=(mtmp_b,))
            P.emit("vector", (lambda s_: lambda e: e.tensor_max(
                msb[:, s_:s_ + 1], msb[:, s_:s_ + 1], mtmp[:]))(s_),
                reads=(msb_b, mtmp_b), writes=(msb_b,))
        for s_ in range(16):
            bank, bank_b = L.ps[5], L.ps_buf[5]
            P.emit("tensor", (lambda s_, bank: lambda e: e.matmul(
                bank[0:1, 0:128], msb[:, s_:s_ + 1], identf[:], start=True, stop=True))(s_, bank),
                reads=(msb_b, identf_b), writes=(bank_b,))
            P.emit("vector", (lambda s_, bank: lambda e: e.tensor_scalar(
                mneg[0:1, s_ * 128:(s_ + 1) * 128], bank[0:1, 0:128],
                -1.0, None, ALU.mult))(s_, bank),
                reads=(bank_b,), writes=(mneg_b,))

        for g in range(4):
            obank, obank_b = L.ps[2], L.ps_buf[2]
            lbank, lbank_b = L.ps[3], L.ps_buf[3]
            first = True
            for u in range(16 * g, 64):
                bank, bank_b = L.ps[u % 2], L.ps_buf[u % 2]
                if u <= 16 * g + 15:
                    P.emit("vector", (lambda u, g: lambda e: e.tensor_tensor(
                        krow[:].rearrange("a (j k) -> a j k", j=4),
                        mneg[0:1, g * 512:(g + 1) * 512].rearrange("a (j k) -> a j k", j=4),
                        kills[0:1, u * 16 + 4 * g:u * 16 + 4 * g + 4]
                        .unsqueeze(2).broadcast_to([1, 4, 128]),
                        ALU.add))(u, g),
                        reads=(mneg_b, kills_b), writes=(krow_b,))
                    rrow, rrow_b2 = krow, krow_b
                else:
                    rrow, rrow_b2 = mneg, mneg_b
                rsl = slice(0, 512) if rrow is krow else slice(g * 512, (g + 1) * 512)
                P.emit("tensor", (lambda bank, rrow, rsl: lambda e: e.matmul(
                    bank[:], onesr[0:1, 0:128], rrow[0:1, rsl],
                    start=True, stop=False))(bank, rrow, rsl),
                    reads=(onesr_b, rrow_b2), writes=(bank_b,))
                P.emit("tensor", (lambda u, g, bank, kth: lambda e: e.matmul(
                    bank[:], kth[:, u * 128:(u + 1) * 128],
                    qh[:, g * 512:(g + 1) * 512], start=False, stop=True))(u, g, bank, kth),
                    reads=(kth_b, qh_b), writes=(bank_b,))
                pts, pts_b = pts2[u % 2]
                P.emit("scalar", (lambda bank, pts: lambda e: e.activation(
                    pts[:], bank[:], AF.Exp))(bank, pts),
                    reads=(bank_b,), writes=(pts_b,))
                P.emit("tensor", (lambda u, first, obank, vh, pts: lambda e: e.matmul(
                    obank[:], vh[:, u * 128:(u + 1) * 128], pts[:],
                    start=first, stop=False))(u, first, obank, vh, pts),
                    reads=(vh_b, pts_b), writes=(obank_b,))
                P.emit("tensor", (lambda first, lbank, pts: lambda e: e.matmul(
                    lbank[0:1, 0:512], onesc[:], pts[:],
                    start=first, stop=False))(first, lbank, pts),
                    reads=(onesc_b, pts_b), writes=(lbank_b,))
                first = False
            for j in range(4):
                s_ = 4 * g + j
                bank, bank_b = L.ps[j % 2], L.ps_buf[j % 2]
                P.emit("tensor", (lambda s_, bank: lambda e: e.matmul(
                    bank[:, 0:128], onesr[0:1, 0:128],
                    mneg[0:1, s_ * 128:(s_ + 1) * 128], start=True, stop=False))(s_, bank),
                    reads=(onesr_b, mneg_b), writes=(bank_b,))
                P.emit("tensor", (lambda s_, bank: lambda e: e.matmul(
                    bank[:, 0:128], ktoh[:, s_ * 128:(s_ + 1) * 128],
                    qh[:, s_ * 128:(s_ + 1) * 128], start=False, stop=True))(s_, bank),
                    reads=(ktoh_b, qh_b), writes=(bank_b,))
                P.emit("scalar", (lambda bank: lambda e: e.activation(
                    ptd[:], bank[:, 0:128], AF.Exp))(bank),
                    reads=(bank_b,), writes=(ptd_b,))
                P.emit("vector", lambda e: e.tensor_tensor(
                    ptd[:], ptd[:], trim[:], ALU.mult),
                    reads=(ptd_b, trim_b), writes=(ptd_b,))
                P.emit("tensor", (lambda s_, j, obank: lambda e: e.matmul(
                    obank[:, j * 128:(j + 1) * 128], voh[:, s_ * 128:(s_ + 1) * 128],
                    ptd[:], start=False, stop=(j == 3)))(s_, j, obank),
                    reads=(voh_b, ptd_b), writes=(obank_b,))
                P.emit("tensor", (lambda j, lbank: lambda e: e.matmul(
                    lbank[0:1, j * 128:(j + 1) * 128], onesc[:], ptd[:],
                    start=False, stop=(j == 3)))(j, lbank),
                    reads=(onesc_b, ptd_b), writes=(lbank_b,))
            P.emit("vector", (lambda lbank: lambda e: e.tensor_copy(
                lrow[:], lbank[0:1, 0:512]))(lbank),
                reads=(lbank_b,), writes=(lrow_b,))
            P.emit("vector", lambda e: e.reciprocal(linv[:], lrow[:]),
                   reads=(lrow_b,), writes=(linv_b,))
            bcb, bcb_b = L.ps[4], L.ps_buf[4]
            P.emit("tensor", (lambda bcb: lambda e: e.matmul(
                bcb[:], onesrf[:], linv[:], start=True, stop=True))(bcb),
                reads=(onesrf_b, linv_b), writes=(bcb_b,))
            P.emit("vector", (lambda bcb: lambda e: e.tensor_copy(
                linvbc[:], bcb[:]))(bcb),
                reads=(bcb_b,), writes=(linvbc_b,))
            P.emit("vector", (lambda h, g, obank: lambda e: e.tensor_tensor(
                otsb[:, (h * 4 + g) * 512:(h * 4 + g + 1) * 512], obank[:],
                linvbc[:], ALU.mult))(h, g, obank),
                reads=(obank_b, linvbc_b), writes=(otsb_b,))

    store_ops = []
    for qg in range(8):
        for oslab in range(4):
            for dc_ in range(16):
                wt, wt_b = wot[dc_ % 2]
                P.emit("gpsimd", (lambda dc_, oslab, wt: lambda e: e.dma_start(
                    out=wt[:], in_=woB[dc_, :, oslab * 512:(oslab + 1) * 512]))(dc_, oslab, wt),
                    writes=(wt_b,), dma=True)
                for j in range(2):
                    qt = qg * 2 + j
                    bank, bank_b = L.ps[j], L.ps_buf[j]
                    P.emit("tensor", (lambda dc_, qt, wt, bank: lambda e: e.matmul(
                        bank[:], otsb[:, dc_ * TOK + qt * 128:dc_ * TOK + (qt + 1) * 128],
                        wt[:], start=(dc_ == 0), stop=(dc_ == 15)))(dc_, qt, wt, bank),
                        reads=(otsb_b, wt_b), writes=(bank_b,))
            for j in range(2):
                bank, bank_b = L.ps[j], L.ps_buf[j]
                P.emit("vector", (lambda j, oslab, bank: lambda e: e.tensor_tensor(
                    ao4[:, j * D + oslab * 512:j * D + (oslab + 1) * 512], bank[:],
                    bobc[:, oslab * 512:(oslab + 1) * 512], ALU.add))(j, oslab, bank),
                    reads=(bank_b, bobc_b), writes=(ao4_b,))
        for j in range(2):
            qt = qg * 2 + j
            xi, xi_b = xq[j % 2]
            P.emit(x2eng, (lambda qt, xi: lambda e: x2load(e, qt, xi[:]))(qt, xi),
                   writes=(xi_b,), dma=True, deps=(x2dep,))
            ao_sl, ao_sl_b = ao4, ao4_b

            def _mk_ln(j):
                pass
            n = 4
            for i in range(n):
                P.emit("vector", (lambda i, j: lambda e: e.bn_stats(
                    stats[:, i * 6:(i + 1) * 6],
                    ao4[:, j * D + i * 512:j * D + (i + 1) * 512]))(i, j),
                    reads=(ao4_b,), writes=(stats_b,))
            P.emit("vector", lambda e: e.bn_aggr(stats[:, 24:26], stats[:, 0:24]),
                   reads=(stats_b,), writes=(stats_b,))
            P.emit("vector", lambda e: e.tensor_scalar(stats[:, 26:27], stats[:, 24:25],
                                                       -1.0, None, ALU.mult),
                   reads=(stats_b,), writes=(stats_b,))
            P.emit("scalar", lambda e: e.activation(stats[:, 27:28], stats[:, 25:26],
                                                    AF.Sqrt, bias=L.consts[:, 0:1]),
                   reads=(stats_b, L.consts_b), writes=(stats_b,))
            P.emit("vector", lambda e: e.reciprocal(stats[:, 28:29], stats[:, 27:28]),
                   reads=(stats_b,), writes=(stats_b,))
            P.emit("vector", (lambda j: lambda e: e.tensor_scalar(
                yt[:], ao4[:, j * D:(j + 1) * D], stats[:, 26:27], stats[:, 28:29],
                ALU.add, ALU.mult))(j),
                reads=(ao4_b, stats_b), writes=(yt_b,))
            P.emit("vector", (lambda xi: lambda e: e.tensor_tensor(
                yt[:], yt[:], xi[:], ALU.add))(xi),
                reads=(yt_b, xi_b), writes=(yt_b,))
            store_ops.append(P.emit("sync", (lambda qt: lambda e: e.dma_start(
                out=x3[qt * 128:(qt + 1) * 128, :], in_=yt[:]))(qt),
                reads=(yt_b,), dma=True))

    P.emit("sync", lambda e: e.nop(), deps=store_ops)
    if fin:
        P.finalize(nc, es)
    return nc


def host_prep_l1_attn(inputs, core):
    b, c = core // 4, core % 4
    Wo = inputs["L1_attn_Wo"]
    woB = np.ascontiguousarray(Wo.reshape(16, 128, D))
    boR = inputs["L1_attn_bo"].reshape(1, D)
    kill = np.zeros((64, 16), dtype=np.float32)
    for u in range(64):
        for s_ in range(16):
            if u <= 4 * s_ + c:
                kill[u, s_] = NEG
    killR = kill.reshape(1, 64 * 16).astype(bf16)
    idx = np.arange(128)
    triA = np.where(idx[None, :] <= idx[:, None], 0.0, NEG).astype(np.float32)
    triM = (idx[:, None] <= idx[None, :]).astype(bf16)
    return dict(woB=woB, bo_row=boR, kill_slots=killR, tri_add=triA, tri01T=triM,
                ident_f=np.eye(128, dtype=np.float32),
                ones_row=np.ones((1, 512), dtype=bf16),
                ones_col=np.ones((128, 1), dtype=bf16),
                ones_row_f=np.ones((1, 128), dtype=np.float32))



def build_l0_attn(nc, es, L=None, io=None):
    fin = L is None
    if L is None:
        L = LaunchCtx(nc, es)
    P = L.P

    xo = dram(io, nc, "x_own", [TOK, D], F32, "ExternalInput")
    xh = dram(io, nc, "x_halo", [1024, D], F32, "ExternalInput")
    wqB = dram(io, nc, "wqB", [16, 128, D], F32, "ExternalInput")
    wkB = dram(io, nc, "wkB", [16, 128, 512], F32, "ExternalInput")
    wvB = dram(io, nc, "wvB", [16, 128, 512], F32, "ExternalInput")
    woB = dram(io, nc, "woB", [16, 128, D], F32, "ExternalInput")
    gainR = dram(io, nc, "gain_row", [1, 16], F32, "ExternalInput")
    roc = dram(io, nc, "rope_own_c", [TOK, 64], F32, "ExternalInput")
    ros = dram(io, nc, "rope_own_s", [TOK, 64], F32, "ExternalInput")
    rhc = dram(io, nc, "rope_halo_c", [1024, 64], F32, "ExternalInput")
    rhs_ = dram(io, nc, "rope_halo_s", [1024, 64], F32, "ExternalInput")
    mskI = dram(io, nc, "maskadd", [2, 8, 5, 128, 128], F32, "ExternalInput")
    identi = dram(io, nc, "ident_bf", [128, 128], BF16, "ExternalInput")
    onesrfi = dram(io, nc, "ones_row_f", [1, 128], F32, "ExternalInput")
    x1 = dram(io, nc, "x1", [TOK, D], F32, "ExternalOutput")

    ident, ident_b = L.sb("identt", [128, 128], BF16)
    onesrf, onesrf_b = L.sb("onesrf", [1, 128], F32)
    gains, gains_b = L.sb("gains", [1, 16], F32)
    gsc, gsc_b = L.sb("gsc", [128, 16], F32)
    wks, wks_b = L.sb("wks", [128, 16 * 512], BF16)
    wvs, wvs_b = L.sb("wvs", [128, 16 * 512], BF16)
    wqsl, wqsl_b = L.sb("wqsl", [128, 16 * 512], BF16)
    ropeo, ropeo_b = L.sb("ropeo", [128, 16 * 2 * 64], F32)
    ropeh, ropeh_b = L.sb("ropeh", [128, 8 * 2 * 64], F32)
    xi, xi_b = L.sb("xi", [128, D], F32)
    t0bf, t0bf_b = L.sb("t0bf", [128, D], BF16)
    stats, stats_b = L.sb("stats", [128, 32], F32)
    u16, u16_b = L.sb("u16", [128, 4096], BF16)
    tT, tT_b = u16[:, 0:2048], u16_b
    t0T = [L.sb(f"t0T{i}", [128, 16 * 128], BF16) for i in range(8)]
    ktblk, ktblk_b = L.sb("ktblk", [128, 4 * 12 * 128], BF16)
    vblk, vblk_b = L.sb("vblk", [128, 12 * 512], BF16)
    kvtmp, kvtmp_b = L.sb("kvtmp", [128, 512], F32)
    kvsq, kvsq_b = L.sb("kvsq", [128, 512], F32)
    kvss, kvss_b = L.sb("kvss", [128, 16], F32)
    krope, krope_b = L.sb("krope", [128, 512], BF16)
    qpart, qpart_b = L.sb("qpart", [128, 512], F32)
    qro, qro_b = L.sb("qro", [128, 512], BF16)
    qtos, qtos_b = u16[:, 0:4096], u16_b
    msk, msk_b = L.sb("msk", [128, 5 * 128], F32)
    ssb, ssb_b = L.sb("ssb", [128, 640], F32)
    mcol, mcol_b = L.sb("mcol", [128, 1], F32)
    lcol, lcol_b = L.sb("lcol", [128, 1], F32)
    licol, licol_b = L.sb("licol", [128, 1], F32)
    psb_, psb_b_ = L.sb("psb_", [128, 640], BF16)
    ptsb, ptsb_b = L.sb("ptsb", [128, 640], BF16)
    oblk, oblk_b = L.sb("oblk", [128, 8 * D], BF16)
    ot2, ot2_b = u16, u16_b
    wots = [L.sb(f"wots{i}", [128, 512], BF16) for i in range(2)]
    ao2, ao2_b = L.sb("ao2", [128, 2 * D], F32)
    yt, yt_b = u16[:].bitcast(F32), u16_b

    P.emit("sync", lambda e: e.dma_start(out=ident[:], in_=identi[:]), writes=(ident_b,), dma=True)
    P.emit("sync", lambda e: e.dma_start(out=onesrf[:], in_=onesrfi[:]), writes=(onesrf_b,), dma=True)
    P.emit("sync", lambda e: e.dma_start(out=gains[:], in_=gainR[:]), writes=(gains_b,), dma=True)
    bank, bank_b = L.ps[4], L.ps_buf[4]
    P.emit("tensor", (lambda bank: lambda e: e.matmul(
        bank[:, 0:16], onesrf[:], gains[:], start=True, stop=True))(bank),
        reads=(onesrf_b, gains_b), writes=(bank_b,))
    P.emit("vector", (lambda bank: lambda e: e.tensor_scalar(
        gsc[:], bank[:, 0:16], float(1.0 / np.sqrt(128.0)), None, ALU.mult))(bank),
        reads=(bank_b,), writes=(gsc_b,))
    for dc_ in range(DC):
        P.emit("gpsimd", (lambda dc_: lambda e: e.dma_start(
            out=wks[:, dc_ * 512:(dc_ + 1) * 512], in_=wkB[dc_]))(dc_),
            writes=(wks_b,), dma=True)
        P.emit("gpsimd", (lambda dc_: lambda e: e.dma_start(
            out=wvs[:, dc_ * 512:(dc_ + 1) * 512], in_=wvB[dc_]))(dc_),
            writes=(wvs_b,), dma=True)
    for dst_t, dst_tb, srcs, nt_ in ((ropeo, ropeo_b, (roc, ros), 16),
                                     (ropeh, ropeh_b, (rhc, rhs_), 8)):
        for ci, srct in enumerate(srcs):
            P.emit("sync", (lambda dst_t, srct, ci, nt_: lambda e: e.dma_start(
                out=dst_t[:].rearrange("p (t two f) -> p t two f", t=nt_, two=2)[:, :, ci, :],
                in_=srct[:].rearrange("(t p) f -> p t f", p=128)))(dst_t, srct, ci, nt_),
                writes=(dst_tb,), dma=True)

    store_ops = []
    for blk in range(2):
        for m in range(12):
            if m < 4:
                src = xh[blk * 512 + m * 128: blk * 512 + (m + 1) * 128, :]
            else:
                src = xo[blk * 1024 + (m - 4) * 128: blk * 1024 + (m - 3) * 128, :]
            P.emit("sync", (lambda src: lambda e: e.dma_start(out=xi[:], in_=src))(src),
                   writes=(xi_b,), dma=True)
            emit_ln_tile(L, t0bf, t0bf_b, xi, xi_b, stats, stats_b)
            if m >= 4:
                dstT, dstT_b = t0T[m - 4]
            else:
                dstT, dstT_b = tT, tT_b
            emit_transposes(L, dstT, dstT_b, t0bf, t0bf_b, ident, ident_b,
                            [(dc_ * 128, dc_ * 128) for dc_ in range(DC)])
            bank, bank_b = L.ps[4], L.ps_buf[4]
            for dc_ in range(DC):
                P.emit("tensor", (lambda dc_, dstT, bank: lambda e: e.matmul(
                    bank[:], dstT[:, dc_ * 128:(dc_ + 1) * 128],
                    wks[:, dc_ * 512:(dc_ + 1) * 512],
                    start=(dc_ == 0), stop=(dc_ == DC - 1)))(dc_, dstT, bank),
                    reads=(dstT_b, wks_b), writes=(bank_b,))
            P.emit("vector", (lambda bank: lambda e: e.tensor_copy(
                kvtmp[:], bank[:]))(bank), reads=(bank_b,), writes=(kvtmp_b,))
            P.emit("scalar", lambda e: e.activation(kvsq[:], kvtmp[:], AF.Square),
                   reads=(kvtmp_b,), writes=(kvsq_b,))
            P.emit("vector", lambda e: e.tensor_reduce(
                kvss[:, 0:4], kvsq[:].rearrange("p (h d) -> p h d", h=4), AX.X, ALU.add),
                reads=(kvsq_b,), writes=(kvss_b,))
            P.emit("scalar", lambda e: e.activation(
                kvss[:, 4:8], kvss[:, 0:4], AF.Sqrt, bias=L.consts[:, 1:2],
                scale=float(1.0 / 128.0)),
                reads=(kvss_b, L.consts_b), writes=(kvss_b,))
            P.emit("vector", lambda e: e.reciprocal(kvss[:, 8:12], kvss[:, 4:8]),
                   reads=(kvss_b,), writes=(kvss_b,))
            P.emit("vector", lambda e: e.tensor_tensor(
                kvtmp[:].rearrange("p (h d) -> p h d", h=4), kvtmp[:].rearrange("p (h d) -> p h d", h=4),
                kvss[:, 8:12].unsqueeze(2).broadcast_to([128, 4, 128]), ALU.mult),
                reads=(kvtmp_b, kvss_b), writes=(kvtmp_b,))
            rop, rop_b = (ropeh, ropeh_b) if m < 4 else (ropeo, ropeo_b)
            ti = (blk * 4 + m) if m < 4 else (blk * 8 + (m - 4))
            cview = rop[:, (2 * ti) * 64:(2 * ti + 1) * 64].unsqueeze(1).broadcast_to([128, 4, 64])
            sview = rop[:, (2 * ti + 1) * 64:(2 * ti + 2) * 64].unsqueeze(1).broadcast_to([128, 4, 64])
            x1v = kvtmp[:].rearrange("p (h two d) -> p h two d", h=4, two=2)[:, :, 0, :]
            x2v = kvtmp[:].rearrange("p (h two d) -> p h two d", h=4, two=2)[:, :, 1, :]
            k1v = krope[:].rearrange("p (h two d) -> p h two d", h=4, two=2)[:, :, 0, :]
            k2v = krope[:].rearrange("p (h two d) -> p h two d", h=4, two=2)[:, :, 1, :]
            r1v = kvsq[:].rearrange("p (h d) -> p h d", h=4)[:, :, 0:64]
            r2v = kvsq[:].rearrange("p (h d) -> p h d", h=4)[:, :, 64:128]
            P.emit("vector", (lambda x1v, cview: lambda e: e.tensor_tensor(
                r1v, x1v, cview, ALU.mult))(x1v, cview),
                reads=(kvtmp_b, rop_b), writes=(kvsq_b,))
            P.emit("vector", (lambda x2v, sview: lambda e: e.tensor_tensor(
                r2v, x2v, sview, ALU.mult))(x2v, sview),
                reads=(kvtmp_b, rop_b), writes=(kvsq_b,))
            P.emit("vector", (lambda k1v: lambda e: e.tensor_tensor(
                k1v, r1v, r2v, ALU.add))(k1v),
                reads=(kvsq_b,), writes=(krope_b,))
            P.emit("vector", (lambda x2v, cview: lambda e: e.tensor_tensor(
                r1v, x2v, cview, ALU.mult))(x2v, cview),
                reads=(kvtmp_b, rop_b), writes=(kvsq_b,))
            P.emit("vector", (lambda x1v, sview: lambda e: e.tensor_tensor(
                r2v, x1v, sview, ALU.mult))(x1v, sview),
                reads=(kvtmp_b, rop_b), writes=(kvsq_b,))
            P.emit("vector", (lambda k2v: lambda e: e.tensor_tensor(
                k2v, r1v, r2v, ALU.subtract))(k2v),
                reads=(kvsq_b,), writes=(krope_b,))
            emit_transposes(L, ktblk, ktblk_b, krope, krope_b, ident, ident_b,
                            [(kv * 128, (kv * 12 + m) * 128) for kv in range(4)])
            bank, bank_b = L.ps[5], L.ps_buf[5]
            for dc_ in range(DC):
                P.emit("tensor", (lambda dc_, dstT, bank: lambda e: e.matmul(
                    bank[:], dstT[:, dc_ * 128:(dc_ + 1) * 128],
                    wvs[:, dc_ * 512:(dc_ + 1) * 512],
                    start=(dc_ == 0), stop=(dc_ == DC - 1)))(dc_, dstT, bank),
                    reads=(dstT_b, wvs_b), writes=(bank_b,))
            P.emit("vector", (lambda m, bank: lambda e: e.tensor_copy(
                vblk[:, m * 512:(m + 1) * 512], bank[:]))(m, bank),
                reads=(bank_b,), writes=(vblk_b,))

        for oslab in range(4):
            for dc_ in range(DC):
                P.emit("gpsimd", (lambda dc_, oslab, wqsl: lambda e: e.dma_start(
                    out=wqsl[:, dc_ * 512:(dc_ + 1) * 512],
                    in_=wqB[dc_, :, oslab * 512:(oslab + 1) * 512]))(dc_, oslab, wqsl),
                    writes=(wqsl_b,), dma=True)
            for qt in range(8):
                dstT, dstT_b = t0T[qt]
                bank, bank_b = L.ps[3], L.ps_buf[3]
                for dc_ in range(DC):
                    P.emit("tensor", (lambda dc_, dstT, bank, wqsl: lambda e: e.matmul(
                        bank[:], dstT[:, dc_ * 128:(dc_ + 1) * 128],
                        wqsl[:, dc_ * 512:(dc_ + 1) * 512],
                        start=(dc_ == 0), stop=(dc_ == DC - 1)))(dc_, dstT, bank, wqsl),
                        reads=(dstT_b, wqsl_b), writes=(bank_b,))
                P.emit("vector", (lambda bank: lambda e: e.tensor_copy(
                    qpart[:], bank[:]))(bank), reads=(bank_b,), writes=(qpart_b,))
                P.emit("scalar", lambda e: e.activation(kvsq[:], qpart[:], AF.Square),
                       reads=(qpart_b,), writes=(kvsq_b,))
                P.emit("vector", lambda e: e.tensor_reduce(
                    kvss[:, 0:4], kvsq[:].rearrange("p (h d) -> p h d", h=4), AX.X, ALU.add),
                    reads=(kvsq_b,), writes=(kvss_b,))
                P.emit("scalar", lambda e: e.activation(
                    kvss[:, 4:8], kvss[:, 0:4], AF.Sqrt, bias=L.consts[:, 1:2],
                    scale=float(1.0 / 128.0)),
                    reads=(kvss_b, L.consts_b), writes=(kvss_b,))
                P.emit("vector", lambda e: e.reciprocal(kvss[:, 8:12], kvss[:, 4:8]),
                       reads=(kvss_b,), writes=(kvss_b,))
                P.emit("vector", lambda e: e.tensor_tensor(
                    qpart[:].rearrange("p (h d) -> p h d", h=4), qpart[:].rearrange("p (h d) -> p h d", h=4),
                    kvss[:, 8:12].unsqueeze(2).broadcast_to([128, 4, 128]), ALU.mult),
                    reads=(qpart_b, kvss_b), writes=(qpart_b,))
                ti = blk * 8 + qt
                cview = ropeo[:, (2 * ti) * 64:(2 * ti + 1) * 64].unsqueeze(1).broadcast_to([128, 4, 64])
                sview = ropeo[:, (2 * ti + 1) * 64:(2 * ti + 2) * 64].unsqueeze(1).broadcast_to([128, 4, 64])
                x1v = qpart[:].rearrange("p (h two d) -> p h two d", h=4, two=2)[:, :, 0, :]
                x2v = qpart[:].rearrange("p (h two d) -> p h two d", h=4, two=2)[:, :, 1, :]
                q1v = qro[:].rearrange("p (h two d) -> p h two d", h=4, two=2)[:, :, 0, :]
                q2v = qro[:].rearrange("p (h two d) -> p h two d", h=4, two=2)[:, :, 1, :]
                r1v = kvsq[:].rearrange("p (h d) -> p h d", h=4)[:, :, 0:64]
                r2v = kvsq[:].rearrange("p (h d) -> p h d", h=4)[:, :, 64:128]
                P.emit("vector", (lambda x1v, cview: lambda e: e.tensor_tensor(
                    r1v, x1v, cview, ALU.mult))(x1v, cview),
                    reads=(qpart_b, ropeo_b), writes=(kvsq_b,))
                P.emit("vector", (lambda x2v, sview: lambda e: e.tensor_tensor(
                    r2v, x2v, sview, ALU.mult))(x2v, sview),
                    reads=(qpart_b, ropeo_b), writes=(kvsq_b,))
                P.emit("vector", (lambda q1v: lambda e: e.tensor_tensor(
                    q1v, r1v, r2v, ALU.add))(q1v),
                    reads=(kvsq_b,), writes=(qro_b,))
                P.emit("vector", (lambda x2v, cview: lambda e: e.tensor_tensor(
                    r1v, x2v, cview, ALU.mult))(x2v, cview),
                    reads=(qpart_b, ropeo_b), writes=(kvsq_b,))
                P.emit("vector", (lambda x1v, sview: lambda e: e.tensor_tensor(
                    r2v, x1v, sview, ALU.mult))(x1v, sview),
                    reads=(qpart_b, ropeo_b), writes=(kvsq_b,))
                P.emit("vector", (lambda q2v: lambda e: e.tensor_tensor(
                    q2v, r1v, r2v, ALU.subtract))(q2v),
                    reads=(kvsq_b,), writes=(qro_b,))
                emit_transposes(L, qtos, qtos_b, qro, qro_b, ident, ident_b,
                                [(hl * 128, (hl * 8 + qt) * 128) for hl in range(4)])
            for qt in range(8):
                P.emit("sync", (lambda blk, qt: lambda e: e.dma_start(
                    out=msk[:].rearrange("p (r k) -> p r k", r=5),
                    in_=mskI[blk, qt].rearrange("r p k -> p r k")))(blk, qt),
                    writes=(msk_b,), dma=True)
                for hl in range(4):
                    h = oslab * 4 + hl
                    kv = h // 4
                    b0, b0_b = L.ps[0], L.ps_buf[0]
                    b1, b1_b = L.ps[1], L.ps_buf[1]
                    P.emit("tensor", (lambda hl, qt, kv, b0: lambda e: e.matmul(
                        b0[:], qtos[:, (hl * 8 + qt) * 128:(hl * 8 + qt + 1) * 128],
                        ktblk[:, (kv * 12 + qt) * 128:(kv * 12 + qt) * 128 + 512],
                        start=True, stop=True))(hl, qt, kv, b0),
                        reads=(qtos_b, ktblk_b), writes=(b0_b,))
                    P.emit("tensor", (lambda hl, qt, kv, b1: lambda e: e.matmul(
                        b1[:, 0:128], qtos[:, (hl * 8 + qt) * 128:(hl * 8 + qt + 1) * 128],
                        ktblk[:, (kv * 12 + qt + 4) * 128:(kv * 12 + qt + 5) * 128],
                        start=True, stop=True))(hl, qt, kv, b1),
                        reads=(qtos_b, ktblk_b), writes=(b1_b,))
                    P.emit("vector", (lambda h, b0: lambda e: e.scalar_tensor_tensor(
                        ssb[:, 0:512], b0[:], gsc[:, h:h + 1], msk[:, 0:512],
                        ALU.mult, ALU.add))(h, b0),
                        reads=(b0_b, gsc_b, msk_b), writes=(ssb_b,))
                    P.emit("vector", (lambda h, b1: lambda e: e.scalar_tensor_tensor(
                        ssb[:, 512:640], b1[:, 0:128], gsc[:, h:h + 1], msk[:, 512:640],
                        ALU.mult, ALU.add))(h, b1),
                        reads=(b1_b, gsc_b, msk_b), writes=(ssb_b,))
                    P.emit("vector", lambda e: e.reduce_max(mcol[:], ssb[:], AX.X),
                           reads=(ssb_b,), writes=(mcol_b,))
                    P.emit("vector", lambda e: e.tensor_scalar(
                        mcol[:], mcol[:], -1.0, None, ALU.mult),
                        reads=(mcol_b,), writes=(mcol_b,))
                    P.emit("scalar", lambda e: e.activation(
                        psb_[:], ssb[:], AF.Exp, bias=mcol[:], accum_out=lcol[:]),
                        reads=(ssb_b, mcol_b), writes=(psb_b_, lcol_b))
                    P.emit("vector", lambda e: e.reciprocal(licol[:], lcol[:]),
                           reads=(lcol_b,), writes=(licol_b,))
                    emit_transposes(L, ptsb, ptsb_b, psb_, psb_b_, ident, ident_b,
                                    [(w * 128, w * 128) for w in range(5)])
                    ob, ob_b = L.ps[2], L.ps_buf[2]
                    for w in range(5):
                        P.emit("tensor", (lambda w, qt, kv, ob: lambda e: e.matmul(
                            ob[:, 0:128], ptsb[:, w * 128:(w + 1) * 128],
                            vblk[:, (qt + w) * 512 + kv * 128:(qt + w) * 512 + (kv + 1) * 128],
                            start=(w == 0), stop=(w == 4)))(w, qt, kv, ob),
                            reads=(ptsb_b, vblk_b), writes=(ob_b,))
                    P.emit("vector", (lambda qt, h, ob: lambda e: e.tensor_scalar(
                        oblk[:, qt * D + h * 128:qt * D + (h + 1) * 128], ob[:, 0:128],
                        licol[:], None, ALU.mult))(qt, h, ob),
                        reads=(ob_b, licol_b), writes=(oblk_b,))

        for qgrp in range(4):
            for j in range(2):
                qt = qgrp * 2 + j
                emit_transposes(L, ot2, ot2_b, oblk, oblk_b, ident, ident_b,
                                [(qt * D + dc_ * 128, (j * 16 + dc_) * 128) for dc_ in range(DC)])
            for oslab in range(4):
                for dc_ in range(DC):
                    wt, wt_b = wots[dc_ % 2]
                    P.emit("gpsimd", (lambda dc_, oslab, wt: lambda e: e.dma_start(
                        out=wt[:], in_=woB[dc_, :, oslab * 512:(oslab + 1) * 512]))(dc_, oslab, wt),
                        writes=(wt_b,), dma=True)
                    for j in range(2):
                        bank, bank_b = L.ps[j], L.ps_buf[j]
                        P.emit("tensor", (lambda j, dc_, wt, bank: lambda e: e.matmul(
                            bank[:], ot2[:, (j * 16 + dc_) * 128:(j * 16 + dc_ + 1) * 128],
                            wt[:], start=(dc_ == 0), stop=(dc_ == DC - 1)))(j, dc_, wt, bank),
                            reads=(ot2_b, wt_b), writes=(bank_b,))
                for j in range(2):
                    bank, bank_b = L.ps[j], L.ps_buf[j]
                    P.emit("vector", (lambda j, oslab, bank: lambda e: e.tensor_copy(
                        ao2[:, j * D + oslab * 512:j * D + (oslab + 1) * 512], bank[:]))(j, oslab, bank),
                        reads=(bank_b,), writes=(ao2_b,))
            for j in range(2):
                qt = qgrp * 2 + j
                row0 = blk * 1024 + qt * 128
                P.emit("sync", (lambda row0: lambda e: e.dma_start(
                    out=xi[:], in_=xo[row0:row0 + 128, :]))(row0),
                    writes=(xi_b,), dma=True)
                for i in range(4):
                    P.emit("vector", (lambda i, j: lambda e: e.bn_stats(
                        stats[:, i * 6:(i + 1) * 6],
                        ao2[:, j * D + i * 512:j * D + (i + 1) * 512]))(i, j),
                        reads=(ao2_b,), writes=(stats_b,))
                P.emit("vector", lambda e: e.bn_aggr(stats[:, 24:26], stats[:, 0:24]),
                       reads=(stats_b,), writes=(stats_b,))
                P.emit("vector", lambda e: e.tensor_scalar(stats[:, 26:27], stats[:, 24:25],
                                                           -1.0, None, ALU.mult),
                       reads=(stats_b,), writes=(stats_b,))
                P.emit("scalar", lambda e: e.activation(stats[:, 27:28], stats[:, 25:26],
                                                        AF.Sqrt, bias=L.consts[:, 0:1]),
                       reads=(stats_b, L.consts_b), writes=(stats_b,))
                P.emit("vector", lambda e: e.reciprocal(stats[:, 28:29], stats[:, 27:28]),
                       reads=(stats_b,), writes=(stats_b,))
                P.emit("vector", (lambda j: lambda e: e.tensor_scalar(
                    yt[:], ao2[:, j * D:(j + 1) * D], stats[:, 26:27], stats[:, 28:29],
                    ALU.add, ALU.mult))(j),
                    reads=(ao2_b, stats_b), writes=(yt_b,))
                P.emit("vector", lambda e: e.tensor_tensor(yt[:], yt[:], xi[:], ALU.add),
                       reads=(yt_b, xi_b), writes=(yt_b,))
                store_ops.append(P.emit("sync", (lambda row0: lambda e: e.dma_start(
                    out=x1[row0:row0 + 128, :], in_=yt[:]))(row0),
                    reads=(yt_b,), dma=True))

    P.emit("sync", lambda e: e.nop(), deps=store_ops)
    if fin:
        P.finalize(nc, es)
    return nc


def host_prep_l0_attn(inputs, core):
    b, c = core // 4, core % 4
    blocks = [c, 7 - c]
    x = inputs["L0_attn_x"][b]
    xo = np.concatenate([x[j * 1024:(j + 1) * 1024] for j in blocks], axis=0)
    halos = []
    for j in blocks:
        lo = j * 1024 - 512
        if lo < 0:
            halos.append(np.zeros((512, D), np.float32))
        else:
            halos.append(x[lo:lo + 512])
    xh = np.concatenate(halos, axis=0)
    wqB = np.ascontiguousarray(inputs["L0_attn_Wq"].T.reshape(16, 128, D))
    wkB = np.ascontiguousarray(inputs["L0_attn_Wk"].T.reshape(16, 128, 512))
    wvB = np.ascontiguousarray(inputs["L0_attn_Wv"].T.reshape(16, 128, 512))
    woB = np.ascontiguousarray(inputs["L0_attn_Wo"].T.reshape(16, 128, D))
    gain = inputs["L0_attn_q_gain"].reshape(1, 16)
    inv_freq = (np.float32(10000.0) ** (-np.arange(0, 64, dtype=np.float32) / np.float32(64))).astype(np.float32)
    own_rows = np.concatenate([np.arange(j * 1024, (j + 1) * 1024) for j in blocks])
    halo_rows = np.concatenate([np.arange(max(0, j * 1024 - 512), max(0, j * 1024 - 512) + 512)
                                if j * 1024 - 512 >= 0 else np.zeros(512, np.int64)
                                for j in blocks])
    fo = own_rows.astype(np.float32)[:, None] * inv_freq[None, :]
    fh = halo_rows.astype(np.float32)[:, None] * inv_freq[None, :]
    mask = np.empty((2, 8, 5, 128, 128), np.float32)
    i_ = np.arange(128)
    for bi, j in enumerate(blocks):
        bstart = j * 1024
        for qt in range(8):
            q0 = bstart + qt * 128
            for w in range(5):
                k0 = bstart - 512 + (qt + w) * 128
                kk = k0 + i_[None, :]
                qq = q0 + i_[:, None]
                ok = (kk <= qq) & (kk >= qq - (WIN - 1)) & (kk >= 0)
                mask[bi, qt, w] = np.where(ok, 0.0, NEG)
    return dict(x_own=xo, x_halo=xh, wqB=wqB, wkB=wkB, wvB=wvB, woB=woB,
                gain_row=gain,
                rope_own_c=np.cos(fo).astype(np.float32),
                rope_own_s=np.sin(fo).astype(np.float32),
                rope_halo_c=np.cos(fh).astype(np.float32),
                rope_halo_s=np.sin(fh).astype(np.float32),
                maskadd=mask,
                ident_bf=np.eye(128, dtype=bf16),
                ones_row_f=np.ones((1, 128), dtype=np.float32))



def build_l0_ffn(nc, es, L=None, io=None):
    fin = L is None
    if L is None:
        L = LaunchCtx(nc, es)
    P = L.P

    x1 = dram(io, nc, "x1", [TOK, D], F32, "ExternalInput")
    w1 = dram(io, nc, "w1", [128, 128, D], F32, "ExternalInput")
    wp = dram(io, nc, "wp", [16, 128, FF0], F32, "ExternalInput")
    identi = dram(io, nc, "ident_bf", [128, 128], BF16, "ExternalInput")
    x2 = dram(io, nc, "x2", [TOK, D], F32, "ExternalOutput")
    w1hat = dram(io, nc, "w1hat", [2 * FF0, D], BF16, "Internal")
    wphat = dram(io, nc, "wphat", [D, FF0], BF16, "Internal")
    hscr = dram(io, nc, "hscr", [TOK, FF0], BF16, "Internal")

    ident, ident_b = L.sb("identt", [128, 128], BF16)
    uq, uq_b = L.sb("uq", [128, 2 * FF0], BF16)
    wqb, wqb_b = uq[:, 0:FF0], uq_b
    q0, q0_b = uq[:, FF0:2 * FF0], uq_b
    asum, asum_b = L.sb("asum", [128, 128], F32)
    scl, scl_b = L.sb("scl", [128, 128], BF16)
    rec, rec_b = L.sb("rec", [128, 128], F32)
    xi, xi_b = L.sb("xi", [128, D], F32)
    t1f, t1f_b = L.sb("t1f", [128, D], F32)
    t1bf, t1bf_b = L.sb("t1bf", [128, D], BF16)
    stats, stats_b = L.sb("stats", [128, 32], F32)
    t1T, t1T_b = L.sb("t1T", [128, DC * TOK], BF16)
    gusl, gusl_b = L.sb("gusl", [128, 64 * 512], BF16)
    sg, sg_b = L.sb("sg", [128, 512], F32)
    hseg, hseg_b = L.sb("hseg", [128, 512], F32)
    hbf, hbf_b = L.sb("hbf", [128, 512], BF16)
    hsq, hsq_b = L.sb("hsq", [128, 512], F32)
    hcol, hcol_b = L.sb("hcol", [128, 1], F32)
    hss, hss_b = L.sb("hss", [128, 48], F32)
    wpt = [L.sb(f"wpt{i}", [128, 512], BF16) for i in range(2)]
    outt4, outt4_b = uq[:].bitcast(F32), uq_b
    yt, yt_b = L.sb("yt", [128, D], F32)

    P.emit("sync", lambda e: e.dma_start(out=ident[:], in_=identi[:]), writes=(ident_b,), dma=True)
    P.emit("vector", lambda e: e.memset(hss[:, 0:16], 0.0), writes=(hss_b,))

    store_ops = []

    def quant_tile(src_ap, dst_ap, width, ng, eng):
        P.emit("gpsimd", (lambda src_ap: lambda e: e.dma_start(
            out=wqb[:, 0:width], in_=src_ap))(src_ap),
            writes=(wqb_b,), dma=True)
        P.emit("vector", (lambda width, ng: lambda e: e.tensor_reduce(
            asum[:, 0:ng], wqb[:, 0:width].rearrange("p (g k) -> p g k", k=64),
            AX.X, ALU.add, apply_absolute_value=True))(width, ng),
            reads=(wqb_b,), writes=(asum_b,))
        P.emit(eng, (lambda ng: lambda e: e.tensor_scalar(
            scl[:, 0:ng], asum[:, 0:ng], float(1.0 / 64.0), 1e-8, ALU.mult, ALU.max))(ng),
            reads=(asum_b,), writes=(scl_b,))
        P.emit("vector", (lambda ng: lambda e: e.reciprocal(
            rec[:, 0:ng], scl[:, 0:ng]))(ng),
            reads=(scl_b,), writes=(rec_b,))
        P.emit(eng, (lambda width, ng: lambda e: e.tensor_tensor(
            q0[:, 0:width].rearrange("p (g k) -> p g k", k=64),
            wqb[:, 0:width].rearrange("p (g k) -> p g k", k=64),
            rec[:, 0:ng].unsqueeze(2).broadcast_to([128, ng, 64]), ALU.mult))(width, ng),
            reads=(wqb_b, rec_b), writes=(q0_b,))
        P.emit(eng, (lambda width: lambda e: e.tensor_scalar(
            q0[:, 0:width], q0[:, 0:width], MAGIC, MAGIC, ALU.add, ALU.subtract))(width),
            reads=(q0_b,), writes=(q0_b,))
        P.emit(eng, (lambda width: lambda e: e.tensor_scalar(
            q0[:, 0:width], q0[:, 0:width], 1.0, -1.0, ALU.min, ALU.max))(width),
            reads=(q0_b,), writes=(q0_b,))
        P.emit(eng, (lambda width, ng: lambda e: e.tensor_tensor(
            q0[:, 0:width].rearrange("p (g k) -> p g k", k=64),
            q0[:, 0:width].rearrange("p (g k) -> p g k", k=64),
            scl[:, 0:ng].unsqueeze(2).broadcast_to([128, ng, 64]), ALU.mult))(width, ng),
            reads=(q0_b, scl_b), writes=(q0_b,))
        return P.emit("sync", (lambda dst_ap, width: lambda e: e.dma_start(
            out=dst_ap, in_=q0[:, 0:width]))(dst_ap, width),
            reads=(q0_b,), dma=True)

    wh_ops = []
    for ht in range(128):
        wh_ops.append(quant_tile(w1[ht], w1hat[ht * 128:(ht + 1) * 128, :], D, 32,
                                 "gpsimd" if ht % 2 else "vector"))
    for pt in range(16):
        wh_ops.append(quant_tile(wp[pt], wphat[pt * 128:(pt + 1) * 128, :], FF0, 128,
                                 "gpsimd" if pt % 2 else "vector"))

    for qt in range(NT):
        P.emit("sync", (lambda qt: lambda e: e.dma_start(
            out=xi[:], in_=x1[qt * 128:(qt + 1) * 128, :]))(qt),
            writes=(xi_b,), dma=True)
        emit_ln_tile(L, t1f, t1f_b, xi, xi_b, stats, stats_b)
        P.emit("vector", lambda e: e.tensor_copy(t1bf[:], t1f[:]),
               reads=(t1f_b,), writes=(t1bf_b,))
        emit_transposes(L, t1T, t1T_b, t1bf, t1bf_b, ident, ident_b,
                        [(dc_ * 128, (dc_ * NT + qt) * 128) for dc_ in range(DC)])

    for p in range(16):
        for gi in range(2):
            hbase = gi * FF0 + p * 512
            for dc_ in range(DC):
                P.emit("scalar", (lambda gi, dc_, hbase: lambda e: e.dma_start_transpose(
                    gusl[:, (gi * 16 + dc_) * 512:(gi * 16 + dc_ + 1) * 512],
                    w1hat[hbase:hbase + 512, dc_ * 128:(dc_ + 1) * 128]))(gi, dc_, hbase),
                    reads=(), writes=(gusl_b,), dma=True, deps=wh_ops)
        for qt in range(NT):
            bg, bg_b = L.ps[0], L.ps_buf[0]
            bu, bu_b = L.ps[1], L.ps_buf[1]
            for dc_ in range(DC):
                P.emit("tensor", (lambda qt, dc_, bg: lambda e: e.matmul(
                    bg[:], t1T[:, (dc_ * NT + qt) * 128:(dc_ * NT + qt) * 128 + 128],
                    gusl[:, dc_ * 512:(dc_ + 1) * 512],
                    start=(dc_ == 0), stop=(dc_ == DC - 1)))(qt, dc_, bg),
                    reads=(t1T_b, gusl_b), writes=(bg_b,))
            for dc_ in range(DC):
                P.emit("tensor", (lambda qt, dc_, bu: lambda e: e.matmul(
                    bu[:], t1T[:, (dc_ * NT + qt) * 128:(dc_ * NT + qt) * 128 + 128],
                    gusl[:, (16 + dc_) * 512:(16 + dc_ + 1) * 512],
                    start=(dc_ == 0), stop=(dc_ == DC - 1)))(qt, dc_, bu),
                    reads=(t1T_b, gusl_b), writes=(bu_b,))
            P.emit("scalar", (lambda bg: lambda e: e.activation(
                sg[:], bg[:], AF.Silu))(bg),
                reads=(bg_b,), writes=(sg_b,))
            P.emit("vector", (lambda bu: lambda e: e.tensor_tensor(
                hseg[:], sg[:], bu[:], ALU.mult))(bu),
                reads=(sg_b, bu_b), writes=(hseg_b,))
            P.emit("scalar", lambda e: e.activation(hsq[:], hseg[:], AF.Square),
                   reads=(hseg_b,), writes=(hsq_b,))
            P.emit("vector", lambda e: e.tensor_reduce(hcol[:], hsq[:], AX.X, ALU.add),
                   reads=(hsq_b,), writes=(hcol_b,))
            P.emit("vector", (lambda qt: lambda e: e.tensor_tensor(
                hss[:, qt:qt + 1], hss[:, qt:qt + 1], hcol[:], ALU.add))(qt),
                reads=(hss_b, hcol_b), writes=(hss_b,))
            P.emit("vector", lambda e: e.tensor_copy(hbf[:], hseg[:]),
                   reads=(hseg_b,), writes=(hbf_b,))
            store_ops.append(P.emit("sync", (lambda qt, p: lambda e: e.dma_start(
                out=hscr[qt * 128:(qt + 1) * 128, p * 512:(p + 1) * 512],
                in_=hbf[:]))(qt, p),
                reads=(hbf_b,), dma=True))
    h_done = list(store_ops)
    store_ops = []

    P.emit("scalar", lambda e: e.activation(
        hss[:, 16:32], hss[:, 0:16], AF.Sqrt, bias=L.consts[:, 1:2],
        scale=float(1.0 / FF0)),
        reads=(hss_b, L.consts_b), writes=(hss_b,))
    P.emit("vector", lambda e: e.reciprocal(hss[:, 32:48], hss[:, 16:32]),
           reads=(hss_b,), writes=(hss_b,))

    for qg in range(4):
        for hc in range(64):
            P.emit("scalar", (lambda qg, hc: lambda e: e.dma_start_transpose(
                gusl[:, hc * 512:(hc + 1) * 512],
                hscr[qg * 512:(qg + 1) * 512, hc * 128:(hc + 1) * 128]))(qg, hc),
                writes=(gusl_b,), dma=True, deps=h_done)
        for ds_ in range(4):
            for hc in range(64):
                wt, wt_b = wpt[hc % 2]
                P.emit("scalar", (lambda ds_, hc, wt: lambda e: e.dma_start_transpose(
                    wt[:], wphat[ds_ * 512:(ds_ + 1) * 512, hc * 128:(hc + 1) * 128]))(ds_, hc, wt),
                    writes=(wt_b,), dma=True, deps=wh_ops)
                for j in range(4):
                    bank, bank_b = L.ps[j], L.ps_buf[j]
                    P.emit("tensor", (lambda hc, j, wt, bank: lambda e: e.matmul(
                        bank[:], gusl[:, hc * 512 + j * 128:hc * 512 + (j + 1) * 128],
                        wt[:], start=(hc == 0), stop=(hc == 63)))(hc, j, wt, bank),
                        reads=(gusl_b, wt_b), writes=(bank_b,))
            for j in range(4):
                qt = qg * 4 + j
                bank, bank_b = L.ps[j], L.ps_buf[j]
                P.emit("vector", (lambda qt, j, ds_, bank: lambda e: e.tensor_scalar(
                    outt4[:, j * D + ds_ * 512:j * D + (ds_ + 1) * 512], bank[:],
                    hss[:, 32 + qt:33 + qt], None, ALU.mult))(qt, j, ds_, bank),
                    reads=(bank_b, hss_b), writes=(outt4_b,))
        for j in range(4):
            qt = qg * 4 + j
            P.emit("sync", (lambda qt: lambda e: e.dma_start(
                out=xi[:], in_=x1[qt * 128:(qt + 1) * 128, :]))(qt),
                writes=(xi_b,), dma=True)
            for i2 in range(4):
                P.emit("vector", (lambda i2, j: lambda e: e.bn_stats(
                    stats[:, i2 * 6:(i2 + 1) * 6],
                    outt4[:, j * D + i2 * 512:j * D + (i2 + 1) * 512]))(i2, j),
                    reads=(outt4_b,), writes=(stats_b,))
            P.emit("vector", lambda e: e.bn_aggr(stats[:, 24:26], stats[:, 0:24]),
                   reads=(stats_b,), writes=(stats_b,))
            P.emit("vector", lambda e: e.tensor_scalar(stats[:, 26:27], stats[:, 24:25],
                                                       -1.0, None, ALU.mult),
                   reads=(stats_b,), writes=(stats_b,))
            P.emit("scalar", lambda e: e.activation(stats[:, 27:28], stats[:, 25:26],
                                                    AF.Sqrt, bias=L.consts[:, 0:1]),
                   reads=(stats_b, L.consts_b), writes=(stats_b,))
            P.emit("vector", lambda e: e.reciprocal(stats[:, 28:29], stats[:, 27:28]),
                   reads=(stats_b,), writes=(stats_b,))
            P.emit("vector", (lambda j: lambda e: e.tensor_scalar(
                yt[:], outt4[:, j * D:(j + 1) * D], stats[:, 26:27], stats[:, 28:29],
                ALU.add, ALU.mult))(j),
                reads=(outt4_b, stats_b), writes=(yt_b,))
            P.emit("vector", lambda e: e.tensor_tensor(yt[:], yt[:], xi[:], ALU.add),
                   reads=(yt_b, xi_b), writes=(yt_b,))
            store_ops.append(P.emit("sync", (lambda qt: lambda e: e.dma_start(
                out=x2[qt * 128:(qt + 1) * 128, :], in_=yt[:]))(qt),
                reads=(yt_b,), dma=True))

    P.emit("sync", lambda e: e.nop(), deps=store_ops)
    if fin:
        P.finalize(nc, es)
    return nc


def host_prep_l0_ffn(inputs):
    w1 = np.ascontiguousarray(inputs["L0_ffn_w_gate_up"].reshape(128, 128, D))
    wp = np.ascontiguousarray(inputs["L0_ffn_w_proj"].reshape(16, 128, FF0))
    return dict(w1=w1, wp=wp, ident_bf=np.eye(128, dtype=bf16))



_BUILT = {}


def _get_nc(name, builder):
    if name not in _BUILT:
        nc = bass.Bass("TRN2", target_bir_lowering=False, debug=False)
        es = ExitStack()
        es.__enter__()
        builder(nc, es)
        _BUILT[name] = (nc, es)
    return _BUILT[name][0]


def kernel_multi(**inputs):
    cores = list(range(8))

    nc = _get_nc("l0a", build_l0_attn)
    maps = [host_prep_l0_attn(inputs, c) for c in cores]
    r1 = run_bass_kernel_spmd(nc, maps, core_ids=cores).results
    x1s = [r["x1"] for r in r1]

    nc = _get_nc("l0f", build_l0_ffn)
    shared = host_prep_l0_ffn(inputs)
    maps = [dict(x1=x1s[c], **shared) for c in cores]
    r2 = run_bass_kernel_spmd(nc, maps, core_ids=cores).results
    x2s = [r["x2"] for r in r2]

    x2_full = np.empty((B, S, D), np.float32)
    for c in cores:
        b, rows = contiguous_rows(c)
        x2_full[b][rows] = x2s[c]
    x2i = []
    for c in cores:
        b, rows = interleaved_rows(c)
        x2i.append(np.ascontiguousarray(x2_full[b][rows]))

    nc = _get_nc("l1q", build_l1_qkv)
    maps = [dict(x2=x2i[c], **host_prep_l1_qkv(inputs, c)) for c in cores]
    r3 = run_bass_kernel_spmd(nc, maps, core_ids=cores).results
    qts = [r["qt_o"] for r in r3]
    kts = [r["kt_o"] for r in r3]
    vs = [r["v_o"] for r in r3]

    nc = _get_nc("l1a", build_l1_attn)
    maps = []
    for c in cores:
        b = c // 4
        ktg = np.concatenate([kts[4 * b + r] for r in range(4)], axis=0)
        vg = np.concatenate([vs[4 * b + r] for r in range(4)], axis=0)
        maps.append(dict(qto=qts[c], ktg=ktg, vg=vg, kto=kts[c], vo=vs[c],
                         x2=x2i[c], **host_prep_l1_attn(inputs, c)))
    r4 = run_bass_kernel_spmd(nc, maps, core_ids=cores).results
    x3s = [r["x3"] for r in r4]

    nc = _get_nc("l1f", build_l1_ffn)
    shared = host_prep_l1_ffn(inputs)
    maps = [dict(x3=x3s[c], **shared) for c in cores]
    r5 = run_bass_kernel_spmd(nc, maps, core_ids=cores).results

    out = np.empty((B, S, D), np.float32)
    for c in cores:
        b, rows = interleaved_rows(c)
        out[b][rows] = r5[c]["y"]
    return out



def _slot_src(s_):
    if s_ <= 7:
        r, pos = s_ // 2, s_ % 2
    else:
        if s_ % 2 == 1:
            r, pos = (15 - s_) // 2, 3
        else:
            r, pos = (14 - s_) // 2, 2
    return r * 512 + pos * 128


def build_fused(nc, es):
    L0 = LaunchCtx(nc, es)
    L0.arena = es.enter_context(nc.sbuf_tensor("arena", [128, 206 * 1024],
                                               mybir.dt.uint8))
    P = L0.P
    RG = [[0, 1, 2, 3], [4, 5, 6, 7]]
    E = "ExternalInput"

    def d(name, shape, dt_, kind="ExternalInput"):
        return nc.dram_tensor(name, shape, dt_, kind=kind).ap()

    ident_bf = d("ident_bf", [128, 128], BF16)
    ident_f = d("ident_f", [128, 128], F32)
    ones_row = d("ones_row", [1, 512], BF16)
    ones_col = d("ones_col", [128, 1], BF16)
    ones_row_f = d("ones_row_f", [1, 128], F32)

    io1 = dict(ident_bf=ident_bf, ones_row_f=ones_row_f,
               x_own=d("x_own", [TOK, D], F32), x_halo=d("x_halo", [1024, D], F32),
               wqB=d("wqB", [16, 128, D], F32), wkB=d("wkB", [16, 128, 512], F32),
               wvB=d("wvB", [16, 128, 512], F32), woB=d("woB", [16, 128, D], F32),
               gain_row=d("gain_row", [1, 16], F32),
               rope_own_c=d("rope_own_c", [TOK, 64], F32),
               rope_own_s=d("rope_own_s", [TOK, 64], F32),
               rope_halo_c=d("rope_halo_c", [1024, 64], F32),
               rope_halo_s=d("rope_halo_s", [1024, 64], F32),
               maskadd=d("maskadd", [2, 8, 5, 128, 128], F32))
    io2 = dict(ident_bf=ident_bf,
               w1=d("w1", [128, 128, D], F32), wp=d("wp", [16, 128, FF0], F32))
    io3 = dict(ident_bf=ident_bf, ones_row_f=ones_row_f,
               wqA=d("wqA", [16, 128, 16 * 128], F32),
               wkA=d("wkA", [16, 128, 16 * 128], F32),
               wvB=d("wvB1", [16, 128, D], F32),
               bqP=d("bqP", [128, 16], F32), bkP=d("bkP", [128, 16], F32),
               bv_row=d("bv_row", [1, D], F32),
               ropeq_c=d("ropeq_c", [64, TOK], F32),
               ropeq_s=d("ropeq_s", [64, TOK], F32),
               ropek_c=d("ropek_c", [64, TOK], F32),
               ropek_s=d("ropek_s", [64, TOK], F32))
    io4 = dict(ident_f=ident_f, ones_row=ones_row, ones_col=ones_col,
               ones_row_f=ones_row_f,
               woB=d("woB1", [16, 128, D], F32),
               bo_row=d("bo_row", [1, D], F32),
               kill_slots=d("kill_slots", [1, 64 * 16], BF16),
               tri_add=d("tri_add", [128, 128], F32),
               tri01T=d("tri01T", [128, 128], BF16))
    io5 = dict(ident_bf=ident_bf, ident_f=ident_f, ones_row=ones_row,
               ones_col=ones_col,
               wg=d("wg", [64, 128, 16 * 128], I32),
               wu=d("wu", [64, 128, 16 * 128], I32),
               wd=d("wd", [64, 128, D], I32),
               gsc=d("gsc", [128, 64], F32), usc=d("usc", [128, 64], F32),
               dsc_row=d("dsc_row", [1, D], F32))

    x1i = d("x1i", [TOK, D], F32, "Internal")
    x2in = d("cc_x2_in", [TOK, D], F32, "Internal")
    x2g = d("cc_x2_g", [4 * TOK, D], F32, "Internal")
    qtoi = d("qto_i", [D, TOK], BF16, "Internal")
    ktoi = d("kto_i", [D, TOK], BF16, "Internal")
    voi = d("vo_i", [TOK, D], BF16, "Internal")
    ktgi = d("ktg_i", [4 * D, TOK], BF16, "Internal")
    vgi = d("vg_i", [4 * TOK, D], BF16, "Internal")
    x3i = d("x3i", [TOK, D], F32, "Internal")
    yout = d("y", [TOK, D], F32, "ExternalOutput")

    x2idx = d("x2idx", [128, 16], I32)
    x2idxs = es.enter_context(nc.sbuf_tensor("x2idxs", [128, 16], I32))
    x2idxs_b = Buf()

    def x2_load(e, qt, dst):
        return e.indirect_dma_start(
            dst, None, x2g[:, :],
            bass.IndirectOffsetOnAxis(ap=x2idxs[:, qt:qt + 1], axis=0))

    io1["x1"] = x1i
    io2.update(x1=x1i, x2=x2in)
    io3.update(x2=x2g, qt_o=qtoi, kt_o=ktoi, v_o=voi)
    io4.update(qto=qtoi, ktg=ktgi, vg=vgi, kto=ktoi, vo=voi, x2=x2g, x3=x3i)
    io5.update(x3=x3i, y=yout)

    def stage(builder, io):
        Ls = LaunchCtx(nc, es, shared=L0)
        builder(nc, es, L=Ls, io=io)
        P.barrier()

    cdep = P.emit("sync", lambda e: e.dma_start(out=x2idxs[:], in_=x2idx[:]),
                  writes=(x2idxs_b,), dma=True)
    io3.update(x2_load=x2_load, x2_dep=cdep, x2_eng="gpsimd")
    io4.update(x2_load=x2_load, x2_dep=cdep, x2_eng="gpsimd")

    stage(build_l0_attn, io1)
    stage(build_l0_ffn, io2)
    P.emit("gpsimd", lambda e: e.collective_compute(
        "AllGather", ALU.bypass, RG,
        [x2in[:, :]], [x2g[:, :]]), dma=True, coll=True)
    P.barrier()
    stage(build_l1_qkv, io3)
    P.emit("gpsimd", lambda e: e.collective_compute(
        "AllGather", ALU.bypass, RG,
        [ktoi[:, :]], [ktgi[:, :]]), dma=True, coll=True)
    P.emit("gpsimd", lambda e: e.collective_compute(
        "AllGather", ALU.bypass, RG,
        [voi[:, :]], [vgi[:, :]]), dma=True, coll=True)
    P.barrier()
    stage(build_l1_attn, io4)
    stage(build_l1_ffn, io5)

    P.finalize(nc, es)
    return nc


def _a_const_host(s_):
    t0 = 63 - 4 * s_
    j = t0 // 8
    r = min(j, 7 - j)
    return r * 2048 + (0 if j == r else 1024) + (t0 - 8 * j) * 128


def host_prep_fused(inputs, core):
    c = core % 4
    idx = np.empty((128, 16), np.int32)
    for s_ in range(16):
        idx[:, s_] = _a_const_host(s_) - 128 * c + np.arange(128)
    m = {"x2idx": idx}
    m.update(host_prep_l0_attn(inputs, core))
    m.update(host_prep_l0_ffn(inputs))
    q = host_prep_l1_qkv(inputs, core)
    m["wvB1"] = q.pop("wvB")
    m.update(q)
    a = host_prep_l1_attn(inputs, core)
    m["woB1"] = a.pop("woB")
    m.update(a)
    m.update(host_prep_l1_ffn(inputs))
    return m


def kernel_fused(**inputs):
    cores = list(range(8))
    nc = _get_nc("fused", build_fused)
    maps = [host_prep_fused(inputs, c) for c in cores]
    res = run_bass_kernel_spmd(nc, maps, core_ids=cores).results
    out = np.empty((B, S, D), np.float32)
    for c in cores:
        b, rows = interleaved_rows(c)
        out[b][rows] = res[c]["y"]
    return out


def kernel(**inputs):
    return kernel_fused(**inputs)



def build_half_a(nc, es):
    L0 = LaunchCtx(nc, es)
    L0.arena = es.enter_context(nc.sbuf_tensor("arena", [128, 206 * 1024],
                                               mybir.dt.uint8))
    P = L0.P

    def d(name, shape, dt_, kind="ExternalInput"):
        return nc.dram_tensor(name, shape, dt_, kind=kind).ap()

    ident_bf = d("ident_bf", [128, 128], BF16)
    ones_row_f = d("ones_row_f", [1, 128], F32)
    io1 = dict(ident_bf=ident_bf, ones_row_f=ones_row_f,
               x_own=d("x_own", [TOK, D], F32), x_halo=d("x_halo", [1024, D], F32),
               wqB=d("wqB", [16, 128, D], F32), wkB=d("wkB", [16, 128, 512], F32),
               wvB=d("wvB", [16, 128, 512], F32), woB=d("woB", [16, 128, D], F32),
               gain_row=d("gain_row", [1, 16], F32),
               rope_own_c=d("rope_own_c", [TOK, 64], F32),
               rope_own_s=d("rope_own_s", [TOK, 64], F32),
               rope_halo_c=d("rope_halo_c", [1024, 64], F32),
               rope_halo_s=d("rope_halo_s", [1024, 64], F32),
               maskadd=d("maskadd", [2, 8, 5, 128, 128], F32))
    io2 = dict(ident_bf=ident_bf,
               w1=d("w1", [128, 128, D], F32), wp=d("wp", [16, 128, FF0], F32))
    io3 = dict(ident_bf=ident_bf, ones_row_f=ones_row_f,
               wqA=d("wqA", [16, 128, 16 * 128], F32),
               wkA=d("wkA", [16, 128, 16 * 128], F32),
               wvB=d("wvB1", [16, 128, D], F32),
               bqP=d("bqP", [128, 16], F32), bkP=d("bkP", [128, 16], F32),
               bv_row=d("bv_row", [1, D], F32),
               ropeq_c=d("ropeq_c", [64, TOK], F32),
               ropeq_s=d("ropeq_s", [64, TOK], F32),
               ropek_c=d("ropek_c", [64, TOK], F32),
               ropek_s=d("ropek_s", [64, TOK], F32))
    x1i = d("x1i", [TOK, D], F32, "Internal")
    x2o = d("x2", [TOK, D], F32, "ExternalOutput")
    io1["x1"] = x1i
    io2.update(x1=x1i, x2=x2o)
    io3.update(x2=x2o,
               qt_o=d("qt_o", [D, TOK], BF16, "ExternalOutput"),
               kt_o=d("kt_o", [D, TOK], BF16, "ExternalOutput"),
               v_o=d("v_o", [TOK, D], BF16, "ExternalOutput"))

    def stage(builder, io):
        Ls = LaunchCtx(nc, es, shared=L0)
        builder(nc, es, L=Ls, io=io)
        P.barrier()

    stage(build_l0_attn, io1)
    stage(build_l0_ffn, io2)
    stage(build_l1_qkv, io3)
    P.finalize(nc, es)
    return nc


def build_half_b(nc, es):
    L0 = LaunchCtx(nc, es)
    L0.arena = es.enter_context(nc.sbuf_tensor("arena", [128, 206 * 1024],
                                               mybir.dt.uint8))
    P = L0.P

    def d(name, shape, dt_, kind="ExternalInput"):
        return nc.dram_tensor(name, shape, dt_, kind=kind).ap()

    io4 = dict(ident_f=d("ident_f", [128, 128], F32),
               ones_row=d("ones_row", [1, 512], BF16),
               ones_col=d("ones_col", [128, 1], BF16),
               ones_row_f=d("ones_row_f", [1, 128], F32),
               woB=d("woB1", [16, 128, D], F32),
               bo_row=d("bo_row", [1, D], F32),
               kill_slots=d("kill_slots", [1, 64 * 16], BF16),
               tri_add=d("tri_add", [128, 128], F32),
               tri01T=d("tri01T", [128, 128], BF16),
               qto=d("qto", [D, TOK], BF16), ktg=d("ktg", [4 * D, TOK], BF16),
               vg=d("vg", [4 * TOK, D], BF16), kto=d("kto", [D, TOK], BF16),
               vo=d("vo", [TOK, D], BF16),
               x2=d("x2", [TOK, D], F32))
    io5 = dict(ident_bf=d("ident_bf", [128, 128], BF16),
               ident_f=io4["ident_f"], ones_row=io4["ones_row"],
               ones_col=io4["ones_col"],
               wg=d("wg", [64, 128, 16 * 128], I32),
               wu=d("wu", [64, 128, 16 * 128], I32),
               wd=d("wd", [64, 128, D], I32),
               gsc=d("gsc", [128, 64], F32), usc=d("usc", [128, 64], F32),
               dsc_row=d("dsc_row", [1, D], F32))
    x3i = d("x3i", [TOK, D], F32, "Internal")
    io4["x3"] = x3i
    io5.update(x3=x3i, y=d("y", [TOK, D], F32, "ExternalOutput"))

    def stage(builder, io):
        Ls = LaunchCtx(nc, es, shared=L0)
        builder(nc, es, L=Ls, io=io)
        P.barrier()

    stage(build_l1_attn, io4)
    stage(build_l1_ffn, io5)
    P.finalize(nc, es)
    return nc


def kernel_2launch(**inputs):
    cores = list(range(8))
    nc = _get_nc("halfa", build_half_a)
    maps = []
    for c in cores:
        m = {}
        m.update(host_prep_l0_attn(inputs, c))
        m.update(host_prep_l0_ffn(inputs))
        b, crows = contiguous_rows(c)
        q = host_prep_l1_qkv(inputs, c, rows=crows)
        m["wvB1"] = q.pop("wvB")
        m.update(q)
        maps.append(m)
    ra = run_bass_kernel_spmd(nc, maps, core_ids=cores).results

    x2f = np.empty((B, S, D), np.float32)
    qtf = np.empty((B, D, S), bf16)
    ktf = np.empty((B, D, S), bf16)
    vf = np.empty((B, S, D), bf16)
    for c in cores:
        b, crows = contiguous_rows(c)
        x2f[b][crows] = ra[c]["x2"]
        qtf[b][:, crows] = ra[c]["qt_o"]
        ktf[b][:, crows] = ra[c]["kt_o"]
        vf[b][crows] = ra[c]["v_o"]

    nc = _get_nc("halfb", build_half_b)
    maps = []
    for c in cores:
        b, irows = interleaved_rows(c)
        m = host_prep_l1_attn(inputs, c)
        m["woB1"] = m.pop("woB")
        f = host_prep_l1_ffn(inputs)
        m.update(f)
        m["qto"] = np.ascontiguousarray(qtf[b][:, irows])
        m["kto"] = np.ascontiguousarray(ktf[b][:, irows])
        m["vo"] = np.ascontiguousarray(vf[b][irows])
        ktg = np.empty((4 * D, TOK), bf16)
        vg = np.empty((4 * TOK, D), bf16)
        for r in range(4):
            _, rrows = interleaved_rows(4 * b + r)
            ktg[r * D:(r + 1) * D, :] = ktf[b][:, rrows]
            vg[r * TOK:(r + 1) * TOK, :] = vf[b][rrows]
        m["ktg"] = ktg
        m["vg"] = vg
        m["x2"] = np.ascontiguousarray(x2f[b][irows])
        maps.append(m)
    rb = run_bass_kernel_spmd(nc, maps, core_ids=cores).results
    out = np.empty((B, S, D), np.float32)
    for c in cores:
        b, irows = interleaved_rows(c)
        out[b][irows] = rb[c]["y"]
    return out


def kernel(**inputs):
    return kernel_2launch(**inputs)


# version 06
# speedup vs baseline: 1.1709x; 1.1709x over previous
import numpy as np
import ml_dtypes
import concourse.bass as bass
import concourse.mybir as mybir
from concourse.bass_utils import run_bass_kernel_spmd
from contextlib import ExitStack

F32 = mybir.dt.float32
BF16 = mybir.dt.bfloat16
I32 = mybir.dt.int32
AF = mybir.ActivationFunctionType
ALU = mybir.AluOpType
AX = mybir.AxisListType
bf16 = ml_dtypes.bfloat16

B, S, D = 2, 8192, 2048
TOK = 2048
NT = TOK // 128
DC = D // 128
HD = 128
WIN = 512
FF0 = 8192
EPS_RMS = 1.1920928955078125e-07
EPS_LN = 1e-5
MAGIC = float(np.float32(1.5 * 2 ** 23))
NEG = -1.0e30

ENGINES = ("sync", "gpsimd", "vector", "scalar", "tensor")


class Buf:
    __slots__ = ("w", "rs")

    def __init__(self):
        self.w = None
        self.rs = []


class Prog:

    def __init__(self):
        self.ops = []

    def emit(self, eng, fn, reads=(), writes=(), dma=False, deps=(), coll=False):
        i = len(self.ops)
        dep = set(d for d in deps if d is not None)
        for b in reads:
            if b.w is not None:
                dep.add(b.w)
        for b in writes:
            if b.w is not None:
                dep.add(b.w)
            dep.update(b.rs)
        dep.discard(i)
        self.ops.append({"eng": eng, "fn": fn, "deps": dep, "dma": dma, "i": i,
                         "coll": coll})
        for b in reads:
            b.rs.append(i)
        for b in writes:
            b.w = i
            b.rs = []
        return i

    N_DMA_SEMS = 12

    def barrier(self):
        deps = set()
        last = {}
        ndma = {}
        for op in reversed(self.ops):
            e = op["eng"]
            if op["dma"]:
                if ndma.get(e, 0) < self.N_DMA_SEMS:
                    deps.add(op["i"])
                    ndma[e] = ndma.get(e, 0) + 1
            elif e not in last:
                last[e] = op["i"]
                deps.add(op["i"])
        deps = sorted(deps)
        for e in ENGINES:
            self.emit(e, lambda eng: eng.nop(), deps=deps)

    def finalize(self, nc, es):
        by_eng = {e: [] for e in ENGINES}
        for op in self.ops:
            by_eng[op["eng"]].append(op)

        consumers = {}
        for op in self.ops:
            for d in op["deps"]:
                consumers.setdefault(d, []).append(op["i"])

        rr = {}
        last_on_sem = {}
        ncoll = [0]
        for op in self.ops:
            if op.get("coll"):
                op["skey"] = f"coll{ncoll[0]}"
                ncoll[0] += 1
            elif op["dma"]:
                j = rr.get(op["eng"], 0)
                rr[op["eng"]] = j + 1
                k = f"{op['eng']}_dma{j % self.N_DMA_SEMS}"
                prev = last_on_sem.get(k)
                if prev is not None:
                    extra = [c for c in consumers.get(prev, [prev])
                             if c < op["i"]]
                    if not extra:
                        extra = [prev]
                    op["deps"].update(extra)
                last_on_sem[k] = op["i"]
                op["skey"] = k
            else:
                op["skey"] = op["eng"]

        need_sig = set()
        for op in self.ops:
            need_sig.update(op["deps"])
        semval = {}
        for op in self.ops:
            if op["i"] in need_sig:
                op["inc"] = 16 if op["dma"] else 1
                k = op["skey"]
                semval[k] = semval.get(k, 0) + op["inc"]
                op["sig"] = semval[k]
            else:
                op["sig"] = None

        sem_keys = sorted({op["skey"] for op in self.ops if op["sig"] is not None})
        sems = {k: es.enter_context(nc.semaphore(f"s_{k}")) for k in sem_keys}

        for op in self.ops:
            waits = {}
            for d in op["deps"]:
                dop = self.ops[d]
                k, v = dop["skey"], dop["sig"]
                if waits.get(k, 0) < v:
                    waits[k] = v
            op["waits"] = waits

        block = es.enter_context(nc.Block())
        for eng_name in ENGINES:
            deco = getattr(block, eng_name)

            def _mk(eng_name):
                ops = by_eng[eng_name]

                def _body(eng):
                    water = {}
                    for op in ops:
                        for k, v in sorted(op["waits"].items()):
                            if water.get(k, 0) < v:
                                eng.wait_ge(sems[k], v)
                                water[k] = v
                        inst = op["fn"](eng)
                        if op["sig"] is not None:
                            inst.then_inc(sems[op["skey"]], op["inc"])
                return _body

            deco(_mk(eng_name))
        return sems


def dram(io, nc, name, shape, dt_, kind):
    if io is not None and name in io:
        return io[name]
    return nc.dram_tensor(name, shape, dt_, kind=kind).ap()


DT_SIZE = {F32: 4, BF16: 2, I32: 4}


class LaunchCtx:
    def __init__(self, nc, es, shared=None):
        self.nc = nc
        self.es = es
        if shared is None:
            self.P = Prog()
            self.arena = None
            self.ps = [es.enter_context(nc.psum_tensor(f"psf{i}", [128, 512], F32))
                       for i in range(6)]
            self.psb = [es.enter_context(nc.psum_tensor(f"psb{i}", [128, 1024], BF16))
                        for i in range(2)]
            self.ps_buf = [Buf() for _ in self.ps]
            self.psb_buf = [Buf() for _ in self.psb]
            self.consts = es.enter_context(nc.sbuf_tensor("consts", [128, 8], F32))
            self.consts_b = Buf()
            self.P.emit("vector", lambda e: e.memset(self.consts[:, 0:1], EPS_LN),
                        writes=(self.consts_b,))
            self.P.emit("vector", lambda e: e.memset(self.consts[:, 1:2], EPS_RMS),
                        writes=(self.consts_b,))
        else:
            self.P = shared.P
            self.arena = shared.arena
            self.arena_off = [0]
            self.ps = shared.ps
            self.psb = shared.psb
            self.ps_buf = shared.ps_buf
            self.psb_buf = shared.psb_buf
            self.consts = shared.consts
            self.consts_b = shared.consts_b

    def sb(self, name, shape, dtype):
        if self.arena is None:
            t = self.es.enter_context(self.nc.sbuf_tensor(name, shape, dtype))
            return t, Buf()
        nb = int(np.prod(shape[1:])) * DT_SIZE[dtype]
        off = (self.arena_off[0] + 63) & ~63
        self.arena_off[0] = off + nb
        assert self.arena_off[0] <= self.arena.shape[1], \
            f"arena overflow at {name}: {self.arena_off[0]}"
        ap = self.arena[:, off:off + nb].bitcast(dtype)
        if shape[0] != 128:
            ap = ap[0:shape[0], :]
        return ap, Buf()


def emit_ln_tile(L, dst, dst_b, src, src_b, stats, stats_b, width=2048,
                 eps_col=0, out_kind="normal", res=None, res_b=None):
    P = L.P
    n = width // 512
    for i in range(n):
        P.emit("vector", (lambda i: lambda e: e.bn_stats(
            stats[:, i * 6:(i + 1) * 6], src[:, i * 512:(i + 1) * 512]))(i),
            reads=(src_b,), writes=(stats_b,))
    P.emit("vector", lambda e: e.bn_aggr(stats[:, 24:26], stats[:, 0:6 * n]),
           reads=(stats_b,), writes=(stats_b,))
    P.emit("vector", lambda e: e.tensor_scalar(stats[:, 26:27], stats[:, 24:25],
                                               -1.0, None, ALU.mult),
           reads=(stats_b,), writes=(stats_b,))
    P.emit("scalar", lambda e: e.activation(stats[:, 27:28], stats[:, 25:26],
                                            AF.Sqrt, bias=L.consts[:, eps_col:eps_col + 1]),
           reads=(stats_b, L.consts_b), writes=(stats_b,))
    P.emit("vector", lambda e: e.reciprocal(stats[:, 28:29], stats[:, 27:28]),
           reads=(stats_b,), writes=(stats_b,))
    P.emit("vector", lambda e: e.tensor_scalar(
        dst[:], src[:], stats[:, 26:27], stats[:, 28:29], ALU.add, ALU.mult),
        reads=(src_b, stats_b), writes=(dst_b,))
    if out_kind == "resid":
        P.emit("vector", lambda e: e.tensor_tensor(dst[:], dst[:], res[:], ALU.add),
               reads=(dst_b, res_b), writes=(dst_b,))


def emit_transposes(L, dst, dst_b, src, src_b, ident, ident_b, chunk_map):
    P = L.P
    for base in range(0, len(chunk_map), 8):
        grp = chunk_map[base:base + 8]
        bi = (base // 8) % len(L.psb)
        bank, bank_b = L.psb[bi], L.psb_buf[bi]
        for k, (sc, _) in enumerate(grp):
            P.emit("tensor", (lambda k, sc, bank: lambda e: e.transpose(
                bank[:, k * 128:(k + 1) * 128], src[:, sc:sc + 128], ident[:]))(k, sc, bank),
                reads=(src_b, ident_b), writes=(bank_b,))
        for k, (_, dc_) in enumerate(grp):
            P.emit("vector", (lambda k, dc_, bank: lambda e: e.tensor_copy(
                dst[:, dc_:dc_ + 128], bank[:, k * 128:(k + 1) * 128]))(k, dc_, bank),
                reads=(bank_b,), writes=(dst_b,))



def build_l1_ffn(nc, es, L=None, io=None):
    fin = L is None
    if L is None:
        L = LaunchCtx(nc, es)
    P = L.P

    x3 = dram(io, nc, "x3", [TOK, D], F32, "ExternalInput")
    wg = dram(io, nc, "wg", [64, 128, 16 * 128], I32, "ExternalInput")
    wu = dram(io, nc, "wu", [64, 128, 16 * 128], I32, "ExternalInput")
    wd = dram(io, nc, "wd", [64, 128, D], I32, "ExternalInput")
    gsc = dram(io, nc, "gsc", [128, 64], F32, "ExternalInput")
    usc = dram(io, nc, "usc", [128, 64], F32, "ExternalInput")
    dscr = dram(io, nc, "dsc_row", [1, D], F32, "ExternalInput")
    identi = dram(io, nc, "ident_bf", [128, 128], BF16, "ExternalInput")
    identfi = dram(io, nc, "ident_f", [128, 128], F32, "ExternalInput")
    onesri = dram(io, nc, "ones_row", [1, 512], BF16, "ExternalInput")
    onesci = dram(io, nc, "ones_col", [128, 1], BF16, "ExternalInput")
    y = dram(io, nc, "y", [TOK, D], F32, "ExternalOutput")
    acc = dram(io, nc, "acc_scr", [TOK, D], F32, "Internal")

    ident, ident_b = L.sb("identt", [128, 128], BF16)
    identf, identf_b = L.sb("identtf", [128, 128], F32)
    onesr, onesr_b = L.sb("onesr", [1, 512], BF16)
    onesc, onesc_b = L.sb("onesc", [128, 1], BF16)
    onesrf, onesrf_b = L.sb("onesrf", [1, 128], F32)
    gscs, gscs_b = L.sb("gscs", [128, 64], F32)
    uscs, uscs_b = L.sb("uscs", [128, 64], F32)
    dsb, dsb_b = L.sb("ds_bc", [128, D], F32)
    dsrow, dsrow_b = L.sb("dsrow", [1, D], F32)
    xq = [L.sb(f"xq{i}", [128, D], F32) for i in range(2)]
    t3f, t3f_b = L.sb("t3f", [128, D], F32)
    t3bf, t3bf_b = L.sb("t3bf", [128, D], BF16)
    stats, stats_b = L.sb("stats", [128, 32], F32)
    t3T, t3T_b = L.sb("t3T", [128, DC * TOK], BF16)
    rsneg, rsneg_b = L.sb("rsneg", [1, 2 * TOK], BF16)
    rrow, rrow_b = L.sb("rrow", [1, 128], F32)
    wgu = [L.sb(f"wgu{i}", [128, 16 * 128], BF16) for i in range(4)]
    wdt = [L.sb(f"wdt{i}", [128, 512], BF16) for i in range(2)]
    hbuf, hbuf_b = L.sb("hbuf", [128, 8 * TOK], BF16)
    gtmp, gtmp_b = L.sb("gtmp", [128, 512], F32)
    utmp, utmp_b = L.sb("utmp", [128, 512], F32)
    sgt, sgt_b = L.sb("sgt", [128, 512], F32)
    hs, hs_b = L.sb("hs", [1, TOK], F32)
    hstmp, hstmp_b = L.sb("hstmp", [1, 512], F32)
    hcol, hcol_b = L.sb("hcol", [128, 1], F32)
    dtile, dtile_b = L.sb("dtile", [128, 512], F32)
    outt, outt_b = L.sb("outt", [128, D], F32)
    yt, yt_b = L.sb("yt", [128, D], F32)

    acc_bufs = [[Buf() for _ in range(4)] for _ in range(NT)]

    P.emit("sync", lambda e: e.dma_start(out=ident[:], in_=identi[:]), writes=(ident_b,), dma=True)
    P.emit("sync", lambda e: e.dma_start(out=identf[:], in_=identfi[:]), writes=(identf_b,), dma=True)
    P.emit("sync", lambda e: e.dma_start(out=onesr[:], in_=onesri[:]), writes=(onesr_b,), dma=True)
    P.emit("sync", lambda e: e.dma_start(out=onesc[:], in_=onesci[:]), writes=(onesc_b,), dma=True)
    P.emit("sync", lambda e: e.dma_start(out=gscs[:], in_=gsc[:]), writes=(gscs_b,), dma=True)
    P.emit("sync", lambda e: e.dma_start(out=uscs[:], in_=usc[:]), writes=(uscs_b,), dma=True)
    P.emit("sync", lambda e: e.dma_start(out=dsrow[:], in_=dscr[:]), writes=(dsrow_b,), dma=True)
    P.emit("vector", lambda e: e.memset(onesrf[:], 1.0), writes=(onesrf_b,))
    P.emit("vector", lambda e: e.memset(hs[:], 0.0), writes=(hs_b,))

    for i in range(4):
        bank, bank_b = L.ps[4], L.ps_buf[4]
        P.emit("tensor", (lambda i, bank: lambda e: e.matmul(
            bank[:], onesrf[:], dsrow[:, i * 512:(i + 1) * 512], start=True, stop=True))(i, bank),
            reads=(onesrf_b, dsrow_b), writes=(bank_b,))
        P.emit("vector", (lambda i, bank: lambda e: e.tensor_copy(
            dsb[:, i * 512:(i + 1) * 512], bank[:]))(i, bank),
            reads=(bank_b,), writes=(dsb_b,))

    for qt in range(NT):
        xi, xi_b = xq[qt % 2]
        P.emit("sync", (lambda qt, xi: lambda e: e.dma_start(
            out=xi[:], in_=x3[qt * 128:(qt + 1) * 128, :]))(qt, xi),
            writes=(xi_b,), dma=True)
        emit_ln_tile(L, t3f, t3f_b, xi, xi_b, stats, stats_b)
        P.emit("vector", lambda e: e.tensor_copy(t3bf[:], t3f[:]),
               reads=(t3f_b,), writes=(t3bf_b,))
        P.emit("vector", lambda e: e.tensor_reduce(
            stats[:, 29:30], t3bf[:], AX.X, ALU.add),
            reads=(t3bf_b,), writes=(stats_b,))
        bank, bank_b = L.ps[5], L.ps_buf[5]
        P.emit("tensor", (lambda bank: lambda e: e.matmul(
            bank[0:1, 0:128], stats[:, 29:30], identf[:], start=True, stop=True))(bank),
            reads=(stats_b, identf_b), writes=(bank_b,))
        P.emit("vector", (lambda bank: lambda e: e.tensor_scalar(
            rrow[:], bank[0:1, 0:128], -128.0, None, ALU.mult))(bank),
            reads=(bank_b,), writes=(rrow_b,))
        P.emit("vector", (lambda qt: lambda e: e.tensor_copy(
            rsneg[0:1, qt * 128:(qt + 1) * 128], rrow[:]))(qt),
            reads=(rrow_b,), writes=(rsneg_b,))
        P.emit("vector", (lambda qt: lambda e: e.tensor_tensor(
            rsneg[0:1, TOK + qt * 128:TOK + (qt + 1) * 128], rrow[:],
            rsneg[0:1, qt * 128:(qt + 1) * 128], ALU.subtract))(qt),
            reads=(rrow_b, rsneg_b), writes=(rsneg_b,))
        emit_transposes(L, t3T, t3T_b, t3bf, t3bf_b, ident, ident_b,
                        [(dc_ * 128, (dc_ * NT + qt) * 128) for dc_ in range(DC)])

    for bb in range(8):
        for oc in range(8):
            ocg = bb * 8 + oc
            specs = ((wg, gscs, gscs_b, gtmp, gtmp_b), (wu, uscs, uscs_b, utmp, utmp_b))
            for gi, (w_dram, sc_s, sc_b, tmp, tmp_b) in enumerate(specs):
                wt, wt_b = wgu[(oc % 2) * 2 + gi]
                P.emit("gpsimd", (lambda ocg, w_dram, wt: lambda e: e.dma_start(
                    out=wt[:], in_=w_dram[ocg]))(ocg, w_dram, wt),
                    writes=(wt_b,), dma=True)
            for qg in range(4):
                for gi, (w_dram, sc_s, sc_b, tmp, tmp_b) in enumerate(specs):
                    wt, wt_b = wgu[(oc % 2) * 2 + gi]
                    bank, bank_b = L.ps[(oc % 2) * 2 + gi], L.ps_buf[(oc % 2) * 2 + gi]
                    P.emit("tensor", (lambda qg, bank: lambda e: e.matmul(
                        bank[:], onesr[0:1, 0:128],
                        rsneg[0:1, qg * 512:(qg + 1) * 512],
                        start=True, stop=False))(qg, bank),
                        reads=(onesr_b, rsneg_b), writes=(bank_b,))
                    P.emit("tensor", (lambda qg, bank: lambda e: e.matmul(
                        bank[:], onesr[0:1, 0:128],
                        rsneg[0:1, TOK + qg * 512:TOK + (qg + 1) * 512],
                        start=False, stop=False))(qg, bank),
                        reads=(onesr_b, rsneg_b), writes=(bank_b,))
                    for dc_ in range(DC):
                        P.emit("tensor", (lambda qg, dc_, wt, bank: lambda e: e.matmul(
                            bank[:], wt[:, dc_ * 128:(dc_ + 1) * 128],
                            t3T[:, (dc_ * 4 + qg) * 512:(dc_ * 4 + qg + 1) * 512],
                            start=False, stop=(dc_ == DC - 1)))(qg, dc_, wt, bank),
                            reads=(wt_b, t3T_b), writes=(bank_b,))
                    P.emit("vector", (lambda ocg, bank, tmp, sc_s, sc_b: lambda e: e.tensor_scalar(
                        tmp[:], bank[:], sc_s[:, ocg:ocg + 1], None, ALU.mult))(ocg, bank, tmp, sc_s, sc_b),
                        reads=(bank_b, sc_b), writes=(tmp_b,))
                P.emit("scalar", lambda e: e.activation(sgt[:], gtmp[:], AF.Silu),
                       reads=(gtmp_b,), writes=(sgt_b,))
                P.emit("vector", (lambda oc, qg: lambda e: e.tensor_tensor(
                    hbuf[:, (oc * 4 + qg) * 512:(oc * 4 + qg + 1) * 512],
                    sgt[:], utmp[:], ALU.mult))(oc, qg),
                    reads=(sgt_b, utmp_b), writes=(hbuf_b,))
            bank, bank_b = L.ps[4], L.ps_buf[4]
            for qg in range(4):
                P.emit("tensor", (lambda oc, qg, bank: lambda e: e.matmul(
                    bank[0:1, 0:512], onesc[:],
                    hbuf[:, (oc * 4 + qg) * 512:(oc * 4 + qg + 1) * 512],
                    start=True, stop=True))(oc, qg, bank),
                    reads=(onesc_b, hbuf_b), writes=(bank_b,))
                P.emit("vector", (lambda qg, bank: lambda e: e.tensor_copy(
                    hstmp[:], bank[0:1, 0:512]))(qg, bank),
                    reads=(bank_b,), writes=(hstmp_b,))
                P.emit("vector", (lambda qg: lambda e: e.tensor_tensor(
                    hs[0:1, qg * 512:(qg + 1) * 512],
                    hs[0:1, qg * 512:(qg + 1) * 512], hstmp[:], ALU.add))(qg),
                    reads=(hs_b, hstmp_b), writes=(hs_b,))
        for qgrp in range(4):
            for ds_ in range(4):
                for oc in range(8):
                    wt, wt_b = wdt[oc % 2]
                    ocg = bb * 8 + oc
                    P.emit("gpsimd", (lambda ocg, ds_, wt: lambda e: e.dma_start(
                        out=wt[:], in_=wd[ocg, :, ds_ * 512:(ds_ + 1) * 512]))(ocg, ds_, wt),
                        writes=(wt_b,), dma=True)
                    for jj in range(4):
                        qt = qgrp * 4 + jj
                        bank, bank_b = L.ps[jj], L.ps_buf[jj]
                        P.emit("tensor", (lambda oc, qt, jj, wt, bank: lambda e: e.matmul(
                            bank[:], hbuf[:, oc * TOK + qt * 128:oc * TOK + (qt + 1) * 128],
                            wt[:], start=(oc == 0), stop=(oc == 7)))(oc, qt, jj, wt, bank),
                            reads=(hbuf_b, wt_b), writes=(bank_b,))
                for jj in range(4):
                    qt = qgrp * 4 + jj
                    bank, bank_b = L.ps[jj], L.ps_buf[jj]
                    ab = acc_bufs[qt][ds_]
                    if bb == 0:
                        P.emit("vector", (lambda bank: lambda e: e.tensor_copy(
                            dtile[:], bank[:]))(bank),
                            reads=(bank_b,), writes=(dtile_b,))
                    else:
                        P.emit("sync", (lambda qt, ds_: lambda e: e.dma_start(
                            out=dtile[:], in_=acc[qt * 128:(qt + 1) * 128,
                                                  ds_ * 512:(ds_ + 1) * 512]))(qt, ds_),
                            reads=(ab,), writes=(dtile_b,), dma=True)
                        P.emit("vector", (lambda bank: lambda e: e.tensor_tensor(
                            dtile[:], dtile[:], bank[:], ALU.add))(bank),
                            reads=(dtile_b, bank_b), writes=(dtile_b,))
                    P.emit("sync", (lambda qt, ds_: lambda e: e.dma_start(
                        out=acc[qt * 128:(qt + 1) * 128, ds_ * 512:(ds_ + 1) * 512],
                        in_=dtile[:]))(qt, ds_),
                        reads=(dtile_b,), writes=(ab,), dma=True)

    store_ops = []
    for qt in range(NT):
        xi, xi_b = xq[qt % 2]
        P.emit("sync", (lambda qt, xi: lambda e: e.dma_start(
            out=xi[:], in_=x3[qt * 128:(qt + 1) * 128, :]))(qt, xi),
            writes=(xi_b,), dma=True)
        bank, bank_b = L.ps[5], L.ps_buf[5]
        P.emit("tensor", (lambda qt, bank: lambda e: e.matmul(
            bank[:, 0:128], hs[0:1, qt * 128:(qt + 1) * 128], onesrf[:],
            start=True, stop=True))(qt, bank),
            reads=(hs_b, onesrf_b), writes=(bank_b,))
        P.emit("vector", (lambda bank: lambda e: e.tensor_scalar(
            hcol[:], bank[:, 0:1], -128.0, None, ALU.mult))(bank),
            reads=(bank_b,), writes=(hcol_b,))
        for ds_ in range(4):
            ab = acc_bufs[qt][ds_]
            P.emit("sync", (lambda qt, ds_: lambda e: e.dma_start(
                out=dtile[:], in_=acc[qt * 128:(qt + 1) * 128,
                                      ds_ * 512:(ds_ + 1) * 512]))(qt, ds_),
                reads=(ab,), writes=(dtile_b,), dma=True)
            P.emit("vector", (lambda ds_: lambda e: e.tensor_scalar(
                outt[:, ds_ * 512:(ds_ + 1) * 512], dtile[:], hcol[:], None,
                ALU.add))(ds_),
                reads=(dtile_b, hcol_b), writes=(outt_b,))
        P.emit("vector", lambda e: e.tensor_tensor(outt[:], outt[:], dsb[:], ALU.mult),
               reads=(outt_b, dsb_b), writes=(outt_b,))
        emit_ln_tile(L, yt, yt_b, outt, outt_b, stats, stats_b,
                     out_kind="resid", res=xi, res_b=xi_b)
        store_ops.append(P.emit("sync", (lambda qt: lambda e: e.dma_start(
            out=y[qt * 128:(qt + 1) * 128, :], in_=yt[:]))(qt),
            reads=(yt_b,), dma=True))

    P.emit("sync", lambda e: e.nop(), deps=store_ops)
    if fin:
        P.finalize(nc, es)
    return nc


def _const_inputs():
    return dict(ident_bf=np.eye(128, dtype=bf16),
                ident_f=np.eye(128, dtype=np.float32),
                ones_row=np.ones((1, 512), dtype=bf16),
                ones_col=np.ones((128, 1), dtype=bf16))


def host_prep_l1_ffn(inputs):
    gw = inputs["L1_ffn_gate_w"]
    uw = inputs["L1_ffn_up_w"]
    dw = inputs["L1_ffn_down_w"]
    def _gu(w):
        wr = w.reshape(64, 128, 16, 128)
        return np.ascontiguousarray(wr.transpose(0, 3, 2, 1)).reshape(64, 128, 16 * 128)
    wgA = _gu(gw)
    wuA = _gu(uw)
    wdA = np.ascontiguousarray(dw.T).reshape(64, 128, D)
    gsc = np.ascontiguousarray(inputs["L1_ffn_gate_scale"].reshape(64, 128).T)
    usc = np.ascontiguousarray(inputs["L1_ffn_up_scale"].reshape(64, 128).T)
    dsc = np.ascontiguousarray(inputs["L1_ffn_down_scale"].reshape(1, D))
    d = dict(wg=wgA, wu=wuA, wd=wdA, gsc=gsc, usc=usc, dsc_row=dsc)
    d.update(_const_inputs())
    return d


def contiguous_rows(core):
    b, c = core // 4, core % 4
    r1 = np.arange(c * 1024, (c + 1) * 1024)
    r2 = np.arange((7 - c) * 1024, (8 - c) * 1024)
    return b, np.concatenate([r1, r2])


def interleaved_rows(core):
    b, c = core // 4, core % 4
    tiles = [63 - 4 * s - c for s in range(16)]
    rows = np.concatenate([np.arange(t * 128, (t + 1) * 128) for t in tiles])
    return b, rows


def kernel(**inputs):
    raise NotImplementedError("under construction")



def build_l1_qkv(nc, es, L=None, io=None):
    fin = L is None
    if L is None:
        L = LaunchCtx(nc, es)
    P = L.P

    x2 = dram(io, nc, "x2", [TOK, D], F32, "ExternalInput")
    wqA = dram(io, nc, "wqA", [16, 128, 16 * 128], F32, "ExternalInput")
    wkA = dram(io, nc, "wkA", [16, 128, 16 * 128], F32, "ExternalInput")
    wvB = dram(io, nc, "wvB", [16, 128, D], F32, "ExternalInput")
    bqP = dram(io, nc, "bqP", [128, 16], F32, "ExternalInput")
    bkP = dram(io, nc, "bkP", [128, 16], F32, "ExternalInput")
    bvR = dram(io, nc, "bv_row", [1, D], F32, "ExternalInput")
    rqc = dram(io, nc, "ropeq_c", [64, TOK], F32, "ExternalInput")
    rqs = dram(io, nc, "ropeq_s", [64, TOK], F32, "ExternalInput")
    rkc = dram(io, nc, "ropek_c", [64, TOK], F32, "ExternalInput")
    rks = dram(io, nc, "ropek_s", [64, TOK], F32, "ExternalInput")
    identi = dram(io, nc, "ident_bf", [128, 128], BF16, "ExternalInput")
    onesrfi = dram(io, nc, "ones_row_f", [1, 128], F32, "ExternalInput")
    qt_o = dram(io, nc, "qt_o", [D, TOK], BF16, "ExternalOutput")
    kt_o = dram(io, nc, "kt_o", [D, TOK], BF16, "ExternalOutput")
    v_o = dram(io, nc, "v_o", [TOK, D], BF16, "ExternalOutput")

    x2load = (io.get("x2_load") if io else None) or (
        lambda e, qt, dst: e.dma_start(out=dst, in_=x2[qt * 128:(qt + 1) * 128, :]))
    x2eng = (io.get("x2_eng") if io else None) or "sync"
    x2dep = io.get("x2_dep") if io else None
    ident, ident_b = L.sb("identt", [128, 128], BF16)
    onesrf, onesrf_b = L.sb("onesrf", [1, 128], F32)
    bvrow, bvrow_b = L.sb("bvrow", [1, D], F32)
    bvbc, bvbc_b = L.sb("bvbc", [128, D], F32)
    bqs, bqs_b = L.sb("bqs", [128, 16], F32)
    bks, bks_b = L.sb("bks", [128, 16], F32)
    rq = [L.sb(f"rq{i}", [128, TOK], F32) for i in range(2)]
    rk = [L.sb(f"rk{i}", [128, TOK], F32) for i in range(2)]
    xq = [L.sb(f"xq{i}", [128, D], F32) for i in range(2)]
    t2f, t2f_b = L.sb("t2f", [128, D], F32)
    t2bf, t2bf_b = L.sb("t2bf", [128, D], BF16)
    stats, stats_b = L.sb("stats", [128, 32], F32)
    t2T, t2T_b = L.sb("t2T", [128, DC * TOK], BF16)
    wqk = [L.sb(f"wqk{i}", [128, 16 * 128], BF16) for i in range(2)]
    wvt = [L.sb(f"wvt{i}", [128, 512], BF16) for i in range(2)]
    ptmp, ptmp_b = L.sb("ptmp", [128, 512], F32)
    r1, r1_b = L.sb("ropet", [64, 512], F32)
    r2, r2_b = L.sb("ropet2", [64, 512], F32)
    qko, qko_b = L.sb("qko", [128, 512], BF16)
    vsb, vsb_b = L.sb("vsb", [128, D], BF16)

    P.emit("sync", lambda e: e.dma_start(out=ident[:], in_=identi[:]), writes=(ident_b,), dma=True)
    P.emit("sync", lambda e: e.dma_start(out=onesrf[:], in_=onesrfi[:]), writes=(onesrf_b,), dma=True)
    P.emit("sync", lambda e: e.dma_start(out=bvrow[:], in_=bvR[:]), writes=(bvrow_b,), dma=True)
    P.emit("sync", lambda e: e.dma_start(out=bqs[:], in_=bqP[:]), writes=(bqs_b,), dma=True)
    P.emit("sync", lambda e: e.dma_start(out=bks[:], in_=bkP[:]), writes=(bks_b,), dma=True)
    for (tt_, ttb_), srct in ((rq[0], rqc), (rq[1], rqs), (rk[0], rkc), (rk[1], rks)):
        P.emit("sync", (lambda tt_, srct: lambda e: e.dma_start(
            out=tt_[0:64, :], in_=srct[:]))(tt_, srct), writes=(ttb_,), dma=True)
        P.emit("sync", (lambda tt_, srct: lambda e: e.dma_start(
            out=tt_[64:128, :], in_=srct[:]))(tt_, srct), writes=(ttb_,), dma=True)
    for i in range(4):
        bank, bank_b = L.ps[4], L.ps_buf[4]
        P.emit("tensor", (lambda i, bank: lambda e: e.matmul(
            bank[:], onesrf[:], bvrow[:, i * 512:(i + 1) * 512], start=True, stop=True))(i, bank),
            reads=(onesrf_b, bvrow_b), writes=(bank_b,))
        P.emit("vector", (lambda i, bank: lambda e: e.tensor_copy(
            bvbc[:, i * 512:(i + 1) * 512], bank[:]))(i, bank),
            reads=(bank_b,), writes=(bvbc_b,))

    for qt in range(NT):
        xi, xi_b = xq[qt % 2]
        P.emit(x2eng, (lambda qt, xi: lambda e: x2load(e, qt, xi[:]))(qt, xi),
               writes=(xi_b,), dma=True, deps=(x2dep,))
        emit_ln_tile(L, t2f, t2f_b, xi, xi_b, stats, stats_b)
        P.emit("vector", lambda e: e.tensor_copy(t2bf[:], t2f[:]),
               reads=(t2f_b,), writes=(t2bf_b,))
        emit_transposes(L, t2T, t2T_b, t2bf, t2bf_b, ident, ident_b,
                        [(dc_ * 128, (dc_ * NT + qt) * 128) for dc_ in range(DC)])

    store_ops = []
    for wi, (wA, bb, bb_b, rc, rs_, out_d) in enumerate(
            ((wqA, bqs, bqs_b, rq[0], rq[1], qt_o),
             (wkA, bks, bks_b, rk[0], rk[1], kt_o))):
        for oc in range(16):
            wt, wt_b = wqk[oc % 2]
            P.emit("gpsimd", (lambda oc, wA, wt: lambda e: e.dma_start(
                out=wt[:], in_=wA[oc]))(oc, wA, wt),
                writes=(wt_b,), dma=True)
            for qg in range(4):
                bank, bank_b = L.ps[wi], L.ps_buf[wi]
                for dc_ in range(DC):
                    P.emit("tensor", (lambda qg, dc_, wt, bank: lambda e: e.matmul(
                        bank[:], wt[:, dc_ * 128:(dc_ + 1) * 128],
                        t2T[:, (dc_ * 4 + qg) * 512:(dc_ * 4 + qg + 1) * 512],
                        start=(dc_ == 0), stop=(dc_ == DC - 1)))(qg, dc_, wt, bank),
                        reads=(wt_b, t2T_b), writes=(bank_b,))
                P.emit("vector", (lambda oc, bank, bb: lambda e: e.tensor_scalar(
                    ptmp[:], bank[:], bb[:, oc:oc + 1], None, ALU.add))(oc, bank, bb),
                    reads=(bank_b, bb_b), writes=(ptmp_b,))
                rcs, rcb = rc
                rss, rsb = rs_
                sl = slice(qg * 512, (qg + 1) * 512)
                P.emit("vector", (lambda sl, rcs: lambda e: e.tensor_tensor(
                    r1[:], ptmp[0:64, :], rcs[0:64, sl], ALU.mult))(sl, rcs),
                    reads=(ptmp_b, rcb), writes=(r1_b,))
                P.emit("vector", (lambda sl, rss: lambda e: e.tensor_tensor(
                    r2[:], ptmp[64:128, :], rss[64:128, sl], ALU.mult))(sl, rss),
                    reads=(ptmp_b, rsb), writes=(r2_b,))
                P.emit("vector", lambda e: e.tensor_tensor(
                    qko[0:64, :], r1[:], r2[:], ALU.subtract),
                    reads=(r1_b, r2_b), writes=(qko_b,))
                P.emit("vector", (lambda sl, rcs: lambda e: e.tensor_tensor(
                    r1[:], ptmp[64:128, :], rcs[64:128, sl], ALU.mult))(sl, rcs),
                    reads=(ptmp_b, rcb), writes=(r1_b,))
                P.emit("vector", (lambda sl, rss: lambda e: e.tensor_tensor(
                    r2[:], ptmp[0:64, :], rss[0:64, sl], ALU.mult))(sl, rss),
                    reads=(ptmp_b, rsb), writes=(r2_b,))
                P.emit("vector", lambda e: e.tensor_tensor(
                    qko[64:128, :], r1[:], r2[:], ALU.add),
                    reads=(r1_b, r2_b), writes=(qko_b,))
                store_ops.append(P.emit("sync", (lambda oc, qg, out_d: lambda e: e.dma_start(
                    out=out_d[oc * 128:(oc + 1) * 128, qg * 512:(qg + 1) * 512],
                    in_=qko[:]))(oc, qg, out_d),
                    reads=(qko_b,), dma=True))

    wvsl, wvsl_b = L.sb("wvsl", [128, 16 * 512], BF16)
    for oslab in range(4):
        for dc_ in range(DC):
            P.emit("gpsimd", (lambda oslab, dc_: lambda e: e.dma_start(
                out=wvsl[:, dc_ * 512:(dc_ + 1) * 512],
                in_=wvB[dc_, :, oslab * 512:(oslab + 1) * 512]))(oslab, dc_),
                writes=(wvsl_b,), dma=True)
        for qg in range(4):
            for dc_ in range(DC):
                for j in range(4):
                    qt = qg * 4 + j
                    bank, bank_b = L.ps[2 + j], L.ps_buf[2 + j]
                    P.emit("tensor", (lambda qt, dc_, j, bank: lambda e: e.matmul(
                        bank[:], t2T[:, (dc_ * NT + qt) * 128:(dc_ * NT + qt) * 128 + 128],
                        wvsl[:, dc_ * 512:(dc_ + 1) * 512],
                        start=(dc_ == 0), stop=(dc_ == DC - 1)))(qt, dc_, j, bank),
                        reads=(t2T_b, wvsl_b), writes=(bank_b,))
            for j in range(4):
                qt = qg * 4 + j
                bank, bank_b = L.ps[2 + j], L.ps_buf[2 + j]
                P.emit("vector", (lambda oslab, bank: lambda e: e.tensor_tensor(
                    vsb[:, oslab * 512:(oslab + 1) * 512], bank[:],
                    bvbc[:, oslab * 512:(oslab + 1) * 512], ALU.add))(oslab, bank),
                    reads=(bank_b, bvbc_b), writes=(vsb_b,))
                store_ops.append(P.emit("sync", (lambda qt, oslab: lambda e: e.dma_start(
                    out=v_o[qt * 128:(qt + 1) * 128, oslab * 512:(oslab + 1) * 512],
                    in_=vsb[:, oslab * 512:(oslab + 1) * 512]))(qt, oslab),
                    reads=(vsb_b,), dma=True))

    P.emit("sync", lambda e: e.nop(), deps=store_ops)
    if fin:
        P.finalize(nc, es)
    return nc


_ROPE_PERM = None


def _rope_perm():
    global _ROPE_PERM
    if _ROPE_PERM is None:
        p = []
        for h in range(16):
            p += [h * 128 + 2 * i for i in range(64)]
            p += [h * 128 + 2 * i + 1 for i in range(64)]
        _ROPE_PERM = np.array(p)
    return _ROPE_PERM


def host_prep_l1_qkv(inputs, core, rows=None):
    if rows is None:
        b, rows = interleaved_rows(core)
    perm = _rope_perm()
    Wq = inputs["L1_attn_Wq"][:, perm]
    Wk = inputs["L1_attn_Wk"][:, perm]
    Wv = inputs["L1_attn_Wv"]
    bq = inputs["L1_attn_bq"][perm]
    bk = inputs["L1_attn_bk"][perm]

    def _tileA(W):
        Wr = W.reshape(16, 128, 16, 128)
        return np.ascontiguousarray(Wr.transpose(2, 1, 0, 3)).reshape(16, 128, 16 * 128)

    wqA = _tileA(Wq)
    wkA = _tileA(Wk)
    wvB = np.ascontiguousarray(Wv.reshape(16, 128, D))
    bqP = np.ascontiguousarray(bq.reshape(16, 128).T)
    bkP = np.ascontiguousarray(bk.reshape(16, 128).T)
    bvR = inputs["L1_attn_bv"].reshape(1, D)
    pos = rows.astype(np.float32)
    i = np.arange(64, dtype=np.float32)
    theta = np.power(np.float32(10000.0), -2.0 * i / 128.0).astype(np.float32)
    ang = theta[:, None] * pos[None, :]
    c = np.cos(ang).astype(np.float32)
    s = np.sin(ang).astype(np.float32)
    sc = np.float32(1.0 / np.sqrt(128.0))
    return dict(wqA=wqA, wkA=wkA, wvB=wvB, bqP=bqP, bkP=bkP, bv_row=bvR,
                ropeq_c=c * sc, ropeq_s=s * sc, ropek_c=c, ropek_s=s,
                ident_bf=np.eye(128, dtype=bf16),
                ones_row_f=np.ones((1, 128), dtype=np.float32))



def build_l1_attn(nc, es, L=None, io=None):
    fin = L is None
    if L is None:
        L = LaunchCtx(nc, es)
    P = L.P

    qto = dram(io, nc, "qto", [D, TOK], BF16, "ExternalInput")
    ktg = dram(io, nc, "ktg", [4 * D, TOK], BF16, "ExternalInput")
    vg = dram(io, nc, "vg", [4 * TOK, D], BF16, "ExternalInput")
    kto = dram(io, nc, "kto", [D, TOK], BF16, "ExternalInput")
    vo = dram(io, nc, "vo", [TOK, D], BF16, "ExternalInput")
    x2 = dram(io, nc, "x2", [TOK, D], F32, "ExternalInput")
    woB = dram(io, nc, "woB", [16, 128, D], F32, "ExternalInput")
    boR = dram(io, nc, "bo_row", [1, D], F32, "ExternalInput")
    killR = dram(io, nc, "kill_slots", [1, 64 * 16], BF16, "ExternalInput")
    triA = dram(io, nc, "tri_add", [128, 128], F32, "ExternalInput")
    triM = dram(io, nc, "tri01T", [128, 128], BF16, "ExternalInput")
    identfi = dram(io, nc, "ident_f", [128, 128], F32, "ExternalInput")
    onesri = dram(io, nc, "ones_row", [1, 512], BF16, "ExternalInput")
    onesci = dram(io, nc, "ones_col", [128, 1], BF16, "ExternalInput")
    onesrfi = dram(io, nc, "ones_row_f", [1, 128], F32, "ExternalInput")
    x3 = dram(io, nc, "x3", [TOK, D], F32, "ExternalOutput")

    x2load = (io.get("x2_load") if io else None) or (
        lambda e, qt, dst: e.dma_start(out=dst, in_=x2[qt * 128:(qt + 1) * 128, :]))
    x2eng = (io.get("x2_eng") if io else None) or "sync"
    x2dep = io.get("x2_dep") if io else None
    ktg_v = ktg.rearrange("(r n) m -> r n m", r=4)
    vg_v = vg.rearrange("(r n) m -> r n m", r=4)

    identf, identf_b = L.sb("identtf", [128, 128], F32)
    onesr, onesr_b = L.sb("onesr", [1, 512], BF16)
    onesc, onesc_b = L.sb("onesc", [128, 1], BF16)
    onesrf, onesrf_b = L.sb("onesrf", [1, 128], F32)
    kills, kills_b = L.sb("kills", [1, 64 * 16], BF16)
    tria, tria_b = L.sb("tria", [128, 128], F32)
    trim, trim_b = L.sb("trim", [128, 128], BF16)
    borow, borow_b = L.sb("borow", [1, D], F32)
    bobc, bobc_b = L.sb("bobc", [128, D], F32)
    qh, qh_b = L.sb("qh", [128, TOK], BF16)
    ktoh, ktoh_b = L.sb("ktoh", [128, TOK], BF16)
    voh, voh_b = L.sb("voh", [128, TOK], BF16)
    _kth = L.sb("kth0", [128, 4 * TOK], BF16)
    kth2 = [_kth, _kth]
    _vh = L.sb("vh0", [128, 4 * TOK], BF16)
    vh2 = [_vh, _vh]
    msb, msb_b = L.sb("msb", [128, 16], F32)
    mtmp, mtmp_b = L.sb("mtmp", [128, 1], F32)
    mneg, mneg_b = L.sb("mneg", [1, TOK], BF16)
    mrow, mrow_b = L.sb("mrow", [1, 128], F32)
    krow, krow_b = L.sb("krow", [1, 512], BF16)
    sdt, sdt_b = L.sb("sdt", [128, 128], F32)
    pts2 = [L.sb(f"pts{i}", [128, 512], BF16) for i in range(2)]
    ptd, ptd_b = L.sb("ptd", [128, 128], BF16)
    lrow, lrow_b = L.sb("lrow", [1, 512], F32)
    linv, linv_b = L.sb("linv", [1, 512], F32)
    linvbc, linvbc_b = L.sb("linvbc", [128, 512], F32)
    otsb, otsb_b = L.sb("otsb", [128, 16 * TOK], BF16)
    wot = [L.sb(f"wot{i}", [128, 512], BF16) for i in range(2)]
    ao4, ao4_b = L.sb("ao4", [128, 4 * D], F32)
    xq = [L.sb("xq0", [128, D], F32)]
    xq = [xq[0], xq[0]]
    stats, stats_b = L.sb("stats", [128, 32], F32)
    yt, yt_b = L.sb("yt", [128, D], F32)

    for t, bb_, src in ((identf, identf_b, identfi), (onesr, onesr_b, onesri),
                        (onesc, onesc_b, onesci), (onesrf, onesrf_b, onesrfi),
                        (kills, kills_b, killR), (tria, tria_b, triA),
                        (trim, trim_b, triM), (borow, borow_b, boR)):
        P.emit("sync", (lambda t, src: lambda e: e.dma_start(out=t[:], in_=src[:]))(t, src),
               writes=(bb_,), dma=True)
    for i in range(4):
        bank, bank_b = L.ps[4], L.ps_buf[4]
        P.emit("tensor", (lambda i, bank: lambda e: e.matmul(
            bank[:], onesrf[:], borow[:, i * 512:(i + 1) * 512], start=True, stop=True))(i, bank),
            reads=(onesrf_b, borow_b), writes=(bank_b,))
        P.emit("vector", (lambda i, bank: lambda e: e.tensor_copy(
            bobc[:, i * 512:(i + 1) * 512], bank[:]))(i, bank),
            reads=(bank_b,), writes=(bobc_b,))

    for h in range(16):
        kth, kth_b = kth2[h % 2]
        vh, vh_b = vh2[h % 2]
        P.emit("sync", (lambda h: lambda e: e.dma_start(
            out=qh[:], in_=qto[h * 128:(h + 1) * 128, :]))(h),
            writes=(qh_b,), dma=True)
        P.emit("sync", (lambda h: lambda e: e.dma_start(
            out=ktoh[:], in_=kto[h * 128:(h + 1) * 128, :]))(h),
            writes=(ktoh_b,), dma=True)
        P.emit("sync", (lambda h: lambda e: e.dma_start(
            out=voh[:].rearrange("p (s d) -> p s d", s=16),
            in_=vo[:, h * 128:(h + 1) * 128]
            .rearrange("(s p) d -> p s d", p=128)))(h),
            writes=(voh_b,), dma=True)
        for ug in range(16):
            P.emit("sync", (lambda h, ug, kth: lambda e: e.dma_start(
                out=kth[:, ug * 512:(ug + 1) * 512].rearrange("n (r m) -> n r m", r=4),
                in_=ktg_v[:, h * 128:(h + 1) * 128, ug * 128:(ug + 1) * 128]
                .rearrange("r n m -> n r m")))(h, ug, kth),
                writes=(kth_b,), dma=True)
            P.emit("sync", (lambda h, ug, vh: lambda e: e.dma_start(
                out=vh[:, ug * 512:(ug + 1) * 512].rearrange("t (r d) -> t r d", r=4),
                in_=vg_v[:, ug * 128:(ug + 1) * 128, h * 128:(h + 1) * 128]
                .rearrange("r t d -> t r d")))(h, ug, vh),
                writes=(vh_b,), dma=True)

        for s_ in range(16):
            for ug in range(s_, 16):
                bank, bank_b = L.ps[ug % 2], L.ps_buf[ug % 2]
                P.emit("tensor", (lambda s_, ug, bank, kth: lambda e: e.matmul(
                    bank[:], qh[:, s_ * 128:(s_ + 1) * 128],
                    kth[:, ug * 512:(ug + 1) * 512], start=True, stop=True))(s_, ug, bank, kth),
                    reads=(qh_b, kth_b), writes=(bank_b,))
                if ug == s_:
                    P.emit("vector", (lambda s_, bank: lambda e: e.reduce_max(
                        msb[:, s_:s_ + 1], bank[:], AX.X))(s_, bank),
                        reads=(bank_b,), writes=(msb_b,))
                else:
                    P.emit("vector", (lambda bank: lambda e: e.reduce_max(
                        mtmp[:], bank[:], AX.X))(bank),
                        reads=(bank_b,), writes=(mtmp_b,))
                    P.emit("vector", (lambda s_: lambda e: e.tensor_max(
                        msb[:, s_:s_ + 1], msb[:, s_:s_ + 1], mtmp[:]))(s_),
                        reads=(msb_b, mtmp_b), writes=(msb_b,))
            bank, bank_b = L.ps[2], L.ps_buf[2]
            P.emit("tensor", (lambda s_, bank: lambda e: e.matmul(
                bank[:, 0:128], qh[:, s_ * 128:(s_ + 1) * 128],
                ktoh[:, s_ * 128:(s_ + 1) * 128], start=True, stop=True))(s_, bank),
                reads=(qh_b, ktoh_b), writes=(bank_b,))
            P.emit("vector", (lambda bank: lambda e: e.scalar_tensor_tensor(
                sdt[:], bank[:, 0:128], 1.0, tria[:], ALU.mult, ALU.add))(bank),
                reads=(bank_b, tria_b), writes=(sdt_b,))
            P.emit("vector", lambda e: e.reduce_max(mtmp[:], sdt[:], AX.X),
                   reads=(sdt_b,), writes=(mtmp_b,))
            P.emit("vector", (lambda s_: lambda e: e.tensor_max(
                msb[:, s_:s_ + 1], msb[:, s_:s_ + 1], mtmp[:]))(s_),
                reads=(msb_b, mtmp_b), writes=(msb_b,))
        for s_ in range(16):
            bank, bank_b = L.ps[5], L.ps_buf[5]
            P.emit("tensor", (lambda s_, bank: lambda e: e.matmul(
                bank[0:1, 0:128], msb[:, s_:s_ + 1], identf[:], start=True, stop=True))(s_, bank),
                reads=(msb_b, identf_b), writes=(bank_b,))
            P.emit("vector", (lambda s_, bank: lambda e: e.tensor_scalar(
                mneg[0:1, s_ * 128:(s_ + 1) * 128], bank[0:1, 0:128],
                -1.0, None, ALU.mult))(s_, bank),
                reads=(bank_b,), writes=(mneg_b,))

        for g in range(4):
            obank, obank_b = L.ps[2], L.ps_buf[2]
            lbank, lbank_b = L.ps[3], L.ps_buf[3]
            first = True
            for u in range(16 * g, 64):
                bank, bank_b = L.ps[u % 2], L.ps_buf[u % 2]
                if u <= 16 * g + 15:
                    P.emit("vector", (lambda u, g: lambda e: e.tensor_tensor(
                        krow[:].rearrange("a (j k) -> a j k", j=4),
                        mneg[0:1, g * 512:(g + 1) * 512].rearrange("a (j k) -> a j k", j=4),
                        kills[0:1, u * 16 + 4 * g:u * 16 + 4 * g + 4]
                        .unsqueeze(2).broadcast_to([1, 4, 128]),
                        ALU.add))(u, g),
                        reads=(mneg_b, kills_b), writes=(krow_b,))
                    rrow, rrow_b2 = krow, krow_b
                else:
                    rrow, rrow_b2 = mneg, mneg_b
                rsl = slice(0, 512) if rrow is krow else slice(g * 512, (g + 1) * 512)
                P.emit("tensor", (lambda bank, rrow, rsl: lambda e: e.matmul(
                    bank[:], onesr[0:1, 0:128], rrow[0:1, rsl],
                    start=True, stop=False))(bank, rrow, rsl),
                    reads=(onesr_b, rrow_b2), writes=(bank_b,))
                P.emit("tensor", (lambda u, g, bank, kth: lambda e: e.matmul(
                    bank[:], kth[:, u * 128:(u + 1) * 128],
                    qh[:, g * 512:(g + 1) * 512], start=False, stop=True))(u, g, bank, kth),
                    reads=(kth_b, qh_b), writes=(bank_b,))
                pts, pts_b = pts2[u % 2]
                P.emit("scalar", (lambda bank, pts: lambda e: e.activation(
                    pts[:], bank[:], AF.Exp))(bank, pts),
                    reads=(bank_b,), writes=(pts_b,))
                P.emit("tensor", (lambda u, first, obank, vh, pts: lambda e: e.matmul(
                    obank[:], vh[:, u * 128:(u + 1) * 128], pts[:],
                    start=first, stop=False))(u, first, obank, vh, pts),
                    reads=(vh_b, pts_b), writes=(obank_b,))
                P.emit("tensor", (lambda first, lbank, pts: lambda e: e.matmul(
                    lbank[0:1, 0:512], onesc[:], pts[:],
                    start=first, stop=False))(first, lbank, pts),
                    reads=(onesc_b, pts_b), writes=(lbank_b,))
                first = False
            for j in range(4):
                s_ = 4 * g + j
                bank, bank_b = L.ps[j % 2], L.ps_buf[j % 2]
                P.emit("tensor", (lambda s_, bank: lambda e: e.matmul(
                    bank[:, 0:128], onesr[0:1, 0:128],
                    mneg[0:1, s_ * 128:(s_ + 1) * 128], start=True, stop=False))(s_, bank),
                    reads=(onesr_b, mneg_b), writes=(bank_b,))
                P.emit("tensor", (lambda s_, bank: lambda e: e.matmul(
                    bank[:, 0:128], ktoh[:, s_ * 128:(s_ + 1) * 128],
                    qh[:, s_ * 128:(s_ + 1) * 128], start=False, stop=True))(s_, bank),
                    reads=(ktoh_b, qh_b), writes=(bank_b,))
                P.emit("scalar", (lambda bank: lambda e: e.activation(
                    ptd[:], bank[:, 0:128], AF.Exp))(bank),
                    reads=(bank_b,), writes=(ptd_b,))
                P.emit("vector", lambda e: e.tensor_tensor(
                    ptd[:], ptd[:], trim[:], ALU.mult),
                    reads=(ptd_b, trim_b), writes=(ptd_b,))
                P.emit("tensor", (lambda s_, j, obank: lambda e: e.matmul(
                    obank[:, j * 128:(j + 1) * 128], voh[:, s_ * 128:(s_ + 1) * 128],
                    ptd[:], start=False, stop=(j == 3)))(s_, j, obank),
                    reads=(voh_b, ptd_b), writes=(obank_b,))
                P.emit("tensor", (lambda j, lbank: lambda e: e.matmul(
                    lbank[0:1, j * 128:(j + 1) * 128], onesc[:], ptd[:],
                    start=False, stop=(j == 3)))(j, lbank),
                    reads=(onesc_b, ptd_b), writes=(lbank_b,))
            P.emit("vector", (lambda lbank: lambda e: e.tensor_copy(
                lrow[:], lbank[0:1, 0:512]))(lbank),
                reads=(lbank_b,), writes=(lrow_b,))
            P.emit("vector", lambda e: e.reciprocal(linv[:], lrow[:]),
                   reads=(lrow_b,), writes=(linv_b,))
            bcb, bcb_b = L.ps[4], L.ps_buf[4]
            P.emit("tensor", (lambda bcb: lambda e: e.matmul(
                bcb[:], onesrf[:], linv[:], start=True, stop=True))(bcb),
                reads=(onesrf_b, linv_b), writes=(bcb_b,))
            P.emit("vector", (lambda bcb: lambda e: e.tensor_copy(
                linvbc[:], bcb[:]))(bcb),
                reads=(bcb_b,), writes=(linvbc_b,))
            P.emit("vector", (lambda h, g, obank: lambda e: e.tensor_tensor(
                otsb[:, (h * 4 + g) * 512:(h * 4 + g + 1) * 512], obank[:],
                linvbc[:], ALU.mult))(h, g, obank),
                reads=(obank_b, linvbc_b), writes=(otsb_b,))

    store_ops = []
    for qg in range(4):
        for oslab in range(4):
            for dc_ in range(16):
                wt, wt_b = wot[dc_ % 2]
                P.emit("gpsimd", (lambda dc_, oslab, wt: lambda e: e.dma_start(
                    out=wt[:], in_=woB[dc_, :, oslab * 512:(oslab + 1) * 512]))(dc_, oslab, wt),
                    writes=(wt_b,), dma=True)
                for j in range(4):
                    qt = qg * 4 + j
                    bank, bank_b = L.ps[j], L.ps_buf[j]
                    P.emit("tensor", (lambda dc_, qt, wt, bank: lambda e: e.matmul(
                        bank[:], otsb[:, dc_ * TOK + qt * 128:dc_ * TOK + (qt + 1) * 128],
                        wt[:], start=(dc_ == 0), stop=(dc_ == 15)))(dc_, qt, wt, bank),
                        reads=(otsb_b, wt_b), writes=(bank_b,))
            for j in range(4):
                bank, bank_b = L.ps[j], L.ps_buf[j]
                P.emit("vector", (lambda j, oslab, bank: lambda e: e.tensor_tensor(
                    ao4[:, j * D + oslab * 512:j * D + (oslab + 1) * 512], bank[:],
                    bobc[:, oslab * 512:(oslab + 1) * 512], ALU.add))(j, oslab, bank),
                    reads=(bank_b, bobc_b), writes=(ao4_b,))
        for j in range(4):
            qt = qg * 4 + j
            xi, xi_b = xq[j % 2]
            P.emit(x2eng, (lambda qt, xi: lambda e: x2load(e, qt, xi[:]))(qt, xi),
                   writes=(xi_b,), dma=True, deps=(x2dep,))
            ao_sl, ao_sl_b = ao4, ao4_b

            def _mk_ln(j):
                pass
            n = 4
            for i in range(n):
                P.emit("vector", (lambda i, j: lambda e: e.bn_stats(
                    stats[:, i * 6:(i + 1) * 6],
                    ao4[:, j * D + i * 512:j * D + (i + 1) * 512]))(i, j),
                    reads=(ao4_b,), writes=(stats_b,))
            P.emit("vector", lambda e: e.bn_aggr(stats[:, 24:26], stats[:, 0:24]),
                   reads=(stats_b,), writes=(stats_b,))
            P.emit("vector", lambda e: e.tensor_scalar(stats[:, 26:27], stats[:, 24:25],
                                                       -1.0, None, ALU.mult),
                   reads=(stats_b,), writes=(stats_b,))
            P.emit("scalar", lambda e: e.activation(stats[:, 27:28], stats[:, 25:26],
                                                    AF.Sqrt, bias=L.consts[:, 0:1]),
                   reads=(stats_b, L.consts_b), writes=(stats_b,))
            P.emit("vector", lambda e: e.reciprocal(stats[:, 28:29], stats[:, 27:28]),
                   reads=(stats_b,), writes=(stats_b,))
            P.emit("vector", (lambda j: lambda e: e.tensor_scalar(
                yt[:], ao4[:, j * D:(j + 1) * D], stats[:, 26:27], stats[:, 28:29],
                ALU.add, ALU.mult))(j),
                reads=(ao4_b, stats_b), writes=(yt_b,))
            P.emit("vector", (lambda xi: lambda e: e.tensor_tensor(
                yt[:], yt[:], xi[:], ALU.add))(xi),
                reads=(yt_b, xi_b), writes=(yt_b,))
            store_ops.append(P.emit("sync", (lambda qt: lambda e: e.dma_start(
                out=x3[qt * 128:(qt + 1) * 128, :], in_=yt[:]))(qt),
                reads=(yt_b,), dma=True))

    P.emit("sync", lambda e: e.nop(), deps=store_ops)
    if fin:
        P.finalize(nc, es)
    return nc


def host_prep_l1_attn(inputs, core):
    b, c = core // 4, core % 4
    Wo = inputs["L1_attn_Wo"]
    woB = np.ascontiguousarray(Wo.reshape(16, 128, D))
    boR = inputs["L1_attn_bo"].reshape(1, D)
    kill = np.zeros((64, 16), dtype=np.float32)
    for u in range(64):
        for s_ in range(16):
            if u <= 4 * s_ + c:
                kill[u, s_] = NEG
    killR = kill.reshape(1, 64 * 16).astype(bf16)
    idx = np.arange(128)
    triA = np.where(idx[None, :] <= idx[:, None], 0.0, NEG).astype(np.float32)
    triM = (idx[:, None] <= idx[None, :]).astype(bf16)
    return dict(woB=woB, bo_row=boR, kill_slots=killR, tri_add=triA, tri01T=triM,
                ident_f=np.eye(128, dtype=np.float32),
                ones_row=np.ones((1, 512), dtype=bf16),
                ones_col=np.ones((128, 1), dtype=bf16),
                ones_row_f=np.ones((1, 128), dtype=np.float32))



def build_l0_attn(nc, es, L=None, io=None):
    fin = L is None
    if L is None:
        L = LaunchCtx(nc, es)
    P = L.P

    xo = dram(io, nc, "x_own", [TOK, D], F32, "ExternalInput")
    xh = dram(io, nc, "x_halo", [1024, D], F32, "ExternalInput")
    wqB = dram(io, nc, "wqB", [16, 128, D], F32, "ExternalInput")
    wkB = dram(io, nc, "wkB", [16, 128, 512], F32, "ExternalInput")
    wvB = dram(io, nc, "wvB", [16, 128, 512], F32, "ExternalInput")
    woB = dram(io, nc, "woB", [16, 128, D], F32, "ExternalInput")
    gainR = dram(io, nc, "gain_row", [1, 16], F32, "ExternalInput")
    roc = dram(io, nc, "rope_own_c", [TOK, 64], F32, "ExternalInput")
    ros = dram(io, nc, "rope_own_s", [TOK, 64], F32, "ExternalInput")
    rhc = dram(io, nc, "rope_halo_c", [1024, 64], F32, "ExternalInput")
    rhs_ = dram(io, nc, "rope_halo_s", [1024, 64], F32, "ExternalInput")
    mskI = dram(io, nc, "maskadd", [2, 8, 5, 128, 128], F32, "ExternalInput")
    identi = dram(io, nc, "ident_bf", [128, 128], BF16, "ExternalInput")
    onesrfi = dram(io, nc, "ones_row_f", [1, 128], F32, "ExternalInput")
    x1 = dram(io, nc, "x1", [TOK, D], F32, "ExternalOutput")

    ident, ident_b = L.sb("identt", [128, 128], BF16)
    onesrf, onesrf_b = L.sb("onesrf", [1, 128], F32)
    gains, gains_b = L.sb("gains", [1, 16], F32)
    gsc, gsc_b = L.sb("gsc", [128, 16], F32)
    wks, wks_b = L.sb("wks", [128, 16 * 512], BF16)
    wvs, wvs_b = L.sb("wvs", [128, 16 * 512], BF16)
    wqsl, wqsl_b = L.sb("wqsl", [128, 16 * 512], BF16)
    ropeo, ropeo_b = L.sb("ropeo", [128, 16 * 2 * 64], F32)
    ropeh, ropeh_b = L.sb("ropeh", [128, 8 * 2 * 64], F32)
    xi, xi_b = L.sb("xi", [128, D], F32)
    t0bf, t0bf_b = L.sb("t0bf", [128, D], BF16)
    stats, stats_b = L.sb("stats", [128, 32], F32)
    u16, u16_b = L.sb("u16", [128, 4096], BF16)
    tT, tT_b = u16[:, 0:2048], u16_b
    t0T = [L.sb(f"t0T{i}", [128, 16 * 128], BF16) for i in range(8)]
    ktblk, ktblk_b = L.sb("ktblk", [128, 4 * 12 * 128], BF16)
    vblk, vblk_b = L.sb("vblk", [128, 12 * 512], BF16)
    kvtmp, kvtmp_b = L.sb("kvtmp", [128, 512], F32)
    kvsq, kvsq_b = L.sb("kvsq", [128, 512], F32)
    kvss, kvss_b = L.sb("kvss", [128, 16], F32)
    krope, krope_b = L.sb("krope", [128, 512], BF16)
    qpart, qpart_b = L.sb("qpart", [128, 512], F32)
    qro, qro_b = L.sb("qro", [128, 512], BF16)
    qtos, qtos_b = u16[:, 0:4096], u16_b
    msk, msk_b = L.sb("msk", [128, 5 * 128], F32)
    ssb, ssb_b = L.sb("ssb", [128, 640], F32)
    mcol, mcol_b = L.sb("mcol", [128, 1], F32)
    lcol, lcol_b = L.sb("lcol", [128, 1], F32)
    licol, licol_b = L.sb("licol", [128, 1], F32)
    psb_, psb_b_ = L.sb("psb_", [128, 640], BF16)
    ptsb, ptsb_b = L.sb("ptsb", [128, 640], BF16)
    oblk, oblk_b = L.sb("oblk", [128, 8 * D], BF16)
    ot2, ot2_b = u16, u16_b
    wots = [L.sb(f"wots{i}", [128, 512], BF16) for i in range(2)]
    ao2, ao2_b = L.sb("ao2", [128, 2 * D], F32)
    yt, yt_b = u16[:].bitcast(F32), u16_b

    P.emit("sync", lambda e: e.dma_start(out=ident[:], in_=identi[:]), writes=(ident_b,), dma=True)
    P.emit("sync", lambda e: e.dma_start(out=onesrf[:], in_=onesrfi[:]), writes=(onesrf_b,), dma=True)
    P.emit("sync", lambda e: e.dma_start(out=gains[:], in_=gainR[:]), writes=(gains_b,), dma=True)
    bank, bank_b = L.ps[4], L.ps_buf[4]
    P.emit("tensor", (lambda bank: lambda e: e.matmul(
        bank[:, 0:16], onesrf[:], gains[:], start=True, stop=True))(bank),
        reads=(onesrf_b, gains_b), writes=(bank_b,))
    P.emit("vector", (lambda bank: lambda e: e.tensor_scalar(
        gsc[:], bank[:, 0:16], float(1.0 / np.sqrt(128.0)), None, ALU.mult))(bank),
        reads=(bank_b,), writes=(gsc_b,))
    for dc_ in range(DC):
        P.emit("gpsimd", (lambda dc_: lambda e: e.dma_start(
            out=wks[:, dc_ * 512:(dc_ + 1) * 512], in_=wkB[dc_]))(dc_),
            writes=(wks_b,), dma=True)
        P.emit("gpsimd", (lambda dc_: lambda e: e.dma_start(
            out=wvs[:, dc_ * 512:(dc_ + 1) * 512], in_=wvB[dc_]))(dc_),
            writes=(wvs_b,), dma=True)
    for dst_t, dst_tb, srcs, nt_ in ((ropeo, ropeo_b, (roc, ros), 16),
                                     (ropeh, ropeh_b, (rhc, rhs_), 8)):
        for ci, srct in enumerate(srcs):
            P.emit("sync", (lambda dst_t, srct, ci, nt_: lambda e: e.dma_start(
                out=dst_t[:].rearrange("p (t two f) -> p t two f", t=nt_, two=2)[:, :, ci, :],
                in_=srct[:].rearrange("(t p) f -> p t f", p=128)))(dst_t, srct, ci, nt_),
                writes=(dst_tb,), dma=True)

    store_ops = []
    for blk in range(2):
        for m in range(12):
            if m < 4:
                src = xh[blk * 512 + m * 128: blk * 512 + (m + 1) * 128, :]
            else:
                src = xo[blk * 1024 + (m - 4) * 128: blk * 1024 + (m - 3) * 128, :]
            P.emit("sync", (lambda src: lambda e: e.dma_start(out=xi[:], in_=src))(src),
                   writes=(xi_b,), dma=True)
            emit_ln_tile(L, t0bf, t0bf_b, xi, xi_b, stats, stats_b)
            if m >= 4:
                dstT, dstT_b = t0T[m - 4]
            else:
                dstT, dstT_b = tT, tT_b
            emit_transposes(L, dstT, dstT_b, t0bf, t0bf_b, ident, ident_b,
                            [(dc_ * 128, dc_ * 128) for dc_ in range(DC)])
            bank, bank_b = L.ps[4], L.ps_buf[4]
            for dc_ in range(DC):
                P.emit("tensor", (lambda dc_, dstT, bank: lambda e: e.matmul(
                    bank[:], dstT[:, dc_ * 128:(dc_ + 1) * 128],
                    wks[:, dc_ * 512:(dc_ + 1) * 512],
                    start=(dc_ == 0), stop=(dc_ == DC - 1)))(dc_, dstT, bank),
                    reads=(dstT_b, wks_b), writes=(bank_b,))
            P.emit("vector", (lambda bank: lambda e: e.tensor_copy(
                kvtmp[:], bank[:]))(bank), reads=(bank_b,), writes=(kvtmp_b,))
            P.emit("scalar", lambda e: e.activation(kvsq[:], kvtmp[:], AF.Square),
                   reads=(kvtmp_b,), writes=(kvsq_b,))
            P.emit("vector", lambda e: e.tensor_reduce(
                kvss[:, 0:4], kvsq[:].rearrange("p (h d) -> p h d", h=4), AX.X, ALU.add),
                reads=(kvsq_b,), writes=(kvss_b,))
            P.emit("scalar", lambda e: e.activation(
                kvss[:, 4:8], kvss[:, 0:4], AF.Sqrt, bias=L.consts[:, 1:2],
                scale=float(1.0 / 128.0)),
                reads=(kvss_b, L.consts_b), writes=(kvss_b,))
            P.emit("vector", lambda e: e.reciprocal(kvss[:, 8:12], kvss[:, 4:8]),
                   reads=(kvss_b,), writes=(kvss_b,))
            P.emit("vector", lambda e: e.tensor_tensor(
                kvtmp[:].rearrange("p (h d) -> p h d", h=4), kvtmp[:].rearrange("p (h d) -> p h d", h=4),
                kvss[:, 8:12].unsqueeze(2).broadcast_to([128, 4, 128]), ALU.mult),
                reads=(kvtmp_b, kvss_b), writes=(kvtmp_b,))
            rop, rop_b = (ropeh, ropeh_b) if m < 4 else (ropeo, ropeo_b)
            ti = (blk * 4 + m) if m < 4 else (blk * 8 + (m - 4))
            cview = rop[:, (2 * ti) * 64:(2 * ti + 1) * 64].unsqueeze(1).broadcast_to([128, 4, 64])
            sview = rop[:, (2 * ti + 1) * 64:(2 * ti + 2) * 64].unsqueeze(1).broadcast_to([128, 4, 64])
            x1v = kvtmp[:].rearrange("p (h two d) -> p h two d", h=4, two=2)[:, :, 0, :]
            x2v = kvtmp[:].rearrange("p (h two d) -> p h two d", h=4, two=2)[:, :, 1, :]
            k1v = krope[:].rearrange("p (h two d) -> p h two d", h=4, two=2)[:, :, 0, :]
            k2v = krope[:].rearrange("p (h two d) -> p h two d", h=4, two=2)[:, :, 1, :]
            r1v = kvsq[:].rearrange("p (h d) -> p h d", h=4)[:, :, 0:64]
            r2v = kvsq[:].rearrange("p (h d) -> p h d", h=4)[:, :, 64:128]
            P.emit("vector", (lambda x1v, cview: lambda e: e.tensor_tensor(
                r1v, x1v, cview, ALU.mult))(x1v, cview),
                reads=(kvtmp_b, rop_b), writes=(kvsq_b,))
            P.emit("vector", (lambda x2v, sview: lambda e: e.tensor_tensor(
                r2v, x2v, sview, ALU.mult))(x2v, sview),
                reads=(kvtmp_b, rop_b), writes=(kvsq_b,))
            P.emit("vector", (lambda k1v: lambda e: e.tensor_tensor(
                k1v, r1v, r2v, ALU.add))(k1v),
                reads=(kvsq_b,), writes=(krope_b,))
            P.emit("vector", (lambda x2v, cview: lambda e: e.tensor_tensor(
                r1v, x2v, cview, ALU.mult))(x2v, cview),
                reads=(kvtmp_b, rop_b), writes=(kvsq_b,))
            P.emit("vector", (lambda x1v, sview: lambda e: e.tensor_tensor(
                r2v, x1v, sview, ALU.mult))(x1v, sview),
                reads=(kvtmp_b, rop_b), writes=(kvsq_b,))
            P.emit("vector", (lambda k2v: lambda e: e.tensor_tensor(
                k2v, r1v, r2v, ALU.subtract))(k2v),
                reads=(kvsq_b,), writes=(krope_b,))
            emit_transposes(L, ktblk, ktblk_b, krope, krope_b, ident, ident_b,
                            [(kv * 128, (kv * 12 + m) * 128) for kv in range(4)])
            bank, bank_b = L.ps[5], L.ps_buf[5]
            for dc_ in range(DC):
                P.emit("tensor", (lambda dc_, dstT, bank: lambda e: e.matmul(
                    bank[:], dstT[:, dc_ * 128:(dc_ + 1) * 128],
                    wvs[:, dc_ * 512:(dc_ + 1) * 512],
                    start=(dc_ == 0), stop=(dc_ == DC - 1)))(dc_, dstT, bank),
                    reads=(dstT_b, wvs_b), writes=(bank_b,))
            P.emit("vector", (lambda m, bank: lambda e: e.tensor_copy(
                vblk[:, m * 512:(m + 1) * 512], bank[:]))(m, bank),
                reads=(bank_b,), writes=(vblk_b,))

        for oslab in range(4):
            for dc_ in range(DC):
                P.emit("gpsimd", (lambda dc_, oslab, wqsl: lambda e: e.dma_start(
                    out=wqsl[:, dc_ * 512:(dc_ + 1) * 512],
                    in_=wqB[dc_, :, oslab * 512:(oslab + 1) * 512]))(dc_, oslab, wqsl),
                    writes=(wqsl_b,), dma=True)
            for qt in range(8):
                dstT, dstT_b = t0T[qt]
                bank, bank_b = L.ps[3], L.ps_buf[3]
                for dc_ in range(DC):
                    P.emit("tensor", (lambda dc_, dstT, bank, wqsl: lambda e: e.matmul(
                        bank[:], dstT[:, dc_ * 128:(dc_ + 1) * 128],
                        wqsl[:, dc_ * 512:(dc_ + 1) * 512],
                        start=(dc_ == 0), stop=(dc_ == DC - 1)))(dc_, dstT, bank, wqsl),
                        reads=(dstT_b, wqsl_b), writes=(bank_b,))
                P.emit("vector", (lambda bank: lambda e: e.tensor_copy(
                    qpart[:], bank[:]))(bank), reads=(bank_b,), writes=(qpart_b,))
                P.emit("scalar", lambda e: e.activation(kvsq[:], qpart[:], AF.Square),
                       reads=(qpart_b,), writes=(kvsq_b,))
                P.emit("vector", lambda e: e.tensor_reduce(
                    kvss[:, 0:4], kvsq[:].rearrange("p (h d) -> p h d", h=4), AX.X, ALU.add),
                    reads=(kvsq_b,), writes=(kvss_b,))
                P.emit("scalar", lambda e: e.activation(
                    kvss[:, 4:8], kvss[:, 0:4], AF.Sqrt, bias=L.consts[:, 1:2],
                    scale=float(1.0 / 128.0)),
                    reads=(kvss_b, L.consts_b), writes=(kvss_b,))
                P.emit("vector", lambda e: e.reciprocal(kvss[:, 8:12], kvss[:, 4:8]),
                       reads=(kvss_b,), writes=(kvss_b,))
                P.emit("vector", lambda e: e.tensor_tensor(
                    qpart[:].rearrange("p (h d) -> p h d", h=4), qpart[:].rearrange("p (h d) -> p h d", h=4),
                    kvss[:, 8:12].unsqueeze(2).broadcast_to([128, 4, 128]), ALU.mult),
                    reads=(qpart_b, kvss_b), writes=(qpart_b,))
                ti = blk * 8 + qt
                cview = ropeo[:, (2 * ti) * 64:(2 * ti + 1) * 64].unsqueeze(1).broadcast_to([128, 4, 64])
                sview = ropeo[:, (2 * ti + 1) * 64:(2 * ti + 2) * 64].unsqueeze(1).broadcast_to([128, 4, 64])
                x1v = qpart[:].rearrange("p (h two d) -> p h two d", h=4, two=2)[:, :, 0, :]
                x2v = qpart[:].rearrange("p (h two d) -> p h two d", h=4, two=2)[:, :, 1, :]
                q1v = qro[:].rearrange("p (h two d) -> p h two d", h=4, two=2)[:, :, 0, :]
                q2v = qro[:].rearrange("p (h two d) -> p h two d", h=4, two=2)[:, :, 1, :]
                r1v = kvsq[:].rearrange("p (h d) -> p h d", h=4)[:, :, 0:64]
                r2v = kvsq[:].rearrange("p (h d) -> p h d", h=4)[:, :, 64:128]
                P.emit("vector", (lambda x1v, cview: lambda e: e.tensor_tensor(
                    r1v, x1v, cview, ALU.mult))(x1v, cview),
                    reads=(qpart_b, ropeo_b), writes=(kvsq_b,))
                P.emit("vector", (lambda x2v, sview: lambda e: e.tensor_tensor(
                    r2v, x2v, sview, ALU.mult))(x2v, sview),
                    reads=(qpart_b, ropeo_b), writes=(kvsq_b,))
                P.emit("vector", (lambda q1v: lambda e: e.tensor_tensor(
                    q1v, r1v, r2v, ALU.add))(q1v),
                    reads=(kvsq_b,), writes=(qro_b,))
                P.emit("vector", (lambda x2v, cview: lambda e: e.tensor_tensor(
                    r1v, x2v, cview, ALU.mult))(x2v, cview),
                    reads=(qpart_b, ropeo_b), writes=(kvsq_b,))
                P.emit("vector", (lambda x1v, sview: lambda e: e.tensor_tensor(
                    r2v, x1v, sview, ALU.mult))(x1v, sview),
                    reads=(qpart_b, ropeo_b), writes=(kvsq_b,))
                P.emit("vector", (lambda q2v: lambda e: e.tensor_tensor(
                    q2v, r1v, r2v, ALU.subtract))(q2v),
                    reads=(kvsq_b,), writes=(qro_b,))
                emit_transposes(L, qtos, qtos_b, qro, qro_b, ident, ident_b,
                                [(hl * 128, (hl * 8 + qt) * 128) for hl in range(4)])
            for qt in range(8):
                P.emit("sync", (lambda blk, qt: lambda e: e.dma_start(
                    out=msk[:].rearrange("p (r k) -> p r k", r=5),
                    in_=mskI[blk, qt].rearrange("r p k -> p r k")))(blk, qt),
                    writes=(msk_b,), dma=True)
                for hl in range(4):
                    h = oslab * 4 + hl
                    kv = h // 4
                    b0, b0_b = L.ps[0], L.ps_buf[0]
                    b1, b1_b = L.ps[1], L.ps_buf[1]
                    P.emit("tensor", (lambda hl, qt, kv, b0: lambda e: e.matmul(
                        b0[:], qtos[:, (hl * 8 + qt) * 128:(hl * 8 + qt + 1) * 128],
                        ktblk[:, (kv * 12 + qt) * 128:(kv * 12 + qt) * 128 + 512],
                        start=True, stop=True))(hl, qt, kv, b0),
                        reads=(qtos_b, ktblk_b), writes=(b0_b,))
                    P.emit("tensor", (lambda hl, qt, kv, b1: lambda e: e.matmul(
                        b1[:, 0:128], qtos[:, (hl * 8 + qt) * 128:(hl * 8 + qt + 1) * 128],
                        ktblk[:, (kv * 12 + qt + 4) * 128:(kv * 12 + qt + 5) * 128],
                        start=True, stop=True))(hl, qt, kv, b1),
                        reads=(qtos_b, ktblk_b), writes=(b1_b,))
                    P.emit("vector", (lambda h, b0: lambda e: e.scalar_tensor_tensor(
                        ssb[:, 0:512], b0[:], gsc[:, h:h + 1], msk[:, 0:512],
                        ALU.mult, ALU.add))(h, b0),
                        reads=(b0_b, gsc_b, msk_b), writes=(ssb_b,))
                    P.emit("vector", (lambda h, b1: lambda e: e.scalar_tensor_tensor(
                        ssb[:, 512:640], b1[:, 0:128], gsc[:, h:h + 1], msk[:, 512:640],
                        ALU.mult, ALU.add))(h, b1),
                        reads=(b1_b, gsc_b, msk_b), writes=(ssb_b,))
                    P.emit("vector", lambda e: e.reduce_max(mcol[:], ssb[:], AX.X),
                           reads=(ssb_b,), writes=(mcol_b,))
                    P.emit("vector", lambda e: e.tensor_scalar(
                        mcol[:], mcol[:], -1.0, None, ALU.mult),
                        reads=(mcol_b,), writes=(mcol_b,))
                    P.emit("scalar", lambda e: e.activation(
                        psb_[:], ssb[:], AF.Exp, bias=mcol[:], accum_out=lcol[:]),
                        reads=(ssb_b, mcol_b), writes=(psb_b_, lcol_b))
                    P.emit("vector", lambda e: e.reciprocal(licol[:], lcol[:]),
                           reads=(lcol_b,), writes=(licol_b,))
                    emit_transposes(L, ptsb, ptsb_b, psb_, psb_b_, ident, ident_b,
                                    [(w * 128, w * 128) for w in range(5)])
                    ob, ob_b = L.ps[2], L.ps_buf[2]
                    for w in range(5):
                        P.emit("tensor", (lambda w, qt, kv, ob: lambda e: e.matmul(
                            ob[:, 0:128], ptsb[:, w * 128:(w + 1) * 128],
                            vblk[:, (qt + w) * 512 + kv * 128:(qt + w) * 512 + (kv + 1) * 128],
                            start=(w == 0), stop=(w == 4)))(w, qt, kv, ob),
                            reads=(ptsb_b, vblk_b), writes=(ob_b,))
                    P.emit("vector", (lambda qt, h, ob: lambda e: e.tensor_scalar(
                        oblk[:, qt * D + h * 128:qt * D + (h + 1) * 128], ob[:, 0:128],
                        licol[:], None, ALU.mult))(qt, h, ob),
                        reads=(ob_b, licol_b), writes=(oblk_b,))

        for qgrp in range(4):
            for j in range(2):
                qt = qgrp * 2 + j
                emit_transposes(L, ot2, ot2_b, oblk, oblk_b, ident, ident_b,
                                [(qt * D + dc_ * 128, (j * 16 + dc_) * 128) for dc_ in range(DC)])
            for oslab in range(4):
                for dc_ in range(DC):
                    wt, wt_b = wots[dc_ % 2]
                    P.emit("gpsimd", (lambda dc_, oslab, wt: lambda e: e.dma_start(
                        out=wt[:], in_=woB[dc_, :, oslab * 512:(oslab + 1) * 512]))(dc_, oslab, wt),
                        writes=(wt_b,), dma=True)
                    for j in range(2):
                        bank, bank_b = L.ps[j], L.ps_buf[j]
                        P.emit("tensor", (lambda j, dc_, wt, bank: lambda e: e.matmul(
                            bank[:], ot2[:, (j * 16 + dc_) * 128:(j * 16 + dc_ + 1) * 128],
                            wt[:], start=(dc_ == 0), stop=(dc_ == DC - 1)))(j, dc_, wt, bank),
                            reads=(ot2_b, wt_b), writes=(bank_b,))
                for j in range(2):
                    bank, bank_b = L.ps[j], L.ps_buf[j]
                    P.emit("vector", (lambda j, oslab, bank: lambda e: e.tensor_copy(
                        ao2[:, j * D + oslab * 512:j * D + (oslab + 1) * 512], bank[:]))(j, oslab, bank),
                        reads=(bank_b,), writes=(ao2_b,))
            for j in range(2):
                qt = qgrp * 2 + j
                row0 = blk * 1024 + qt * 128
                P.emit("sync", (lambda row0: lambda e: e.dma_start(
                    out=xi[:], in_=xo[row0:row0 + 128, :]))(row0),
                    writes=(xi_b,), dma=True)
                for i in range(4):
                    P.emit("vector", (lambda i, j: lambda e: e.bn_stats(
                        stats[:, i * 6:(i + 1) * 6],
                        ao2[:, j * D + i * 512:j * D + (i + 1) * 512]))(i, j),
                        reads=(ao2_b,), writes=(stats_b,))
                P.emit("vector", lambda e: e.bn_aggr(stats[:, 24:26], stats[:, 0:24]),
                       reads=(stats_b,), writes=(stats_b,))
                P.emit("vector", lambda e: e.tensor_scalar(stats[:, 26:27], stats[:, 24:25],
                                                           -1.0, None, ALU.mult),
                       reads=(stats_b,), writes=(stats_b,))
                P.emit("scalar", lambda e: e.activation(stats[:, 27:28], stats[:, 25:26],
                                                        AF.Sqrt, bias=L.consts[:, 0:1]),
                       reads=(stats_b, L.consts_b), writes=(stats_b,))
                P.emit("vector", lambda e: e.reciprocal(stats[:, 28:29], stats[:, 27:28]),
                       reads=(stats_b,), writes=(stats_b,))
                P.emit("vector", (lambda j: lambda e: e.tensor_scalar(
                    yt[:], ao2[:, j * D:(j + 1) * D], stats[:, 26:27], stats[:, 28:29],
                    ALU.add, ALU.mult))(j),
                    reads=(ao2_b, stats_b), writes=(yt_b,))
                P.emit("vector", lambda e: e.tensor_tensor(yt[:], yt[:], xi[:], ALU.add),
                       reads=(yt_b, xi_b), writes=(yt_b,))
                store_ops.append(P.emit("sync", (lambda row0: lambda e: e.dma_start(
                    out=x1[row0:row0 + 128, :], in_=yt[:]))(row0),
                    reads=(yt_b,), dma=True))

    P.emit("sync", lambda e: e.nop(), deps=store_ops)
    if fin:
        P.finalize(nc, es)
    return nc


def host_prep_l0_attn(inputs, core):
    b, c = core // 4, core % 4
    blocks = [c, 7 - c]
    x = inputs["L0_attn_x"][b]
    xo = np.concatenate([x[j * 1024:(j + 1) * 1024] for j in blocks], axis=0)
    halos = []
    for j in blocks:
        lo = j * 1024 - 512
        if lo < 0:
            halos.append(np.zeros((512, D), np.float32))
        else:
            halos.append(x[lo:lo + 512])
    xh = np.concatenate(halos, axis=0)
    wqB = np.ascontiguousarray(inputs["L0_attn_Wq"].T.reshape(16, 128, D))
    wkB = np.ascontiguousarray(inputs["L0_attn_Wk"].T.reshape(16, 128, 512))
    wvB = np.ascontiguousarray(inputs["L0_attn_Wv"].T.reshape(16, 128, 512))
    woB = np.ascontiguousarray(inputs["L0_attn_Wo"].T.reshape(16, 128, D))
    gain = inputs["L0_attn_q_gain"].reshape(1, 16)
    inv_freq = (np.float32(10000.0) ** (-np.arange(0, 64, dtype=np.float32) / np.float32(64))).astype(np.float32)
    own_rows = np.concatenate([np.arange(j * 1024, (j + 1) * 1024) for j in blocks])
    halo_rows = np.concatenate([np.arange(max(0, j * 1024 - 512), max(0, j * 1024 - 512) + 512)
                                if j * 1024 - 512 >= 0 else np.zeros(512, np.int64)
                                for j in blocks])
    fo = own_rows.astype(np.float32)[:, None] * inv_freq[None, :]
    fh = halo_rows.astype(np.float32)[:, None] * inv_freq[None, :]
    mask = np.empty((2, 8, 5, 128, 128), np.float32)
    i_ = np.arange(128)
    for bi, j in enumerate(blocks):
        bstart = j * 1024
        for qt in range(8):
            q0 = bstart + qt * 128
            for w in range(5):
                k0 = bstart - 512 + (qt + w) * 128
                kk = k0 + i_[None, :]
                qq = q0 + i_[:, None]
                ok = (kk <= qq) & (kk >= qq - (WIN - 1)) & (kk >= 0)
                mask[bi, qt, w] = np.where(ok, 0.0, NEG)
    return dict(x_own=xo, x_halo=xh, wqB=wqB, wkB=wkB, wvB=wvB, woB=woB,
                gain_row=gain,
                rope_own_c=np.cos(fo).astype(np.float32),
                rope_own_s=np.sin(fo).astype(np.float32),
                rope_halo_c=np.cos(fh).astype(np.float32),
                rope_halo_s=np.sin(fh).astype(np.float32),
                maskadd=mask,
                ident_bf=np.eye(128, dtype=bf16),
                ones_row_f=np.ones((1, 128), dtype=np.float32))



def build_l0_ffn(nc, es, L=None, io=None):
    fin = L is None
    if L is None:
        L = LaunchCtx(nc, es)
    P = L.P

    x1 = dram(io, nc, "x1", [TOK, D], F32, "ExternalInput")
    w1 = dram(io, nc, "w1", [128, 128, D], F32, "ExternalInput")
    wp = dram(io, nc, "wp", [16, 128, FF0], F32, "ExternalInput")
    identi = dram(io, nc, "ident_bf", [128, 128], BF16, "ExternalInput")
    x2 = dram(io, nc, "x2", [TOK, D], F32, "ExternalOutput")
    w1hat = dram(io, nc, "w1hat", [2 * FF0, D], BF16, "Internal")
    wphat = dram(io, nc, "wphat", [D, FF0], BF16, "Internal")
    hscr = dram(io, nc, "hscr", [TOK, FF0], BF16, "Internal")

    ident, ident_b = L.sb("identt", [128, 128], BF16)
    uq, uq_b = L.sb("uq", [128, 2 * FF0], BF16)
    wqb, wqb_b = uq[:, 0:FF0], uq_b
    q0, q0_b = uq[:, FF0:2 * FF0], uq_b
    asum, asum_b = L.sb("asum", [128, 128], F32)
    scl, scl_b = L.sb("scl", [128, 128], BF16)
    rec, rec_b = L.sb("rec", [128, 128], F32)
    xi, xi_b = L.sb("xi", [128, D], F32)
    t1f, t1f_b = L.sb("t1f", [128, D], F32)
    t1bf, t1bf_b = L.sb("t1bf", [128, D], BF16)
    stats, stats_b = L.sb("stats", [128, 32], F32)
    t1T, t1T_b = L.sb("t1T", [128, DC * TOK], BF16)
    gusl, gusl_b = L.sb("gusl", [128, 64 * 512], BF16)
    sg, sg_b = L.sb("sg", [128, 512], F32)
    hseg, hseg_b = L.sb("hseg", [128, 512], F32)
    hbf, hbf_b = L.sb("hbf", [128, 512], BF16)
    hsq, hsq_b = L.sb("hsq", [128, 512], F32)
    hcol, hcol_b = L.sb("hcol", [128, 1], F32)
    hss, hss_b = L.sb("hss", [128, 48], F32)
    wpt = [L.sb(f"wpt{i}", [128, 512], BF16) for i in range(2)]
    outt4, outt4_b = uq[:].bitcast(F32), uq_b
    yt, yt_b = L.sb("yt", [128, D], F32)

    P.emit("sync", lambda e: e.dma_start(out=ident[:], in_=identi[:]), writes=(ident_b,), dma=True)
    P.emit("vector", lambda e: e.memset(hss[:, 0:16], 0.0), writes=(hss_b,))

    store_ops = []

    def quant_tile(src_ap, dst_ap, width, ng, eng):
        P.emit("gpsimd", (lambda src_ap: lambda e: e.dma_start(
            out=wqb[:, 0:width], in_=src_ap))(src_ap),
            writes=(wqb_b,), dma=True)
        P.emit("vector", (lambda width, ng: lambda e: e.tensor_reduce(
            asum[:, 0:ng], wqb[:, 0:width].rearrange("p (g k) -> p g k", k=64),
            AX.X, ALU.add, apply_absolute_value=True))(width, ng),
            reads=(wqb_b,), writes=(asum_b,))
        P.emit(eng, (lambda ng: lambda e: e.tensor_scalar(
            scl[:, 0:ng], asum[:, 0:ng], float(1.0 / 64.0), 1e-8, ALU.mult, ALU.max))(ng),
            reads=(asum_b,), writes=(scl_b,))
        P.emit("vector", (lambda ng: lambda e: e.reciprocal(
            rec[:, 0:ng], scl[:, 0:ng]))(ng),
            reads=(scl_b,), writes=(rec_b,))
        P.emit(eng, (lambda width, ng: lambda e: e.tensor_tensor(
            q0[:, 0:width].rearrange("p (g k) -> p g k", k=64),
            wqb[:, 0:width].rearrange("p (g k) -> p g k", k=64),
            rec[:, 0:ng].unsqueeze(2).broadcast_to([128, ng, 64]), ALU.mult))(width, ng),
            reads=(wqb_b, rec_b), writes=(q0_b,))
        P.emit(eng, (lambda width: lambda e: e.tensor_scalar(
            q0[:, 0:width], q0[:, 0:width], MAGIC, MAGIC, ALU.add, ALU.subtract))(width),
            reads=(q0_b,), writes=(q0_b,))
        P.emit(eng, (lambda width: lambda e: e.tensor_scalar(
            q0[:, 0:width], q0[:, 0:width], 1.0, -1.0, ALU.min, ALU.max))(width),
            reads=(q0_b,), writes=(q0_b,))
        P.emit(eng, (lambda width, ng: lambda e: e.tensor_tensor(
            q0[:, 0:width].rearrange("p (g k) -> p g k", k=64),
            q0[:, 0:width].rearrange("p (g k) -> p g k", k=64),
            scl[:, 0:ng].unsqueeze(2).broadcast_to([128, ng, 64]), ALU.mult))(width, ng),
            reads=(q0_b, scl_b), writes=(q0_b,))
        return P.emit("sync", (lambda dst_ap, width: lambda e: e.dma_start(
            out=dst_ap, in_=q0[:, 0:width]))(dst_ap, width),
            reads=(q0_b,), dma=True)

    wh_ops = []
    for ht in range(128):
        wh_ops.append(quant_tile(w1[ht], w1hat[ht * 128:(ht + 1) * 128, :], D, 32,
                                 "vector"))
    for pt in range(16):
        wh_ops.append(quant_tile(wp[pt], wphat[pt * 128:(pt + 1) * 128, :], FF0, 128,
                                 "vector"))

    for qt in range(NT):
        P.emit("sync", (lambda qt: lambda e: e.dma_start(
            out=xi[:], in_=x1[qt * 128:(qt + 1) * 128, :]))(qt),
            writes=(xi_b,), dma=True)
        emit_ln_tile(L, t1f, t1f_b, xi, xi_b, stats, stats_b)
        P.emit("vector", lambda e: e.tensor_copy(t1bf[:], t1f[:]),
               reads=(t1f_b,), writes=(t1bf_b,))
        emit_transposes(L, t1T, t1T_b, t1bf, t1bf_b, ident, ident_b,
                        [(dc_ * 128, (dc_ * NT + qt) * 128) for dc_ in range(DC)])

    for p in range(16):
        for gi in range(2):
            hbase = gi * FF0 + p * 512
            for dc_ in range(DC):
                P.emit("scalar" if dc_ % 2 else "sync",
                       (lambda gi, dc_, hbase: lambda e: e.dma_start_transpose(
                           gusl[:, (gi * 16 + dc_) * 512:(gi * 16 + dc_ + 1) * 512],
                           w1hat[hbase:hbase + 512, dc_ * 128:(dc_ + 1) * 128]))(gi, dc_, hbase),
                       reads=(), writes=(gusl_b,), dma=True, deps=wh_ops)
        for qt in range(NT):
            bg, bg_b = L.ps[0], L.ps_buf[0]
            bu, bu_b = L.ps[1], L.ps_buf[1]
            for dc_ in range(DC):
                P.emit("tensor", (lambda qt, dc_, bg: lambda e: e.matmul(
                    bg[:], t1T[:, (dc_ * NT + qt) * 128:(dc_ * NT + qt) * 128 + 128],
                    gusl[:, dc_ * 512:(dc_ + 1) * 512],
                    start=(dc_ == 0), stop=(dc_ == DC - 1)))(qt, dc_, bg),
                    reads=(t1T_b, gusl_b), writes=(bg_b,))
            for dc_ in range(DC):
                P.emit("tensor", (lambda qt, dc_, bu: lambda e: e.matmul(
                    bu[:], t1T[:, (dc_ * NT + qt) * 128:(dc_ * NT + qt) * 128 + 128],
                    gusl[:, (16 + dc_) * 512:(16 + dc_ + 1) * 512],
                    start=(dc_ == 0), stop=(dc_ == DC - 1)))(qt, dc_, bu),
                    reads=(t1T_b, gusl_b), writes=(bu_b,))
            P.emit("scalar", (lambda bg: lambda e: e.activation(
                sg[:], bg[:], AF.Silu))(bg),
                reads=(bg_b,), writes=(sg_b,))
            P.emit("vector", (lambda bu: lambda e: e.tensor_tensor(
                hseg[:], sg[:], bu[:], ALU.mult))(bu),
                reads=(sg_b, bu_b), writes=(hseg_b,))
            P.emit("scalar", lambda e: e.activation(hsq[:], hseg[:], AF.Square),
                   reads=(hseg_b,), writes=(hsq_b,))
            P.emit("vector", lambda e: e.tensor_reduce(hcol[:], hsq[:], AX.X, ALU.add),
                   reads=(hsq_b,), writes=(hcol_b,))
            P.emit("vector", (lambda qt: lambda e: e.tensor_tensor(
                hss[:, qt:qt + 1], hss[:, qt:qt + 1], hcol[:], ALU.add))(qt),
                reads=(hss_b, hcol_b), writes=(hss_b,))
            P.emit("vector", lambda e: e.tensor_copy(hbf[:], hseg[:]),
                   reads=(hseg_b,), writes=(hbf_b,))
            store_ops.append(P.emit("sync", (lambda qt, p: lambda e: e.dma_start(
                out=hscr[qt * 128:(qt + 1) * 128, p * 512:(p + 1) * 512],
                in_=hbf[:]))(qt, p),
                reads=(hbf_b,), dma=True))
    h_done = list(store_ops)
    store_ops = []

    P.emit("scalar", lambda e: e.activation(
        hss[:, 16:32], hss[:, 0:16], AF.Sqrt, bias=L.consts[:, 1:2],
        scale=float(1.0 / FF0)),
        reads=(hss_b, L.consts_b), writes=(hss_b,))
    P.emit("vector", lambda e: e.reciprocal(hss[:, 32:48], hss[:, 16:32]),
           reads=(hss_b,), writes=(hss_b,))

    for qg in range(4):
        for hc in range(64):
            P.emit("scalar" if hc % 2 else "sync",
                   (lambda qg, hc: lambda e: e.dma_start_transpose(
                       gusl[:, hc * 512:(hc + 1) * 512],
                       hscr[qg * 512:(qg + 1) * 512, hc * 128:(hc + 1) * 128]))(qg, hc),
                   writes=(gusl_b,), dma=True, deps=h_done)
        for ds_ in range(4):
            for hc in range(64):
                wt, wt_b = wpt[hc % 2]
                P.emit("scalar", (lambda ds_, hc, wt: lambda e: e.dma_start_transpose(
                    wt[:], wphat[ds_ * 512:(ds_ + 1) * 512, hc * 128:(hc + 1) * 128]))(ds_, hc, wt),
                    writes=(wt_b,), dma=True, deps=wh_ops)
                for j in range(4):
                    bank, bank_b = L.ps[j], L.ps_buf[j]
                    P.emit("tensor", (lambda hc, j, wt, bank: lambda e: e.matmul(
                        bank[:], gusl[:, hc * 512 + j * 128:hc * 512 + (j + 1) * 128],
                        wt[:], start=(hc == 0), stop=(hc == 63)))(hc, j, wt, bank),
                        reads=(gusl_b, wt_b), writes=(bank_b,))
            for j in range(4):
                qt = qg * 4 + j
                bank, bank_b = L.ps[j], L.ps_buf[j]
                P.emit("vector", (lambda qt, j, ds_, bank: lambda e: e.tensor_scalar(
                    outt4[:, j * D + ds_ * 512:j * D + (ds_ + 1) * 512], bank[:],
                    hss[:, 32 + qt:33 + qt], None, ALU.mult))(qt, j, ds_, bank),
                    reads=(bank_b, hss_b), writes=(outt4_b,))
        for j in range(4):
            qt = qg * 4 + j
            P.emit("sync", (lambda qt: lambda e: e.dma_start(
                out=xi[:], in_=x1[qt * 128:(qt + 1) * 128, :]))(qt),
                writes=(xi_b,), dma=True)
            for i2 in range(4):
                P.emit("vector", (lambda i2, j: lambda e: e.bn_stats(
                    stats[:, i2 * 6:(i2 + 1) * 6],
                    outt4[:, j * D + i2 * 512:j * D + (i2 + 1) * 512]))(i2, j),
                    reads=(outt4_b,), writes=(stats_b,))
            P.emit("vector", lambda e: e.bn_aggr(stats[:, 24:26], stats[:, 0:24]),
                   reads=(stats_b,), writes=(stats_b,))
            P.emit("vector", lambda e: e.tensor_scalar(stats[:, 26:27], stats[:, 24:25],
                                                       -1.0, None, ALU.mult),
                   reads=(stats_b,), writes=(stats_b,))
            P.emit("scalar", lambda e: e.activation(stats[:, 27:28], stats[:, 25:26],
                                                    AF.Sqrt, bias=L.consts[:, 0:1]),
                   reads=(stats_b, L.consts_b), writes=(stats_b,))
            P.emit("vector", lambda e: e.reciprocal(stats[:, 28:29], stats[:, 27:28]),
                   reads=(stats_b,), writes=(stats_b,))
            P.emit("vector", (lambda j: lambda e: e.tensor_scalar(
                yt[:], outt4[:, j * D:(j + 1) * D], stats[:, 26:27], stats[:, 28:29],
                ALU.add, ALU.mult))(j),
                reads=(outt4_b, stats_b), writes=(yt_b,))
            P.emit("vector", lambda e: e.tensor_tensor(yt[:], yt[:], xi[:], ALU.add),
                   reads=(yt_b, xi_b), writes=(yt_b,))
            store_ops.append(P.emit("sync", (lambda qt: lambda e: e.dma_start(
                out=x2[qt * 128:(qt + 1) * 128, :], in_=yt[:]))(qt),
                reads=(yt_b,), dma=True))

    P.emit("sync", lambda e: e.nop(), deps=store_ops)
    if fin:
        P.finalize(nc, es)
    return nc


def host_prep_l0_ffn(inputs):
    w1 = np.ascontiguousarray(inputs["L0_ffn_w_gate_up"].reshape(128, 128, D))
    wp = np.ascontiguousarray(inputs["L0_ffn_w_proj"].reshape(16, 128, FF0))
    return dict(w1=w1, wp=wp, ident_bf=np.eye(128, dtype=bf16))



_BUILT = {}


def _get_nc(name, builder):
    if name not in _BUILT:
        nc = bass.Bass("TRN2", target_bir_lowering=False, debug=False)
        es = ExitStack()
        es.__enter__()
        builder(nc, es)
        _BUILT[name] = (nc, es)
    return _BUILT[name][0]


def kernel_multi(**inputs):
    cores = list(range(8))

    nc = _get_nc("l0a", build_l0_attn)
    maps = [host_prep_l0_attn(inputs, c) for c in cores]
    r1 = run_bass_kernel_spmd(nc, maps, core_ids=cores).results
    x1s = [r["x1"] for r in r1]

    nc = _get_nc("l0f", build_l0_ffn)
    shared = host_prep_l0_ffn(inputs)
    maps = [dict(x1=x1s[c], **shared) for c in cores]
    r2 = run_bass_kernel_spmd(nc, maps, core_ids=cores).results
    x2s = [r["x2"] for r in r2]

    x2_full = np.empty((B, S, D), np.float32)
    for c in cores:
        b, rows = contiguous_rows(c)
        x2_full[b][rows] = x2s[c]
    x2i = []
    for c in cores:
        b, rows = interleaved_rows(c)
        x2i.append(np.ascontiguousarray(x2_full[b][rows]))

    nc = _get_nc("l1q", build_l1_qkv)
    maps = [dict(x2=x2i[c], **host_prep_l1_qkv(inputs, c)) for c in cores]
    r3 = run_bass_kernel_spmd(nc, maps, core_ids=cores).results
    qts = [r["qt_o"] for r in r3]
    kts = [r["kt_o"] for r in r3]
    vs = [r["v_o"] for r in r3]

    nc = _get_nc("l1a", build_l1_attn)
    maps = []
    for c in cores:
        b = c // 4
        ktg = np.concatenate([kts[4 * b + r] for r in range(4)], axis=0)
        vg = np.concatenate([vs[4 * b + r] for r in range(4)], axis=0)
        maps.append(dict(qto=qts[c], ktg=ktg, vg=vg, kto=kts[c], vo=vs[c],
                         x2=x2i[c], **host_prep_l1_attn(inputs, c)))
    r4 = run_bass_kernel_spmd(nc, maps, core_ids=cores).results
    x3s = [r["x3"] for r in r4]

    nc = _get_nc("l1f", build_l1_ffn)
    shared = host_prep_l1_ffn(inputs)
    maps = [dict(x3=x3s[c], **shared) for c in cores]
    r5 = run_bass_kernel_spmd(nc, maps, core_ids=cores).results

    out = np.empty((B, S, D), np.float32)
    for c in cores:
        b, rows = interleaved_rows(c)
        out[b][rows] = r5[c]["y"]
    return out



def _slot_src(s_):
    if s_ <= 7:
        r, pos = s_ // 2, s_ % 2
    else:
        if s_ % 2 == 1:
            r, pos = (15 - s_) // 2, 3
        else:
            r, pos = (14 - s_) // 2, 2
    return r * 512 + pos * 128


def build_fused(nc, es):
    L0 = LaunchCtx(nc, es)
    L0.arena = es.enter_context(nc.sbuf_tensor("arena", [128, 206 * 1024],
                                               mybir.dt.uint8))
    P = L0.P
    RG = [[0, 1, 2, 3], [4, 5, 6, 7]]
    E = "ExternalInput"

    def d(name, shape, dt_, kind="ExternalInput"):
        return nc.dram_tensor(name, shape, dt_, kind=kind).ap()

    ident_bf = d("ident_bf", [128, 128], BF16)
    ident_f = d("ident_f", [128, 128], F32)
    ones_row = d("ones_row", [1, 512], BF16)
    ones_col = d("ones_col", [128, 1], BF16)
    ones_row_f = d("ones_row_f", [1, 128], F32)

    io1 = dict(ident_bf=ident_bf, ones_row_f=ones_row_f,
               x_own=d("x_own", [TOK, D], F32), x_halo=d("x_halo", [1024, D], F32),
               wqB=d("wqB", [16, 128, D], F32), wkB=d("wkB", [16, 128, 512], F32),
               wvB=d("wvB", [16, 128, 512], F32), woB=d("woB", [16, 128, D], F32),
               gain_row=d("gain_row", [1, 16], F32),
               rope_own_c=d("rope_own_c", [TOK, 64], F32),
               rope_own_s=d("rope_own_s", [TOK, 64], F32),
               rope_halo_c=d("rope_halo_c", [1024, 64], F32),
               rope_halo_s=d("rope_halo_s", [1024, 64], F32),
               maskadd=d("maskadd", [2, 8, 5, 128, 128], F32))
    io2 = dict(ident_bf=ident_bf,
               w1=d("w1", [128, 128, D], F32), wp=d("wp", [16, 128, FF0], F32))
    io3 = dict(ident_bf=ident_bf, ones_row_f=ones_row_f,
               wqA=d("wqA", [16, 128, 16 * 128], F32),
               wkA=d("wkA", [16, 128, 16 * 128], F32),
               wvB=d("wvB1", [16, 128, D], F32),
               bqP=d("bqP", [128, 16], F32), bkP=d("bkP", [128, 16], F32),
               bv_row=d("bv_row", [1, D], F32),
               ropeq_c=d("ropeq_c", [64, TOK], F32),
               ropeq_s=d("ropeq_s", [64, TOK], F32),
               ropek_c=d("ropek_c", [64, TOK], F32),
               ropek_s=d("ropek_s", [64, TOK], F32))
    io4 = dict(ident_f=ident_f, ones_row=ones_row, ones_col=ones_col,
               ones_row_f=ones_row_f,
               woB=d("woB1", [16, 128, D], F32),
               bo_row=d("bo_row", [1, D], F32),
               kill_slots=d("kill_slots", [1, 64 * 16], BF16),
               tri_add=d("tri_add", [128, 128], F32),
               tri01T=d("tri01T", [128, 128], BF16))
    io5 = dict(ident_bf=ident_bf, ident_f=ident_f, ones_row=ones_row,
               ones_col=ones_col,
               wg=d("wg", [64, 128, 16 * 128], I32),
               wu=d("wu", [64, 128, 16 * 128], I32),
               wd=d("wd", [64, 128, D], I32),
               gsc=d("gsc", [128, 64], F32), usc=d("usc", [128, 64], F32),
               dsc_row=d("dsc_row", [1, D], F32))

    x1i = d("x1i", [TOK, D], F32, "Internal")
    x2in = d("cc_x2_in", [TOK, D], F32, "Internal")
    x2g = d("cc_x2_g", [4 * TOK, D], F32, "Internal")
    qtoi = d("qto_i", [D, TOK], BF16, "Internal")
    ktoi = d("kto_i", [D, TOK], BF16, "Internal")
    voi = d("vo_i", [TOK, D], BF16, "Internal")
    ktgi = d("ktg_i", [4 * D, TOK], BF16, "Internal")
    vgi = d("vg_i", [4 * TOK, D], BF16, "Internal")
    x3i = d("x3i", [TOK, D], F32, "Internal")
    yout = d("y", [TOK, D], F32, "ExternalOutput")

    x2idx = d("x2idx", [128, 16], I32)
    x2idxs = es.enter_context(nc.sbuf_tensor("x2idxs", [128, 16], I32))
    x2idxs_b = Buf()

    def x2_load(e, qt, dst):
        return e.indirect_dma_start(
            dst, None, x2g[:, :],
            bass.IndirectOffsetOnAxis(ap=x2idxs[:, qt:qt + 1], axis=0))

    io1["x1"] = x1i
    io2.update(x1=x1i, x2=x2in)
    io3.update(x2=x2g, qt_o=qtoi, kt_o=ktoi, v_o=voi)
    io4.update(qto=qtoi, ktg=ktgi, vg=vgi, kto=ktoi, vo=voi, x2=x2g, x3=x3i)
    io5.update(x3=x3i, y=yout)

    def stage(builder, io):
        Ls = LaunchCtx(nc, es, shared=L0)
        builder(nc, es, L=Ls, io=io)
        P.barrier()

    cdep = P.emit("sync", lambda e: e.dma_start(out=x2idxs[:], in_=x2idx[:]),
                  writes=(x2idxs_b,), dma=True)
    io3.update(x2_load=x2_load, x2_dep=cdep, x2_eng="gpsimd")
    io4.update(x2_load=x2_load, x2_dep=cdep, x2_eng="gpsimd")

    stage(build_l0_attn, io1)
    stage(build_l0_ffn, io2)
    P.emit("gpsimd", lambda e: e.collective_compute(
        "AllGather", ALU.bypass, RG,
        [x2in[:, :]], [x2g[:, :]]), dma=True, coll=True)
    P.barrier()
    stage(build_l1_qkv, io3)
    P.emit("gpsimd", lambda e: e.collective_compute(
        "AllGather", ALU.bypass, RG,
        [ktoi[:, :]], [ktgi[:, :]]), dma=True, coll=True)
    P.emit("gpsimd", lambda e: e.collective_compute(
        "AllGather", ALU.bypass, RG,
        [voi[:, :]], [vgi[:, :]]), dma=True, coll=True)
    P.barrier()
    stage(build_l1_attn, io4)
    stage(build_l1_ffn, io5)

    P.finalize(nc, es)
    return nc


def _a_const_host(s_):
    t0 = 63 - 4 * s_
    j = t0 // 8
    r = min(j, 7 - j)
    return r * 2048 + (0 if j == r else 1024) + (t0 - 8 * j) * 128


def host_prep_fused(inputs, core):
    c = core % 4
    idx = np.empty((128, 16), np.int32)
    for s_ in range(16):
        idx[:, s_] = _a_const_host(s_) - 128 * c + np.arange(128)
    m = {"x2idx": idx}
    m.update(host_prep_l0_attn(inputs, core))
    m.update(host_prep_l0_ffn(inputs))
    q = host_prep_l1_qkv(inputs, core)
    m["wvB1"] = q.pop("wvB")
    m.update(q)
    a = host_prep_l1_attn(inputs, core)
    m["woB1"] = a.pop("woB")
    m.update(a)
    m.update(host_prep_l1_ffn(inputs))
    return m


def kernel_fused(**inputs):
    cores = list(range(8))
    nc = _get_nc("fused", build_fused)
    maps = [host_prep_fused(inputs, c) for c in cores]
    res = run_bass_kernel_spmd(nc, maps, core_ids=cores).results
    out = np.empty((B, S, D), np.float32)
    for c in cores:
        b, rows = interleaved_rows(c)
        out[b][rows] = res[c]["y"]
    return out


def kernel(**inputs):
    return kernel_fused(**inputs)



def build_half_a(nc, es):
    L0 = LaunchCtx(nc, es)
    L0.arena = es.enter_context(nc.sbuf_tensor("arena", [128, 206 * 1024],
                                               mybir.dt.uint8))
    P = L0.P

    def d(name, shape, dt_, kind="ExternalInput"):
        return nc.dram_tensor(name, shape, dt_, kind=kind).ap()

    ident_bf = d("ident_bf", [128, 128], BF16)
    ones_row_f = d("ones_row_f", [1, 128], F32)
    io1 = dict(ident_bf=ident_bf, ones_row_f=ones_row_f,
               x_own=d("x_own", [TOK, D], F32), x_halo=d("x_halo", [1024, D], F32),
               wqB=d("wqB", [16, 128, D], F32), wkB=d("wkB", [16, 128, 512], F32),
               wvB=d("wvB", [16, 128, 512], F32), woB=d("woB", [16, 128, D], F32),
               gain_row=d("gain_row", [1, 16], F32),
               rope_own_c=d("rope_own_c", [TOK, 64], F32),
               rope_own_s=d("rope_own_s", [TOK, 64], F32),
               rope_halo_c=d("rope_halo_c", [1024, 64], F32),
               rope_halo_s=d("rope_halo_s", [1024, 64], F32),
               maskadd=d("maskadd", [2, 8, 5, 128, 128], F32))
    io2 = dict(ident_bf=ident_bf,
               w1=d("w1", [128, 128, D], F32), wp=d("wp", [16, 128, FF0], F32))
    io3 = dict(ident_bf=ident_bf, ones_row_f=ones_row_f,
               wqA=d("wqA", [16, 128, 16 * 128], F32),
               wkA=d("wkA", [16, 128, 16 * 128], F32),
               wvB=d("wvB1", [16, 128, D], F32),
               bqP=d("bqP", [128, 16], F32), bkP=d("bkP", [128, 16], F32),
               bv_row=d("bv_row", [1, D], F32),
               ropeq_c=d("ropeq_c", [64, TOK], F32),
               ropeq_s=d("ropeq_s", [64, TOK], F32),
               ropek_c=d("ropek_c", [64, TOK], F32),
               ropek_s=d("ropek_s", [64, TOK], F32))
    x1i = d("x1i", [TOK, D], F32, "Internal")
    x2o = d("x2", [TOK, D], F32, "ExternalOutput")
    io1["x1"] = x1i
    io2.update(x1=x1i, x2=x2o)
    io3.update(x2=x2o,
               qt_o=d("qt_o", [D, TOK], BF16, "ExternalOutput"),
               kt_o=d("kt_o", [D, TOK], BF16, "ExternalOutput"),
               v_o=d("v_o", [TOK, D], BF16, "ExternalOutput"))

    def stage(builder, io):
        Ls = LaunchCtx(nc, es, shared=L0)
        builder(nc, es, L=Ls, io=io)
        P.barrier()

    stage(build_l0_attn, io1)
    stage(build_l0_ffn, io2)
    stage(build_l1_qkv, io3)
    P.finalize(nc, es)
    return nc


def build_half_b(nc, es):
    L0 = LaunchCtx(nc, es)
    L0.arena = es.enter_context(nc.sbuf_tensor("arena", [128, 206 * 1024],
                                               mybir.dt.uint8))
    P = L0.P

    def d(name, shape, dt_, kind="ExternalInput"):
        return nc.dram_tensor(name, shape, dt_, kind=kind).ap()

    io4 = dict(ident_f=d("ident_f", [128, 128], F32),
               ones_row=d("ones_row", [1, 512], BF16),
               ones_col=d("ones_col", [128, 1], BF16),
               ones_row_f=d("ones_row_f", [1, 128], F32),
               woB=d("woB1", [16, 128, D], F32),
               bo_row=d("bo_row", [1, D], F32),
               kill_slots=d("kill_slots", [1, 64 * 16], BF16),
               tri_add=d("tri_add", [128, 128], F32),
               tri01T=d("tri01T", [128, 128], BF16),
               qto=d("qto", [D, TOK], BF16), ktg=d("ktg", [4 * D, TOK], BF16),
               vg=d("vg", [4 * TOK, D], BF16), kto=d("kto", [D, TOK], BF16),
               vo=d("vo", [TOK, D], BF16),
               x2=d("x2", [TOK, D], F32))
    io5 = dict(ident_bf=d("ident_bf", [128, 128], BF16),
               ident_f=io4["ident_f"], ones_row=io4["ones_row"],
               ones_col=io4["ones_col"],
               wg=d("wg", [64, 128, 16 * 128], I32),
               wu=d("wu", [64, 128, 16 * 128], I32),
               wd=d("wd", [64, 128, D], I32),
               gsc=d("gsc", [128, 64], F32), usc=d("usc", [128, 64], F32),
               dsc_row=d("dsc_row", [1, D], F32))
    x3i = d("x3i", [TOK, D], F32, "Internal")
    io4["x3"] = x3i
    io5.update(x3=x3i, y=d("y", [TOK, D], F32, "ExternalOutput"))

    def stage(builder, io):
        Ls = LaunchCtx(nc, es, shared=L0)
        builder(nc, es, L=Ls, io=io)
        P.barrier()

    stage(build_l1_attn, io4)
    stage(build_l1_ffn, io5)
    P.finalize(nc, es)
    return nc


def kernel_2launch(**inputs):
    cores = list(range(8))
    nc = _get_nc("halfa", build_half_a)
    maps = []
    for c in cores:
        m = {}
        m.update(host_prep_l0_attn(inputs, c))
        m.update(host_prep_l0_ffn(inputs))
        b, crows = contiguous_rows(c)
        q = host_prep_l1_qkv(inputs, c, rows=crows)
        m["wvB1"] = q.pop("wvB")
        m.update(q)
        maps.append(m)
    ra = run_bass_kernel_spmd(nc, maps, core_ids=cores).results

    x2f = np.empty((B, S, D), np.float32)
    qtf = np.empty((B, D, S), bf16)
    ktf = np.empty((B, D, S), bf16)
    vf = np.empty((B, S, D), bf16)
    for c in cores:
        b, crows = contiguous_rows(c)
        x2f[b][crows] = ra[c]["x2"]
        qtf[b][:, crows] = ra[c]["qt_o"]
        ktf[b][:, crows] = ra[c]["kt_o"]
        vf[b][crows] = ra[c]["v_o"]

    nc = _get_nc("halfb", build_half_b)
    maps = []
    for c in cores:
        b, irows = interleaved_rows(c)
        m = host_prep_l1_attn(inputs, c)
        m["woB1"] = m.pop("woB")
        f = host_prep_l1_ffn(inputs)
        m.update(f)
        m["qto"] = np.ascontiguousarray(qtf[b][:, irows])
        m["kto"] = np.ascontiguousarray(ktf[b][:, irows])
        m["vo"] = np.ascontiguousarray(vf[b][irows])
        ktg = np.empty((4 * D, TOK), bf16)
        vg = np.empty((4 * TOK, D), bf16)
        for r in range(4):
            _, rrows = interleaved_rows(4 * b + r)
            ktg[r * D:(r + 1) * D, :] = ktf[b][:, rrows]
            vg[r * TOK:(r + 1) * TOK, :] = vf[b][rrows]
        m["ktg"] = ktg
        m["vg"] = vg
        m["x2"] = np.ascontiguousarray(x2f[b][irows])
        maps.append(m)
    rb = run_bass_kernel_spmd(nc, maps, core_ids=cores).results
    out = np.empty((B, S, D), np.float32)
    for c in cores:
        b, irows = interleaved_rows(c)
        out[b][irows] = rb[c]["y"]
    return out


def kernel(**inputs):
    return kernel_2launch(**inputs)


# version 08
# speedup vs baseline: 1.1981x; 1.0232x over previous
import numpy as np
import ml_dtypes
import concourse.bass as bass
import concourse.mybir as mybir
from concourse.bass_utils import run_bass_kernel_spmd
from contextlib import ExitStack

F32 = mybir.dt.float32
BF16 = mybir.dt.bfloat16
I32 = mybir.dt.int32
AF = mybir.ActivationFunctionType
ALU = mybir.AluOpType
AX = mybir.AxisListType
bf16 = ml_dtypes.bfloat16

B, S, D = 2, 8192, 2048
TOK = 2048
NT = TOK // 128
DC = D // 128
HD = 128
WIN = 512
FF0 = 8192
EPS_RMS = 1.1920928955078125e-07
EPS_LN = 1e-5
MAGIC = float(np.float32(1.5 * 2 ** 23))
NEG = -1.0e30

ENGINES = ("sync", "gpsimd", "vector", "scalar", "tensor")


class Buf:
    __slots__ = ("w", "rs")

    def __init__(self):
        self.w = None
        self.rs = []


class Prog:

    def __init__(self):
        self.ops = []

    def emit(self, eng, fn, reads=(), writes=(), dma=False, deps=(), coll=False):
        i = len(self.ops)
        dep = set(d for d in deps if d is not None)
        for b in reads:
            if b.w is not None:
                dep.add(b.w)
        for b in writes:
            if b.w is not None:
                dep.add(b.w)
            dep.update(b.rs)
        dep.discard(i)
        self.ops.append({"eng": eng, "fn": fn, "deps": dep, "dma": dma, "i": i,
                         "coll": coll})
        for b in reads:
            b.rs.append(i)
        for b in writes:
            b.w = i
            b.rs = []
        return i

    N_DMA_SEMS = 12

    def barrier(self):
        deps = set()
        last = {}
        ndma = {}
        for op in reversed(self.ops):
            e = op["eng"]
            if op["dma"]:
                if ndma.get(e, 0) < self.N_DMA_SEMS:
                    deps.add(op["i"])
                    ndma[e] = ndma.get(e, 0) + 1
            elif e not in last:
                last[e] = op["i"]
                deps.add(op["i"])
        deps = sorted(deps)
        for e in ENGINES:
            self.emit(e, lambda eng: eng.nop(), deps=deps)

    def finalize(self, nc, es):
        by_eng = {e: [] for e in ENGINES}
        for op in self.ops:
            by_eng[op["eng"]].append(op)

        consumers = {}
        for op in self.ops:
            for d in op["deps"]:
                consumers.setdefault(d, []).append(op["i"])

        rr = {}
        last_on_sem = {}
        ncoll = [0]
        for op in self.ops:
            if op.get("coll"):
                op["skey"] = f"coll{ncoll[0]}"
                ncoll[0] += 1
            elif op["dma"]:
                j = rr.get(op["eng"], 0)
                rr[op["eng"]] = j + 1
                k = f"{op['eng']}_dma{j % self.N_DMA_SEMS}"
                prev = last_on_sem.get(k)
                if prev is not None:
                    extra = [c for c in consumers.get(prev, [prev])
                             if c < op["i"]]
                    if not extra:
                        extra = [prev]
                    op["deps"].update(extra)
                last_on_sem[k] = op["i"]
                op["skey"] = k
            else:
                op["skey"] = op["eng"]

        need_sig = set()
        for op in self.ops:
            need_sig.update(op["deps"])
        semval = {}
        for op in self.ops:
            if op["i"] in need_sig:
                op["inc"] = 16 if op["dma"] else 1
                k = op["skey"]
                semval[k] = semval.get(k, 0) + op["inc"]
                op["sig"] = semval[k]
            else:
                op["sig"] = None

        sem_keys = sorted({op["skey"] for op in self.ops if op["sig"] is not None})
        sems = {k: es.enter_context(nc.semaphore(f"s_{k}")) for k in sem_keys}

        for op in self.ops:
            waits = {}
            for d in op["deps"]:
                dop = self.ops[d]
                k, v = dop["skey"], dop["sig"]
                if waits.get(k, 0) < v:
                    waits[k] = v
            op["waits"] = waits

        block = es.enter_context(nc.Block())
        for eng_name in ENGINES:
            deco = getattr(block, eng_name)

            def _mk(eng_name):
                ops = by_eng[eng_name]

                def _body(eng):
                    water = {}
                    for op in ops:
                        for k, v in sorted(op["waits"].items()):
                            if water.get(k, 0) < v:
                                eng.wait_ge(sems[k], v)
                                water[k] = v
                        inst = op["fn"](eng)
                        if op["sig"] is not None:
                            inst.then_inc(sems[op["skey"]], op["inc"])
                return _body

            deco(_mk(eng_name))
        return sems


def dram(io, nc, name, shape, dt_, kind):
    if io is not None and name in io:
        return io[name]
    return nc.dram_tensor(name, shape, dt_, kind=kind).ap()


DT_SIZE = {F32: 4, BF16: 2, I32: 4}


class LaunchCtx:
    def __init__(self, nc, es, shared=None):
        self.nc = nc
        self.es = es
        if shared is None:
            self.P = Prog()
            self.arena = None
            self.ps = [es.enter_context(nc.psum_tensor(f"psf{i}", [128, 512], F32))
                       for i in range(6)]
            self.psb = [es.enter_context(nc.psum_tensor(f"psb{i}", [128, 1024], BF16))
                        for i in range(2)]
            self.ps_buf = [Buf() for _ in self.ps]
            self.psb_buf = [Buf() for _ in self.psb]
            self.consts = es.enter_context(nc.sbuf_tensor("consts", [128, 8], F32))
            self.consts_b = Buf()
            self.P.emit("vector", lambda e: e.memset(self.consts[:, 0:1], EPS_LN),
                        writes=(self.consts_b,))
            self.P.emit("vector", lambda e: e.memset(self.consts[:, 1:2], EPS_RMS),
                        writes=(self.consts_b,))
        else:
            self.P = shared.P
            self.arena = shared.arena
            self.arena_off = [0]
            self.ps = shared.ps
            self.psb = shared.psb
            self.ps_buf = shared.ps_buf
            self.psb_buf = shared.psb_buf
            self.consts = shared.consts
            self.consts_b = shared.consts_b

    def sb(self, name, shape, dtype):
        if self.arena is None:
            t = self.es.enter_context(self.nc.sbuf_tensor(name, shape, dtype))
            return t, Buf()
        nb = int(np.prod(shape[1:])) * DT_SIZE[dtype]
        off = (self.arena_off[0] + 63) & ~63
        self.arena_off[0] = off + nb
        assert self.arena_off[0] <= self.arena.shape[1], \
            f"arena overflow at {name}: {self.arena_off[0]}"
        ap = self.arena[:, off:off + nb].bitcast(dtype)
        if shape[0] != 128:
            ap = ap[0:shape[0], :]
        return ap, Buf()


def emit_ln_tile(L, dst, dst_b, src, src_b, stats, stats_b, width=2048,
                 eps_col=0, out_kind="normal", res=None, res_b=None):
    P = L.P
    n = width // 512
    for i in range(n):
        P.emit("vector", (lambda i: lambda e: e.bn_stats(
            stats[:, i * 6:(i + 1) * 6], src[:, i * 512:(i + 1) * 512]))(i),
            reads=(src_b,), writes=(stats_b,))
    P.emit("vector", lambda e: e.bn_aggr(stats[:, 24:26], stats[:, 0:6 * n]),
           reads=(stats_b,), writes=(stats_b,))
    P.emit("vector", lambda e: e.tensor_scalar(stats[:, 26:27], stats[:, 24:25],
                                               -1.0, None, ALU.mult),
           reads=(stats_b,), writes=(stats_b,))
    P.emit("scalar", lambda e: e.activation(stats[:, 27:28], stats[:, 25:26],
                                            AF.Sqrt, bias=L.consts[:, eps_col:eps_col + 1]),
           reads=(stats_b, L.consts_b), writes=(stats_b,))
    P.emit("vector", lambda e: e.reciprocal(stats[:, 28:29], stats[:, 27:28]),
           reads=(stats_b,), writes=(stats_b,))
    P.emit("vector", lambda e: e.tensor_scalar(
        dst[:], src[:], stats[:, 26:27], stats[:, 28:29], ALU.add, ALU.mult),
        reads=(src_b, stats_b), writes=(dst_b,))
    if out_kind == "resid":
        P.emit("vector", lambda e: e.tensor_tensor(dst[:], dst[:], res[:], ALU.add),
               reads=(dst_b, res_b), writes=(dst_b,))


def emit_transposes(L, dst, dst_b, src, src_b, ident, ident_b, chunk_map):
    P = L.P
    for base in range(0, len(chunk_map), 8):
        grp = chunk_map[base:base + 8]
        bi = (base // 8) % len(L.psb)
        bank, bank_b = L.psb[bi], L.psb_buf[bi]
        for k, (sc, _) in enumerate(grp):
            P.emit("tensor", (lambda k, sc, bank: lambda e: e.transpose(
                bank[:, k * 128:(k + 1) * 128], src[:, sc:sc + 128], ident[:]))(k, sc, bank),
                reads=(src_b, ident_b), writes=(bank_b,))
        for k, (_, dc_) in enumerate(grp):
            P.emit("vector", (lambda k, dc_, bank: lambda e: e.tensor_copy(
                dst[:, dc_:dc_ + 128], bank[:, k * 128:(k + 1) * 128]))(k, dc_, bank),
                reads=(bank_b,), writes=(dst_b,))



def build_l1_ffn(nc, es, L=None, io=None):
    fin = L is None
    if L is None:
        L = LaunchCtx(nc, es)
    P = L.P

    x3 = dram(io, nc, "x3", [TOK, D], F32, "ExternalInput")
    wg = dram(io, nc, "wg", [64, 128, 16 * 128], I32, "ExternalInput")
    wu = dram(io, nc, "wu", [64, 128, 16 * 128], I32, "ExternalInput")
    wd = dram(io, nc, "wd", [64, 128, D], I32, "ExternalInput")
    gsc = dram(io, nc, "gsc", [128, 64], F32, "ExternalInput")
    usc = dram(io, nc, "usc", [128, 64], F32, "ExternalInput")
    dscr = dram(io, nc, "dsc_row", [1, D], F32, "ExternalInput")
    identi = dram(io, nc, "ident_bf", [128, 128], BF16, "ExternalInput")
    identfi = dram(io, nc, "ident_f", [128, 128], F32, "ExternalInput")
    onesri = dram(io, nc, "ones_row", [1, 512], BF16, "ExternalInput")
    onesci = dram(io, nc, "ones_col", [128, 1], BF16, "ExternalInput")
    y = dram(io, nc, "y", [TOK, D], F32, "ExternalOutput")
    acc = dram(io, nc, "acc_scr", [TOK, D], F32, "Internal")

    ident, ident_b = L.sb("identt", [128, 128], BF16)
    identf, identf_b = L.sb("identtf", [128, 128], F32)
    onesr, onesr_b = L.sb("onesr", [1, 512], BF16)
    onesc, onesc_b = L.sb("onesc", [128, 1], BF16)
    onesrf, onesrf_b = L.sb("onesrf", [1, 128], F32)
    gscs, gscs_b = L.sb("gscs", [128, 64], F32)
    uscs, uscs_b = L.sb("uscs", [128, 64], F32)
    dsb, dsb_b = L.sb("ds_bc", [128, D], F32)
    dsrow, dsrow_b = L.sb("dsrow", [1, D], F32)
    xq = [L.sb(f"xq{i}", [128, D], F32) for i in range(2)]
    t3f, t3f_b = L.sb("t3f", [128, D], F32)
    t3bf, t3bf_b = L.sb("t3bf", [128, D], BF16)
    stats, stats_b = L.sb("stats", [128, 32], F32)
    t3T, t3T_b = L.sb("t3T", [128, DC * TOK], BF16)
    rsneg, rsneg_b = L.sb("rsneg", [1, 2 * TOK], BF16)
    rrow, rrow_b = L.sb("rrow", [1, 128], F32)
    wgu = [L.sb(f"wgu{i}", [128, 16 * 128], BF16) for i in range(4)]
    wdt = [L.sb(f"wdt{i}", [128, 512], BF16) for i in range(2)]
    hbuf, hbuf_b = L.sb("hbuf", [128, 8 * TOK], BF16)
    gtmp, gtmp_b = L.sb("gtmp", [128, 512], F32)
    utmp, utmp_b = L.sb("utmp", [128, 512], F32)
    sgt, sgt_b = L.sb("sgt", [128, 512], F32)
    hs, hs_b = L.sb("hs", [1, TOK], F32)
    hstmp, hstmp_b = L.sb("hstmp", [1, 512], F32)
    hcol, hcol_b = L.sb("hcol", [128, 1], F32)
    dtile, dtile_b = L.sb("dtile", [128, 512], F32)
    outt, outt_b = L.sb("outt", [128, D], F32)
    yt, yt_b = L.sb("yt", [128, D], F32)

    acc_bufs = [[Buf() for _ in range(4)] for _ in range(NT)]

    P.emit("sync", lambda e: e.dma_start(out=ident[:], in_=identi[:]), writes=(ident_b,), dma=True)
    P.emit("sync", lambda e: e.dma_start(out=identf[:], in_=identfi[:]), writes=(identf_b,), dma=True)
    P.emit("sync", lambda e: e.dma_start(out=onesr[:], in_=onesri[:]), writes=(onesr_b,), dma=True)
    P.emit("sync", lambda e: e.dma_start(out=onesc[:], in_=onesci[:]), writes=(onesc_b,), dma=True)
    P.emit("sync", lambda e: e.dma_start(out=gscs[:], in_=gsc[:]), writes=(gscs_b,), dma=True)
    P.emit("sync", lambda e: e.dma_start(out=uscs[:], in_=usc[:]), writes=(uscs_b,), dma=True)
    P.emit("sync", lambda e: e.dma_start(out=dsrow[:], in_=dscr[:]), writes=(dsrow_b,), dma=True)
    P.emit("vector", lambda e: e.memset(onesrf[:], 1.0), writes=(onesrf_b,))
    P.emit("vector", lambda e: e.memset(hs[:], 0.0), writes=(hs_b,))

    for i in range(4):
        bank, bank_b = L.ps[4], L.ps_buf[4]
        P.emit("tensor", (lambda i, bank: lambda e: e.matmul(
            bank[:], onesrf[:], dsrow[:, i * 512:(i + 1) * 512], start=True, stop=True))(i, bank),
            reads=(onesrf_b, dsrow_b), writes=(bank_b,))
        P.emit("vector", (lambda i, bank: lambda e: e.tensor_copy(
            dsb[:, i * 512:(i + 1) * 512], bank[:]))(i, bank),
            reads=(bank_b,), writes=(dsb_b,))

    for qt in range(NT):
        xi, xi_b = xq[qt % 2]
        P.emit("sync", (lambda qt, xi: lambda e: e.dma_start(
            out=xi[:], in_=x3[qt * 128:(qt + 1) * 128, :]))(qt, xi),
            writes=(xi_b,), dma=True)
        emit_ln_tile(L, t3f, t3f_b, xi, xi_b, stats, stats_b)
        P.emit("vector", lambda e: e.tensor_copy(t3bf[:], t3f[:]),
               reads=(t3f_b,), writes=(t3bf_b,))
        P.emit("vector", lambda e: e.tensor_reduce(
            stats[:, 29:30], t3bf[:], AX.X, ALU.add),
            reads=(t3bf_b,), writes=(stats_b,))
        bank, bank_b = L.ps[5], L.ps_buf[5]
        P.emit("tensor", (lambda bank: lambda e: e.matmul(
            bank[0:1, 0:128], stats[:, 29:30], identf[:], start=True, stop=True))(bank),
            reads=(stats_b, identf_b), writes=(bank_b,))
        P.emit("vector", (lambda bank: lambda e: e.tensor_scalar(
            rrow[:], bank[0:1, 0:128], -128.0, None, ALU.mult))(bank),
            reads=(bank_b,), writes=(rrow_b,))
        P.emit("vector", (lambda qt: lambda e: e.tensor_copy(
            rsneg[0:1, qt * 128:(qt + 1) * 128], rrow[:]))(qt),
            reads=(rrow_b,), writes=(rsneg_b,))
        P.emit("vector", (lambda qt: lambda e: e.tensor_tensor(
            rsneg[0:1, TOK + qt * 128:TOK + (qt + 1) * 128], rrow[:],
            rsneg[0:1, qt * 128:(qt + 1) * 128], ALU.subtract))(qt),
            reads=(rrow_b, rsneg_b), writes=(rsneg_b,))
        emit_transposes(L, t3T, t3T_b, t3bf, t3bf_b, ident, ident_b,
                        [(dc_ * 128, (dc_ * NT + qt) * 128) for dc_ in range(DC)])

    for bb in range(8):
        for oc in range(8):
            ocg = bb * 8 + oc
            specs = ((wg, gscs, gscs_b, gtmp, gtmp_b), (wu, uscs, uscs_b, utmp, utmp_b))
            for gi, (w_dram, sc_s, sc_b, tmp, tmp_b) in enumerate(specs):
                wt, wt_b = wgu[(oc % 2) * 2 + gi]
                P.emit("gpsimd", (lambda ocg, w_dram, wt: lambda e: e.dma_start(
                    out=wt[:], in_=w_dram[ocg]))(ocg, w_dram, wt),
                    writes=(wt_b,), dma=True)
            for qg in range(4):
                for gi, (w_dram, sc_s, sc_b, tmp, tmp_b) in enumerate(specs):
                    wt, wt_b = wgu[(oc % 2) * 2 + gi]
                    bank, bank_b = L.ps[(oc % 2) * 2 + gi], L.ps_buf[(oc % 2) * 2 + gi]
                    P.emit("tensor", (lambda qg, bank: lambda e: e.matmul(
                        bank[:], onesr[0:1, 0:128],
                        rsneg[0:1, qg * 512:(qg + 1) * 512],
                        start=True, stop=False))(qg, bank),
                        reads=(onesr_b, rsneg_b), writes=(bank_b,))
                    P.emit("tensor", (lambda qg, bank: lambda e: e.matmul(
                        bank[:], onesr[0:1, 0:128],
                        rsneg[0:1, TOK + qg * 512:TOK + (qg + 1) * 512],
                        start=False, stop=False))(qg, bank),
                        reads=(onesr_b, rsneg_b), writes=(bank_b,))
                    for dc_ in range(DC):
                        P.emit("tensor", (lambda qg, dc_, wt, bank: lambda e: e.matmul(
                            bank[:], wt[:, dc_ * 128:(dc_ + 1) * 128],
                            t3T[:, (dc_ * 4 + qg) * 512:(dc_ * 4 + qg + 1) * 512],
                            start=False, stop=(dc_ == DC - 1)))(qg, dc_, wt, bank),
                            reads=(wt_b, t3T_b), writes=(bank_b,))
                    P.emit("vector", (lambda ocg, bank, tmp, sc_s, sc_b: lambda e: e.tensor_scalar(
                        tmp[:], bank[:], sc_s[:, ocg:ocg + 1], None, ALU.mult))(ocg, bank, tmp, sc_s, sc_b),
                        reads=(bank_b, sc_b), writes=(tmp_b,))
                P.emit("scalar", lambda e: e.activation(sgt[:], gtmp[:], AF.Silu),
                       reads=(gtmp_b,), writes=(sgt_b,))
                P.emit("vector", (lambda oc, qg: lambda e: e.tensor_tensor(
                    hbuf[:, (oc * 4 + qg) * 512:(oc * 4 + qg + 1) * 512],
                    sgt[:], utmp[:], ALU.mult))(oc, qg),
                    reads=(sgt_b, utmp_b), writes=(hbuf_b,))
            bank, bank_b = L.ps[4], L.ps_buf[4]
            for qg in range(4):
                P.emit("tensor", (lambda oc, qg, bank: lambda e: e.matmul(
                    bank[0:1, 0:512], onesc[:],
                    hbuf[:, (oc * 4 + qg) * 512:(oc * 4 + qg + 1) * 512],
                    start=True, stop=True))(oc, qg, bank),
                    reads=(onesc_b, hbuf_b), writes=(bank_b,))
                P.emit("vector", (lambda qg, bank: lambda e: e.tensor_copy(
                    hstmp[:], bank[0:1, 0:512]))(qg, bank),
                    reads=(bank_b,), writes=(hstmp_b,))
                P.emit("vector", (lambda qg: lambda e: e.tensor_tensor(
                    hs[0:1, qg * 512:(qg + 1) * 512],
                    hs[0:1, qg * 512:(qg + 1) * 512], hstmp[:], ALU.add))(qg),
                    reads=(hs_b, hstmp_b), writes=(hs_b,))
        for qgrp in range(4):
            for ds_ in range(4):
                for oc in range(8):
                    wt, wt_b = wdt[oc % 2]
                    ocg = bb * 8 + oc
                    P.emit("gpsimd", (lambda ocg, ds_, wt: lambda e: e.dma_start(
                        out=wt[:], in_=wd[ocg, :, ds_ * 512:(ds_ + 1) * 512]))(ocg, ds_, wt),
                        writes=(wt_b,), dma=True)
                    for jj in range(4):
                        qt = qgrp * 4 + jj
                        bank, bank_b = L.ps[jj], L.ps_buf[jj]
                        P.emit("tensor", (lambda oc, qt, jj, wt, bank: lambda e: e.matmul(
                            bank[:], hbuf[:, oc * TOK + qt * 128:oc * TOK + (qt + 1) * 128],
                            wt[:], start=(oc == 0), stop=(oc == 7)))(oc, qt, jj, wt, bank),
                            reads=(hbuf_b, wt_b), writes=(bank_b,))
                for jj in range(4):
                    qt = qgrp * 4 + jj
                    bank, bank_b = L.ps[jj], L.ps_buf[jj]
                    ab = acc_bufs[qt][ds_]
                    if bb == 0:
                        P.emit("vector", (lambda bank: lambda e: e.tensor_copy(
                            dtile[:], bank[:]))(bank),
                            reads=(bank_b,), writes=(dtile_b,))
                    else:
                        P.emit("sync", (lambda qt, ds_: lambda e: e.dma_start(
                            out=dtile[:], in_=acc[qt * 128:(qt + 1) * 128,
                                                  ds_ * 512:(ds_ + 1) * 512]))(qt, ds_),
                            reads=(ab,), writes=(dtile_b,), dma=True)
                        P.emit("vector", (lambda bank: lambda e: e.tensor_tensor(
                            dtile[:], dtile[:], bank[:], ALU.add))(bank),
                            reads=(dtile_b, bank_b), writes=(dtile_b,))
                    P.emit("sync", (lambda qt, ds_: lambda e: e.dma_start(
                        out=acc[qt * 128:(qt + 1) * 128, ds_ * 512:(ds_ + 1) * 512],
                        in_=dtile[:]))(qt, ds_),
                        reads=(dtile_b,), writes=(ab,), dma=True)

    store_ops = []
    for qt in range(NT):
        xi, xi_b = xq[qt % 2]
        P.emit("sync", (lambda qt, xi: lambda e: e.dma_start(
            out=xi[:], in_=x3[qt * 128:(qt + 1) * 128, :]))(qt, xi),
            writes=(xi_b,), dma=True)
        bank, bank_b = L.ps[5], L.ps_buf[5]
        P.emit("tensor", (lambda qt, bank: lambda e: e.matmul(
            bank[:, 0:128], hs[0:1, qt * 128:(qt + 1) * 128], onesrf[:],
            start=True, stop=True))(qt, bank),
            reads=(hs_b, onesrf_b), writes=(bank_b,))
        P.emit("vector", (lambda bank: lambda e: e.tensor_scalar(
            hcol[:], bank[:, 0:1], -128.0, None, ALU.mult))(bank),
            reads=(bank_b,), writes=(hcol_b,))
        for ds_ in range(4):
            ab = acc_bufs[qt][ds_]
            P.emit("sync", (lambda qt, ds_: lambda e: e.dma_start(
                out=dtile[:], in_=acc[qt * 128:(qt + 1) * 128,
                                      ds_ * 512:(ds_ + 1) * 512]))(qt, ds_),
                reads=(ab,), writes=(dtile_b,), dma=True)
            P.emit("vector", (lambda ds_: lambda e: e.tensor_scalar(
                outt[:, ds_ * 512:(ds_ + 1) * 512], dtile[:], hcol[:], None,
                ALU.add))(ds_),
                reads=(dtile_b, hcol_b), writes=(outt_b,))
        P.emit("vector", lambda e: e.tensor_tensor(outt[:], outt[:], dsb[:], ALU.mult),
               reads=(outt_b, dsb_b), writes=(outt_b,))
        emit_ln_tile(L, yt, yt_b, outt, outt_b, stats, stats_b,
                     out_kind="resid", res=xi, res_b=xi_b)
        store_ops.append(P.emit("sync", (lambda qt: lambda e: e.dma_start(
            out=y[qt * 128:(qt + 1) * 128, :], in_=yt[:]))(qt),
            reads=(yt_b,), dma=True))

    P.emit("sync", lambda e: e.nop(), deps=store_ops)
    if fin:
        P.finalize(nc, es)
    return nc


def _const_inputs():
    return dict(ident_bf=np.eye(128, dtype=bf16),
                ident_f=np.eye(128, dtype=np.float32),
                ones_row=np.ones((1, 512), dtype=bf16),
                ones_col=np.ones((128, 1), dtype=bf16))


def host_prep_l1_ffn(inputs):
    gw = inputs["L1_ffn_gate_w"]
    uw = inputs["L1_ffn_up_w"]
    dw = inputs["L1_ffn_down_w"]
    def _gu(w):
        wr = w.reshape(64, 128, 16, 128)
        return np.ascontiguousarray(wr.transpose(0, 3, 2, 1)).reshape(64, 128, 16 * 128)
    wgA = _gu(gw)
    wuA = _gu(uw)
    wdA = np.ascontiguousarray(dw.T).reshape(64, 128, D)
    gsc = np.ascontiguousarray(inputs["L1_ffn_gate_scale"].reshape(64, 128).T)
    usc = np.ascontiguousarray(inputs["L1_ffn_up_scale"].reshape(64, 128).T)
    dsc = np.ascontiguousarray(inputs["L1_ffn_down_scale"].reshape(1, D))
    d = dict(wg=wgA, wu=wuA, wd=wdA, gsc=gsc, usc=usc, dsc_row=dsc)
    d.update(_const_inputs())
    return d


def contiguous_rows(core):
    b, c = core // 4, core % 4
    r1 = np.arange(c * 1024, (c + 1) * 1024)
    r2 = np.arange((7 - c) * 1024, (8 - c) * 1024)
    return b, np.concatenate([r1, r2])


def interleaved_rows(core):
    b, c = core // 4, core % 4
    tiles = [63 - 4 * s - c for s in range(16)]
    rows = np.concatenate([np.arange(t * 128, (t + 1) * 128) for t in tiles])
    return b, rows


def kernel(**inputs):
    raise NotImplementedError("under construction")



def build_l1_qkv(nc, es, L=None, io=None):
    fin = L is None
    if L is None:
        L = LaunchCtx(nc, es)
    P = L.P

    x2 = dram(io, nc, "x2", [TOK, D], F32, "ExternalInput")
    wqA = dram(io, nc, "wqA", [16, 128, 16 * 128], F32, "ExternalInput")
    wkA = dram(io, nc, "wkA", [16, 128, 16 * 128], F32, "ExternalInput")
    wvB = dram(io, nc, "wvB", [16, 128, D], F32, "ExternalInput")
    bqP = dram(io, nc, "bqP", [128, 16], F32, "ExternalInput")
    bkP = dram(io, nc, "bkP", [128, 16], F32, "ExternalInput")
    bvR = dram(io, nc, "bv_row", [1, D], F32, "ExternalInput")
    rqc = dram(io, nc, "ropeq_c", [64, TOK], F32, "ExternalInput")
    rqs = dram(io, nc, "ropeq_s", [64, TOK], F32, "ExternalInput")
    rkc = dram(io, nc, "ropek_c", [64, TOK], F32, "ExternalInput")
    rks = dram(io, nc, "ropek_s", [64, TOK], F32, "ExternalInput")
    identi = dram(io, nc, "ident_bf", [128, 128], BF16, "ExternalInput")
    onesrfi = dram(io, nc, "ones_row_f", [1, 128], F32, "ExternalInput")
    qt_o = dram(io, nc, "qt_o", [D, TOK], BF16, "ExternalOutput")
    kt_o = dram(io, nc, "kt_o", [D, TOK], BF16, "ExternalOutput")
    v_o = dram(io, nc, "v_o", [TOK, D], BF16, "ExternalOutput")

    x2load = (io.get("x2_load") if io else None) or (
        lambda e, qt, dst: e.dma_start(out=dst, in_=x2[qt * 128:(qt + 1) * 128, :]))
    x2eng = (io.get("x2_eng") if io else None) or "sync"
    x2dep = io.get("x2_dep") if io else None
    ident, ident_b = L.sb("identt", [128, 128], BF16)
    onesrf, onesrf_b = L.sb("onesrf", [1, 128], F32)
    bvrow, bvrow_b = L.sb("bvrow", [1, D], F32)
    bvbc, bvbc_b = L.sb("bvbc", [128, D], F32)
    bqs, bqs_b = L.sb("bqs", [128, 16], F32)
    bks, bks_b = L.sb("bks", [128, 16], F32)
    rq = [L.sb(f"rq{i}", [128, TOK], F32) for i in range(2)]
    rk = [L.sb(f"rk{i}", [128, TOK], F32) for i in range(2)]
    xq = [L.sb(f"xq{i}", [128, D], F32) for i in range(2)]
    t2f, t2f_b = L.sb("t2f", [128, D], F32)
    t2bf, t2bf_b = L.sb("t2bf", [128, D], BF16)
    stats, stats_b = L.sb("stats", [128, 32], F32)
    t2T, t2T_b = L.sb("t2T", [128, DC * TOK], BF16)
    wqk = [L.sb(f"wqk{i}", [128, 16 * 128], BF16) for i in range(2)]
    wvt = [L.sb(f"wvt{i}", [128, 512], BF16) for i in range(2)]
    ptmp, ptmp_b = L.sb("ptmp", [128, 512], F32)
    r1, r1_b = L.sb("ropet", [64, 512], F32)
    r2, r2_b = L.sb("ropet2", [64, 512], F32)
    qko, qko_b = L.sb("qko", [128, 512], BF16)
    vsb, vsb_b = L.sb("vsb", [128, D], BF16)

    P.emit("sync", lambda e: e.dma_start(out=ident[:], in_=identi[:]), writes=(ident_b,), dma=True)
    P.emit("sync", lambda e: e.dma_start(out=onesrf[:], in_=onesrfi[:]), writes=(onesrf_b,), dma=True)
    P.emit("sync", lambda e: e.dma_start(out=bvrow[:], in_=bvR[:]), writes=(bvrow_b,), dma=True)
    P.emit("sync", lambda e: e.dma_start(out=bqs[:], in_=bqP[:]), writes=(bqs_b,), dma=True)
    P.emit("sync", lambda e: e.dma_start(out=bks[:], in_=bkP[:]), writes=(bks_b,), dma=True)
    for (tt_, ttb_), srct in ((rq[0], rqc), (rq[1], rqs), (rk[0], rkc), (rk[1], rks)):
        P.emit("sync", (lambda tt_, srct: lambda e: e.dma_start(
            out=tt_[0:64, :], in_=srct[:]))(tt_, srct), writes=(ttb_,), dma=True)
        P.emit("sync", (lambda tt_, srct: lambda e: e.dma_start(
            out=tt_[64:128, :], in_=srct[:]))(tt_, srct), writes=(ttb_,), dma=True)
    for i in range(4):
        bank, bank_b = L.ps[4], L.ps_buf[4]
        P.emit("tensor", (lambda i, bank: lambda e: e.matmul(
            bank[:], onesrf[:], bvrow[:, i * 512:(i + 1) * 512], start=True, stop=True))(i, bank),
            reads=(onesrf_b, bvrow_b), writes=(bank_b,))
        P.emit("vector", (lambda i, bank: lambda e: e.tensor_copy(
            bvbc[:, i * 512:(i + 1) * 512], bank[:]))(i, bank),
            reads=(bank_b,), writes=(bvbc_b,))

    for qt in range(NT):
        xi, xi_b = xq[qt % 2]
        P.emit(x2eng, (lambda qt, xi: lambda e: x2load(e, qt, xi[:]))(qt, xi),
               writes=(xi_b,), dma=True, deps=(x2dep,))
        emit_ln_tile(L, t2f, t2f_b, xi, xi_b, stats, stats_b)
        P.emit("vector", lambda e: e.tensor_copy(t2bf[:], t2f[:]),
               reads=(t2f_b,), writes=(t2bf_b,))
        emit_transposes(L, t2T, t2T_b, t2bf, t2bf_b, ident, ident_b,
                        [(dc_ * 128, (dc_ * NT + qt) * 128) for dc_ in range(DC)])

    store_ops = []
    for wi, (wA, bb, bb_b, rc, rs_, out_d) in enumerate(
            ((wqA, bqs, bqs_b, rq[0], rq[1], qt_o),
             (wkA, bks, bks_b, rk[0], rk[1], kt_o))):
        for oc in range(16):
            wt, wt_b = wqk[oc % 2]
            P.emit("gpsimd", (lambda oc, wA, wt: lambda e: e.dma_start(
                out=wt[:], in_=wA[oc]))(oc, wA, wt),
                writes=(wt_b,), dma=True)
            for qg in range(4):
                bank, bank_b = L.ps[wi], L.ps_buf[wi]
                for dc_ in range(DC):
                    P.emit("tensor", (lambda qg, dc_, wt, bank: lambda e: e.matmul(
                        bank[:], wt[:, dc_ * 128:(dc_ + 1) * 128],
                        t2T[:, (dc_ * 4 + qg) * 512:(dc_ * 4 + qg + 1) * 512],
                        start=(dc_ == 0), stop=(dc_ == DC - 1)))(qg, dc_, wt, bank),
                        reads=(wt_b, t2T_b), writes=(bank_b,))
                P.emit("vector", (lambda oc, bank, bb: lambda e: e.tensor_scalar(
                    ptmp[:], bank[:], bb[:, oc:oc + 1], None, ALU.add))(oc, bank, bb),
                    reads=(bank_b, bb_b), writes=(ptmp_b,))
                rcs, rcb = rc
                rss, rsb = rs_
                sl = slice(qg * 512, (qg + 1) * 512)
                P.emit("vector", (lambda sl, rcs: lambda e: e.tensor_tensor(
                    r1[:], ptmp[0:64, :], rcs[0:64, sl], ALU.mult))(sl, rcs),
                    reads=(ptmp_b, rcb), writes=(r1_b,))
                P.emit("vector", (lambda sl, rss: lambda e: e.tensor_tensor(
                    r2[:], ptmp[64:128, :], rss[64:128, sl], ALU.mult))(sl, rss),
                    reads=(ptmp_b, rsb), writes=(r2_b,))
                P.emit("vector", lambda e: e.tensor_tensor(
                    qko[0:64, :], r1[:], r2[:], ALU.subtract),
                    reads=(r1_b, r2_b), writes=(qko_b,))
                P.emit("vector", (lambda sl, rcs: lambda e: e.tensor_tensor(
                    r1[:], ptmp[64:128, :], rcs[64:128, sl], ALU.mult))(sl, rcs),
                    reads=(ptmp_b, rcb), writes=(r1_b,))
                P.emit("vector", (lambda sl, rss: lambda e: e.tensor_tensor(
                    r2[:], ptmp[0:64, :], rss[0:64, sl], ALU.mult))(sl, rss),
                    reads=(ptmp_b, rsb), writes=(r2_b,))
                P.emit("vector", lambda e: e.tensor_tensor(
                    qko[64:128, :], r1[:], r2[:], ALU.add),
                    reads=(r1_b, r2_b), writes=(qko_b,))
                store_ops.append(P.emit("sync", (lambda oc, qg, out_d: lambda e: e.dma_start(
                    out=out_d[oc * 128:(oc + 1) * 128, qg * 512:(qg + 1) * 512],
                    in_=qko[:]))(oc, qg, out_d),
                    reads=(qko_b,), dma=True))

    wvsl, wvsl_b = L.sb("wvsl", [128, 16 * 512], BF16)
    for oslab in range(4):
        for dc_ in range(DC):
            P.emit("gpsimd", (lambda oslab, dc_: lambda e: e.dma_start(
                out=wvsl[:, dc_ * 512:(dc_ + 1) * 512],
                in_=wvB[dc_, :, oslab * 512:(oslab + 1) * 512]))(oslab, dc_),
                writes=(wvsl_b,), dma=True)
        for qg in range(4):
            for dc_ in range(DC):
                for j in range(4):
                    qt = qg * 4 + j
                    bank, bank_b = L.ps[2 + j], L.ps_buf[2 + j]
                    P.emit("tensor", (lambda qt, dc_, j, bank: lambda e: e.matmul(
                        bank[:], t2T[:, (dc_ * NT + qt) * 128:(dc_ * NT + qt) * 128 + 128],
                        wvsl[:, dc_ * 512:(dc_ + 1) * 512],
                        start=(dc_ == 0), stop=(dc_ == DC - 1)))(qt, dc_, j, bank),
                        reads=(t2T_b, wvsl_b), writes=(bank_b,))
            for j in range(4):
                qt = qg * 4 + j
                bank, bank_b = L.ps[2 + j], L.ps_buf[2 + j]
                P.emit("vector", (lambda oslab, bank: lambda e: e.tensor_tensor(
                    vsb[:, oslab * 512:(oslab + 1) * 512], bank[:],
                    bvbc[:, oslab * 512:(oslab + 1) * 512], ALU.add))(oslab, bank),
                    reads=(bank_b, bvbc_b), writes=(vsb_b,))
                store_ops.append(P.emit("sync", (lambda qt, oslab: lambda e: e.dma_start(
                    out=v_o[qt * 128:(qt + 1) * 128, oslab * 512:(oslab + 1) * 512],
                    in_=vsb[:, oslab * 512:(oslab + 1) * 512]))(qt, oslab),
                    reads=(vsb_b,), dma=True))

    P.emit("sync", lambda e: e.nop(), deps=store_ops)
    if fin:
        P.finalize(nc, es)
    return nc


_ROPE_PERM = None


def _rope_perm():
    global _ROPE_PERM
    if _ROPE_PERM is None:
        p = []
        for h in range(16):
            p += [h * 128 + 2 * i for i in range(64)]
            p += [h * 128 + 2 * i + 1 for i in range(64)]
        _ROPE_PERM = np.array(p)
    return _ROPE_PERM


def host_prep_l1_qkv(inputs, core, rows=None):
    if rows is None:
        b, rows = interleaved_rows(core)
    perm = _rope_perm()
    Wq = inputs["L1_attn_Wq"][:, perm]
    Wk = inputs["L1_attn_Wk"][:, perm]
    Wv = inputs["L1_attn_Wv"]
    bq = inputs["L1_attn_bq"][perm]
    bk = inputs["L1_attn_bk"][perm]

    def _tileA(W):
        Wr = W.reshape(16, 128, 16, 128)
        return np.ascontiguousarray(Wr.transpose(2, 1, 0, 3)).reshape(16, 128, 16 * 128)

    wqA = _tileA(Wq)
    wkA = _tileA(Wk)
    wvB = np.ascontiguousarray(Wv.reshape(16, 128, D))
    bqP = np.ascontiguousarray(bq.reshape(16, 128).T)
    bkP = np.ascontiguousarray(bk.reshape(16, 128).T)
    bvR = inputs["L1_attn_bv"].reshape(1, D)
    pos = rows.astype(np.float32)
    i = np.arange(64, dtype=np.float32)
    theta = np.power(np.float32(10000.0), -2.0 * i / 128.0).astype(np.float32)
    ang = theta[:, None] * pos[None, :]
    c = np.cos(ang).astype(np.float32)
    s = np.sin(ang).astype(np.float32)
    sc = np.float32(1.0 / np.sqrt(128.0))
    return dict(wqA=wqA, wkA=wkA, wvB=wvB, bqP=bqP, bkP=bkP, bv_row=bvR,
                ropeq_c=c * sc, ropeq_s=s * sc, ropek_c=c, ropek_s=s,
                ident_bf=np.eye(128, dtype=bf16),
                ones_row_f=np.ones((1, 128), dtype=np.float32))



def build_l1_attn(nc, es, L=None, io=None):
    fin = L is None
    if L is None:
        L = LaunchCtx(nc, es)
    P = L.P

    qto = dram(io, nc, "qto", [D, TOK], BF16, "ExternalInput")
    ktg = dram(io, nc, "ktg", [4 * D, TOK], BF16, "ExternalInput")
    vg = dram(io, nc, "vg", [4 * TOK, D], BF16, "ExternalInput")
    kto = dram(io, nc, "kto", [D, TOK], BF16, "ExternalInput")
    vo = dram(io, nc, "vo", [TOK, D], BF16, "ExternalInput")
    x2 = dram(io, nc, "x2", [TOK, D], F32, "ExternalInput")
    woB = dram(io, nc, "woB", [16, 128, D], F32, "ExternalInput")
    boR = dram(io, nc, "bo_row", [1, D], F32, "ExternalInput")
    killR = dram(io, nc, "kill_slots", [1, 64 * 16], BF16, "ExternalInput")
    triA = dram(io, nc, "tri_add", [128, 128], F32, "ExternalInput")
    triM = dram(io, nc, "tri01T", [128, 128], BF16, "ExternalInput")
    identfi = dram(io, nc, "ident_f", [128, 128], F32, "ExternalInput")
    onesri = dram(io, nc, "ones_row", [1, 512], BF16, "ExternalInput")
    onesci = dram(io, nc, "ones_col", [128, 1], BF16, "ExternalInput")
    onesrfi = dram(io, nc, "ones_row_f", [1, 128], F32, "ExternalInput")
    x3 = dram(io, nc, "x3", [TOK, D], F32, "ExternalOutput")

    x2load = (io.get("x2_load") if io else None) or (
        lambda e, qt, dst: e.dma_start(out=dst, in_=x2[qt * 128:(qt + 1) * 128, :]))
    x2eng = (io.get("x2_eng") if io else None) or "sync"
    x2dep = io.get("x2_dep") if io else None
    ktg_v = ktg.rearrange("(r n) m -> r n m", r=4)
    vg_v = vg.rearrange("(r n) m -> r n m", r=4)

    identf, identf_b = L.sb("identtf", [128, 128], F32)
    onesr, onesr_b = L.sb("onesr", [1, 512], BF16)
    onesc, onesc_b = L.sb("onesc", [128, 1], BF16)
    onesrf, onesrf_b = L.sb("onesrf", [1, 128], F32)
    kills, kills_b = L.sb("kills", [1, 64 * 16], BF16)
    tria, tria_b = L.sb("tria", [128, 128], F32)
    trim, trim_b = L.sb("trim", [128, 128], BF16)
    borow, borow_b = L.sb("borow", [1, D], F32)
    bobc, bobc_b = L.sb("bobc", [128, D], F32)
    qh, qh_b = L.sb("qh", [128, TOK], BF16)
    ktoh, ktoh_b = L.sb("ktoh", [128, TOK], BF16)
    voh, voh_b = L.sb("voh", [128, TOK], BF16)
    _kth = L.sb("kth0", [128, 4 * TOK], BF16)
    kth2 = [_kth, _kth]
    _vh = L.sb("vh0", [128, 4 * TOK], BF16)
    vh2 = [_vh, _vh]
    msb, msb_b = L.sb("msb", [128, 16], F32)
    mtmp, mtmp_b = L.sb("mtmp", [128, 1], F32)
    mneg, mneg_b = L.sb("mneg", [1, TOK], BF16)
    mrow, mrow_b = L.sb("mrow", [1, 128], F32)
    krow, krow_b = L.sb("krow", [1, 512], BF16)
    sdt, sdt_b = L.sb("sdt", [128, 128], F32)
    pts2 = [L.sb(f"pts{i}", [128, 512], BF16) for i in range(2)]
    ptd, ptd_b = L.sb("ptd", [128, 128], BF16)
    lrow, lrow_b = L.sb("lrow", [1, 512], F32)
    linv, linv_b = L.sb("linv", [1, 512], F32)
    linvbc, linvbc_b = L.sb("linvbc", [128, 512], F32)
    otsb, otsb_b = L.sb("otsb", [128, 16 * TOK], BF16)
    wot = [L.sb(f"wot{i}", [128, 512], BF16) for i in range(2)]
    ao4, ao4_b = L.sb("ao4", [128, 4 * D], F32)
    xq = [L.sb("xq0", [128, D], F32)]
    xq = [xq[0], xq[0]]
    stats, stats_b = L.sb("stats", [128, 32], F32)
    yt, yt_b = L.sb("yt", [128, D], F32)

    for t, bb_, src in ((identf, identf_b, identfi), (onesr, onesr_b, onesri),
                        (onesc, onesc_b, onesci), (onesrf, onesrf_b, onesrfi),
                        (kills, kills_b, killR), (tria, tria_b, triA),
                        (trim, trim_b, triM), (borow, borow_b, boR)):
        P.emit("sync", (lambda t, src: lambda e: e.dma_start(out=t[:], in_=src[:]))(t, src),
               writes=(bb_,), dma=True)
    for i in range(4):
        bank, bank_b = L.ps[4], L.ps_buf[4]
        P.emit("tensor", (lambda i, bank: lambda e: e.matmul(
            bank[:], onesrf[:], borow[:, i * 512:(i + 1) * 512], start=True, stop=True))(i, bank),
            reads=(onesrf_b, borow_b), writes=(bank_b,))
        P.emit("vector", (lambda i, bank: lambda e: e.tensor_copy(
            bobc[:, i * 512:(i + 1) * 512], bank[:]))(i, bank),
            reads=(bank_b,), writes=(bobc_b,))

    for h in range(16):
        kth, kth_b = kth2[h % 2]
        vh, vh_b = vh2[h % 2]
        P.emit("sync", (lambda h: lambda e: e.dma_start(
            out=qh[:], in_=qto[h * 128:(h + 1) * 128, :]))(h),
            writes=(qh_b,), dma=True)
        P.emit("sync", (lambda h: lambda e: e.dma_start(
            out=ktoh[:], in_=kto[h * 128:(h + 1) * 128, :]))(h),
            writes=(ktoh_b,), dma=True)
        P.emit("sync", (lambda h: lambda e: e.dma_start(
            out=voh[:].rearrange("p (s d) -> p s d", s=16),
            in_=vo[:, h * 128:(h + 1) * 128]
            .rearrange("(s p) d -> p s d", p=128)))(h),
            writes=(voh_b,), dma=True)
        for ug in range(16):
            P.emit("sync", (lambda h, ug, kth: lambda e: e.dma_start(
                out=kth[:, ug * 512:(ug + 1) * 512].rearrange("n (r m) -> n r m", r=4),
                in_=ktg_v[:, h * 128:(h + 1) * 128, ug * 128:(ug + 1) * 128]
                .rearrange("r n m -> n r m")))(h, ug, kth),
                writes=(kth_b,), dma=True)
            P.emit("sync", (lambda h, ug, vh: lambda e: e.dma_start(
                out=vh[:, ug * 512:(ug + 1) * 512].rearrange("t (r d) -> t r d", r=4),
                in_=vg_v[:, ug * 128:(ug + 1) * 128, h * 128:(h + 1) * 128]
                .rearrange("r t d -> t r d")))(h, ug, vh),
                writes=(vh_b,), dma=True)

        for s_ in range(16):
            for ug in range(s_, 16):
                bank, bank_b = L.ps[ug % 2], L.ps_buf[ug % 2]
                P.emit("tensor", (lambda s_, ug, bank, kth: lambda e: e.matmul(
                    bank[:], qh[:, s_ * 128:(s_ + 1) * 128],
                    kth[:, ug * 512:(ug + 1) * 512], start=True, stop=True))(s_, ug, bank, kth),
                    reads=(qh_b, kth_b), writes=(bank_b,))
                if ug == s_:
                    P.emit("vector", (lambda s_, bank: lambda e: e.reduce_max(
                        msb[:, s_:s_ + 1], bank[:], AX.X))(s_, bank),
                        reads=(bank_b,), writes=(msb_b,))
                else:
                    P.emit("vector", (lambda bank: lambda e: e.reduce_max(
                        mtmp[:], bank[:], AX.X))(bank),
                        reads=(bank_b,), writes=(mtmp_b,))
                    P.emit("vector", (lambda s_: lambda e: e.tensor_max(
                        msb[:, s_:s_ + 1], msb[:, s_:s_ + 1], mtmp[:]))(s_),
                        reads=(msb_b, mtmp_b), writes=(msb_b,))
            bank, bank_b = L.ps[2], L.ps_buf[2]
            P.emit("tensor", (lambda s_, bank: lambda e: e.matmul(
                bank[:, 0:128], qh[:, s_ * 128:(s_ + 1) * 128],
                ktoh[:, s_ * 128:(s_ + 1) * 128], start=True, stop=True))(s_, bank),
                reads=(qh_b, ktoh_b), writes=(bank_b,))
            P.emit("vector", (lambda bank: lambda e: e.scalar_tensor_tensor(
                sdt[:], bank[:, 0:128], 1.0, tria[:], ALU.mult, ALU.add))(bank),
                reads=(bank_b, tria_b), writes=(sdt_b,))
            P.emit("vector", lambda e: e.reduce_max(mtmp[:], sdt[:], AX.X),
                   reads=(sdt_b,), writes=(mtmp_b,))
            P.emit("vector", (lambda s_: lambda e: e.tensor_max(
                msb[:, s_:s_ + 1], msb[:, s_:s_ + 1], mtmp[:]))(s_),
                reads=(msb_b, mtmp_b), writes=(msb_b,))
        for s_ in range(16):
            bank, bank_b = L.ps[5], L.ps_buf[5]
            P.emit("tensor", (lambda s_, bank: lambda e: e.matmul(
                bank[0:1, 0:128], msb[:, s_:s_ + 1], identf[:], start=True, stop=True))(s_, bank),
                reads=(msb_b, identf_b), writes=(bank_b,))
            P.emit("vector", (lambda s_, bank: lambda e: e.tensor_scalar(
                mneg[0:1, s_ * 128:(s_ + 1) * 128], bank[0:1, 0:128],
                -1.0, None, ALU.mult))(s_, bank),
                reads=(bank_b,), writes=(mneg_b,))

        for g in range(4):
            obank, obank_b = L.ps[2], L.ps_buf[2]
            lbank, lbank_b = L.ps[3], L.ps_buf[3]
            first = True
            for u in range(16 * g, 64):
                bank, bank_b = L.ps[u % 2], L.ps_buf[u % 2]
                if u <= 16 * g + 15:
                    P.emit("vector", (lambda u, g: lambda e: e.tensor_tensor(
                        krow[:].rearrange("a (j k) -> a j k", j=4),
                        mneg[0:1, g * 512:(g + 1) * 512].rearrange("a (j k) -> a j k", j=4),
                        kills[0:1, u * 16 + 4 * g:u * 16 + 4 * g + 4]
                        .unsqueeze(2).broadcast_to([1, 4, 128]),
                        ALU.add))(u, g),
                        reads=(mneg_b, kills_b), writes=(krow_b,))
                    rrow, rrow_b2 = krow, krow_b
                else:
                    rrow, rrow_b2 = mneg, mneg_b
                rsl = slice(0, 512) if rrow is krow else slice(g * 512, (g + 1) * 512)
                P.emit("tensor", (lambda bank, rrow, rsl: lambda e: e.matmul(
                    bank[:], onesr[0:1, 0:128], rrow[0:1, rsl],
                    start=True, stop=False))(bank, rrow, rsl),
                    reads=(onesr_b, rrow_b2), writes=(bank_b,))
                P.emit("tensor", (lambda u, g, bank, kth: lambda e: e.matmul(
                    bank[:], kth[:, u * 128:(u + 1) * 128],
                    qh[:, g * 512:(g + 1) * 512], start=False, stop=True))(u, g, bank, kth),
                    reads=(kth_b, qh_b), writes=(bank_b,))
                pts, pts_b = pts2[u % 2]
                P.emit("scalar", (lambda bank, pts: lambda e: e.activation(
                    pts[:], bank[:], AF.Exp))(bank, pts),
                    reads=(bank_b,), writes=(pts_b,))
                P.emit("tensor", (lambda u, first, obank, vh, pts: lambda e: e.matmul(
                    obank[:], vh[:, u * 128:(u + 1) * 128], pts[:],
                    start=first, stop=False))(u, first, obank, vh, pts),
                    reads=(vh_b, pts_b), writes=(obank_b,))
                P.emit("tensor", (lambda first, lbank, pts: lambda e: e.matmul(
                    lbank[0:1, 0:512], onesc[:], pts[:],
                    start=first, stop=False))(first, lbank, pts),
                    reads=(onesc_b, pts_b), writes=(lbank_b,))
                first = False
            for j in range(4):
                s_ = 4 * g + j
                bank, bank_b = L.ps[j % 2], L.ps_buf[j % 2]
                P.emit("tensor", (lambda s_, bank: lambda e: e.matmul(
                    bank[:, 0:128], onesr[0:1, 0:128],
                    mneg[0:1, s_ * 128:(s_ + 1) * 128], start=True, stop=False))(s_, bank),
                    reads=(onesr_b, mneg_b), writes=(bank_b,))
                P.emit("tensor", (lambda s_, bank: lambda e: e.matmul(
                    bank[:, 0:128], ktoh[:, s_ * 128:(s_ + 1) * 128],
                    qh[:, s_ * 128:(s_ + 1) * 128], start=False, stop=True))(s_, bank),
                    reads=(ktoh_b, qh_b), writes=(bank_b,))
                P.emit("scalar", (lambda bank: lambda e: e.activation(
                    ptd[:], bank[:, 0:128], AF.Exp))(bank),
                    reads=(bank_b,), writes=(ptd_b,))
                P.emit("vector", lambda e: e.tensor_tensor(
                    ptd[:], ptd[:], trim[:], ALU.mult),
                    reads=(ptd_b, trim_b), writes=(ptd_b,))
                P.emit("tensor", (lambda s_, j, obank: lambda e: e.matmul(
                    obank[:, j * 128:(j + 1) * 128], voh[:, s_ * 128:(s_ + 1) * 128],
                    ptd[:], start=False, stop=(j == 3)))(s_, j, obank),
                    reads=(voh_b, ptd_b), writes=(obank_b,))
                P.emit("tensor", (lambda j, lbank: lambda e: e.matmul(
                    lbank[0:1, j * 128:(j + 1) * 128], onesc[:], ptd[:],
                    start=False, stop=(j == 3)))(j, lbank),
                    reads=(onesc_b, ptd_b), writes=(lbank_b,))
            P.emit("vector", (lambda lbank: lambda e: e.tensor_copy(
                lrow[:], lbank[0:1, 0:512]))(lbank),
                reads=(lbank_b,), writes=(lrow_b,))
            P.emit("vector", lambda e: e.reciprocal(linv[:], lrow[:]),
                   reads=(lrow_b,), writes=(linv_b,))
            bcb, bcb_b = L.ps[4], L.ps_buf[4]
            P.emit("tensor", (lambda bcb: lambda e: e.matmul(
                bcb[:], onesrf[:], linv[:], start=True, stop=True))(bcb),
                reads=(onesrf_b, linv_b), writes=(bcb_b,))
            P.emit("vector", (lambda bcb: lambda e: e.tensor_copy(
                linvbc[:], bcb[:]))(bcb),
                reads=(bcb_b,), writes=(linvbc_b,))
            P.emit("vector", (lambda h, g, obank: lambda e: e.tensor_tensor(
                otsb[:, (h * 4 + g) * 512:(h * 4 + g + 1) * 512], obank[:],
                linvbc[:], ALU.mult))(h, g, obank),
                reads=(obank_b, linvbc_b), writes=(otsb_b,))

    store_ops = []
    for qg in range(4):
        for oslab in range(4):
            for dc_ in range(16):
                wt, wt_b = wot[dc_ % 2]
                P.emit("gpsimd", (lambda dc_, oslab, wt: lambda e: e.dma_start(
                    out=wt[:], in_=woB[dc_, :, oslab * 512:(oslab + 1) * 512]))(dc_, oslab, wt),
                    writes=(wt_b,), dma=True)
                for j in range(4):
                    qt = qg * 4 + j
                    bank, bank_b = L.ps[j], L.ps_buf[j]
                    P.emit("tensor", (lambda dc_, qt, wt, bank: lambda e: e.matmul(
                        bank[:], otsb[:, dc_ * TOK + qt * 128:dc_ * TOK + (qt + 1) * 128],
                        wt[:], start=(dc_ == 0), stop=(dc_ == 15)))(dc_, qt, wt, bank),
                        reads=(otsb_b, wt_b), writes=(bank_b,))
            for j in range(4):
                bank, bank_b = L.ps[j], L.ps_buf[j]
                P.emit("vector", (lambda j, oslab, bank: lambda e: e.tensor_tensor(
                    ao4[:, j * D + oslab * 512:j * D + (oslab + 1) * 512], bank[:],
                    bobc[:, oslab * 512:(oslab + 1) * 512], ALU.add))(j, oslab, bank),
                    reads=(bank_b, bobc_b), writes=(ao4_b,))
        for j in range(4):
            qt = qg * 4 + j
            xi, xi_b = xq[j % 2]
            P.emit(x2eng, (lambda qt, xi: lambda e: x2load(e, qt, xi[:]))(qt, xi),
                   writes=(xi_b,), dma=True, deps=(x2dep,))
            ao_sl, ao_sl_b = ao4, ao4_b

            def _mk_ln(j):
                pass
            n = 4
            for i in range(n):
                P.emit("vector", (lambda i, j: lambda e: e.bn_stats(
                    stats[:, i * 6:(i + 1) * 6],
                    ao4[:, j * D + i * 512:j * D + (i + 1) * 512]))(i, j),
                    reads=(ao4_b,), writes=(stats_b,))
            P.emit("vector", lambda e: e.bn_aggr(stats[:, 24:26], stats[:, 0:24]),
                   reads=(stats_b,), writes=(stats_b,))
            P.emit("vector", lambda e: e.tensor_scalar(stats[:, 26:27], stats[:, 24:25],
                                                       -1.0, None, ALU.mult),
                   reads=(stats_b,), writes=(stats_b,))
            P.emit("scalar", lambda e: e.activation(stats[:, 27:28], stats[:, 25:26],
                                                    AF.Sqrt, bias=L.consts[:, 0:1]),
                   reads=(stats_b, L.consts_b), writes=(stats_b,))
            P.emit("vector", lambda e: e.reciprocal(stats[:, 28:29], stats[:, 27:28]),
                   reads=(stats_b,), writes=(stats_b,))
            P.emit("vector", (lambda j: lambda e: e.tensor_scalar(
                yt[:], ao4[:, j * D:(j + 1) * D], stats[:, 26:27], stats[:, 28:29],
                ALU.add, ALU.mult))(j),
                reads=(ao4_b, stats_b), writes=(yt_b,))
            P.emit("vector", (lambda xi: lambda e: e.tensor_tensor(
                yt[:], yt[:], xi[:], ALU.add))(xi),
                reads=(yt_b, xi_b), writes=(yt_b,))
            store_ops.append(P.emit("sync", (lambda qt: lambda e: e.dma_start(
                out=x3[qt * 128:(qt + 1) * 128, :], in_=yt[:]))(qt),
                reads=(yt_b,), dma=True))

    P.emit("sync", lambda e: e.nop(), deps=store_ops)
    if fin:
        P.finalize(nc, es)
    return nc


def host_prep_l1_attn(inputs, core):
    b, c = core // 4, core % 4
    Wo = inputs["L1_attn_Wo"]
    woB = np.ascontiguousarray(Wo.reshape(16, 128, D))
    boR = inputs["L1_attn_bo"].reshape(1, D)
    kill = np.zeros((64, 16), dtype=np.float32)
    for u in range(64):
        for s_ in range(16):
            if u <= 4 * s_ + c:
                kill[u, s_] = NEG
    killR = kill.reshape(1, 64 * 16).astype(bf16)
    idx = np.arange(128)
    triA = np.where(idx[None, :] <= idx[:, None], 0.0, NEG).astype(np.float32)
    triM = (idx[:, None] <= idx[None, :]).astype(bf16)
    return dict(woB=woB, bo_row=boR, kill_slots=killR, tri_add=triA, tri01T=triM,
                ident_f=np.eye(128, dtype=np.float32),
                ones_row=np.ones((1, 512), dtype=bf16),
                ones_col=np.ones((128, 1), dtype=bf16),
                ones_row_f=np.ones((1, 128), dtype=np.float32))



def build_l0_attn(nc, es, L=None, io=None):
    fin = L is None
    if L is None:
        L = LaunchCtx(nc, es)
    P = L.P

    xo = dram(io, nc, "x_own", [TOK, D], F32, "ExternalInput")
    xh = dram(io, nc, "x_halo", [1024, D], F32, "ExternalInput")
    wqB = dram(io, nc, "wqB", [16, 128, D], F32, "ExternalInput")
    wkB = dram(io, nc, "wkB", [16, 128, 512], F32, "ExternalInput")
    wvB = dram(io, nc, "wvB", [16, 128, 512], F32, "ExternalInput")
    woB = dram(io, nc, "woB", [16, 128, D], F32, "ExternalInput")
    gainR = dram(io, nc, "gain_row", [1, 16], F32, "ExternalInput")
    roc = dram(io, nc, "rope_own_c", [TOK, 64], F32, "ExternalInput")
    ros = dram(io, nc, "rope_own_s", [TOK, 64], F32, "ExternalInput")
    rhc = dram(io, nc, "rope_halo_c", [1024, 64], F32, "ExternalInput")
    rhs_ = dram(io, nc, "rope_halo_s", [1024, 64], F32, "ExternalInput")
    mskI = dram(io, nc, "maskadd", [2, 8, 5, 128, 128], F32, "ExternalInput")
    identi = dram(io, nc, "ident_bf", [128, 128], BF16, "ExternalInput")
    onesrfi = dram(io, nc, "ones_row_f", [1, 128], F32, "ExternalInput")
    x1 = dram(io, nc, "x1", [TOK, D], F32, "ExternalOutput")

    ident, ident_b = L.sb("identt", [128, 128], BF16)
    onesrf, onesrf_b = L.sb("onesrf", [1, 128], F32)
    gains, gains_b = L.sb("gains", [1, 16], F32)
    gsc, gsc_b = L.sb("gsc", [128, 16], F32)
    wks, wks_b = L.sb("wks", [128, 16 * 512], BF16)
    wvs, wvs_b = L.sb("wvs", [128, 16 * 512], BF16)
    wqsl, wqsl_b = L.sb("wqsl", [128, 16 * 512], BF16)
    ropeo, ropeo_b = L.sb("ropeo", [128, 16 * 2 * 64], F32)
    ropeh, ropeh_b = L.sb("ropeh", [128, 8 * 2 * 64], F32)
    xi, xi_b = L.sb("xi", [128, D], F32)
    t0bf, t0bf_b = L.sb("t0bf", [128, D], BF16)
    stats, stats_b = L.sb("stats", [128, 32], F32)
    u16, u16_b = L.sb("u16", [128, 4096], BF16)
    tT, tT_b = u16[:, 0:2048], u16_b
    t0T = [L.sb(f"t0T{i}", [128, 16 * 128], BF16) for i in range(8)]
    ktblk, ktblk_b = L.sb("ktblk", [128, 4 * 12 * 128], BF16)
    vblk, vblk_b = L.sb("vblk", [128, 12 * 512], BF16)
    kvtmp, kvtmp_b = L.sb("kvtmp", [128, 512], F32)
    kvsq, kvsq_b = L.sb("kvsq", [128, 512], F32)
    kvss, kvss_b = L.sb("kvss", [128, 16], F32)
    krope, krope_b = L.sb("krope", [128, 512], BF16)
    qpart, qpart_b = L.sb("qpart", [128, 512], F32)
    qro, qro_b = L.sb("qro", [128, 512], BF16)
    qtos, qtos_b = u16[:, 0:4096], u16_b
    msk, msk_b = L.sb("msk", [128, 5 * 128], F32)
    ssb, ssb_b = L.sb("ssb", [128, 640], F32)
    mcol, mcol_b = L.sb("mcol", [128, 1], F32)
    lcol, lcol_b = L.sb("lcol", [128, 1], F32)
    licol, licol_b = L.sb("licol", [128, 1], F32)
    psb_, psb_b_ = L.sb("psb_", [128, 640], BF16)
    ptsb, ptsb_b = L.sb("ptsb", [128, 640], BF16)
    oblk, oblk_b = L.sb("oblk", [128, 8 * D], BF16)
    ot2, ot2_b = u16, u16_b
    wots = [L.sb(f"wots{i}", [128, 512], BF16) for i in range(2)]
    ao2, ao2_b = L.sb("ao2", [128, 2 * D], F32)
    yt, yt_b = u16[:].bitcast(F32), u16_b

    P.emit("sync", lambda e: e.dma_start(out=ident[:], in_=identi[:]), writes=(ident_b,), dma=True)
    P.emit("sync", lambda e: e.dma_start(out=onesrf[:], in_=onesrfi[:]), writes=(onesrf_b,), dma=True)
    P.emit("sync", lambda e: e.dma_start(out=gains[:], in_=gainR[:]), writes=(gains_b,), dma=True)
    bank, bank_b = L.ps[4], L.ps_buf[4]
    P.emit("tensor", (lambda bank: lambda e: e.matmul(
        bank[:, 0:16], onesrf[:], gains[:], start=True, stop=True))(bank),
        reads=(onesrf_b, gains_b), writes=(bank_b,))
    P.emit("vector", (lambda bank: lambda e: e.tensor_scalar(
        gsc[:], bank[:, 0:16], float(1.0 / np.sqrt(128.0)), None, ALU.mult))(bank),
        reads=(bank_b,), writes=(gsc_b,))
    for dc_ in range(DC):
        P.emit("gpsimd", (lambda dc_: lambda e: e.dma_start(
            out=wks[:, dc_ * 512:(dc_ + 1) * 512], in_=wkB[dc_]))(dc_),
            writes=(wks_b,), dma=True)
        P.emit("gpsimd", (lambda dc_: lambda e: e.dma_start(
            out=wvs[:, dc_ * 512:(dc_ + 1) * 512], in_=wvB[dc_]))(dc_),
            writes=(wvs_b,), dma=True)
    for dst_t, dst_tb, srcs, nt_ in ((ropeo, ropeo_b, (roc, ros), 16),
                                     (ropeh, ropeh_b, (rhc, rhs_), 8)):
        for ci, srct in enumerate(srcs):
            P.emit("sync", (lambda dst_t, srct, ci, nt_: lambda e: e.dma_start(
                out=dst_t[:].rearrange("p (t two f) -> p t two f", t=nt_, two=2)[:, :, ci, :],
                in_=srct[:].rearrange("(t p) f -> p t f", p=128)))(dst_t, srct, ci, nt_),
                writes=(dst_tb,), dma=True)

    store_ops = []
    for blk in range(2):
        for m in range(12):
            if m < 4:
                src = xh[blk * 512 + m * 128: blk * 512 + (m + 1) * 128, :]
            else:
                src = xo[blk * 1024 + (m - 4) * 128: blk * 1024 + (m - 3) * 128, :]
            P.emit("sync", (lambda src: lambda e: e.dma_start(out=xi[:], in_=src))(src),
                   writes=(xi_b,), dma=True)
            emit_ln_tile(L, t0bf, t0bf_b, xi, xi_b, stats, stats_b)
            if m >= 4:
                dstT, dstT_b = t0T[m - 4]
            else:
                dstT, dstT_b = tT, tT_b
            emit_transposes(L, dstT, dstT_b, t0bf, t0bf_b, ident, ident_b,
                            [(dc_ * 128, dc_ * 128) for dc_ in range(DC)])
            bank, bank_b = L.ps[4], L.ps_buf[4]
            for dc_ in range(DC):
                P.emit("tensor", (lambda dc_, dstT, bank: lambda e: e.matmul(
                    bank[:], dstT[:, dc_ * 128:(dc_ + 1) * 128],
                    wks[:, dc_ * 512:(dc_ + 1) * 512],
                    start=(dc_ == 0), stop=(dc_ == DC - 1)))(dc_, dstT, bank),
                    reads=(dstT_b, wks_b), writes=(bank_b,))
            P.emit("vector", (lambda bank: lambda e: e.tensor_copy(
                kvtmp[:], bank[:]))(bank), reads=(bank_b,), writes=(kvtmp_b,))
            P.emit("scalar", lambda e: e.activation(kvsq[:], kvtmp[:], AF.Square),
                   reads=(kvtmp_b,), writes=(kvsq_b,))
            P.emit("vector", lambda e: e.tensor_reduce(
                kvss[:, 0:4], kvsq[:].rearrange("p (h d) -> p h d", h=4), AX.X, ALU.add),
                reads=(kvsq_b,), writes=(kvss_b,))
            P.emit("scalar", lambda e: e.activation(
                kvss[:, 4:8], kvss[:, 0:4], AF.Sqrt, bias=L.consts[:, 1:2],
                scale=float(1.0 / 128.0)),
                reads=(kvss_b, L.consts_b), writes=(kvss_b,))
            P.emit("vector", lambda e: e.reciprocal(kvss[:, 8:12], kvss[:, 4:8]),
                   reads=(kvss_b,), writes=(kvss_b,))
            P.emit("vector", lambda e: e.tensor_tensor(
                kvtmp[:].rearrange("p (h d) -> p h d", h=4), kvtmp[:].rearrange("p (h d) -> p h d", h=4),
                kvss[:, 8:12].unsqueeze(2).broadcast_to([128, 4, 128]), ALU.mult),
                reads=(kvtmp_b, kvss_b), writes=(kvtmp_b,))
            rop, rop_b = (ropeh, ropeh_b) if m < 4 else (ropeo, ropeo_b)
            ti = (blk * 4 + m) if m < 4 else (blk * 8 + (m - 4))
            cview = rop[:, (2 * ti) * 64:(2 * ti + 1) * 64].unsqueeze(1).broadcast_to([128, 4, 64])
            sview = rop[:, (2 * ti + 1) * 64:(2 * ti + 2) * 64].unsqueeze(1).broadcast_to([128, 4, 64])
            x1v = kvtmp[:].rearrange("p (h two d) -> p h two d", h=4, two=2)[:, :, 0, :]
            x2v = kvtmp[:].rearrange("p (h two d) -> p h two d", h=4, two=2)[:, :, 1, :]
            k1v = krope[:].rearrange("p (h two d) -> p h two d", h=4, two=2)[:, :, 0, :]
            k2v = krope[:].rearrange("p (h two d) -> p h two d", h=4, two=2)[:, :, 1, :]
            r1v = kvsq[:].rearrange("p (h d) -> p h d", h=4)[:, :, 0:64]
            r2v = kvsq[:].rearrange("p (h d) -> p h d", h=4)[:, :, 64:128]
            P.emit("vector", (lambda x1v, cview: lambda e: e.tensor_tensor(
                r1v, x1v, cview, ALU.mult))(x1v, cview),
                reads=(kvtmp_b, rop_b), writes=(kvsq_b,))
            P.emit("vector", (lambda x2v, sview: lambda e: e.tensor_tensor(
                r2v, x2v, sview, ALU.mult))(x2v, sview),
                reads=(kvtmp_b, rop_b), writes=(kvsq_b,))
            P.emit("vector", (lambda k1v: lambda e: e.tensor_tensor(
                k1v, r1v, r2v, ALU.add))(k1v),
                reads=(kvsq_b,), writes=(krope_b,))
            P.emit("vector", (lambda x2v, cview: lambda e: e.tensor_tensor(
                r1v, x2v, cview, ALU.mult))(x2v, cview),
                reads=(kvtmp_b, rop_b), writes=(kvsq_b,))
            P.emit("vector", (lambda x1v, sview: lambda e: e.tensor_tensor(
                r2v, x1v, sview, ALU.mult))(x1v, sview),
                reads=(kvtmp_b, rop_b), writes=(kvsq_b,))
            P.emit("vector", (lambda k2v: lambda e: e.tensor_tensor(
                k2v, r1v, r2v, ALU.subtract))(k2v),
                reads=(kvsq_b,), writes=(krope_b,))
            emit_transposes(L, ktblk, ktblk_b, krope, krope_b, ident, ident_b,
                            [(kv * 128, (kv * 12 + m) * 128) for kv in range(4)])
            bank, bank_b = L.ps[5], L.ps_buf[5]
            for dc_ in range(DC):
                P.emit("tensor", (lambda dc_, dstT, bank: lambda e: e.matmul(
                    bank[:], dstT[:, dc_ * 128:(dc_ + 1) * 128],
                    wvs[:, dc_ * 512:(dc_ + 1) * 512],
                    start=(dc_ == 0), stop=(dc_ == DC - 1)))(dc_, dstT, bank),
                    reads=(dstT_b, wvs_b), writes=(bank_b,))
            P.emit("vector", (lambda m, bank: lambda e: e.tensor_copy(
                vblk[:, m * 512:(m + 1) * 512], bank[:]))(m, bank),
                reads=(bank_b,), writes=(vblk_b,))

        for oslab in range(4):
            for dc_ in range(DC):
                P.emit("gpsimd", (lambda dc_, oslab, wqsl: lambda e: e.dma_start(
                    out=wqsl[:, dc_ * 512:(dc_ + 1) * 512],
                    in_=wqB[dc_, :, oslab * 512:(oslab + 1) * 512]))(dc_, oslab, wqsl),
                    writes=(wqsl_b,), dma=True)
            for qt in range(8):
                dstT, dstT_b = t0T[qt]
                bank, bank_b = L.ps[3], L.ps_buf[3]
                for dc_ in range(DC):
                    P.emit("tensor", (lambda dc_, dstT, bank, wqsl: lambda e: e.matmul(
                        bank[:], dstT[:, dc_ * 128:(dc_ + 1) * 128],
                        wqsl[:, dc_ * 512:(dc_ + 1) * 512],
                        start=(dc_ == 0), stop=(dc_ == DC - 1)))(dc_, dstT, bank, wqsl),
                        reads=(dstT_b, wqsl_b), writes=(bank_b,))
                P.emit("vector", (lambda bank: lambda e: e.tensor_copy(
                    qpart[:], bank[:]))(bank), reads=(bank_b,), writes=(qpart_b,))
                P.emit("scalar", lambda e: e.activation(kvsq[:], qpart[:], AF.Square),
                       reads=(qpart_b,), writes=(kvsq_b,))
                P.emit("vector", lambda e: e.tensor_reduce(
                    kvss[:, 0:4], kvsq[:].rearrange("p (h d) -> p h d", h=4), AX.X, ALU.add),
                    reads=(kvsq_b,), writes=(kvss_b,))
                P.emit("scalar", lambda e: e.activation(
                    kvss[:, 4:8], kvss[:, 0:4], AF.Sqrt, bias=L.consts[:, 1:2],
                    scale=float(1.0 / 128.0)),
                    reads=(kvss_b, L.consts_b), writes=(kvss_b,))
                P.emit("vector", lambda e: e.reciprocal(kvss[:, 8:12], kvss[:, 4:8]),
                       reads=(kvss_b,), writes=(kvss_b,))
                P.emit("vector", lambda e: e.tensor_tensor(
                    qpart[:].rearrange("p (h d) -> p h d", h=4), qpart[:].rearrange("p (h d) -> p h d", h=4),
                    kvss[:, 8:12].unsqueeze(2).broadcast_to([128, 4, 128]), ALU.mult),
                    reads=(qpart_b, kvss_b), writes=(qpart_b,))
                ti = blk * 8 + qt
                cview = ropeo[:, (2 * ti) * 64:(2 * ti + 1) * 64].unsqueeze(1).broadcast_to([128, 4, 64])
                sview = ropeo[:, (2 * ti + 1) * 64:(2 * ti + 2) * 64].unsqueeze(1).broadcast_to([128, 4, 64])
                x1v = qpart[:].rearrange("p (h two d) -> p h two d", h=4, two=2)[:, :, 0, :]
                x2v = qpart[:].rearrange("p (h two d) -> p h two d", h=4, two=2)[:, :, 1, :]
                q1v = qro[:].rearrange("p (h two d) -> p h two d", h=4, two=2)[:, :, 0, :]
                q2v = qro[:].rearrange("p (h two d) -> p h two d", h=4, two=2)[:, :, 1, :]
                r1v = kvsq[:].rearrange("p (h d) -> p h d", h=4)[:, :, 0:64]
                r2v = kvsq[:].rearrange("p (h d) -> p h d", h=4)[:, :, 64:128]
                P.emit("vector", (lambda x1v, cview: lambda e: e.tensor_tensor(
                    r1v, x1v, cview, ALU.mult))(x1v, cview),
                    reads=(qpart_b, ropeo_b), writes=(kvsq_b,))
                P.emit("vector", (lambda x2v, sview: lambda e: e.tensor_tensor(
                    r2v, x2v, sview, ALU.mult))(x2v, sview),
                    reads=(qpart_b, ropeo_b), writes=(kvsq_b,))
                P.emit("vector", (lambda q1v: lambda e: e.tensor_tensor(
                    q1v, r1v, r2v, ALU.add))(q1v),
                    reads=(kvsq_b,), writes=(qro_b,))
                P.emit("vector", (lambda x2v, cview: lambda e: e.tensor_tensor(
                    r1v, x2v, cview, ALU.mult))(x2v, cview),
                    reads=(qpart_b, ropeo_b), writes=(kvsq_b,))
                P.emit("vector", (lambda x1v, sview: lambda e: e.tensor_tensor(
                    r2v, x1v, sview, ALU.mult))(x1v, sview),
                    reads=(qpart_b, ropeo_b), writes=(kvsq_b,))
                P.emit("vector", (lambda q2v: lambda e: e.tensor_tensor(
                    q2v, r1v, r2v, ALU.subtract))(q2v),
                    reads=(kvsq_b,), writes=(qro_b,))
                emit_transposes(L, qtos, qtos_b, qro, qro_b, ident, ident_b,
                                [(hl * 128, (hl * 8 + qt) * 128) for hl in range(4)])
            for qt in range(8):
                P.emit("sync", (lambda blk, qt: lambda e: e.dma_start(
                    out=msk[:].rearrange("p (r k) -> p r k", r=5),
                    in_=mskI[blk, qt].rearrange("r p k -> p r k")))(blk, qt),
                    writes=(msk_b,), dma=True)
                for hl in range(4):
                    h = oslab * 4 + hl
                    kv = h // 4
                    b0, b0_b = L.ps[0], L.ps_buf[0]
                    b1, b1_b = L.ps[1], L.ps_buf[1]
                    P.emit("tensor", (lambda hl, qt, kv, b0: lambda e: e.matmul(
                        b0[:], qtos[:, (hl * 8 + qt) * 128:(hl * 8 + qt + 1) * 128],
                        ktblk[:, (kv * 12 + qt) * 128:(kv * 12 + qt) * 128 + 512],
                        start=True, stop=True))(hl, qt, kv, b0),
                        reads=(qtos_b, ktblk_b), writes=(b0_b,))
                    P.emit("tensor", (lambda hl, qt, kv, b1: lambda e: e.matmul(
                        b1[:, 0:128], qtos[:, (hl * 8 + qt) * 128:(hl * 8 + qt + 1) * 128],
                        ktblk[:, (kv * 12 + qt + 4) * 128:(kv * 12 + qt + 5) * 128],
                        start=True, stop=True))(hl, qt, kv, b1),
                        reads=(qtos_b, ktblk_b), writes=(b1_b,))
                    P.emit("vector", (lambda h, b0: lambda e: e.scalar_tensor_tensor(
                        ssb[:, 0:512], b0[:], gsc[:, h:h + 1], msk[:, 0:512],
                        ALU.mult, ALU.add))(h, b0),
                        reads=(b0_b, gsc_b, msk_b), writes=(ssb_b,))
                    P.emit("vector", (lambda h, b1: lambda e: e.scalar_tensor_tensor(
                        ssb[:, 512:640], b1[:, 0:128], gsc[:, h:h + 1], msk[:, 512:640],
                        ALU.mult, ALU.add))(h, b1),
                        reads=(b1_b, gsc_b, msk_b), writes=(ssb_b,))
                    P.emit("vector", lambda e: e.reduce_max(mcol[:], ssb[:], AX.X),
                           reads=(ssb_b,), writes=(mcol_b,))
                    P.emit("vector", lambda e: e.tensor_scalar(
                        mcol[:], mcol[:], -1.0, None, ALU.mult),
                        reads=(mcol_b,), writes=(mcol_b,))
                    P.emit("scalar", lambda e: e.activation(
                        psb_[:], ssb[:], AF.Exp, bias=mcol[:], accum_out=lcol[:]),
                        reads=(ssb_b, mcol_b), writes=(psb_b_, lcol_b))
                    P.emit("vector", lambda e: e.reciprocal(licol[:], lcol[:]),
                           reads=(lcol_b,), writes=(licol_b,))
                    emit_transposes(L, ptsb, ptsb_b, psb_, psb_b_, ident, ident_b,
                                    [(w * 128, w * 128) for w in range(5)])
                    ob, ob_b = L.ps[2], L.ps_buf[2]
                    for w in range(5):
                        P.emit("tensor", (lambda w, qt, kv, ob: lambda e: e.matmul(
                            ob[:, 0:128], ptsb[:, w * 128:(w + 1) * 128],
                            vblk[:, (qt + w) * 512 + kv * 128:(qt + w) * 512 + (kv + 1) * 128],
                            start=(w == 0), stop=(w == 4)))(w, qt, kv, ob),
                            reads=(ptsb_b, vblk_b), writes=(ob_b,))
                    P.emit("vector", (lambda qt, h, ob: lambda e: e.tensor_scalar(
                        oblk[:, qt * D + h * 128:qt * D + (h + 1) * 128], ob[:, 0:128],
                        licol[:], None, ALU.mult))(qt, h, ob),
                        reads=(ob_b, licol_b), writes=(oblk_b,))

        for qgrp in range(4):
            for j in range(2):
                qt = qgrp * 2 + j
                emit_transposes(L, ot2, ot2_b, oblk, oblk_b, ident, ident_b,
                                [(qt * D + dc_ * 128, (j * 16 + dc_) * 128) for dc_ in range(DC)])
            for oslab in range(4):
                for dc_ in range(DC):
                    wt, wt_b = wots[dc_ % 2]
                    P.emit("gpsimd", (lambda dc_, oslab, wt: lambda e: e.dma_start(
                        out=wt[:], in_=woB[dc_, :, oslab * 512:(oslab + 1) * 512]))(dc_, oslab, wt),
                        writes=(wt_b,), dma=True)
                    for j in range(2):
                        bank, bank_b = L.ps[j], L.ps_buf[j]
                        P.emit("tensor", (lambda j, dc_, wt, bank: lambda e: e.matmul(
                            bank[:], ot2[:, (j * 16 + dc_) * 128:(j * 16 + dc_ + 1) * 128],
                            wt[:], start=(dc_ == 0), stop=(dc_ == DC - 1)))(j, dc_, wt, bank),
                            reads=(ot2_b, wt_b), writes=(bank_b,))
                for j in range(2):
                    bank, bank_b = L.ps[j], L.ps_buf[j]
                    P.emit("vector", (lambda j, oslab, bank: lambda e: e.tensor_copy(
                        ao2[:, j * D + oslab * 512:j * D + (oslab + 1) * 512], bank[:]))(j, oslab, bank),
                        reads=(bank_b,), writes=(ao2_b,))
            for j in range(2):
                qt = qgrp * 2 + j
                row0 = blk * 1024 + qt * 128
                P.emit("sync", (lambda row0: lambda e: e.dma_start(
                    out=xi[:], in_=xo[row0:row0 + 128, :]))(row0),
                    writes=(xi_b,), dma=True)
                for i in range(4):
                    P.emit("vector", (lambda i, j: lambda e: e.bn_stats(
                        stats[:, i * 6:(i + 1) * 6],
                        ao2[:, j * D + i * 512:j * D + (i + 1) * 512]))(i, j),
                        reads=(ao2_b,), writes=(stats_b,))
                P.emit("vector", lambda e: e.bn_aggr(stats[:, 24:26], stats[:, 0:24]),
                       reads=(stats_b,), writes=(stats_b,))
                P.emit("vector", lambda e: e.tensor_scalar(stats[:, 26:27], stats[:, 24:25],
                                                           -1.0, None, ALU.mult),
                       reads=(stats_b,), writes=(stats_b,))
                P.emit("scalar", lambda e: e.activation(stats[:, 27:28], stats[:, 25:26],
                                                        AF.Sqrt, bias=L.consts[:, 0:1]),
                       reads=(stats_b, L.consts_b), writes=(stats_b,))
                P.emit("vector", lambda e: e.reciprocal(stats[:, 28:29], stats[:, 27:28]),
                       reads=(stats_b,), writes=(stats_b,))
                P.emit("vector", (lambda j: lambda e: e.tensor_scalar(
                    yt[:], ao2[:, j * D:(j + 1) * D], stats[:, 26:27], stats[:, 28:29],
                    ALU.add, ALU.mult))(j),
                    reads=(ao2_b, stats_b), writes=(yt_b,))
                P.emit("vector", lambda e: e.tensor_tensor(yt[:], yt[:], xi[:], ALU.add),
                       reads=(yt_b, xi_b), writes=(yt_b,))
                store_ops.append(P.emit("sync", (lambda row0: lambda e: e.dma_start(
                    out=x1[row0:row0 + 128, :], in_=yt[:]))(row0),
                    reads=(yt_b,), dma=True))

    P.emit("sync", lambda e: e.nop(), deps=store_ops)
    if fin:
        P.finalize(nc, es)
    return nc


def host_prep_l0_attn(inputs, core):
    b, c = core // 4, core % 4
    blocks = [c, 7 - c]
    x = inputs["L0_attn_x"][b]
    xo = np.concatenate([x[j * 1024:(j + 1) * 1024] for j in blocks], axis=0)
    halos = []
    for j in blocks:
        lo = j * 1024 - 512
        if lo < 0:
            halos.append(np.zeros((512, D), np.float32))
        else:
            halos.append(x[lo:lo + 512])
    xh = np.concatenate(halos, axis=0)
    wqB = np.ascontiguousarray(inputs["L0_attn_Wq"].T.reshape(16, 128, D))
    wkB = np.ascontiguousarray(inputs["L0_attn_Wk"].T.reshape(16, 128, 512))
    wvB = np.ascontiguousarray(inputs["L0_attn_Wv"].T.reshape(16, 128, 512))
    woB = np.ascontiguousarray(inputs["L0_attn_Wo"].T.reshape(16, 128, D))
    gain = inputs["L0_attn_q_gain"].reshape(1, 16)
    inv_freq = (np.float32(10000.0) ** (-np.arange(0, 64, dtype=np.float32) / np.float32(64))).astype(np.float32)
    own_rows = np.concatenate([np.arange(j * 1024, (j + 1) * 1024) for j in blocks])
    halo_rows = np.concatenate([np.arange(max(0, j * 1024 - 512), max(0, j * 1024 - 512) + 512)
                                if j * 1024 - 512 >= 0 else np.zeros(512, np.int64)
                                for j in blocks])
    fo = own_rows.astype(np.float32)[:, None] * inv_freq[None, :]
    fh = halo_rows.astype(np.float32)[:, None] * inv_freq[None, :]
    mask = np.empty((2, 8, 5, 128, 128), np.float32)
    i_ = np.arange(128)
    for bi, j in enumerate(blocks):
        bstart = j * 1024
        for qt in range(8):
            q0 = bstart + qt * 128
            for w in range(5):
                k0 = bstart - 512 + (qt + w) * 128
                kk = k0 + i_[None, :]
                qq = q0 + i_[:, None]
                ok = (kk <= qq) & (kk >= qq - (WIN - 1)) & (kk >= 0)
                mask[bi, qt, w] = np.where(ok, 0.0, NEG)
    return dict(x_own=xo, x_halo=xh, wqB=wqB, wkB=wkB, wvB=wvB, woB=woB,
                gain_row=gain,
                rope_own_c=np.cos(fo).astype(np.float32),
                rope_own_s=np.sin(fo).astype(np.float32),
                rope_halo_c=np.cos(fh).astype(np.float32),
                rope_halo_s=np.sin(fh).astype(np.float32),
                maskadd=mask,
                ident_bf=np.eye(128, dtype=bf16),
                ones_row_f=np.ones((1, 128), dtype=np.float32))



def build_l0_ffn(nc, es, L=None, io=None):
    fin = L is None
    if L is None:
        L = LaunchCtx(nc, es)
    P = L.P

    x1 = dram(io, nc, "x1", [TOK, D], F32, "ExternalInput")
    w1 = dram(io, nc, "w1", [128, 128, D], F32, "ExternalInput")
    wp = dram(io, nc, "wp", [16, 128, FF0], F32, "ExternalInput")
    identi = dram(io, nc, "ident_bf", [128, 128], BF16, "ExternalInput")
    x2 = dram(io, nc, "x2", [TOK, D], F32, "ExternalOutput")
    w1hat = dram(io, nc, "w1hat", [2 * FF0, D], BF16, "Internal")
    wphat = dram(io, nc, "wphat", [D, FF0], BF16, "Internal")
    hscr = dram(io, nc, "hscr", [TOK, FF0], BF16, "Internal")

    ident, ident_b = L.sb("identt", [128, 128], BF16)
    uqs = [L.sb(f"uq{i}", [128, 2 * 2048], BF16) for i in range(2)]
    qsc = [L.sb(f"qsc{i}", [128, 96], F32) for i in range(2)]
    outt4, outt4_b2 = L.sb("outt4b", [128, 4 * D], F32)
    xi, xi_b = L.sb("xi", [128, D], F32)
    t1bf, t1bf_b = L.sb("t1bf", [128, D], BF16)
    stats, stats_b = L.sb("stats", [128, 32], F32)
    t1T, t1T_b = L.sb("t1T", [128, DC * TOK], BF16)
    gusl, gusl_b = L.sb("gusl", [128, 64 * 512], BF16)
    sg, sg_b = L.sb("sg", [128, 512], F32)
    hseg, hseg_b = L.sb("hseg", [128, 512], F32)
    hbf, hbf_b = L.sb("hbf", [128, 512], BF16)
    hsq, hsq_b = L.sb("hsq", [128, 512], F32)
    hcol, hcol_b = L.sb("hcol", [128, 1], F32)
    hss, hss_b = L.sb("hss", [128, 48], F32)
    wpt = [L.sb(f"wpt{i}", [128, 512], BF16) for i in range(2)]
    outt4_b = outt4_b2
    yt, yt_b = L.sb("yt", [128, D], F32)

    P.emit("sync", lambda e: e.dma_start(out=ident[:], in_=identi[:]), writes=(ident_b,), dma=True)
    P.emit("vector", lambda e: e.memset(hss[:, 0:16], 0.0), writes=(hss_b,))

    store_ops = []

    qn = [0]

    def quant_tile(src_ap, dst_ap):
        i = qn[0] % 2
        qn[0] += 1
        uq, uq_b = uqs[i]
        sc, sc_b = qsc[i]
        wqb = uq[:, 0:2048]
        q0 = uq[:, 2048:4096]
        asum = sc[:, 0:32]
        scl = sc[:, 32:64].bitcast(BF16)[:, 0:32]
        rec = sc[:, 64:96]
        P.emit("gpsimd", (lambda src_ap, wqb: lambda e: e.dma_start(
            out=wqb, in_=src_ap))(src_ap, wqb),
            writes=(uq_b,), dma=True)
        P.emit("vector", (lambda wqb, asum: lambda e: e.tensor_reduce(
            asum, wqb.rearrange("p (g k) -> p g k", k=64),
            AX.X, ALU.add, apply_absolute_value=True))(wqb, asum),
            reads=(uq_b,), writes=(sc_b,))
        P.emit("vector", (lambda asum, scl: lambda e: e.tensor_scalar(
            scl, asum, float(1.0 / 64.0), 1e-8, ALU.mult, ALU.max))(asum, scl),
            reads=(sc_b,), writes=(sc_b,))
        P.emit("vector", (lambda scl, rec: lambda e: e.reciprocal(rec, scl))(scl, rec),
               reads=(sc_b,), writes=(sc_b,))
        P.emit("vector", (lambda q0, wqb, rec: lambda e: e.tensor_tensor(
            q0.rearrange("p (g k) -> p g k", k=64),
            wqb.rearrange("p (g k) -> p g k", k=64),
            rec.unsqueeze(2).broadcast_to([128, 32, 64]), ALU.mult))(q0, wqb, rec),
            reads=(uq_b, sc_b), writes=(uq_b,))
        P.emit("vector", (lambda q0: lambda e: e.tensor_scalar(
            q0, q0, MAGIC, MAGIC, ALU.add, ALU.subtract))(q0),
            reads=(uq_b,), writes=(uq_b,))
        P.emit("vector", (lambda q0: lambda e: e.tensor_scalar(
            q0, q0, 1.0, -1.0, ALU.min, ALU.max))(q0),
            reads=(uq_b,), writes=(uq_b,))
        P.emit("vector", (lambda q0, scl: lambda e: e.tensor_tensor(
            q0.rearrange("p (g k) -> p g k", k=64),
            q0.rearrange("p (g k) -> p g k", k=64),
            scl.unsqueeze(2).broadcast_to([128, 32, 64]), ALU.mult))(q0, scl),
            reads=(uq_b, sc_b), writes=(uq_b,))
        return P.emit("sync", (lambda dst_ap, q0: lambda e: e.dma_start(
            out=dst_ap, in_=q0))(dst_ap, q0),
            reads=(uq_b,), dma=True)

    w1_ops = {}
    wp_ops = {}

    def quant_w1(ht):
        if ht not in w1_ops:
            w1_ops[ht] = quant_tile(w1[ht], w1hat[ht * 128:(ht + 1) * 128, :])

    def quant_wp(pt, k):
        if (pt, k) not in wp_ops:
            wp_ops[(pt, k)] = quant_tile(
                wp[pt][:, k * 2048:(k + 1) * 2048],
                wphat[pt * 128:(pt + 1) * 128, k * 2048:(k + 1) * 2048])

    for qt in range(NT):
        P.emit("sync", (lambda qt: lambda e: e.dma_start(
            out=xi[:], in_=x1[qt * 128:(qt + 1) * 128, :]))(qt),
            writes=(xi_b,), dma=True)
        emit_ln_tile(L, t1bf, t1bf_b, xi, xi_b, stats, stats_b)
        emit_transposes(L, t1T, t1T_b, t1bf, t1bf_b, ident, ident_b,
                        [(dc_ * 128, (dc_ * NT + qt) * 128) for dc_ in range(DC)])

    for k in range(2):
        for ht in range(4 * 0, 4 * 0 + 4):
            quant_w1(ht)
            quant_w1(64 + ht)
    for p in range(16):
        if p + 1 < 16:
            for ht in range(4 * (p + 1), 4 * (p + 1) + 4):
                quant_w1(ht)
                quant_w1(64 + ht)
        elif p == 15:
            for pt in range(16):
                for k in range(4):
                    quant_wp(pt, k)
        for gi in range(2):
            hbase = gi * FF0 + p * 512
            hts = [hbase // 128 + kk for kk in range(4)]
            tdeps = [w1_ops[t] for t in hts]
            for dc_ in range(DC):
                P.emit("scalar" if dc_ % 2 else "sync",
                       (lambda gi, dc_, hbase: lambda e: e.dma_start_transpose(
                           gusl[:, (gi * 16 + dc_) * 512:(gi * 16 + dc_ + 1) * 512],
                           w1hat[hbase:hbase + 512, dc_ * 128:(dc_ + 1) * 128]))(gi, dc_, hbase),
                       reads=(), writes=(gusl_b,), dma=True, deps=tdeps)
        for qt in range(NT):
            bg, bg_b = L.ps[0], L.ps_buf[0]
            bu, bu_b = L.ps[1], L.ps_buf[1]
            for dc_ in range(DC):
                P.emit("tensor", (lambda qt, dc_, bg: lambda e: e.matmul(
                    bg[:], t1T[:, (dc_ * NT + qt) * 128:(dc_ * NT + qt) * 128 + 128],
                    gusl[:, dc_ * 512:(dc_ + 1) * 512],
                    start=(dc_ == 0), stop=(dc_ == DC - 1)))(qt, dc_, bg),
                    reads=(t1T_b, gusl_b), writes=(bg_b,))
            for dc_ in range(DC):
                P.emit("tensor", (lambda qt, dc_, bu: lambda e: e.matmul(
                    bu[:], t1T[:, (dc_ * NT + qt) * 128:(dc_ * NT + qt) * 128 + 128],
                    gusl[:, (16 + dc_) * 512:(16 + dc_ + 1) * 512],
                    start=(dc_ == 0), stop=(dc_ == DC - 1)))(qt, dc_, bu),
                    reads=(t1T_b, gusl_b), writes=(bu_b,))
            P.emit("scalar", (lambda bg: lambda e: e.activation(
                sg[:], bg[:], AF.Silu))(bg),
                reads=(bg_b,), writes=(sg_b,))
            P.emit("vector", (lambda bu: lambda e: e.tensor_tensor(
                hseg[:], sg[:], bu[:], ALU.mult))(bu),
                reads=(sg_b, bu_b), writes=(hseg_b,))
            P.emit("scalar", lambda e: e.activation(hsq[:], hseg[:], AF.Square),
                   reads=(hseg_b,), writes=(hsq_b,))
            P.emit("vector", lambda e: e.tensor_reduce(hcol[:], hsq[:], AX.X, ALU.add),
                   reads=(hsq_b,), writes=(hcol_b,))
            P.emit("vector", (lambda qt: lambda e: e.tensor_tensor(
                hss[:, qt:qt + 1], hss[:, qt:qt + 1], hcol[:], ALU.add))(qt),
                reads=(hss_b, hcol_b), writes=(hss_b,))
            P.emit("vector", lambda e: e.tensor_copy(hbf[:], hseg[:]),
                   reads=(hseg_b,), writes=(hbf_b,))
            store_ops.append(P.emit("sync", (lambda qt, p: lambda e: e.dma_start(
                out=hscr[qt * 128:(qt + 1) * 128, p * 512:(p + 1) * 512],
                in_=hbf[:]))(qt, p),
                reads=(hbf_b,), dma=True))
    h_done = list(store_ops)
    store_ops = []

    P.emit("scalar", lambda e: e.activation(
        hss[:, 16:32], hss[:, 0:16], AF.Sqrt, bias=L.consts[:, 1:2],
        scale=float(1.0 / FF0)),
        reads=(hss_b, L.consts_b), writes=(hss_b,))
    P.emit("vector", lambda e: e.reciprocal(hss[:, 32:48], hss[:, 16:32]),
           reads=(hss_b,), writes=(hss_b,))

    for qg in range(4):
        for hc in range(64):
            P.emit("scalar" if hc % 2 else "sync",
                   (lambda qg, hc: lambda e: e.dma_start_transpose(
                       gusl[:, hc * 512:(hc + 1) * 512],
                       hscr[qg * 512:(qg + 1) * 512, hc * 128:(hc + 1) * 128]))(qg, hc),
                   writes=(gusl_b,), dma=True, deps=h_done)
        for ds_ in range(4):
            for hc in range(64):
                wt, wt_b = wpt[hc % 2]
                wdeps = [wp_ops[(pt, hc // 16)] for pt in range(4 * ds_, 4 * ds_ + 4)]
                P.emit("scalar", (lambda ds_, hc, wt: lambda e: e.dma_start_transpose(
                    wt[:], wphat[ds_ * 512:(ds_ + 1) * 512, hc * 128:(hc + 1) * 128]))(ds_, hc, wt),
                    writes=(wt_b,), dma=True, deps=wdeps)
                for j in range(4):
                    bank, bank_b = L.ps[j], L.ps_buf[j]
                    P.emit("tensor", (lambda hc, j, wt, bank: lambda e: e.matmul(
                        bank[:], gusl[:, hc * 512 + j * 128:hc * 512 + (j + 1) * 128],
                        wt[:], start=(hc == 0), stop=(hc == 63)))(hc, j, wt, bank),
                        reads=(gusl_b, wt_b), writes=(bank_b,))
            for j in range(4):
                qt = qg * 4 + j
                bank, bank_b = L.ps[j], L.ps_buf[j]
                P.emit("vector", (lambda qt, j, ds_, bank: lambda e: e.tensor_scalar(
                    outt4[:, j * D + ds_ * 512:j * D + (ds_ + 1) * 512], bank[:],
                    hss[:, 32 + qt:33 + qt], None, ALU.mult))(qt, j, ds_, bank),
                    reads=(bank_b, hss_b), writes=(outt4_b,))
        for j in range(4):
            qt = qg * 4 + j
            P.emit("sync", (lambda qt: lambda e: e.dma_start(
                out=xi[:], in_=x1[qt * 128:(qt + 1) * 128, :]))(qt),
                writes=(xi_b,), dma=True)
            for i2 in range(4):
                P.emit("vector", (lambda i2, j: lambda e: e.bn_stats(
                    stats[:, i2 * 6:(i2 + 1) * 6],
                    outt4[:, j * D + i2 * 512:j * D + (i2 + 1) * 512]))(i2, j),
                    reads=(outt4_b,), writes=(stats_b,))
            P.emit("vector", lambda e: e.bn_aggr(stats[:, 24:26], stats[:, 0:24]),
                   reads=(stats_b,), writes=(stats_b,))
            P.emit("vector", lambda e: e.tensor_scalar(stats[:, 26:27], stats[:, 24:25],
                                                       -1.0, None, ALU.mult),
                   reads=(stats_b,), writes=(stats_b,))
            P.emit("scalar", lambda e: e.activation(stats[:, 27:28], stats[:, 25:26],
                                                    AF.Sqrt, bias=L.consts[:, 0:1]),
                   reads=(stats_b, L.consts_b), writes=(stats_b,))
            P.emit("vector", lambda e: e.reciprocal(stats[:, 28:29], stats[:, 27:28]),
                   reads=(stats_b,), writes=(stats_b,))
            P.emit("vector", (lambda j: lambda e: e.tensor_scalar(
                yt[:], outt4[:, j * D:(j + 1) * D], stats[:, 26:27], stats[:, 28:29],
                ALU.add, ALU.mult))(j),
                reads=(outt4_b, stats_b), writes=(yt_b,))
            P.emit("vector", lambda e: e.tensor_tensor(yt[:], yt[:], xi[:], ALU.add),
                   reads=(yt_b, xi_b), writes=(yt_b,))
            store_ops.append(P.emit("sync", (lambda qt: lambda e: e.dma_start(
                out=x2[qt * 128:(qt + 1) * 128, :], in_=yt[:]))(qt),
                reads=(yt_b,), dma=True))

    P.emit("sync", lambda e: e.nop(), deps=store_ops)
    if fin:
        P.finalize(nc, es)
    return nc


def host_prep_l0_ffn(inputs):
    w1 = np.ascontiguousarray(inputs["L0_ffn_w_gate_up"].reshape(128, 128, D))
    wp = np.ascontiguousarray(inputs["L0_ffn_w_proj"].reshape(16, 128, FF0))
    return dict(w1=w1, wp=wp, ident_bf=np.eye(128, dtype=bf16))



_BUILT = {}


def _get_nc(name, builder):
    if name not in _BUILT:
        nc = bass.Bass("TRN2", target_bir_lowering=False, debug=False)
        es = ExitStack()
        es.__enter__()
        builder(nc, es)
        _BUILT[name] = (nc, es)
    return _BUILT[name][0]


def kernel_multi(**inputs):
    cores = list(range(8))

    nc = _get_nc("l0a", build_l0_attn)
    maps = [host_prep_l0_attn(inputs, c) for c in cores]
    r1 = run_bass_kernel_spmd(nc, maps, core_ids=cores).results
    x1s = [r["x1"] for r in r1]

    nc = _get_nc("l0f", build_l0_ffn)
    shared = host_prep_l0_ffn(inputs)
    maps = [dict(x1=x1s[c], **shared) for c in cores]
    r2 = run_bass_kernel_spmd(nc, maps, core_ids=cores).results
    x2s = [r["x2"] for r in r2]

    x2_full = np.empty((B, S, D), np.float32)
    for c in cores:
        b, rows = contiguous_rows(c)
        x2_full[b][rows] = x2s[c]
    x2i = []
    for c in cores:
        b, rows = interleaved_rows(c)
        x2i.append(np.ascontiguousarray(x2_full[b][rows]))

    nc = _get_nc("l1q", build_l1_qkv)
    maps = [dict(x2=x2i[c], **host_prep_l1_qkv(inputs, c)) for c in cores]
    r3 = run_bass_kernel_spmd(nc, maps, core_ids=cores).results
    qts = [r["qt_o"] for r in r3]
    kts = [r["kt_o"] for r in r3]
    vs = [r["v_o"] for r in r3]

    nc = _get_nc("l1a", build_l1_attn)
    maps = []
    for c in cores:
        b = c // 4
        ktg = np.concatenate([kts[4 * b + r] for r in range(4)], axis=0)
        vg = np.concatenate([vs[4 * b + r] for r in range(4)], axis=0)
        maps.append(dict(qto=qts[c], ktg=ktg, vg=vg, kto=kts[c], vo=vs[c],
                         x2=x2i[c], **host_prep_l1_attn(inputs, c)))
    r4 = run_bass_kernel_spmd(nc, maps, core_ids=cores).results
    x3s = [r["x3"] for r in r4]

    nc = _get_nc("l1f", build_l1_ffn)
    shared = host_prep_l1_ffn(inputs)
    maps = [dict(x3=x3s[c], **shared) for c in cores]
    r5 = run_bass_kernel_spmd(nc, maps, core_ids=cores).results

    out = np.empty((B, S, D), np.float32)
    for c in cores:
        b, rows = interleaved_rows(c)
        out[b][rows] = r5[c]["y"]
    return out



def _slot_src(s_):
    if s_ <= 7:
        r, pos = s_ // 2, s_ % 2
    else:
        if s_ % 2 == 1:
            r, pos = (15 - s_) // 2, 3
        else:
            r, pos = (14 - s_) // 2, 2
    return r * 512 + pos * 128


def build_fused(nc, es):
    L0 = LaunchCtx(nc, es)
    L0.arena = es.enter_context(nc.sbuf_tensor("arena", [128, 207 * 1024],
                                               mybir.dt.uint8))
    P = L0.P
    RG = [[0, 1, 2, 3], [4, 5, 6, 7]]
    E = "ExternalInput"

    def d(name, shape, dt_, kind="ExternalInput"):
        return nc.dram_tensor(name, shape, dt_, kind=kind).ap()

    ident_bf = d("ident_bf", [128, 128], BF16)
    ident_f = d("ident_f", [128, 128], F32)
    ones_row = d("ones_row", [1, 512], BF16)
    ones_col = d("ones_col", [128, 1], BF16)
    ones_row_f = d("ones_row_f", [1, 128], F32)

    io1 = dict(ident_bf=ident_bf, ones_row_f=ones_row_f,
               x_own=d("x_own", [TOK, D], F32), x_halo=d("x_halo", [1024, D], F32),
               wqB=d("wqB", [16, 128, D], F32), wkB=d("wkB", [16, 128, 512], F32),
               wvB=d("wvB", [16, 128, 512], F32), woB=d("woB", [16, 128, D], F32),
               gain_row=d("gain_row", [1, 16], F32),
               rope_own_c=d("rope_own_c", [TOK, 64], F32),
               rope_own_s=d("rope_own_s", [TOK, 64], F32),
               rope_halo_c=d("rope_halo_c", [1024, 64], F32),
               rope_halo_s=d("rope_halo_s", [1024, 64], F32),
               maskadd=d("maskadd", [2, 8, 5, 128, 128], F32))
    io2 = dict(ident_bf=ident_bf,
               w1=d("w1", [128, 128, D], F32), wp=d("wp", [16, 128, FF0], F32))
    io3 = dict(ident_bf=ident_bf, ones_row_f=ones_row_f,
               wqA=d("wqA", [16, 128, 16 * 128], F32),
               wkA=d("wkA", [16, 128, 16 * 128], F32),
               wvB=d("wvB1", [16, 128, D], F32),
               bqP=d("bqP", [128, 16], F32), bkP=d("bkP", [128, 16], F32),
               bv_row=d("bv_row", [1, D], F32),
               ropeq_c=d("ropeq_c", [64, TOK], F32),
               ropeq_s=d("ropeq_s", [64, TOK], F32),
               ropek_c=d("ropek_c", [64, TOK], F32),
               ropek_s=d("ropek_s", [64, TOK], F32))
    io4 = dict(ident_f=ident_f, ones_row=ones_row, ones_col=ones_col,
               ones_row_f=ones_row_f,
               woB=d("woB1", [16, 128, D], F32),
               bo_row=d("bo_row", [1, D], F32),
               kill_slots=d("kill_slots", [1, 64 * 16], BF16),
               tri_add=d("tri_add", [128, 128], F32),
               tri01T=d("tri01T", [128, 128], BF16))
    io5 = dict(ident_bf=ident_bf, ident_f=ident_f, ones_row=ones_row,
               ones_col=ones_col,
               wg=d("wg", [64, 128, 16 * 128], I32),
               wu=d("wu", [64, 128, 16 * 128], I32),
               wd=d("wd", [64, 128, D], I32),
               gsc=d("gsc", [128, 64], F32), usc=d("usc", [128, 64], F32),
               dsc_row=d("dsc_row", [1, D], F32))

    x1i = d("x1i", [TOK, D], F32, "Internal")
    x2in = d("cc_x2_in", [TOK, D], F32, "Internal")
    x2g = d("cc_x2_g", [4 * TOK, D], F32, "Internal")
    qtoi = d("qto_i", [D, TOK], BF16, "Internal")
    ktoi = d("kto_i", [D, TOK], BF16, "Internal")
    voi = d("vo_i", [TOK, D], BF16, "Internal")
    ktgi = d("ktg_i", [4 * D, TOK], BF16, "Internal")
    vgi = d("vg_i", [4 * TOK, D], BF16, "Internal")
    x3i = d("x3i", [TOK, D], F32, "Internal")
    yout = d("y", [TOK, D], F32, "ExternalOutput")

    x2idx = d("x2idx", [128, 16], I32)
    x2idxs = es.enter_context(nc.sbuf_tensor("x2idxs", [128, 16], I32))
    x2idxs_b = Buf()

    def x2_load(e, qt, dst):
        return e.indirect_dma_start(
            dst, None, x2g[:, :],
            bass.IndirectOffsetOnAxis(ap=x2idxs[:, qt:qt + 1], axis=0))

    io1["x1"] = x1i
    io2.update(x1=x1i, x2=x2in)
    io3.update(x2=x2g, qt_o=qtoi, kt_o=ktoi, v_o=voi)
    io4.update(qto=qtoi, ktg=ktgi, vg=vgi, kto=ktoi, vo=voi, x2=x2g, x3=x3i)
    io5.update(x3=x3i, y=yout)

    def stage(builder, io):
        Ls = LaunchCtx(nc, es, shared=L0)
        builder(nc, es, L=Ls, io=io)
        P.barrier()

    cdep = P.emit("sync", lambda e: e.dma_start(out=x2idxs[:], in_=x2idx[:]),
                  writes=(x2idxs_b,), dma=True)
    io3.update(x2_load=x2_load, x2_dep=cdep, x2_eng="gpsimd")
    io4.update(x2_load=x2_load, x2_dep=cdep, x2_eng="gpsimd")

    stage(build_l0_attn, io1)
    stage(build_l0_ffn, io2)
    P.emit("gpsimd", lambda e: e.collective_compute(
        "AllGather", ALU.bypass, RG,
        [x2in[:, :]], [x2g[:, :]]), dma=True, coll=True)
    P.barrier()
    stage(build_l1_qkv, io3)
    P.emit("gpsimd", lambda e: e.collective_compute(
        "AllGather", ALU.bypass, RG,
        [ktoi[:, :]], [ktgi[:, :]]), dma=True, coll=True)
    P.emit("gpsimd", lambda e: e.collective_compute(
        "AllGather", ALU.bypass, RG,
        [voi[:, :]], [vgi[:, :]]), dma=True, coll=True)
    P.barrier()
    stage(build_l1_attn, io4)
    stage(build_l1_ffn, io5)

    P.finalize(nc, es)
    return nc


def _a_const_host(s_):
    t0 = 63 - 4 * s_
    j = t0 // 8
    r = min(j, 7 - j)
    return r * 2048 + (0 if j == r else 1024) + (t0 - 8 * j) * 128


def host_prep_fused(inputs, core):
    c = core % 4
    idx = np.empty((128, 16), np.int32)
    for s_ in range(16):
        idx[:, s_] = _a_const_host(s_) - 128 * c + np.arange(128)
    m = {"x2idx": idx}
    m.update(host_prep_l0_attn(inputs, core))
    m.update(host_prep_l0_ffn(inputs))
    q = host_prep_l1_qkv(inputs, core)
    m["wvB1"] = q.pop("wvB")
    m.update(q)
    a = host_prep_l1_attn(inputs, core)
    m["woB1"] = a.pop("woB")
    m.update(a)
    m.update(host_prep_l1_ffn(inputs))
    return m


def kernel_fused(**inputs):
    cores = list(range(8))
    nc = _get_nc("fused", build_fused)
    maps = [host_prep_fused(inputs, c) for c in cores]
    res = run_bass_kernel_spmd(nc, maps, core_ids=cores).results
    out = np.empty((B, S, D), np.float32)
    for c in cores:
        b, rows = interleaved_rows(c)
        out[b][rows] = res[c]["y"]
    return out


def kernel(**inputs):
    return kernel_fused(**inputs)



def build_half_a(nc, es):
    L0 = LaunchCtx(nc, es)
    L0.arena = es.enter_context(nc.sbuf_tensor("arena", [128, 207 * 1024],
                                               mybir.dt.uint8))
    P = L0.P

    def d(name, shape, dt_, kind="ExternalInput"):
        return nc.dram_tensor(name, shape, dt_, kind=kind).ap()

    ident_bf = d("ident_bf", [128, 128], BF16)
    ones_row_f = d("ones_row_f", [1, 128], F32)
    io1 = dict(ident_bf=ident_bf, ones_row_f=ones_row_f,
               x_own=d("x_own", [TOK, D], F32), x_halo=d("x_halo", [1024, D], F32),
               wqB=d("wqB", [16, 128, D], F32), wkB=d("wkB", [16, 128, 512], F32),
               wvB=d("wvB", [16, 128, 512], F32), woB=d("woB", [16, 128, D], F32),
               gain_row=d("gain_row", [1, 16], F32),
               rope_own_c=d("rope_own_c", [TOK, 64], F32),
               rope_own_s=d("rope_own_s", [TOK, 64], F32),
               rope_halo_c=d("rope_halo_c", [1024, 64], F32),
               rope_halo_s=d("rope_halo_s", [1024, 64], F32),
               maskadd=d("maskadd", [2, 8, 5, 128, 128], F32))
    io2 = dict(ident_bf=ident_bf,
               w1=d("w1", [128, 128, D], F32), wp=d("wp", [16, 128, FF0], F32))
    io3 = dict(ident_bf=ident_bf, ones_row_f=ones_row_f,
               wqA=d("wqA", [16, 128, 16 * 128], F32),
               wkA=d("wkA", [16, 128, 16 * 128], F32),
               wvB=d("wvB1", [16, 128, D], F32),
               bqP=d("bqP", [128, 16], F32), bkP=d("bkP", [128, 16], F32),
               bv_row=d("bv_row", [1, D], F32),
               ropeq_c=d("ropeq_c", [64, TOK], F32),
               ropeq_s=d("ropeq_s", [64, TOK], F32),
               ropek_c=d("ropek_c", [64, TOK], F32),
               ropek_s=d("ropek_s", [64, TOK], F32))
    x1i = d("x1i", [TOK, D], F32, "Internal")
    x2o = d("x2", [TOK, D], F32, "ExternalOutput")
    io1["x1"] = x1i
    io2.update(x1=x1i, x2=x2o)
    io3.update(x2=x2o,
               qt_o=d("qt_o", [D, TOK], BF16, "ExternalOutput"),
               kt_o=d("kt_o", [D, TOK], BF16, "ExternalOutput"),
               v_o=d("v_o", [TOK, D], BF16, "ExternalOutput"))

    def stage(builder, io):
        Ls = LaunchCtx(nc, es, shared=L0)
        builder(nc, es, L=Ls, io=io)
        P.barrier()

    stage(build_l0_attn, io1)
    stage(build_l0_ffn, io2)
    stage(build_l1_qkv, io3)
    P.finalize(nc, es)
    return nc


def build_half_b(nc, es):
    L0 = LaunchCtx(nc, es)
    L0.arena = es.enter_context(nc.sbuf_tensor("arena", [128, 207 * 1024],
                                               mybir.dt.uint8))
    P = L0.P

    def d(name, shape, dt_, kind="ExternalInput"):
        return nc.dram_tensor(name, shape, dt_, kind=kind).ap()

    io4 = dict(ident_f=d("ident_f", [128, 128], F32),
               ones_row=d("ones_row", [1, 512], BF16),
               ones_col=d("ones_col", [128, 1], BF16),
               ones_row_f=d("ones_row_f", [1, 128], F32),
               woB=d("woB1", [16, 128, D], F32),
               bo_row=d("bo_row", [1, D], F32),
               kill_slots=d("kill_slots", [1, 64 * 16], BF16),
               tri_add=d("tri_add", [128, 128], F32),
               tri01T=d("tri01T", [128, 128], BF16),
               qto=d("qto", [D, TOK], BF16), ktg=d("ktg", [4 * D, TOK], BF16),
               vg=d("vg", [4 * TOK, D], BF16), kto=d("kto", [D, TOK], BF16),
               vo=d("vo", [TOK, D], BF16),
               x2=d("x2", [TOK, D], F32))
    io5 = dict(ident_bf=d("ident_bf", [128, 128], BF16),
               ident_f=io4["ident_f"], ones_row=io4["ones_row"],
               ones_col=io4["ones_col"],
               wg=d("wg", [64, 128, 16 * 128], I32),
               wu=d("wu", [64, 128, 16 * 128], I32),
               wd=d("wd", [64, 128, D], I32),
               gsc=d("gsc", [128, 64], F32), usc=d("usc", [128, 64], F32),
               dsc_row=d("dsc_row", [1, D], F32))
    x3i = d("x3i", [TOK, D], F32, "Internal")
    io4["x3"] = x3i
    io5.update(x3=x3i, y=d("y", [TOK, D], F32, "ExternalOutput"))

    def stage(builder, io):
        Ls = LaunchCtx(nc, es, shared=L0)
        builder(nc, es, L=Ls, io=io)
        P.barrier()

    stage(build_l1_attn, io4)
    stage(build_l1_ffn, io5)
    P.finalize(nc, es)
    return nc


def kernel_2launch(**inputs):
    cores = list(range(8))
    nc = _get_nc("halfa", build_half_a)
    maps = []
    for c in cores:
        m = {}
        m.update(host_prep_l0_attn(inputs, c))
        m.update(host_prep_l0_ffn(inputs))
        b, crows = contiguous_rows(c)
        q = host_prep_l1_qkv(inputs, c, rows=crows)
        m["wvB1"] = q.pop("wvB")
        m.update(q)
        maps.append(m)
    ra = run_bass_kernel_spmd(nc, maps, core_ids=cores).results

    x2f = np.empty((B, S, D), np.float32)
    qtf = np.empty((B, D, S), bf16)
    ktf = np.empty((B, D, S), bf16)
    vf = np.empty((B, S, D), bf16)
    for c in cores:
        b, crows = contiguous_rows(c)
        x2f[b][crows] = ra[c]["x2"]
        qtf[b][:, crows] = ra[c]["qt_o"]
        ktf[b][:, crows] = ra[c]["kt_o"]
        vf[b][crows] = ra[c]["v_o"]

    nc = _get_nc("halfb", build_half_b)
    maps = []
    for c in cores:
        b, irows = interleaved_rows(c)
        m = host_prep_l1_attn(inputs, c)
        m["woB1"] = m.pop("woB")
        f = host_prep_l1_ffn(inputs)
        m.update(f)
        m["qto"] = np.ascontiguousarray(qtf[b][:, irows])
        m["kto"] = np.ascontiguousarray(ktf[b][:, irows])
        m["vo"] = np.ascontiguousarray(vf[b][irows])
        ktg = np.empty((4 * D, TOK), bf16)
        vg = np.empty((4 * TOK, D), bf16)
        for r in range(4):
            _, rrows = interleaved_rows(4 * b + r)
            ktg[r * D:(r + 1) * D, :] = ktf[b][:, rrows]
            vg[r * TOK:(r + 1) * TOK, :] = vf[b][rrows]
        m["ktg"] = ktg
        m["vg"] = vg
        m["x2"] = np.ascontiguousarray(x2f[b][irows])
        maps.append(m)
    rb = run_bass_kernel_spmd(nc, maps, core_ids=cores).results
    out = np.empty((B, S, D), np.float32)
    for c in cores:
        b, irows = interleaved_rows(c)
        out[b][irows] = rb[c]["y"]
    return out


def kernel(**inputs):
    return kernel_2launch(**inputs)


# version 09
# speedup vs baseline: 1.2337x; 1.0298x over previous
import numpy as np
import ml_dtypes
import concourse.bass as bass
import concourse.mybir as mybir
from concourse.bass_utils import run_bass_kernel_spmd
from contextlib import ExitStack

F32 = mybir.dt.float32
BF16 = mybir.dt.bfloat16
I32 = mybir.dt.int32
AF = mybir.ActivationFunctionType
ALU = mybir.AluOpType
AX = mybir.AxisListType
bf16 = ml_dtypes.bfloat16

B, S, D = 2, 8192, 2048
TOK = 2048
NT = TOK // 128
DC = D // 128
HD = 128
WIN = 512
FF0 = 8192
EPS_RMS = 1.1920928955078125e-07
EPS_LN = 1e-5
MAGIC = float(np.float32(1.5 * 2 ** 23))
NEG = -1.0e30

ENGINES = ("sync", "gpsimd", "vector", "scalar", "tensor")


class Buf:
    __slots__ = ("w", "rs")

    def __init__(self):
        self.w = None
        self.rs = []


class Prog:

    def __init__(self):
        self.ops = []

    def emit(self, eng, fn, reads=(), writes=(), dma=False, deps=(), coll=False):
        i = len(self.ops)
        dep = set(d for d in deps if d is not None)
        for b in reads:
            if b.w is not None:
                dep.add(b.w)
        for b in writes:
            if b.w is not None:
                dep.add(b.w)
            dep.update(b.rs)
        dep.discard(i)
        self.ops.append({"eng": eng, "fn": fn, "deps": dep, "dma": dma, "i": i,
                         "coll": coll})
        for b in reads:
            b.rs.append(i)
        for b in writes:
            b.w = i
            b.rs = []
        return i

    N_DMA_SEMS = 12

    def barrier(self):
        deps = set()
        last = {}
        ndma = {}
        for op in reversed(self.ops):
            e = op["eng"]
            if op["dma"]:
                if ndma.get(e, 0) < self.N_DMA_SEMS:
                    deps.add(op["i"])
                    ndma[e] = ndma.get(e, 0) + 1
            elif e not in last:
                last[e] = op["i"]
                deps.add(op["i"])
        deps = sorted(deps)
        for e in ENGINES:
            self.emit(e, lambda eng: eng.nop(), deps=deps)

    def finalize(self, nc, es):
        by_eng = {e: [] for e in ENGINES}
        for op in self.ops:
            by_eng[op["eng"]].append(op)

        consumers = {}
        for op in self.ops:
            for d in op["deps"]:
                consumers.setdefault(d, []).append(op["i"])

        rr = {}
        last_on_sem = {}
        ncoll = [0]
        for op in self.ops:
            if op.get("coll"):
                op["skey"] = f"coll{ncoll[0]}"
                ncoll[0] += 1
            elif op["dma"]:
                j = rr.get(op["eng"], 0)
                rr[op["eng"]] = j + 1
                k = f"{op['eng']}_dma{j % self.N_DMA_SEMS}"
                prev = last_on_sem.get(k)
                if prev is not None:
                    extra = [c for c in consumers.get(prev, [prev])
                             if c < op["i"]]
                    if not extra:
                        extra = [prev]
                    op["deps"].update(extra)
                last_on_sem[k] = op["i"]
                op["skey"] = k
            else:
                op["skey"] = op["eng"]

        need_sig = set()
        for op in self.ops:
            need_sig.update(op["deps"])
        semval = {}
        for op in self.ops:
            if op["i"] in need_sig:
                op["inc"] = 16 if op["dma"] else 1
                k = op["skey"]
                semval[k] = semval.get(k, 0) + op["inc"]
                op["sig"] = semval[k]
            else:
                op["sig"] = None

        sem_keys = sorted({op["skey"] for op in self.ops if op["sig"] is not None})
        sems = {k: es.enter_context(nc.semaphore(f"s_{k}")) for k in sem_keys}

        for op in self.ops:
            waits = {}
            for d in op["deps"]:
                dop = self.ops[d]
                k, v = dop["skey"], dop["sig"]
                if waits.get(k, 0) < v:
                    waits[k] = v
            op["waits"] = waits

        block = es.enter_context(nc.Block())
        for eng_name in ENGINES:
            deco = getattr(block, eng_name)

            def _mk(eng_name):
                ops = by_eng[eng_name]

                def _body(eng):
                    water = {}
                    for op in ops:
                        for k, v in sorted(op["waits"].items()):
                            if water.get(k, 0) < v:
                                eng.wait_ge(sems[k], v)
                                water[k] = v
                        inst = op["fn"](eng)
                        if op["sig"] is not None:
                            inst.then_inc(sems[op["skey"]], op["inc"])
                return _body

            deco(_mk(eng_name))
        return sems


def dram(io, nc, name, shape, dt_, kind):
    if io is not None and name in io:
        return io[name]
    return nc.dram_tensor(name, shape, dt_, kind=kind).ap()


DT_SIZE = {F32: 4, BF16: 2, I32: 4}


class LaunchCtx:
    def __init__(self, nc, es, shared=None):
        self.nc = nc
        self.es = es
        if shared is None:
            self.P = Prog()
            self.arena = None
            self.ps = [es.enter_context(nc.psum_tensor(f"psf{i}", [128, 512], F32))
                       for i in range(6)]
            self.psb = [es.enter_context(nc.psum_tensor(f"psb{i}", [128, 1024], BF16))
                        for i in range(2)]
            self.ps_buf = [Buf() for _ in self.ps]
            self.psb_buf = [Buf() for _ in self.psb]
            self.consts = es.enter_context(nc.sbuf_tensor("consts", [128, 8], F32))
            self.consts_b = Buf()
            self.P.emit("vector", lambda e: e.memset(self.consts[:, 0:1], EPS_LN),
                        writes=(self.consts_b,))
            self.P.emit("vector", lambda e: e.memset(self.consts[:, 1:2], EPS_RMS),
                        writes=(self.consts_b,))
        else:
            self.P = shared.P
            self.arena = shared.arena
            self.arena_off = [0]
            self.ps = shared.ps
            self.psb = shared.psb
            self.ps_buf = shared.ps_buf
            self.psb_buf = shared.psb_buf
            self.consts = shared.consts
            self.consts_b = shared.consts_b

    def sb(self, name, shape, dtype):
        if self.arena is None:
            t = self.es.enter_context(self.nc.sbuf_tensor(name, shape, dtype))
            return t, Buf()
        nb = int(np.prod(shape[1:])) * DT_SIZE[dtype]
        off = (self.arena_off[0] + 63) & ~63
        self.arena_off[0] = off + nb
        assert self.arena_off[0] <= self.arena.shape[1], \
            f"arena overflow at {name}: {self.arena_off[0]}"
        ap = self.arena[:, off:off + nb].bitcast(dtype)
        if shape[0] != 128:
            ap = ap[0:shape[0], :]
        return ap, Buf()


def emit_ln_tile(L, dst, dst_b, src, src_b, stats, stats_b, width=2048,
                 eps_col=0, out_kind="normal", res=None, res_b=None):
    P = L.P
    n = width // 512
    for i in range(n):
        P.emit("vector", (lambda i: lambda e: e.bn_stats(
            stats[:, i * 6:(i + 1) * 6], src[:, i * 512:(i + 1) * 512]))(i),
            reads=(src_b,), writes=(stats_b,))
    P.emit("vector", lambda e: e.bn_aggr(stats[:, 24:26], stats[:, 0:6 * n]),
           reads=(stats_b,), writes=(stats_b,))
    P.emit("vector", lambda e: e.tensor_scalar(stats[:, 26:27], stats[:, 24:25],
                                               -1.0, None, ALU.mult),
           reads=(stats_b,), writes=(stats_b,))
    P.emit("scalar", lambda e: e.activation(stats[:, 27:28], stats[:, 25:26],
                                            AF.Sqrt, bias=L.consts[:, eps_col:eps_col + 1]),
           reads=(stats_b, L.consts_b), writes=(stats_b,))
    P.emit("vector", lambda e: e.reciprocal(stats[:, 28:29], stats[:, 27:28]),
           reads=(stats_b,), writes=(stats_b,))
    P.emit("vector", lambda e: e.tensor_scalar(
        dst[:], src[:], stats[:, 26:27], stats[:, 28:29], ALU.add, ALU.mult),
        reads=(src_b, stats_b), writes=(dst_b,))
    if out_kind == "resid":
        P.emit("vector", lambda e: e.tensor_tensor(dst[:], dst[:], res[:], ALU.add),
               reads=(dst_b, res_b), writes=(dst_b,))


def emit_transposes(L, dst, dst_b, src, src_b, ident, ident_b, chunk_map):
    P = L.P
    for base in range(0, len(chunk_map), 8):
        grp = chunk_map[base:base + 8]
        bi = (base // 8) % len(L.psb)
        bank, bank_b = L.psb[bi], L.psb_buf[bi]
        for k, (sc, _) in enumerate(grp):
            P.emit("tensor", (lambda k, sc, bank: lambda e: e.transpose(
                bank[:, k * 128:(k + 1) * 128], src[:, sc:sc + 128], ident[:]))(k, sc, bank),
                reads=(src_b, ident_b), writes=(bank_b,))
        for k, (_, dc_) in enumerate(grp):
            P.emit("vector", (lambda k, dc_, bank: lambda e: e.tensor_copy(
                dst[:, dc_:dc_ + 128], bank[:, k * 128:(k + 1) * 128]))(k, dc_, bank),
                reads=(bank_b,), writes=(dst_b,))



def build_l1_ffn(nc, es, L=None, io=None):
    fin = L is None
    if L is None:
        L = LaunchCtx(nc, es)
    P = L.P

    x3 = dram(io, nc, "x3", [TOK, D], F32, "ExternalInput")
    wg = dram(io, nc, "wg", [64, 128, 16 * 128], I32, "ExternalInput")
    wu = dram(io, nc, "wu", [64, 128, 16 * 128], I32, "ExternalInput")
    wd = dram(io, nc, "wd", [64, 128, D], I32, "ExternalInput")
    gsc = dram(io, nc, "gsc", [128, 64], F32, "ExternalInput")
    usc = dram(io, nc, "usc", [128, 64], F32, "ExternalInput")
    dscr = dram(io, nc, "dsc_row", [1, D], F32, "ExternalInput")
    identi = dram(io, nc, "ident_bf", [128, 128], BF16, "ExternalInput")
    identfi = dram(io, nc, "ident_f", [128, 128], F32, "ExternalInput")
    onesri = dram(io, nc, "ones_row", [1, 512], BF16, "ExternalInput")
    onesci = dram(io, nc, "ones_col", [128, 1], BF16, "ExternalInput")
    y = dram(io, nc, "y", [TOK, D], F32, "ExternalOutput")
    acc = dram(io, nc, "acc_scr", [TOK, D], F32, "Internal")

    ident, ident_b = L.sb("identt", [128, 128], BF16)
    identf, identf_b = L.sb("identtf", [128, 128], F32)
    onesr, onesr_b = L.sb("onesr", [1, 512], BF16)
    onesc, onesc_b = L.sb("onesc", [128, 1], BF16)
    onesrf, onesrf_b = L.sb("onesrf", [1, 128], F32)
    gscs, gscs_b = L.sb("gscs", [128, 64], F32)
    uscs, uscs_b = L.sb("uscs", [128, 64], F32)
    dsb, dsb_b = L.sb("ds_bc", [128, D], F32)
    dsrow, dsrow_b = L.sb("dsrow", [1, D], F32)
    xq = [L.sb(f"xq{i}", [128, D], F32) for i in range(2)]
    t3bf, t3bf_b = L.sb("t3bf", [128, D], BF16)
    stats, stats_b = L.sb("stats", [128, 32], F32)
    t3T, t3T_b = L.sb("t3T", [128, DC * TOK], BF16)
    rsneg, rsneg_b = L.sb("rsneg", [1, 2 * TOK], BF16)
    rrow, rrow_b = L.sb("rrow", [1, 128], F32)
    wgu = [L.sb(f"wgu{i}", [128, 16 * 128], BF16) for i in range(4)]
    wdsl, wdsl_b = L.sb("wdsl", [128, 8 * 512], BF16)
    hbuf, hbuf_b = L.sb("hbuf", [128, 8 * TOK], BF16)
    gtmp, gtmp_b = L.sb("gtmp", [128, 512], F32)
    utmp, utmp_b = L.sb("utmp", [128, 512], F32)
    sgt, sgt_b = L.sb("sgt", [128, 512], F32)
    hs, hs_b = L.sb("hs", [1, TOK], F32)
    hstmp, hstmp_b = L.sb("hstmp", [1, 512], F32)
    hcol, hcol_b = L.sb("hcol", [128, 1], F32)
    dtile, dtile_b = L.sb("dtile", [128, 512], F32)
    outt, outt_b = L.sb("outt", [128, D], F32)
    yt, yt_b = L.sb("yt", [128, D], F32)

    acc_bufs = [[Buf() for _ in range(4)] for _ in range(NT)]

    P.emit("sync", lambda e: e.dma_start(out=ident[:], in_=identi[:]), writes=(ident_b,), dma=True)
    P.emit("sync", lambda e: e.dma_start(out=identf[:], in_=identfi[:]), writes=(identf_b,), dma=True)
    P.emit("sync", lambda e: e.dma_start(out=onesr[:], in_=onesri[:]), writes=(onesr_b,), dma=True)
    P.emit("sync", lambda e: e.dma_start(out=onesc[:], in_=onesci[:]), writes=(onesc_b,), dma=True)
    P.emit("sync", lambda e: e.dma_start(out=gscs[:], in_=gsc[:]), writes=(gscs_b,), dma=True)
    P.emit("sync", lambda e: e.dma_start(out=uscs[:], in_=usc[:]), writes=(uscs_b,), dma=True)
    P.emit("sync", lambda e: e.dma_start(out=dsrow[:], in_=dscr[:]), writes=(dsrow_b,), dma=True)
    P.emit("vector", lambda e: e.memset(onesrf[:], 1.0), writes=(onesrf_b,))
    P.emit("vector", lambda e: e.memset(hs[:], 0.0), writes=(hs_b,))

    for i in range(4):
        bank, bank_b = L.ps[4], L.ps_buf[4]
        P.emit("tensor", (lambda i, bank: lambda e: e.matmul(
            bank[:], onesrf[:], dsrow[:, i * 512:(i + 1) * 512], start=True, stop=True))(i, bank),
            reads=(onesrf_b, dsrow_b), writes=(bank_b,))
        P.emit("vector", (lambda i, bank: lambda e: e.tensor_copy(
            dsb[:, i * 512:(i + 1) * 512], bank[:]))(i, bank),
            reads=(bank_b,), writes=(dsb_b,))

    for qt in range(NT):
        xi, xi_b = xq[qt % 2]
        P.emit("sync", (lambda qt, xi: lambda e: e.dma_start(
            out=xi[:], in_=x3[qt * 128:(qt + 1) * 128, :]))(qt, xi),
            writes=(xi_b,), dma=True)
        emit_ln_tile(L, t3bf, t3bf_b, xi, xi_b, stats, stats_b)
        P.emit("vector", lambda e: e.tensor_reduce(
            stats[:, 29:30], t3bf[:], AX.X, ALU.add),
            reads=(t3bf_b,), writes=(stats_b,))
        bank, bank_b = L.ps[5], L.ps_buf[5]
        P.emit("tensor", (lambda bank: lambda e: e.matmul(
            bank[0:1, 0:128], stats[:, 29:30], identf[:], start=True, stop=True))(bank),
            reads=(stats_b, identf_b), writes=(bank_b,))
        P.emit("vector", (lambda bank: lambda e: e.tensor_scalar(
            rrow[:], bank[0:1, 0:128], -128.0, None, ALU.mult))(bank),
            reads=(bank_b,), writes=(rrow_b,))
        P.emit("vector", (lambda qt: lambda e: e.tensor_copy(
            rsneg[0:1, qt * 128:(qt + 1) * 128], rrow[:]))(qt),
            reads=(rrow_b,), writes=(rsneg_b,))
        P.emit("vector", (lambda qt: lambda e: e.tensor_tensor(
            rsneg[0:1, TOK + qt * 128:TOK + (qt + 1) * 128], rrow[:],
            rsneg[0:1, qt * 128:(qt + 1) * 128], ALU.subtract))(qt),
            reads=(rrow_b, rsneg_b), writes=(rsneg_b,))
        emit_transposes(L, t3T, t3T_b, t3bf, t3bf_b, ident, ident_b,
                        [(dc_ * 128, (dc_ * NT + qt) * 128) for dc_ in range(DC)])

    for bb in range(8):
        for oc in range(8):
            ocg = bb * 8 + oc
            specs = ((wg, gscs, gscs_b, gtmp, gtmp_b), (wu, uscs, uscs_b, utmp, utmp_b))
            for gi, (w_dram, sc_s, sc_b, tmp, tmp_b) in enumerate(specs):
                wt, wt_b = wgu[(oc % 2) * 2 + gi]
                P.emit("gpsimd", (lambda ocg, w_dram, wt: lambda e: e.dma_start(
                    out=wt[:], in_=w_dram[ocg]))(ocg, w_dram, wt),
                    writes=(wt_b,), dma=True)
            for qg in range(4):
                for gi, (w_dram, sc_s, sc_b, tmp, tmp_b) in enumerate(specs):
                    wt, wt_b = wgu[(oc % 2) * 2 + gi]
                    bank, bank_b = L.ps[(oc % 2) * 2 + gi], L.ps_buf[(oc % 2) * 2 + gi]
                    P.emit("tensor", (lambda qg, bank: lambda e: e.matmul(
                        bank[:], onesr[0:1, 0:128],
                        rsneg[0:1, qg * 512:(qg + 1) * 512],
                        start=True, stop=False))(qg, bank),
                        reads=(onesr_b, rsneg_b), writes=(bank_b,))
                    P.emit("tensor", (lambda qg, bank: lambda e: e.matmul(
                        bank[:], onesr[0:1, 0:128],
                        rsneg[0:1, TOK + qg * 512:TOK + (qg + 1) * 512],
                        start=False, stop=False))(qg, bank),
                        reads=(onesr_b, rsneg_b), writes=(bank_b,))
                    for dc_ in range(DC):
                        P.emit("tensor", (lambda qg, dc_, wt, bank: lambda e: e.matmul(
                            bank[:], wt[:, dc_ * 128:(dc_ + 1) * 128],
                            t3T[:, (dc_ * 4 + qg) * 512:(dc_ * 4 + qg + 1) * 512],
                            start=False, stop=(dc_ == DC - 1)))(qg, dc_, wt, bank),
                            reads=(wt_b, t3T_b), writes=(bank_b,))
                    P.emit("vector", (lambda ocg, bank, tmp, sc_s, sc_b: lambda e: e.tensor_scalar(
                        tmp[:], bank[:], sc_s[:, ocg:ocg + 1], None, ALU.mult))(ocg, bank, tmp, sc_s, sc_b),
                        reads=(bank_b, sc_b), writes=(tmp_b,))
                P.emit("scalar", lambda e: e.activation(sgt[:], gtmp[:], AF.Silu),
                       reads=(gtmp_b,), writes=(sgt_b,))
                P.emit("vector", (lambda oc, qg: lambda e: e.tensor_tensor(
                    hbuf[:, (oc * 4 + qg) * 512:(oc * 4 + qg + 1) * 512],
                    sgt[:], utmp[:], ALU.mult))(oc, qg),
                    reads=(sgt_b, utmp_b), writes=(hbuf_b,))
            bank, bank_b = L.ps[4], L.ps_buf[4]
            for qg in range(4):
                P.emit("tensor", (lambda oc, qg, bank: lambda e: e.matmul(
                    bank[0:1, 0:512], onesc[:],
                    hbuf[:, (oc * 4 + qg) * 512:(oc * 4 + qg + 1) * 512],
                    start=True, stop=True))(oc, qg, bank),
                    reads=(onesc_b, hbuf_b), writes=(bank_b,))
                P.emit("vector", (lambda qg, bank: lambda e: e.tensor_copy(
                    hstmp[:], bank[0:1, 0:512]))(qg, bank),
                    reads=(bank_b,), writes=(hstmp_b,))
                P.emit("vector", (lambda qg: lambda e: e.tensor_tensor(
                    hs[0:1, qg * 512:(qg + 1) * 512],
                    hs[0:1, qg * 512:(qg + 1) * 512], hstmp[:], ALU.add))(qg),
                    reads=(hs_b, hstmp_b), writes=(hs_b,))
        for ds_ in range(4):
            for oc in range(8):
                ocg = bb * 8 + oc
                P.emit("gpsimd", (lambda ocg, ds_, oc: lambda e: e.dma_start(
                    out=wdsl[:, oc * 512:(oc + 1) * 512],
                    in_=wd[ocg, :, ds_ * 512:(ds_ + 1) * 512]))(ocg, ds_, oc),
                    writes=(wdsl_b,), dma=True)
            for qgrp in range(4):
                for oc in range(8):
                    for jj in range(4):
                        qt = qgrp * 4 + jj
                        bank, bank_b = L.ps[jj], L.ps_buf[jj]
                        P.emit("tensor", (lambda oc, qt, jj, bank: lambda e: e.matmul(
                            bank[:], hbuf[:, oc * TOK + qt * 128:oc * TOK + (qt + 1) * 128],
                            wdsl[:, oc * 512:(oc + 1) * 512],
                            start=(oc == 0), stop=(oc == 7)))(oc, qt, jj, bank),
                            reads=(hbuf_b, wdsl_b), writes=(bank_b,))
                for jj in range(4):
                    qt = qgrp * 4 + jj
                    bank, bank_b = L.ps[jj], L.ps_buf[jj]
                    ab = acc_bufs[qt][ds_]
                    if bb == 0:
                        P.emit("vector", (lambda bank: lambda e: e.tensor_copy(
                            dtile[:], bank[:]))(bank),
                            reads=(bank_b,), writes=(dtile_b,))
                    else:
                        P.emit("sync", (lambda qt, ds_: lambda e: e.dma_start(
                            out=dtile[:], in_=acc[qt * 128:(qt + 1) * 128,
                                                  ds_ * 512:(ds_ + 1) * 512]))(qt, ds_),
                            reads=(ab,), writes=(dtile_b,), dma=True)
                        P.emit("vector", (lambda bank: lambda e: e.tensor_tensor(
                            dtile[:], dtile[:], bank[:], ALU.add))(bank),
                            reads=(dtile_b, bank_b), writes=(dtile_b,))
                    P.emit("sync", (lambda qt, ds_: lambda e: e.dma_start(
                        out=acc[qt * 128:(qt + 1) * 128, ds_ * 512:(ds_ + 1) * 512],
                        in_=dtile[:]))(qt, ds_),
                        reads=(dtile_b,), writes=(ab,), dma=True)

    store_ops = []
    for qt in range(NT):
        xi, xi_b = xq[qt % 2]
        P.emit("sync", (lambda qt, xi: lambda e: e.dma_start(
            out=xi[:], in_=x3[qt * 128:(qt + 1) * 128, :]))(qt, xi),
            writes=(xi_b,), dma=True)
        bank, bank_b = L.ps[5], L.ps_buf[5]
        P.emit("tensor", (lambda qt, bank: lambda e: e.matmul(
            bank[:, 0:128], hs[0:1, qt * 128:(qt + 1) * 128], onesrf[:],
            start=True, stop=True))(qt, bank),
            reads=(hs_b, onesrf_b), writes=(bank_b,))
        P.emit("vector", (lambda bank: lambda e: e.tensor_scalar(
            hcol[:], bank[:, 0:1], -128.0, None, ALU.mult))(bank),
            reads=(bank_b,), writes=(hcol_b,))
        for ds_ in range(4):
            ab = acc_bufs[qt][ds_]
            P.emit("sync", (lambda qt, ds_: lambda e: e.dma_start(
                out=dtile[:], in_=acc[qt * 128:(qt + 1) * 128,
                                      ds_ * 512:(ds_ + 1) * 512]))(qt, ds_),
                reads=(ab,), writes=(dtile_b,), dma=True)
            P.emit("vector", (lambda ds_: lambda e: e.tensor_scalar(
                outt[:, ds_ * 512:(ds_ + 1) * 512], dtile[:], hcol[:], None,
                ALU.add))(ds_),
                reads=(dtile_b, hcol_b), writes=(outt_b,))
        P.emit("vector", lambda e: e.tensor_tensor(outt[:], outt[:], dsb[:], ALU.mult),
               reads=(outt_b, dsb_b), writes=(outt_b,))
        emit_ln_tile(L, yt, yt_b, outt, outt_b, stats, stats_b,
                     out_kind="resid", res=xi, res_b=xi_b)
        store_ops.append(P.emit("sync", (lambda qt: lambda e: e.dma_start(
            out=y[qt * 128:(qt + 1) * 128, :], in_=yt[:]))(qt),
            reads=(yt_b,), dma=True))

    P.emit("sync", lambda e: e.nop(), deps=store_ops)
    if fin:
        P.finalize(nc, es)
    return nc


def _const_inputs():
    return dict(ident_bf=np.eye(128, dtype=bf16),
                ident_f=np.eye(128, dtype=np.float32),
                ones_row=np.ones((1, 512), dtype=bf16),
                ones_col=np.ones((128, 1), dtype=bf16))


def host_prep_l1_ffn(inputs):
    gw = inputs["L1_ffn_gate_w"]
    uw = inputs["L1_ffn_up_w"]
    dw = inputs["L1_ffn_down_w"]
    def _gu(w):
        wr = w.reshape(64, 128, 16, 128)
        return np.ascontiguousarray(wr.transpose(0, 3, 2, 1)).reshape(64, 128, 16 * 128)
    wgA = _gu(gw)
    wuA = _gu(uw)
    wdA = np.ascontiguousarray(dw.T).reshape(64, 128, D)
    gsc = np.ascontiguousarray(inputs["L1_ffn_gate_scale"].reshape(64, 128).T)
    usc = np.ascontiguousarray(inputs["L1_ffn_up_scale"].reshape(64, 128).T)
    dsc = np.ascontiguousarray(inputs["L1_ffn_down_scale"].reshape(1, D))
    d = dict(wg=wgA, wu=wuA, wd=wdA, gsc=gsc, usc=usc, dsc_row=dsc)
    d.update(_const_inputs())
    return d


def contiguous_rows(core):
    b, c = core // 4, core % 4
    r1 = np.arange(c * 1024, (c + 1) * 1024)
    r2 = np.arange((7 - c) * 1024, (8 - c) * 1024)
    return b, np.concatenate([r1, r2])


def interleaved_rows(core):
    b, c = core // 4, core % 4
    tiles = [63 - 4 * s - c for s in range(16)]
    rows = np.concatenate([np.arange(t * 128, (t + 1) * 128) for t in tiles])
    return b, rows


def kernel(**inputs):
    raise NotImplementedError("under construction")



def build_l1_qkv(nc, es, L=None, io=None):
    fin = L is None
    if L is None:
        L = LaunchCtx(nc, es)
    P = L.P

    x2 = dram(io, nc, "x2", [TOK, D], F32, "ExternalInput")
    wqA = dram(io, nc, "wqA", [16, 128, 16 * 128], F32, "ExternalInput")
    wkA = dram(io, nc, "wkA", [16, 128, 16 * 128], F32, "ExternalInput")
    wvB = dram(io, nc, "wvB", [16, 128, D], F32, "ExternalInput")
    bqP = dram(io, nc, "bqP", [128, 16], F32, "ExternalInput")
    bkP = dram(io, nc, "bkP", [128, 16], F32, "ExternalInput")
    bvR = dram(io, nc, "bv_row", [1, D], F32, "ExternalInput")
    rqc = dram(io, nc, "ropeq_c", [64, TOK], F32, "ExternalInput")
    rqs = dram(io, nc, "ropeq_s", [64, TOK], F32, "ExternalInput")
    rkc = dram(io, nc, "ropek_c", [64, TOK], F32, "ExternalInput")
    rks = dram(io, nc, "ropek_s", [64, TOK], F32, "ExternalInput")
    identi = dram(io, nc, "ident_bf", [128, 128], BF16, "ExternalInput")
    onesrfi = dram(io, nc, "ones_row_f", [1, 128], F32, "ExternalInput")
    qt_o = dram(io, nc, "qt_o", [D, TOK], BF16, "ExternalOutput")
    kt_o = dram(io, nc, "kt_o", [D, TOK], BF16, "ExternalOutput")
    v_o = dram(io, nc, "v_o", [TOK, D], BF16, "ExternalOutput")

    x2load = (io.get("x2_load") if io else None) or (
        lambda e, qt, dst: e.dma_start(out=dst, in_=x2[qt * 128:(qt + 1) * 128, :]))
    x2eng = (io.get("x2_eng") if io else None) or "sync"
    x2dep = io.get("x2_dep") if io else None
    ident, ident_b = L.sb("identt", [128, 128], BF16)
    onesrf, onesrf_b = L.sb("onesrf", [1, 128], F32)
    bvrow, bvrow_b = L.sb("bvrow", [1, D], F32)
    bvbc, bvbc_b = L.sb("bvbc", [128, D], F32)
    bqs, bqs_b = L.sb("bqs", [128, 16], F32)
    bks, bks_b = L.sb("bks", [128, 16], F32)
    rq = [L.sb(f"rq{i}", [128, TOK], F32) for i in range(2)]
    rk = [L.sb(f"rk{i}", [128, TOK], F32) for i in range(2)]
    xq = [L.sb(f"xq{i}", [128, D], F32) for i in range(2)]
    t2f, t2f_b = L.sb("t2f", [128, D], F32)
    t2bf, t2bf_b = L.sb("t2bf", [128, D], BF16)
    stats, stats_b = L.sb("stats", [128, 32], F32)
    t2T, t2T_b = L.sb("t2T", [128, DC * TOK], BF16)
    wqk = [L.sb(f"wqk{i}", [128, 16 * 128], BF16) for i in range(2)]
    wvt = [L.sb(f"wvt{i}", [128, 512], BF16) for i in range(2)]
    ptmp, ptmp_b = L.sb("ptmp", [128, 512], F32)
    r1, r1_b = L.sb("ropet", [64, 512], F32)
    r2, r2_b = L.sb("ropet2", [64, 512], F32)
    qko, qko_b = L.sb("qko", [128, 512], BF16)
    vsb, vsb_b = L.sb("vsb", [128, D], BF16)

    P.emit("sync", lambda e: e.dma_start(out=ident[:], in_=identi[:]), writes=(ident_b,), dma=True)
    P.emit("sync", lambda e: e.dma_start(out=onesrf[:], in_=onesrfi[:]), writes=(onesrf_b,), dma=True)
    P.emit("sync", lambda e: e.dma_start(out=bvrow[:], in_=bvR[:]), writes=(bvrow_b,), dma=True)
    P.emit("sync", lambda e: e.dma_start(out=bqs[:], in_=bqP[:]), writes=(bqs_b,), dma=True)
    P.emit("sync", lambda e: e.dma_start(out=bks[:], in_=bkP[:]), writes=(bks_b,), dma=True)
    for (tt_, ttb_), srct in ((rq[0], rqc), (rq[1], rqs), (rk[0], rkc), (rk[1], rks)):
        P.emit("sync", (lambda tt_, srct: lambda e: e.dma_start(
            out=tt_[0:64, :], in_=srct[:]))(tt_, srct), writes=(ttb_,), dma=True)
        P.emit("sync", (lambda tt_, srct: lambda e: e.dma_start(
            out=tt_[64:128, :], in_=srct[:]))(tt_, srct), writes=(ttb_,), dma=True)
    for i in range(4):
        bank, bank_b = L.ps[4], L.ps_buf[4]
        P.emit("tensor", (lambda i, bank: lambda e: e.matmul(
            bank[:], onesrf[:], bvrow[:, i * 512:(i + 1) * 512], start=True, stop=True))(i, bank),
            reads=(onesrf_b, bvrow_b), writes=(bank_b,))
        P.emit("vector", (lambda i, bank: lambda e: e.tensor_copy(
            bvbc[:, i * 512:(i + 1) * 512], bank[:]))(i, bank),
            reads=(bank_b,), writes=(bvbc_b,))

    for qt in range(NT):
        xi, xi_b = xq[qt % 2]
        P.emit(x2eng, (lambda qt, xi: lambda e: x2load(e, qt, xi[:]))(qt, xi),
               writes=(xi_b,), dma=True, deps=(x2dep,))
        emit_ln_tile(L, t2f, t2f_b, xi, xi_b, stats, stats_b)
        P.emit("vector", lambda e: e.tensor_copy(t2bf[:], t2f[:]),
               reads=(t2f_b,), writes=(t2bf_b,))
        emit_transposes(L, t2T, t2T_b, t2bf, t2bf_b, ident, ident_b,
                        [(dc_ * 128, (dc_ * NT + qt) * 128) for dc_ in range(DC)])

    store_ops = []
    for wi, (wA, bb, bb_b, rc, rs_, out_d) in enumerate(
            ((wqA, bqs, bqs_b, rq[0], rq[1], qt_o),
             (wkA, bks, bks_b, rk[0], rk[1], kt_o))):
        for oc in range(16):
            wt, wt_b = wqk[oc % 2]
            P.emit("gpsimd", (lambda oc, wA, wt: lambda e: e.dma_start(
                out=wt[:], in_=wA[oc]))(oc, wA, wt),
                writes=(wt_b,), dma=True)
            for qg in range(4):
                bank, bank_b = L.ps[wi], L.ps_buf[wi]
                for dc_ in range(DC):
                    P.emit("tensor", (lambda qg, dc_, wt, bank: lambda e: e.matmul(
                        bank[:], wt[:, dc_ * 128:(dc_ + 1) * 128],
                        t2T[:, (dc_ * 4 + qg) * 512:(dc_ * 4 + qg + 1) * 512],
                        start=(dc_ == 0), stop=(dc_ == DC - 1)))(qg, dc_, wt, bank),
                        reads=(wt_b, t2T_b), writes=(bank_b,))
                P.emit("vector", (lambda oc, bank, bb: lambda e: e.tensor_scalar(
                    ptmp[:], bank[:], bb[:, oc:oc + 1], None, ALU.add))(oc, bank, bb),
                    reads=(bank_b, bb_b), writes=(ptmp_b,))
                rcs, rcb = rc
                rss, rsb = rs_
                sl = slice(qg * 512, (qg + 1) * 512)
                P.emit("vector", (lambda sl, rcs: lambda e: e.tensor_tensor(
                    r1[:], ptmp[0:64, :], rcs[0:64, sl], ALU.mult))(sl, rcs),
                    reads=(ptmp_b, rcb), writes=(r1_b,))
                P.emit("vector", (lambda sl, rss: lambda e: e.tensor_tensor(
                    r2[:], ptmp[64:128, :], rss[64:128, sl], ALU.mult))(sl, rss),
                    reads=(ptmp_b, rsb), writes=(r2_b,))
                P.emit("vector", lambda e: e.tensor_tensor(
                    qko[0:64, :], r1[:], r2[:], ALU.subtract),
                    reads=(r1_b, r2_b), writes=(qko_b,))
                P.emit("vector", (lambda sl, rcs: lambda e: e.tensor_tensor(
                    r1[:], ptmp[64:128, :], rcs[64:128, sl], ALU.mult))(sl, rcs),
                    reads=(ptmp_b, rcb), writes=(r1_b,))
                P.emit("vector", (lambda sl, rss: lambda e: e.tensor_tensor(
                    r2[:], ptmp[0:64, :], rss[0:64, sl], ALU.mult))(sl, rss),
                    reads=(ptmp_b, rsb), writes=(r2_b,))
                P.emit("vector", lambda e: e.tensor_tensor(
                    qko[64:128, :], r1[:], r2[:], ALU.add),
                    reads=(r1_b, r2_b), writes=(qko_b,))
                store_ops.append(P.emit("sync", (lambda oc, qg, out_d: lambda e: e.dma_start(
                    out=out_d[oc * 128:(oc + 1) * 128, qg * 512:(qg + 1) * 512],
                    in_=qko[:]))(oc, qg, out_d),
                    reads=(qko_b,), dma=True))

    wvsl, wvsl_b = L.sb("wvsl", [128, 16 * 512], BF16)
    for oslab in range(4):
        for dc_ in range(DC):
            P.emit("gpsimd", (lambda oslab, dc_: lambda e: e.dma_start(
                out=wvsl[:, dc_ * 512:(dc_ + 1) * 512],
                in_=wvB[dc_, :, oslab * 512:(oslab + 1) * 512]))(oslab, dc_),
                writes=(wvsl_b,), dma=True)
        for qg in range(4):
            for dc_ in range(DC):
                for j in range(4):
                    qt = qg * 4 + j
                    bank, bank_b = L.ps[2 + j], L.ps_buf[2 + j]
                    P.emit("tensor", (lambda qt, dc_, j, bank: lambda e: e.matmul(
                        bank[:], t2T[:, (dc_ * NT + qt) * 128:(dc_ * NT + qt) * 128 + 128],
                        wvsl[:, dc_ * 512:(dc_ + 1) * 512],
                        start=(dc_ == 0), stop=(dc_ == DC - 1)))(qt, dc_, j, bank),
                        reads=(t2T_b, wvsl_b), writes=(bank_b,))
            for j in range(4):
                qt = qg * 4 + j
                bank, bank_b = L.ps[2 + j], L.ps_buf[2 + j]
                P.emit("vector", (lambda oslab, bank: lambda e: e.tensor_tensor(
                    vsb[:, oslab * 512:(oslab + 1) * 512], bank[:],
                    bvbc[:, oslab * 512:(oslab + 1) * 512], ALU.add))(oslab, bank),
                    reads=(bank_b, bvbc_b), writes=(vsb_b,))
                store_ops.append(P.emit("sync", (lambda qt, oslab: lambda e: e.dma_start(
                    out=v_o[qt * 128:(qt + 1) * 128, oslab * 512:(oslab + 1) * 512],
                    in_=vsb[:, oslab * 512:(oslab + 1) * 512]))(qt, oslab),
                    reads=(vsb_b,), dma=True))

    P.emit("sync", lambda e: e.nop(), deps=store_ops)
    if fin:
        P.finalize(nc, es)
    return nc


_ROPE_PERM = None


def _rope_perm():
    global _ROPE_PERM
    if _ROPE_PERM is None:
        p = []
        for h in range(16):
            p += [h * 128 + 2 * i for i in range(64)]
            p += [h * 128 + 2 * i + 1 for i in range(64)]
        _ROPE_PERM = np.array(p)
    return _ROPE_PERM


def host_prep_l1_qkv(inputs, core, rows=None):
    if rows is None:
        b, rows = interleaved_rows(core)
    perm = _rope_perm()
    Wq = inputs["L1_attn_Wq"][:, perm]
    Wk = inputs["L1_attn_Wk"][:, perm]
    Wv = inputs["L1_attn_Wv"]
    bq = inputs["L1_attn_bq"][perm]
    bk = inputs["L1_attn_bk"][perm]

    def _tileA(W):
        Wr = W.reshape(16, 128, 16, 128)
        return np.ascontiguousarray(Wr.transpose(2, 1, 0, 3)).reshape(16, 128, 16 * 128)

    wqA = _tileA(Wq)
    wkA = _tileA(Wk)
    wvB = np.ascontiguousarray(Wv.reshape(16, 128, D))
    bqP = np.ascontiguousarray(bq.reshape(16, 128).T)
    bkP = np.ascontiguousarray(bk.reshape(16, 128).T)
    bvR = inputs["L1_attn_bv"].reshape(1, D)
    pos = rows.astype(np.float32)
    i = np.arange(64, dtype=np.float32)
    theta = np.power(np.float32(10000.0), -2.0 * i / 128.0).astype(np.float32)
    ang = theta[:, None] * pos[None, :]
    c = np.cos(ang).astype(np.float32)
    s = np.sin(ang).astype(np.float32)
    sc = np.float32(1.0 / np.sqrt(128.0))
    return dict(wqA=wqA, wkA=wkA, wvB=wvB, bqP=bqP, bkP=bkP, bv_row=bvR,
                ropeq_c=c * sc, ropeq_s=s * sc, ropek_c=c, ropek_s=s,
                ident_bf=np.eye(128, dtype=bf16),
                ones_row_f=np.ones((1, 128), dtype=np.float32))



def build_l1_attn(nc, es, L=None, io=None):
    fin = L is None
    if L is None:
        L = LaunchCtx(nc, es)
    P = L.P

    qto = dram(io, nc, "qto", [D, TOK], BF16, "ExternalInput")
    ktg = dram(io, nc, "ktg", [4 * D, TOK], BF16, "ExternalInput")
    vg = dram(io, nc, "vg", [4 * TOK, D], BF16, "ExternalInput")
    kto = dram(io, nc, "kto", [D, TOK], BF16, "ExternalInput")
    vo = dram(io, nc, "vo", [TOK, D], BF16, "ExternalInput")
    x2 = dram(io, nc, "x2", [TOK, D], F32, "ExternalInput")
    woB = dram(io, nc, "woB", [16, 128, D], F32, "ExternalInput")
    boR = dram(io, nc, "bo_row", [1, D], F32, "ExternalInput")
    killR = dram(io, nc, "kill_slots", [1, 64 * 16], BF16, "ExternalInput")
    triA = dram(io, nc, "tri_add", [128, 128], F32, "ExternalInput")
    triM = dram(io, nc, "tri01T", [128, 128], BF16, "ExternalInput")
    identfi = dram(io, nc, "ident_f", [128, 128], F32, "ExternalInput")
    onesri = dram(io, nc, "ones_row", [1, 512], BF16, "ExternalInput")
    onesci = dram(io, nc, "ones_col", [128, 1], BF16, "ExternalInput")
    onesrfi = dram(io, nc, "ones_row_f", [1, 128], F32, "ExternalInput")
    x3 = dram(io, nc, "x3", [TOK, D], F32, "ExternalOutput")

    x2load = (io.get("x2_load") if io else None) or (
        lambda e, qt, dst: e.dma_start(out=dst, in_=x2[qt * 128:(qt + 1) * 128, :]))
    x2eng = (io.get("x2_eng") if io else None) or "sync"
    x2dep = io.get("x2_dep") if io else None
    ktg_v = ktg.rearrange("(r n) m -> r n m", r=4)
    vg_v = vg.rearrange("(r n) m -> r n m", r=4)

    identf, identf_b = L.sb("identtf", [128, 128], F32)
    onesr, onesr_b = L.sb("onesr", [1, 512], BF16)
    onesc, onesc_b = L.sb("onesc", [128, 1], BF16)
    onesrf, onesrf_b = L.sb("onesrf", [1, 128], F32)
    kills, kills_b = L.sb("kills", [1, 64 * 16], BF16)
    tria, tria_b = L.sb("tria", [128, 128], F32)
    trim, trim_b = L.sb("trim", [128, 128], BF16)
    borow, borow_b = L.sb("borow", [1, D], F32)
    bobc, bobc_b = L.sb("bobc", [128, D], F32)
    qh, qh_b = L.sb("qh", [128, TOK], BF16)
    ktoh, ktoh_b = L.sb("ktoh", [128, TOK], BF16)
    voh, voh_b = L.sb("voh", [128, TOK], BF16)
    _kth = L.sb("kth0", [128, 4 * TOK], BF16)
    kth2 = [_kth, _kth]
    _vh = L.sb("vh0", [128, 4 * TOK], BF16)
    vh2 = [_vh, _vh]
    msb, msb_b = L.sb("msb", [128, 16], F32)
    mtmp, mtmp_b = L.sb("mtmp", [128, 1], F32)
    mneg, mneg_b = L.sb("mneg", [1, TOK], BF16)
    mrow, mrow_b = L.sb("mrow", [1, 128], F32)
    krow, krow_b = L.sb("krow", [1, 512], BF16)
    sdt, sdt_b = L.sb("sdt", [128, 128], F32)
    pts2 = [L.sb(f"pts{i}", [128, 512], BF16) for i in range(2)]
    ptd, ptd_b = L.sb("ptd", [128, 128], BF16)
    lrow, lrow_b = L.sb("lrow", [1, 512], F32)
    linv, linv_b = L.sb("linv", [1, 512], F32)
    linvbc, linvbc_b = L.sb("linvbc", [128, 512], F32)
    otsb, otsb_b = L.sb("otsb", [128, 16 * TOK], BF16)
    wot = [L.sb(f"wot{i}", [128, 512], BF16) for i in range(2)]
    ao4, ao4_b = L.sb("ao4", [128, 4 * D], F32)
    xq = [L.sb("xq0", [128, D], F32)]
    xq = [xq[0], xq[0]]
    stats, stats_b = L.sb("stats", [128, 32], F32)
    yt, yt_b = L.sb("yt", [128, D], F32)

    for t, bb_, src in ((identf, identf_b, identfi), (onesr, onesr_b, onesri),
                        (onesc, onesc_b, onesci), (onesrf, onesrf_b, onesrfi),
                        (kills, kills_b, killR), (tria, tria_b, triA),
                        (trim, trim_b, triM), (borow, borow_b, boR)):
        P.emit("sync", (lambda t, src: lambda e: e.dma_start(out=t[:], in_=src[:]))(t, src),
               writes=(bb_,), dma=True)
    for i in range(4):
        bank, bank_b = L.ps[4], L.ps_buf[4]
        P.emit("tensor", (lambda i, bank: lambda e: e.matmul(
            bank[:], onesrf[:], borow[:, i * 512:(i + 1) * 512], start=True, stop=True))(i, bank),
            reads=(onesrf_b, borow_b), writes=(bank_b,))
        P.emit("vector", (lambda i, bank: lambda e: e.tensor_copy(
            bobc[:, i * 512:(i + 1) * 512], bank[:]))(i, bank),
            reads=(bank_b,), writes=(bobc_b,))

    for h in range(16):
        kth, kth_b = kth2[h % 2]
        vh, vh_b = vh2[h % 2]
        P.emit("sync", (lambda h: lambda e: e.dma_start(
            out=qh[:], in_=qto[h * 128:(h + 1) * 128, :]))(h),
            writes=(qh_b,), dma=True)
        P.emit("sync", (lambda h: lambda e: e.dma_start(
            out=ktoh[:], in_=kto[h * 128:(h + 1) * 128, :]))(h),
            writes=(ktoh_b,), dma=True)
        P.emit("sync", (lambda h: lambda e: e.dma_start(
            out=voh[:].rearrange("p (s d) -> p s d", s=16),
            in_=vo[:, h * 128:(h + 1) * 128]
            .rearrange("(s p) d -> p s d", p=128)))(h),
            writes=(voh_b,), dma=True)
        for ug in range(16):
            P.emit("sync", (lambda h, ug, kth: lambda e: e.dma_start(
                out=kth[:, ug * 512:(ug + 1) * 512].rearrange("n (r m) -> n r m", r=4),
                in_=ktg_v[:, h * 128:(h + 1) * 128, ug * 128:(ug + 1) * 128]
                .rearrange("r n m -> n r m")))(h, ug, kth),
                writes=(kth_b,), dma=True)
            P.emit("sync", (lambda h, ug, vh: lambda e: e.dma_start(
                out=vh[:, ug * 512:(ug + 1) * 512].rearrange("t (r d) -> t r d", r=4),
                in_=vg_v[:, ug * 128:(ug + 1) * 128, h * 128:(h + 1) * 128]
                .rearrange("r t d -> t r d")))(h, ug, vh),
                writes=(vh_b,), dma=True)

        for s_ in range(16):
            for ug in range(s_, 16):
                bank, bank_b = L.ps[ug % 2], L.ps_buf[ug % 2]
                P.emit("tensor", (lambda s_, ug, bank, kth: lambda e: e.matmul(
                    bank[:], qh[:, s_ * 128:(s_ + 1) * 128],
                    kth[:, ug * 512:(ug + 1) * 512], start=True, stop=True))(s_, ug, bank, kth),
                    reads=(qh_b, kth_b), writes=(bank_b,))
                if ug == s_:
                    P.emit("vector", (lambda s_, bank: lambda e: e.reduce_max(
                        msb[:, s_:s_ + 1], bank[:], AX.X))(s_, bank),
                        reads=(bank_b,), writes=(msb_b,))
                else:
                    P.emit("vector", (lambda bank: lambda e: e.reduce_max(
                        mtmp[:], bank[:], AX.X))(bank),
                        reads=(bank_b,), writes=(mtmp_b,))
                    P.emit("vector", (lambda s_: lambda e: e.tensor_max(
                        msb[:, s_:s_ + 1], msb[:, s_:s_ + 1], mtmp[:]))(s_),
                        reads=(msb_b, mtmp_b), writes=(msb_b,))
            bank, bank_b = L.ps[2], L.ps_buf[2]
            P.emit("tensor", (lambda s_, bank: lambda e: e.matmul(
                bank[:, 0:128], qh[:, s_ * 128:(s_ + 1) * 128],
                ktoh[:, s_ * 128:(s_ + 1) * 128], start=True, stop=True))(s_, bank),
                reads=(qh_b, ktoh_b), writes=(bank_b,))
            P.emit("vector", (lambda bank: lambda e: e.scalar_tensor_tensor(
                sdt[:], bank[:, 0:128], 1.0, tria[:], ALU.mult, ALU.add))(bank),
                reads=(bank_b, tria_b), writes=(sdt_b,))
            P.emit("vector", lambda e: e.reduce_max(mtmp[:], sdt[:], AX.X),
                   reads=(sdt_b,), writes=(mtmp_b,))
            P.emit("vector", (lambda s_: lambda e: e.tensor_max(
                msb[:, s_:s_ + 1], msb[:, s_:s_ + 1], mtmp[:]))(s_),
                reads=(msb_b, mtmp_b), writes=(msb_b,))
        for s_ in range(16):
            bank, bank_b = L.ps[5], L.ps_buf[5]
            P.emit("tensor", (lambda s_, bank: lambda e: e.matmul(
                bank[0:1, 0:128], msb[:, s_:s_ + 1], identf[:], start=True, stop=True))(s_, bank),
                reads=(msb_b, identf_b), writes=(bank_b,))
            P.emit("vector", (lambda s_, bank: lambda e: e.tensor_scalar(
                mneg[0:1, s_ * 128:(s_ + 1) * 128], bank[0:1, 0:128],
                -1.0, None, ALU.mult))(s_, bank),
                reads=(bank_b,), writes=(mneg_b,))

        for g in range(4):
            obank, obank_b = L.ps[2], L.ps_buf[2]
            lbank, lbank_b = L.ps[3], L.ps_buf[3]
            first = True
            for u in range(16 * g, 64):
                bank, bank_b = L.ps[u % 2], L.ps_buf[u % 2]
                if u <= 16 * g + 15:
                    P.emit("vector", (lambda u, g: lambda e: e.tensor_tensor(
                        krow[:].rearrange("a (j k) -> a j k", j=4),
                        mneg[0:1, g * 512:(g + 1) * 512].rearrange("a (j k) -> a j k", j=4),
                        kills[0:1, u * 16 + 4 * g:u * 16 + 4 * g + 4]
                        .unsqueeze(2).broadcast_to([1, 4, 128]),
                        ALU.add))(u, g),
                        reads=(mneg_b, kills_b), writes=(krow_b,))
                    rrow, rrow_b2 = krow, krow_b
                else:
                    rrow, rrow_b2 = mneg, mneg_b
                rsl = slice(0, 512) if rrow is krow else slice(g * 512, (g + 1) * 512)
                P.emit("tensor", (lambda bank, rrow, rsl: lambda e: e.matmul(
                    bank[:], onesr[0:1, 0:128], rrow[0:1, rsl],
                    start=True, stop=False))(bank, rrow, rsl),
                    reads=(onesr_b, rrow_b2), writes=(bank_b,))
                P.emit("tensor", (lambda u, g, bank, kth: lambda e: e.matmul(
                    bank[:], kth[:, u * 128:(u + 1) * 128],
                    qh[:, g * 512:(g + 1) * 512], start=False, stop=True))(u, g, bank, kth),
                    reads=(kth_b, qh_b), writes=(bank_b,))
                pts, pts_b = pts2[u % 2]
                P.emit("scalar", (lambda bank, pts: lambda e: e.activation(
                    pts[:], bank[:], AF.Exp))(bank, pts),
                    reads=(bank_b,), writes=(pts_b,))
                P.emit("tensor", (lambda u, first, obank, vh, pts: lambda e: e.matmul(
                    obank[:], vh[:, u * 128:(u + 1) * 128], pts[:],
                    start=first, stop=False))(u, first, obank, vh, pts),
                    reads=(vh_b, pts_b), writes=(obank_b,))
                P.emit("tensor", (lambda first, lbank, pts: lambda e: e.matmul(
                    lbank[0:1, 0:512], onesc[:], pts[:],
                    start=first, stop=False))(first, lbank, pts),
                    reads=(onesc_b, pts_b), writes=(lbank_b,))
                first = False
            for j in range(4):
                s_ = 4 * g + j
                bank, bank_b = L.ps[j % 2], L.ps_buf[j % 2]
                P.emit("tensor", (lambda s_, bank: lambda e: e.matmul(
                    bank[:, 0:128], onesr[0:1, 0:128],
                    mneg[0:1, s_ * 128:(s_ + 1) * 128], start=True, stop=False))(s_, bank),
                    reads=(onesr_b, mneg_b), writes=(bank_b,))
                P.emit("tensor", (lambda s_, bank: lambda e: e.matmul(
                    bank[:, 0:128], ktoh[:, s_ * 128:(s_ + 1) * 128],
                    qh[:, s_ * 128:(s_ + 1) * 128], start=False, stop=True))(s_, bank),
                    reads=(ktoh_b, qh_b), writes=(bank_b,))
                P.emit("scalar", (lambda bank: lambda e: e.activation(
                    ptd[:], bank[:, 0:128], AF.Exp))(bank),
                    reads=(bank_b,), writes=(ptd_b,))
                P.emit("vector", lambda e: e.tensor_tensor(
                    ptd[:], ptd[:], trim[:], ALU.mult),
                    reads=(ptd_b, trim_b), writes=(ptd_b,))
                P.emit("tensor", (lambda s_, j, obank: lambda e: e.matmul(
                    obank[:, j * 128:(j + 1) * 128], voh[:, s_ * 128:(s_ + 1) * 128],
                    ptd[:], start=False, stop=(j == 3)))(s_, j, obank),
                    reads=(voh_b, ptd_b), writes=(obank_b,))
                P.emit("tensor", (lambda j, lbank: lambda e: e.matmul(
                    lbank[0:1, j * 128:(j + 1) * 128], onesc[:], ptd[:],
                    start=False, stop=(j == 3)))(j, lbank),
                    reads=(onesc_b, ptd_b), writes=(lbank_b,))
            P.emit("vector", (lambda lbank: lambda e: e.tensor_copy(
                lrow[:], lbank[0:1, 0:512]))(lbank),
                reads=(lbank_b,), writes=(lrow_b,))
            P.emit("vector", lambda e: e.reciprocal(linv[:], lrow[:]),
                   reads=(lrow_b,), writes=(linv_b,))
            bcb, bcb_b = L.ps[4], L.ps_buf[4]
            P.emit("tensor", (lambda bcb: lambda e: e.matmul(
                bcb[:], onesrf[:], linv[:], start=True, stop=True))(bcb),
                reads=(onesrf_b, linv_b), writes=(bcb_b,))
            P.emit("vector", (lambda bcb: lambda e: e.tensor_copy(
                linvbc[:], bcb[:]))(bcb),
                reads=(bcb_b,), writes=(linvbc_b,))
            P.emit("vector", (lambda h, g, obank: lambda e: e.tensor_tensor(
                otsb[:, (h * 4 + g) * 512:(h * 4 + g + 1) * 512], obank[:],
                linvbc[:], ALU.mult))(h, g, obank),
                reads=(obank_b, linvbc_b), writes=(otsb_b,))

    store_ops = []
    for qg in range(4):
        for oslab in range(4):
            for dc_ in range(16):
                wt, wt_b = wot[dc_ % 2]
                P.emit("gpsimd", (lambda dc_, oslab, wt: lambda e: e.dma_start(
                    out=wt[:], in_=woB[dc_, :, oslab * 512:(oslab + 1) * 512]))(dc_, oslab, wt),
                    writes=(wt_b,), dma=True)
                for j in range(4):
                    qt = qg * 4 + j
                    bank, bank_b = L.ps[j], L.ps_buf[j]
                    P.emit("tensor", (lambda dc_, qt, wt, bank: lambda e: e.matmul(
                        bank[:], otsb[:, dc_ * TOK + qt * 128:dc_ * TOK + (qt + 1) * 128],
                        wt[:], start=(dc_ == 0), stop=(dc_ == 15)))(dc_, qt, wt, bank),
                        reads=(otsb_b, wt_b), writes=(bank_b,))
            for j in range(4):
                bank, bank_b = L.ps[j], L.ps_buf[j]
                P.emit("vector", (lambda j, oslab, bank: lambda e: e.tensor_tensor(
                    ao4[:, j * D + oslab * 512:j * D + (oslab + 1) * 512], bank[:],
                    bobc[:, oslab * 512:(oslab + 1) * 512], ALU.add))(j, oslab, bank),
                    reads=(bank_b, bobc_b), writes=(ao4_b,))
        for j in range(4):
            qt = qg * 4 + j
            xi, xi_b = xq[j % 2]
            P.emit(x2eng, (lambda qt, xi: lambda e: x2load(e, qt, xi[:]))(qt, xi),
                   writes=(xi_b,), dma=True, deps=(x2dep,))
            ao_sl, ao_sl_b = ao4, ao4_b

            def _mk_ln(j):
                pass
            n = 4
            for i in range(n):
                P.emit("vector", (lambda i, j: lambda e: e.bn_stats(
                    stats[:, i * 6:(i + 1) * 6],
                    ao4[:, j * D + i * 512:j * D + (i + 1) * 512]))(i, j),
                    reads=(ao4_b,), writes=(stats_b,))
            P.emit("vector", lambda e: e.bn_aggr(stats[:, 24:26], stats[:, 0:24]),
                   reads=(stats_b,), writes=(stats_b,))
            P.emit("vector", lambda e: e.tensor_scalar(stats[:, 26:27], stats[:, 24:25],
                                                       -1.0, None, ALU.mult),
                   reads=(stats_b,), writes=(stats_b,))
            P.emit("scalar", lambda e: e.activation(stats[:, 27:28], stats[:, 25:26],
                                                    AF.Sqrt, bias=L.consts[:, 0:1]),
                   reads=(stats_b, L.consts_b), writes=(stats_b,))
            P.emit("vector", lambda e: e.reciprocal(stats[:, 28:29], stats[:, 27:28]),
                   reads=(stats_b,), writes=(stats_b,))
            P.emit("vector", (lambda j: lambda e: e.tensor_scalar(
                yt[:], ao4[:, j * D:(j + 1) * D], stats[:, 26:27], stats[:, 28:29],
                ALU.add, ALU.mult))(j),
                reads=(ao4_b, stats_b), writes=(yt_b,))
            P.emit("vector", (lambda xi: lambda e: e.tensor_tensor(
                yt[:], yt[:], xi[:], ALU.add))(xi),
                reads=(yt_b, xi_b), writes=(yt_b,))
            store_ops.append(P.emit("sync", (lambda qt: lambda e: e.dma_start(
                out=x3[qt * 128:(qt + 1) * 128, :], in_=yt[:]))(qt),
                reads=(yt_b,), dma=True))

    P.emit("sync", lambda e: e.nop(), deps=store_ops)
    if fin:
        P.finalize(nc, es)
    return nc


def host_prep_l1_attn(inputs, core):
    b, c = core // 4, core % 4
    Wo = inputs["L1_attn_Wo"]
    woB = np.ascontiguousarray(Wo.reshape(16, 128, D))
    boR = inputs["L1_attn_bo"].reshape(1, D)
    kill = np.zeros((64, 16), dtype=np.float32)
    for u in range(64):
        for s_ in range(16):
            if u <= 4 * s_ + c:
                kill[u, s_] = NEG
    killR = kill.reshape(1, 64 * 16).astype(bf16)
    idx = np.arange(128)
    triA = np.where(idx[None, :] <= idx[:, None], 0.0, NEG).astype(np.float32)
    triM = (idx[:, None] <= idx[None, :]).astype(bf16)
    return dict(woB=woB, bo_row=boR, kill_slots=killR, tri_add=triA, tri01T=triM,
                ident_f=np.eye(128, dtype=np.float32),
                ones_row=np.ones((1, 512), dtype=bf16),
                ones_col=np.ones((128, 1), dtype=bf16),
                ones_row_f=np.ones((1, 128), dtype=np.float32))



def build_l0_attn(nc, es, L=None, io=None):
    fin = L is None
    if L is None:
        L = LaunchCtx(nc, es)
    P = L.P

    xo = dram(io, nc, "x_own", [TOK, D], F32, "ExternalInput")
    xh = dram(io, nc, "x_halo", [1024, D], F32, "ExternalInput")
    wqB = dram(io, nc, "wqB", [16, 128, D], F32, "ExternalInput")
    wkB = dram(io, nc, "wkB", [16, 128, 512], F32, "ExternalInput")
    wvB = dram(io, nc, "wvB", [16, 128, 512], F32, "ExternalInput")
    woB = dram(io, nc, "woB", [16, 128, D], F32, "ExternalInput")
    gainR = dram(io, nc, "gain_row", [1, 16], F32, "ExternalInput")
    roc = dram(io, nc, "rope_own_c", [TOK, 64], F32, "ExternalInput")
    ros = dram(io, nc, "rope_own_s", [TOK, 64], F32, "ExternalInput")
    rhc = dram(io, nc, "rope_halo_c", [1024, 64], F32, "ExternalInput")
    rhs_ = dram(io, nc, "rope_halo_s", [1024, 64], F32, "ExternalInput")
    mskI = dram(io, nc, "maskadd", [2, 8, 5, 128, 128], F32, "ExternalInput")
    identi = dram(io, nc, "ident_bf", [128, 128], BF16, "ExternalInput")
    onesrfi = dram(io, nc, "ones_row_f", [1, 128], F32, "ExternalInput")
    x1 = dram(io, nc, "x1", [TOK, D], F32, "ExternalOutput")

    ident, ident_b = L.sb("identt", [128, 128], BF16)
    onesrf, onesrf_b = L.sb("onesrf", [1, 128], F32)
    gains, gains_b = L.sb("gains", [1, 16], F32)
    gsc, gsc_b = L.sb("gsc", [128, 16], F32)
    wks, wks_b = L.sb("wks", [128, 16 * 512], BF16)
    wvs, wvs_b = L.sb("wvs", [128, 16 * 512], BF16)
    wqsl, wqsl_b = L.sb("wqsl", [128, 16 * 512], BF16)
    ropeo, ropeo_b = L.sb("ropeo", [128, 16 * 2 * 64], F32)
    ropeh, ropeh_b = L.sb("ropeh", [128, 8 * 2 * 64], F32)
    xi, xi_b = L.sb("xi", [128, D], F32)
    t0bf, t0bf_b = L.sb("t0bf", [128, D], BF16)
    stats, stats_b = L.sb("stats", [128, 32], F32)
    u16, u16_b = L.sb("u16", [128, 4096], BF16)
    tT, tT_b = u16[:, 0:2048], u16_b
    t0T = [L.sb(f"t0T{i}", [128, 16 * 128], BF16) for i in range(8)]
    ktblk, ktblk_b = L.sb("ktblk", [128, 4 * 12 * 128], BF16)
    vblk, vblk_b = L.sb("vblk", [128, 12 * 512], BF16)
    kvtmp, kvtmp_b = L.sb("kvtmp", [128, 512], F32)
    kvsq, kvsq_b = L.sb("kvsq", [128, 512], F32)
    kvss, kvss_b = L.sb("kvss", [128, 16], F32)
    krope, krope_b = L.sb("krope", [128, 512], BF16)
    qpart, qpart_b = L.sb("qpart", [128, 512], F32)
    qro, qro_b = L.sb("qro", [128, 512], BF16)
    qtos, qtos_b = u16[:, 0:4096], u16_b
    msk, msk_b = L.sb("msk", [128, 5 * 128], F32)
    ssb, ssb_b = L.sb("ssb", [128, 640], F32)
    mcol, mcol_b = L.sb("mcol", [128, 1], F32)
    lcol, lcol_b = L.sb("lcol", [128, 1], F32)
    licol, licol_b = L.sb("licol", [128, 1], F32)
    psb_, psb_b_ = L.sb("psb_", [128, 640], BF16)
    ptsb, ptsb_b = L.sb("ptsb", [128, 640], BF16)
    oblk, oblk_b = L.sb("oblk", [128, 8 * D], BF16)
    ot2, ot2_b = u16, u16_b
    wots = [L.sb(f"wots{i}", [128, 512], BF16) for i in range(2)]
    ao2, ao2_b = L.sb("ao2", [128, 2 * D], F32)
    yt, yt_b = u16[:].bitcast(F32), u16_b

    P.emit("sync", lambda e: e.dma_start(out=ident[:], in_=identi[:]), writes=(ident_b,), dma=True)
    P.emit("sync", lambda e: e.dma_start(out=onesrf[:], in_=onesrfi[:]), writes=(onesrf_b,), dma=True)
    P.emit("sync", lambda e: e.dma_start(out=gains[:], in_=gainR[:]), writes=(gains_b,), dma=True)
    bank, bank_b = L.ps[4], L.ps_buf[4]
    P.emit("tensor", (lambda bank: lambda e: e.matmul(
        bank[:, 0:16], onesrf[:], gains[:], start=True, stop=True))(bank),
        reads=(onesrf_b, gains_b), writes=(bank_b,))
    P.emit("vector", (lambda bank: lambda e: e.tensor_scalar(
        gsc[:], bank[:, 0:16], float(1.0 / np.sqrt(128.0)), None, ALU.mult))(bank),
        reads=(bank_b,), writes=(gsc_b,))
    for dc_ in range(DC):
        P.emit("gpsimd", (lambda dc_: lambda e: e.dma_start(
            out=wks[:, dc_ * 512:(dc_ + 1) * 512], in_=wkB[dc_]))(dc_),
            writes=(wks_b,), dma=True)
        P.emit("gpsimd", (lambda dc_: lambda e: e.dma_start(
            out=wvs[:, dc_ * 512:(dc_ + 1) * 512], in_=wvB[dc_]))(dc_),
            writes=(wvs_b,), dma=True)
    for dst_t, dst_tb, srcs, nt_ in ((ropeo, ropeo_b, (roc, ros), 16),
                                     (ropeh, ropeh_b, (rhc, rhs_), 8)):
        for ci, srct in enumerate(srcs):
            P.emit("sync", (lambda dst_t, srct, ci, nt_: lambda e: e.dma_start(
                out=dst_t[:].rearrange("p (t two f) -> p t two f", t=nt_, two=2)[:, :, ci, :],
                in_=srct[:].rearrange("(t p) f -> p t f", p=128)))(dst_t, srct, ci, nt_),
                writes=(dst_tb,), dma=True)

    store_ops = []
    for blk in range(2):
        for m in range(12):
            if m < 4:
                src = xh[blk * 512 + m * 128: blk * 512 + (m + 1) * 128, :]
            else:
                src = xo[blk * 1024 + (m - 4) * 128: blk * 1024 + (m - 3) * 128, :]
            P.emit("sync", (lambda src: lambda e: e.dma_start(out=xi[:], in_=src))(src),
                   writes=(xi_b,), dma=True)
            emit_ln_tile(L, t0bf, t0bf_b, xi, xi_b, stats, stats_b)
            if m >= 4:
                dstT, dstT_b = t0T[m - 4]
            else:
                dstT, dstT_b = tT, tT_b
            emit_transposes(L, dstT, dstT_b, t0bf, t0bf_b, ident, ident_b,
                            [(dc_ * 128, dc_ * 128) for dc_ in range(DC)])
            bank, bank_b = L.ps[4], L.ps_buf[4]
            for dc_ in range(DC):
                P.emit("tensor", (lambda dc_, dstT, bank: lambda e: e.matmul(
                    bank[:], dstT[:, dc_ * 128:(dc_ + 1) * 128],
                    wks[:, dc_ * 512:(dc_ + 1) * 512],
                    start=(dc_ == 0), stop=(dc_ == DC - 1)))(dc_, dstT, bank),
                    reads=(dstT_b, wks_b), writes=(bank_b,))
            P.emit("vector", (lambda bank: lambda e: e.tensor_copy(
                kvtmp[:], bank[:]))(bank), reads=(bank_b,), writes=(kvtmp_b,))
            P.emit("scalar", lambda e: e.activation(kvsq[:], kvtmp[:], AF.Square),
                   reads=(kvtmp_b,), writes=(kvsq_b,))
            P.emit("vector", lambda e: e.tensor_reduce(
                kvss[:, 0:4], kvsq[:].rearrange("p (h d) -> p h d", h=4), AX.X, ALU.add),
                reads=(kvsq_b,), writes=(kvss_b,))
            P.emit("scalar", lambda e: e.activation(
                kvss[:, 4:8], kvss[:, 0:4], AF.Sqrt, bias=L.consts[:, 1:2],
                scale=float(1.0 / 128.0)),
                reads=(kvss_b, L.consts_b), writes=(kvss_b,))
            P.emit("vector", lambda e: e.reciprocal(kvss[:, 8:12], kvss[:, 4:8]),
                   reads=(kvss_b,), writes=(kvss_b,))
            P.emit("vector", lambda e: e.tensor_tensor(
                kvtmp[:].rearrange("p (h d) -> p h d", h=4), kvtmp[:].rearrange("p (h d) -> p h d", h=4),
                kvss[:, 8:12].unsqueeze(2).broadcast_to([128, 4, 128]), ALU.mult),
                reads=(kvtmp_b, kvss_b), writes=(kvtmp_b,))
            rop, rop_b = (ropeh, ropeh_b) if m < 4 else (ropeo, ropeo_b)
            ti = (blk * 4 + m) if m < 4 else (blk * 8 + (m - 4))
            cview = rop[:, (2 * ti) * 64:(2 * ti + 1) * 64].unsqueeze(1).broadcast_to([128, 4, 64])
            sview = rop[:, (2 * ti + 1) * 64:(2 * ti + 2) * 64].unsqueeze(1).broadcast_to([128, 4, 64])
            x1v = kvtmp[:].rearrange("p (h two d) -> p h two d", h=4, two=2)[:, :, 0, :]
            x2v = kvtmp[:].rearrange("p (h two d) -> p h two d", h=4, two=2)[:, :, 1, :]
            k1v = krope[:].rearrange("p (h two d) -> p h two d", h=4, two=2)[:, :, 0, :]
            k2v = krope[:].rearrange("p (h two d) -> p h two d", h=4, two=2)[:, :, 1, :]
            r1v = kvsq[:].rearrange("p (h d) -> p h d", h=4)[:, :, 0:64]
            r2v = kvsq[:].rearrange("p (h d) -> p h d", h=4)[:, :, 64:128]
            P.emit("vector", (lambda x1v, cview: lambda e: e.tensor_tensor(
                r1v, x1v, cview, ALU.mult))(x1v, cview),
                reads=(kvtmp_b, rop_b), writes=(kvsq_b,))
            P.emit("vector", (lambda x2v, sview: lambda e: e.tensor_tensor(
                r2v, x2v, sview, ALU.mult))(x2v, sview),
                reads=(kvtmp_b, rop_b), writes=(kvsq_b,))
            P.emit("vector", (lambda k1v: lambda e: e.tensor_tensor(
                k1v, r1v, r2v, ALU.add))(k1v),
                reads=(kvsq_b,), writes=(krope_b,))
            P.emit("vector", (lambda x2v, cview: lambda e: e.tensor_tensor(
                r1v, x2v, cview, ALU.mult))(x2v, cview),
                reads=(kvtmp_b, rop_b), writes=(kvsq_b,))
            P.emit("vector", (lambda x1v, sview: lambda e: e.tensor_tensor(
                r2v, x1v, sview, ALU.mult))(x1v, sview),
                reads=(kvtmp_b, rop_b), writes=(kvsq_b,))
            P.emit("vector", (lambda k2v: lambda e: e.tensor_tensor(
                k2v, r1v, r2v, ALU.subtract))(k2v),
                reads=(kvsq_b,), writes=(krope_b,))
            emit_transposes(L, ktblk, ktblk_b, krope, krope_b, ident, ident_b,
                            [(kv * 128, (kv * 12 + m) * 128) for kv in range(4)])
            bank, bank_b = L.ps[5], L.ps_buf[5]
            for dc_ in range(DC):
                P.emit("tensor", (lambda dc_, dstT, bank: lambda e: e.matmul(
                    bank[:], dstT[:, dc_ * 128:(dc_ + 1) * 128],
                    wvs[:, dc_ * 512:(dc_ + 1) * 512],
                    start=(dc_ == 0), stop=(dc_ == DC - 1)))(dc_, dstT, bank),
                    reads=(dstT_b, wvs_b), writes=(bank_b,))
            P.emit("vector", (lambda m, bank: lambda e: e.tensor_copy(
                vblk[:, m * 512:(m + 1) * 512], bank[:]))(m, bank),
                reads=(bank_b,), writes=(vblk_b,))

        for oslab in range(4):
            for dc_ in range(DC):
                P.emit("gpsimd", (lambda dc_, oslab, wqsl: lambda e: e.dma_start(
                    out=wqsl[:, dc_ * 512:(dc_ + 1) * 512],
                    in_=wqB[dc_, :, oslab * 512:(oslab + 1) * 512]))(dc_, oslab, wqsl),
                    writes=(wqsl_b,), dma=True)
            for qt in range(8):
                dstT, dstT_b = t0T[qt]
                bank, bank_b = L.ps[3], L.ps_buf[3]
                for dc_ in range(DC):
                    P.emit("tensor", (lambda dc_, dstT, bank, wqsl: lambda e: e.matmul(
                        bank[:], dstT[:, dc_ * 128:(dc_ + 1) * 128],
                        wqsl[:, dc_ * 512:(dc_ + 1) * 512],
                        start=(dc_ == 0), stop=(dc_ == DC - 1)))(dc_, dstT, bank, wqsl),
                        reads=(dstT_b, wqsl_b), writes=(bank_b,))
                P.emit("vector", (lambda bank: lambda e: e.tensor_copy(
                    qpart[:], bank[:]))(bank), reads=(bank_b,), writes=(qpart_b,))
                P.emit("scalar", lambda e: e.activation(kvsq[:], qpart[:], AF.Square),
                       reads=(qpart_b,), writes=(kvsq_b,))
                P.emit("vector", lambda e: e.tensor_reduce(
                    kvss[:, 0:4], kvsq[:].rearrange("p (h d) -> p h d", h=4), AX.X, ALU.add),
                    reads=(kvsq_b,), writes=(kvss_b,))
                P.emit("scalar", lambda e: e.activation(
                    kvss[:, 4:8], kvss[:, 0:4], AF.Sqrt, bias=L.consts[:, 1:2],
                    scale=float(1.0 / 128.0)),
                    reads=(kvss_b, L.consts_b), writes=(kvss_b,))
                P.emit("vector", lambda e: e.reciprocal(kvss[:, 8:12], kvss[:, 4:8]),
                       reads=(kvss_b,), writes=(kvss_b,))
                P.emit("vector", lambda e: e.tensor_tensor(
                    qpart[:].rearrange("p (h d) -> p h d", h=4), qpart[:].rearrange("p (h d) -> p h d", h=4),
                    kvss[:, 8:12].unsqueeze(2).broadcast_to([128, 4, 128]), ALU.mult),
                    reads=(qpart_b, kvss_b), writes=(qpart_b,))
                ti = blk * 8 + qt
                cview = ropeo[:, (2 * ti) * 64:(2 * ti + 1) * 64].unsqueeze(1).broadcast_to([128, 4, 64])
                sview = ropeo[:, (2 * ti + 1) * 64:(2 * ti + 2) * 64].unsqueeze(1).broadcast_to([128, 4, 64])
                x1v = qpart[:].rearrange("p (h two d) -> p h two d", h=4, two=2)[:, :, 0, :]
                x2v = qpart[:].rearrange("p (h two d) -> p h two d", h=4, two=2)[:, :, 1, :]
                q1v = qro[:].rearrange("p (h two d) -> p h two d", h=4, two=2)[:, :, 0, :]
                q2v = qro[:].rearrange("p (h two d) -> p h two d", h=4, two=2)[:, :, 1, :]
                r1v = kvsq[:].rearrange("p (h d) -> p h d", h=4)[:, :, 0:64]
                r2v = kvsq[:].rearrange("p (h d) -> p h d", h=4)[:, :, 64:128]
                P.emit("vector", (lambda x1v, cview: lambda e: e.tensor_tensor(
                    r1v, x1v, cview, ALU.mult))(x1v, cview),
                    reads=(qpart_b, ropeo_b), writes=(kvsq_b,))
                P.emit("vector", (lambda x2v, sview: lambda e: e.tensor_tensor(
                    r2v, x2v, sview, ALU.mult))(x2v, sview),
                    reads=(qpart_b, ropeo_b), writes=(kvsq_b,))
                P.emit("vector", (lambda q1v: lambda e: e.tensor_tensor(
                    q1v, r1v, r2v, ALU.add))(q1v),
                    reads=(kvsq_b,), writes=(qro_b,))
                P.emit("vector", (lambda x2v, cview: lambda e: e.tensor_tensor(
                    r1v, x2v, cview, ALU.mult))(x2v, cview),
                    reads=(qpart_b, ropeo_b), writes=(kvsq_b,))
                P.emit("vector", (lambda x1v, sview: lambda e: e.tensor_tensor(
                    r2v, x1v, sview, ALU.mult))(x1v, sview),
                    reads=(qpart_b, ropeo_b), writes=(kvsq_b,))
                P.emit("vector", (lambda q2v: lambda e: e.tensor_tensor(
                    q2v, r1v, r2v, ALU.subtract))(q2v),
                    reads=(kvsq_b,), writes=(qro_b,))
                emit_transposes(L, qtos, qtos_b, qro, qro_b, ident, ident_b,
                                [(hl * 128, (hl * 8 + qt) * 128) for hl in range(4)])
            for qt in range(8):
                P.emit("sync", (lambda blk, qt: lambda e: e.dma_start(
                    out=msk[:].rearrange("p (r k) -> p r k", r=5),
                    in_=mskI[blk, qt].rearrange("r p k -> p r k")))(blk, qt),
                    writes=(msk_b,), dma=True)
                for hl in range(4):
                    h = oslab * 4 + hl
                    kv = h // 4
                    b0, b0_b = L.ps[0], L.ps_buf[0]
                    b1, b1_b = L.ps[1], L.ps_buf[1]
                    P.emit("tensor", (lambda hl, qt, kv, b0: lambda e: e.matmul(
                        b0[:], qtos[:, (hl * 8 + qt) * 128:(hl * 8 + qt + 1) * 128],
                        ktblk[:, (kv * 12 + qt) * 128:(kv * 12 + qt) * 128 + 512],
                        start=True, stop=True))(hl, qt, kv, b0),
                        reads=(qtos_b, ktblk_b), writes=(b0_b,))
                    P.emit("tensor", (lambda hl, qt, kv, b1: lambda e: e.matmul(
                        b1[:, 0:128], qtos[:, (hl * 8 + qt) * 128:(hl * 8 + qt + 1) * 128],
                        ktblk[:, (kv * 12 + qt + 4) * 128:(kv * 12 + qt + 5) * 128],
                        start=True, stop=True))(hl, qt, kv, b1),
                        reads=(qtos_b, ktblk_b), writes=(b1_b,))
                    P.emit("vector", (lambda h, b0: lambda e: e.scalar_tensor_tensor(
                        ssb[:, 0:512], b0[:], gsc[:, h:h + 1], msk[:, 0:512],
                        ALU.mult, ALU.add))(h, b0),
                        reads=(b0_b, gsc_b, msk_b), writes=(ssb_b,))
                    P.emit("vector", (lambda h, b1: lambda e: e.scalar_tensor_tensor(
                        ssb[:, 512:640], b1[:, 0:128], gsc[:, h:h + 1], msk[:, 512:640],
                        ALU.mult, ALU.add))(h, b1),
                        reads=(b1_b, gsc_b, msk_b), writes=(ssb_b,))
                    P.emit("vector", lambda e: e.reduce_max(mcol[:], ssb[:], AX.X),
                           reads=(ssb_b,), writes=(mcol_b,))
                    P.emit("vector", lambda e: e.tensor_scalar(
                        mcol[:], mcol[:], -1.0, None, ALU.mult),
                        reads=(mcol_b,), writes=(mcol_b,))
                    P.emit("scalar", lambda e: e.activation(
                        psb_[:], ssb[:], AF.Exp, bias=mcol[:], accum_out=lcol[:]),
                        reads=(ssb_b, mcol_b), writes=(psb_b_, lcol_b))
                    P.emit("vector", lambda e: e.reciprocal(licol[:], lcol[:]),
                           reads=(lcol_b,), writes=(licol_b,))
                    emit_transposes(L, ptsb, ptsb_b, psb_, psb_b_, ident, ident_b,
                                    [(w * 128, w * 128) for w in range(5)])
                    ob, ob_b = L.ps[2], L.ps_buf[2]
                    for w in range(5):
                        P.emit("tensor", (lambda w, qt, kv, ob: lambda e: e.matmul(
                            ob[:, 0:128], ptsb[:, w * 128:(w + 1) * 128],
                            vblk[:, (qt + w) * 512 + kv * 128:(qt + w) * 512 + (kv + 1) * 128],
                            start=(w == 0), stop=(w == 4)))(w, qt, kv, ob),
                            reads=(ptsb_b, vblk_b), writes=(ob_b,))
                    P.emit("vector", (lambda qt, h, ob: lambda e: e.tensor_scalar(
                        oblk[:, qt * D + h * 128:qt * D + (h + 1) * 128], ob[:, 0:128],
                        licol[:], None, ALU.mult))(qt, h, ob),
                        reads=(ob_b, licol_b), writes=(oblk_b,))

        for qgrp in range(4):
            for j in range(2):
                qt = qgrp * 2 + j
                emit_transposes(L, ot2, ot2_b, oblk, oblk_b, ident, ident_b,
                                [(qt * D + dc_ * 128, (j * 16 + dc_) * 128) for dc_ in range(DC)])
            for oslab in range(4):
                for dc_ in range(DC):
                    wt, wt_b = wots[dc_ % 2]
                    P.emit("gpsimd", (lambda dc_, oslab, wt: lambda e: e.dma_start(
                        out=wt[:], in_=woB[dc_, :, oslab * 512:(oslab + 1) * 512]))(dc_, oslab, wt),
                        writes=(wt_b,), dma=True)
                    for j in range(2):
                        bank, bank_b = L.ps[j], L.ps_buf[j]
                        P.emit("tensor", (lambda j, dc_, wt, bank: lambda e: e.matmul(
                            bank[:], ot2[:, (j * 16 + dc_) * 128:(j * 16 + dc_ + 1) * 128],
                            wt[:], start=(dc_ == 0), stop=(dc_ == DC - 1)))(j, dc_, wt, bank),
                            reads=(ot2_b, wt_b), writes=(bank_b,))
                for j in range(2):
                    bank, bank_b = L.ps[j], L.ps_buf[j]
                    P.emit("vector", (lambda j, oslab, bank: lambda e: e.tensor_copy(
                        ao2[:, j * D + oslab * 512:j * D + (oslab + 1) * 512], bank[:]))(j, oslab, bank),
                        reads=(bank_b,), writes=(ao2_b,))
            for j in range(2):
                qt = qgrp * 2 + j
                row0 = blk * 1024 + qt * 128
                P.emit("sync", (lambda row0: lambda e: e.dma_start(
                    out=xi[:], in_=xo[row0:row0 + 128, :]))(row0),
                    writes=(xi_b,), dma=True)
                for i in range(4):
                    P.emit("vector", (lambda i, j: lambda e: e.bn_stats(
                        stats[:, i * 6:(i + 1) * 6],
                        ao2[:, j * D + i * 512:j * D + (i + 1) * 512]))(i, j),
                        reads=(ao2_b,), writes=(stats_b,))
                P.emit("vector", lambda e: e.bn_aggr(stats[:, 24:26], stats[:, 0:24]),
                       reads=(stats_b,), writes=(stats_b,))
                P.emit("vector", lambda e: e.tensor_scalar(stats[:, 26:27], stats[:, 24:25],
                                                           -1.0, None, ALU.mult),
                       reads=(stats_b,), writes=(stats_b,))
                P.emit("scalar", lambda e: e.activation(stats[:, 27:28], stats[:, 25:26],
                                                        AF.Sqrt, bias=L.consts[:, 0:1]),
                       reads=(stats_b, L.consts_b), writes=(stats_b,))
                P.emit("vector", lambda e: e.reciprocal(stats[:, 28:29], stats[:, 27:28]),
                       reads=(stats_b,), writes=(stats_b,))
                P.emit("vector", (lambda j: lambda e: e.tensor_scalar(
                    yt[:], ao2[:, j * D:(j + 1) * D], stats[:, 26:27], stats[:, 28:29],
                    ALU.add, ALU.mult))(j),
                    reads=(ao2_b, stats_b), writes=(yt_b,))
                P.emit("vector", lambda e: e.tensor_tensor(yt[:], yt[:], xi[:], ALU.add),
                       reads=(yt_b, xi_b), writes=(yt_b,))
                store_ops.append(P.emit("sync", (lambda row0: lambda e: e.dma_start(
                    out=x1[row0:row0 + 128, :], in_=yt[:]))(row0),
                    reads=(yt_b,), dma=True))

    P.emit("sync", lambda e: e.nop(), deps=store_ops)
    if fin:
        P.finalize(nc, es)
    return nc


def host_prep_l0_attn(inputs, core):
    b, c = core // 4, core % 4
    blocks = [c, 7 - c]
    x = inputs["L0_attn_x"][b]
    xo = np.concatenate([x[j * 1024:(j + 1) * 1024] for j in blocks], axis=0)
    halos = []
    for j in blocks:
        lo = j * 1024 - 512
        if lo < 0:
            halos.append(np.zeros((512, D), np.float32))
        else:
            halos.append(x[lo:lo + 512])
    xh = np.concatenate(halos, axis=0)
    wqB = np.ascontiguousarray(inputs["L0_attn_Wq"].T.reshape(16, 128, D))
    wkB = np.ascontiguousarray(inputs["L0_attn_Wk"].T.reshape(16, 128, 512))
    wvB = np.ascontiguousarray(inputs["L0_attn_Wv"].T.reshape(16, 128, 512))
    woB = np.ascontiguousarray(inputs["L0_attn_Wo"].T.reshape(16, 128, D))
    gain = inputs["L0_attn_q_gain"].reshape(1, 16)
    inv_freq = (np.float32(10000.0) ** (-np.arange(0, 64, dtype=np.float32) / np.float32(64))).astype(np.float32)
    own_rows = np.concatenate([np.arange(j * 1024, (j + 1) * 1024) for j in blocks])
    halo_rows = np.concatenate([np.arange(max(0, j * 1024 - 512), max(0, j * 1024 - 512) + 512)
                                if j * 1024 - 512 >= 0 else np.zeros(512, np.int64)
                                for j in blocks])
    fo = own_rows.astype(np.float32)[:, None] * inv_freq[None, :]
    fh = halo_rows.astype(np.float32)[:, None] * inv_freq[None, :]
    mask = np.empty((2, 8, 5, 128, 128), np.float32)
    i_ = np.arange(128)
    for bi, j in enumerate(blocks):
        bstart = j * 1024
        for qt in range(8):
            q0 = bstart + qt * 128
            for w in range(5):
                k0 = bstart - 512 + (qt + w) * 128
                kk = k0 + i_[None, :]
                qq = q0 + i_[:, None]
                ok = (kk <= qq) & (kk >= qq - (WIN - 1)) & (kk >= 0)
                mask[bi, qt, w] = np.where(ok, 0.0, NEG)
    return dict(x_own=xo, x_halo=xh, wqB=wqB, wkB=wkB, wvB=wvB, woB=woB,
                gain_row=gain,
                rope_own_c=np.cos(fo).astype(np.float32),
                rope_own_s=np.sin(fo).astype(np.float32),
                rope_halo_c=np.cos(fh).astype(np.float32),
                rope_halo_s=np.sin(fh).astype(np.float32),
                maskadd=mask,
                ident_bf=np.eye(128, dtype=bf16),
                ones_row_f=np.ones((1, 128), dtype=np.float32))



def build_l0_ffn(nc, es, L=None, io=None):
    fin = L is None
    if L is None:
        L = LaunchCtx(nc, es)
    P = L.P

    x1 = dram(io, nc, "x1", [TOK, D], F32, "ExternalInput")
    w1 = dram(io, nc, "w1", [128, 128, D], F32, "ExternalInput")
    wp = dram(io, nc, "wp", [16, 128, FF0], F32, "ExternalInput")
    identi = dram(io, nc, "ident_bf", [128, 128], BF16, "ExternalInput")
    x2 = dram(io, nc, "x2", [TOK, D], F32, "ExternalOutput")
    w1hat = dram(io, nc, "w1hat", [2 * FF0, D], BF16, "Internal")
    wphat = dram(io, nc, "wphat", [D, FF0], BF16, "Internal")
    hscr = dram(io, nc, "hscr", [TOK, FF0], BF16, "Internal")

    ident, ident_b = L.sb("identt", [128, 128], BF16)
    uqs = [L.sb(f"uq{i}", [128, 2 * 2048], BF16) for i in range(2)]
    qsc = [L.sb(f"qsc{i}", [128, 96], F32) for i in range(2)]
    outt4, outt4_b2 = L.sb("outt4b", [128, 4 * D], F32)
    xi, xi_b = L.sb("xi", [128, D], F32)
    t1bf, t1bf_b = L.sb("t1bf", [128, D], BF16)
    stats, stats_b = L.sb("stats", [128, 32], F32)
    t1T, t1T_b = L.sb("t1T", [128, DC * TOK], BF16)
    gusl, gusl_b = L.sb("gusl", [128, 64 * 512], BF16)
    sg, sg_b = L.sb("sg", [128, 512], F32)
    hseg, hseg_b = L.sb("hseg", [128, 512], F32)
    hbf, hbf_b = L.sb("hbf", [128, 512], BF16)
    hsq, hsq_b = L.sb("hsq", [128, 512], F32)
    hcol, hcol_b = L.sb("hcol", [128, 1], F32)
    hss, hss_b = L.sb("hss", [128, 48], F32)
    wpt = [L.sb(f"wpt{i}", [128, 512], BF16) for i in range(2)]
    outt4_b = outt4_b2
    yt, yt_b = L.sb("yt", [128, D], F32)

    P.emit("sync", lambda e: e.dma_start(out=ident[:], in_=identi[:]), writes=(ident_b,), dma=True)
    P.emit("vector", lambda e: e.memset(hss[:, 0:16], 0.0), writes=(hss_b,))

    store_ops = []

    qn = [0]

    def quant_tile(src_ap, dst_ap):
        i = qn[0] % 2
        qn[0] += 1
        uq, uq_b = uqs[i]
        sc, sc_b = qsc[i]
        wqb = uq[:, 0:2048]
        q0 = uq[:, 2048:4096]
        asum = sc[:, 0:32]
        scl = sc[:, 32:64].bitcast(BF16)[:, 0:32]
        rec = sc[:, 64:96]
        P.emit("gpsimd", (lambda src_ap, wqb: lambda e: e.dma_start(
            out=wqb, in_=src_ap))(src_ap, wqb),
            writes=(uq_b,), dma=True)
        P.emit("vector", (lambda wqb, asum: lambda e: e.tensor_reduce(
            asum, wqb.rearrange("p (g k) -> p g k", k=64),
            AX.X, ALU.add, apply_absolute_value=True))(wqb, asum),
            reads=(uq_b,), writes=(sc_b,))
        P.emit("vector", (lambda asum, scl: lambda e: e.tensor_scalar(
            scl, asum, float(1.0 / 64.0), 1e-8, ALU.mult, ALU.max))(asum, scl),
            reads=(sc_b,), writes=(sc_b,))
        P.emit("vector", (lambda scl, rec: lambda e: e.reciprocal(rec, scl))(scl, rec),
               reads=(sc_b,), writes=(sc_b,))
        P.emit("vector", (lambda q0, wqb, rec: lambda e: e.tensor_tensor(
            q0.rearrange("p (g k) -> p g k", k=64),
            wqb.rearrange("p (g k) -> p g k", k=64),
            rec.unsqueeze(2).broadcast_to([128, 32, 64]), ALU.mult))(q0, wqb, rec),
            reads=(uq_b, sc_b), writes=(uq_b,))
        P.emit("vector", (lambda q0: lambda e: e.tensor_scalar(
            q0, q0, MAGIC, MAGIC, ALU.add, ALU.subtract))(q0),
            reads=(uq_b,), writes=(uq_b,))
        P.emit("vector", (lambda q0: lambda e: e.tensor_scalar(
            q0, q0, 1.0, -1.0, ALU.min, ALU.max))(q0),
            reads=(uq_b,), writes=(uq_b,))
        P.emit("vector", (lambda q0, scl: lambda e: e.tensor_tensor(
            q0.rearrange("p (g k) -> p g k", k=64),
            q0.rearrange("p (g k) -> p g k", k=64),
            scl.unsqueeze(2).broadcast_to([128, 32, 64]), ALU.mult))(q0, scl),
            reads=(uq_b, sc_b), writes=(uq_b,))
        return P.emit("sync", (lambda dst_ap, q0: lambda e: e.dma_start(
            out=dst_ap, in_=q0))(dst_ap, q0),
            reads=(uq_b,), dma=True)

    w1_ops = {}
    wp_ops = {}

    def quant_w1(ht):
        if ht not in w1_ops:
            w1_ops[ht] = quant_tile(w1[ht], w1hat[ht * 128:(ht + 1) * 128, :])

    def quant_wp(pt, k):
        if (pt, k) not in wp_ops:
            wp_ops[(pt, k)] = quant_tile(
                wp[pt][:, k * 2048:(k + 1) * 2048],
                wphat[pt * 128:(pt + 1) * 128, k * 2048:(k + 1) * 2048])

    for qt in range(NT):
        P.emit("sync", (lambda qt: lambda e: e.dma_start(
            out=xi[:], in_=x1[qt * 128:(qt + 1) * 128, :]))(qt),
            writes=(xi_b,), dma=True)
        emit_ln_tile(L, t1bf, t1bf_b, xi, xi_b, stats, stats_b)
        emit_transposes(L, t1T, t1T_b, t1bf, t1bf_b, ident, ident_b,
                        [(dc_ * 128, (dc_ * NT + qt) * 128) for dc_ in range(DC)])

    for k in range(2):
        for ht in range(4 * 0, 4 * 0 + 4):
            quant_w1(ht)
            quant_w1(64 + ht)
    for p in range(16):
        if p + 1 < 16:
            for ht in range(4 * (p + 1), 4 * (p + 1) + 4):
                quant_w1(ht)
                quant_w1(64 + ht)
        elif p == 15:
            for pt in range(16):
                for k in range(4):
                    quant_wp(pt, k)
        for gi in range(2):
            hbase = gi * FF0 + p * 512
            hts = [hbase // 128 + kk for kk in range(4)]
            tdeps = [w1_ops[t] for t in hts]
            for dc_ in range(DC):
                P.emit("scalar" if dc_ % 2 else "sync",
                       (lambda gi, dc_, hbase: lambda e: e.dma_start_transpose(
                           gusl[:, (gi * 16 + dc_) * 512:(gi * 16 + dc_ + 1) * 512],
                           w1hat[hbase:hbase + 512, dc_ * 128:(dc_ + 1) * 128]))(gi, dc_, hbase),
                       reads=(), writes=(gusl_b,), dma=True, deps=tdeps)
        for qt in range(NT):
            bg, bg_b = L.ps[0], L.ps_buf[0]
            bu, bu_b = L.ps[1], L.ps_buf[1]
            for dc_ in range(DC):
                P.emit("tensor", (lambda qt, dc_, bg: lambda e: e.matmul(
                    bg[:], t1T[:, (dc_ * NT + qt) * 128:(dc_ * NT + qt) * 128 + 128],
                    gusl[:, dc_ * 512:(dc_ + 1) * 512],
                    start=(dc_ == 0), stop=(dc_ == DC - 1)))(qt, dc_, bg),
                    reads=(t1T_b, gusl_b), writes=(bg_b,))
            for dc_ in range(DC):
                P.emit("tensor", (lambda qt, dc_, bu: lambda e: e.matmul(
                    bu[:], t1T[:, (dc_ * NT + qt) * 128:(dc_ * NT + qt) * 128 + 128],
                    gusl[:, (16 + dc_) * 512:(16 + dc_ + 1) * 512],
                    start=(dc_ == 0), stop=(dc_ == DC - 1)))(qt, dc_, bu),
                    reads=(t1T_b, gusl_b), writes=(bu_b,))
            P.emit("scalar", (lambda bg: lambda e: e.activation(
                sg[:], bg[:], AF.Silu))(bg),
                reads=(bg_b,), writes=(sg_b,))
            P.emit("vector", (lambda bu: lambda e: e.tensor_tensor(
                hseg[:], sg[:], bu[:], ALU.mult))(bu),
                reads=(sg_b, bu_b), writes=(hseg_b,))
            P.emit("scalar", lambda e: e.activation(hsq[:], hseg[:], AF.Square),
                   reads=(hseg_b,), writes=(hsq_b,))
            P.emit("vector", lambda e: e.tensor_reduce(hcol[:], hsq[:], AX.X, ALU.add),
                   reads=(hsq_b,), writes=(hcol_b,))
            P.emit("vector", (lambda qt: lambda e: e.tensor_tensor(
                hss[:, qt:qt + 1], hss[:, qt:qt + 1], hcol[:], ALU.add))(qt),
                reads=(hss_b, hcol_b), writes=(hss_b,))
            P.emit("vector", lambda e: e.tensor_copy(hbf[:], hseg[:]),
                   reads=(hseg_b,), writes=(hbf_b,))
            store_ops.append(P.emit("sync", (lambda qt, p: lambda e: e.dma_start(
                out=hscr[qt * 128:(qt + 1) * 128, p * 512:(p + 1) * 512],
                in_=hbf[:]))(qt, p),
                reads=(hbf_b,), dma=True))
    h_done = list(store_ops)
    store_ops = []

    P.emit("scalar", lambda e: e.activation(
        hss[:, 16:32], hss[:, 0:16], AF.Sqrt, bias=L.consts[:, 1:2],
        scale=float(1.0 / FF0)),
        reads=(hss_b, L.consts_b), writes=(hss_b,))
    P.emit("vector", lambda e: e.reciprocal(hss[:, 32:48], hss[:, 16:32]),
           reads=(hss_b,), writes=(hss_b,))

    for qg in range(4):
        for hc in range(64):
            P.emit("scalar" if hc % 2 else "sync",
                   (lambda qg, hc: lambda e: e.dma_start_transpose(
                       gusl[:, hc * 512:(hc + 1) * 512],
                       hscr[qg * 512:(qg + 1) * 512, hc * 128:(hc + 1) * 128]))(qg, hc),
                   writes=(gusl_b,), dma=True, deps=h_done)
        for ds_ in range(4):
            for hc in range(64):
                wt, wt_b = wpt[hc % 2]
                wdeps = [wp_ops[(pt, hc // 16)] for pt in range(4 * ds_, 4 * ds_ + 4)]
                P.emit("scalar", (lambda ds_, hc, wt: lambda e: e.dma_start_transpose(
                    wt[:], wphat[ds_ * 512:(ds_ + 1) * 512, hc * 128:(hc + 1) * 128]))(ds_, hc, wt),
                    writes=(wt_b,), dma=True, deps=wdeps)
                for j in range(4):
                    bank, bank_b = L.ps[j], L.ps_buf[j]
                    P.emit("tensor", (lambda hc, j, wt, bank: lambda e: e.matmul(
                        bank[:], gusl[:, hc * 512 + j * 128:hc * 512 + (j + 1) * 128],
                        wt[:], start=(hc == 0), stop=(hc == 63)))(hc, j, wt, bank),
                        reads=(gusl_b, wt_b), writes=(bank_b,))
            for j in range(4):
                qt = qg * 4 + j
                bank, bank_b = L.ps[j], L.ps_buf[j]
                P.emit("vector", (lambda qt, j, ds_, bank: lambda e: e.tensor_scalar(
                    outt4[:, j * D + ds_ * 512:j * D + (ds_ + 1) * 512], bank[:],
                    hss[:, 32 + qt:33 + qt], None, ALU.mult))(qt, j, ds_, bank),
                    reads=(bank_b, hss_b), writes=(outt4_b,))
        for j in range(4):
            qt = qg * 4 + j
            P.emit("sync", (lambda qt: lambda e: e.dma_start(
                out=xi[:], in_=x1[qt * 128:(qt + 1) * 128, :]))(qt),
                writes=(xi_b,), dma=True)
            for i2 in range(4):
                P.emit("vector", (lambda i2, j: lambda e: e.bn_stats(
                    stats[:, i2 * 6:(i2 + 1) * 6],
                    outt4[:, j * D + i2 * 512:j * D + (i2 + 1) * 512]))(i2, j),
                    reads=(outt4_b,), writes=(stats_b,))
            P.emit("vector", lambda e: e.bn_aggr(stats[:, 24:26], stats[:, 0:24]),
                   reads=(stats_b,), writes=(stats_b,))
            P.emit("vector", lambda e: e.tensor_scalar(stats[:, 26:27], stats[:, 24:25],
                                                       -1.0, None, ALU.mult),
                   reads=(stats_b,), writes=(stats_b,))
            P.emit("scalar", lambda e: e.activation(stats[:, 27:28], stats[:, 25:26],
                                                    AF.Sqrt, bias=L.consts[:, 0:1]),
                   reads=(stats_b, L.consts_b), writes=(stats_b,))
            P.emit("vector", lambda e: e.reciprocal(stats[:, 28:29], stats[:, 27:28]),
                   reads=(stats_b,), writes=(stats_b,))
            P.emit("vector", (lambda j: lambda e: e.tensor_scalar(
                yt[:], outt4[:, j * D:(j + 1) * D], stats[:, 26:27], stats[:, 28:29],
                ALU.add, ALU.mult))(j),
                reads=(outt4_b, stats_b), writes=(yt_b,))
            P.emit("vector", lambda e: e.tensor_tensor(yt[:], yt[:], xi[:], ALU.add),
                   reads=(yt_b, xi_b), writes=(yt_b,))
            store_ops.append(P.emit("sync", (lambda qt: lambda e: e.dma_start(
                out=x2[qt * 128:(qt + 1) * 128, :], in_=yt[:]))(qt),
                reads=(yt_b,), dma=True))

    P.emit("sync", lambda e: e.nop(), deps=store_ops)
    if fin:
        P.finalize(nc, es)
    return nc


def host_prep_l0_ffn(inputs):
    w1 = np.ascontiguousarray(inputs["L0_ffn_w_gate_up"].reshape(128, 128, D))
    wp = np.ascontiguousarray(inputs["L0_ffn_w_proj"].reshape(16, 128, FF0))
    return dict(w1=w1, wp=wp, ident_bf=np.eye(128, dtype=bf16))



_BUILT = {}


def _get_nc(name, builder):
    if name not in _BUILT:
        nc = bass.Bass("TRN2", target_bir_lowering=False, debug=False)
        es = ExitStack()
        es.__enter__()
        builder(nc, es)
        _BUILT[name] = (nc, es)
    return _BUILT[name][0]


def kernel_multi(**inputs):
    cores = list(range(8))

    nc = _get_nc("l0a", build_l0_attn)
    maps = [host_prep_l0_attn(inputs, c) for c in cores]
    r1 = run_bass_kernel_spmd(nc, maps, core_ids=cores).results
    x1s = [r["x1"] for r in r1]

    nc = _get_nc("l0f", build_l0_ffn)
    shared = host_prep_l0_ffn(inputs)
    maps = [dict(x1=x1s[c], **shared) for c in cores]
    r2 = run_bass_kernel_spmd(nc, maps, core_ids=cores).results
    x2s = [r["x2"] for r in r2]

    x2_full = np.empty((B, S, D), np.float32)
    for c in cores:
        b, rows = contiguous_rows(c)
        x2_full[b][rows] = x2s[c]
    x2i = []
    for c in cores:
        b, rows = interleaved_rows(c)
        x2i.append(np.ascontiguousarray(x2_full[b][rows]))

    nc = _get_nc("l1q", build_l1_qkv)
    maps = [dict(x2=x2i[c], **host_prep_l1_qkv(inputs, c)) for c in cores]
    r3 = run_bass_kernel_spmd(nc, maps, core_ids=cores).results
    qts = [r["qt_o"] for r in r3]
    kts = [r["kt_o"] for r in r3]
    vs = [r["v_o"] for r in r3]

    nc = _get_nc("l1a", build_l1_attn)
    maps = []
    for c in cores:
        b = c // 4
        ktg = np.concatenate([kts[4 * b + r] for r in range(4)], axis=0)
        vg = np.concatenate([vs[4 * b + r] for r in range(4)], axis=0)
        maps.append(dict(qto=qts[c], ktg=ktg, vg=vg, kto=kts[c], vo=vs[c],
                         x2=x2i[c], **host_prep_l1_attn(inputs, c)))
    r4 = run_bass_kernel_spmd(nc, maps, core_ids=cores).results
    x3s = [r["x3"] for r in r4]

    nc = _get_nc("l1f", build_l1_ffn)
    shared = host_prep_l1_ffn(inputs)
    maps = [dict(x3=x3s[c], **shared) for c in cores]
    r5 = run_bass_kernel_spmd(nc, maps, core_ids=cores).results

    out = np.empty((B, S, D), np.float32)
    for c in cores:
        b, rows = interleaved_rows(c)
        out[b][rows] = r5[c]["y"]
    return out



def _slot_src(s_):
    if s_ <= 7:
        r, pos = s_ // 2, s_ % 2
    else:
        if s_ % 2 == 1:
            r, pos = (15 - s_) // 2, 3
        else:
            r, pos = (14 - s_) // 2, 2
    return r * 512 + pos * 128


def build_fused(nc, es):
    L0 = LaunchCtx(nc, es)
    L0.arena = es.enter_context(nc.sbuf_tensor("arena", [128, 207 * 1024],
                                               mybir.dt.uint8))
    P = L0.P
    RG = [[0, 1, 2, 3], [4, 5, 6, 7]]
    E = "ExternalInput"

    def d(name, shape, dt_, kind="ExternalInput"):
        return nc.dram_tensor(name, shape, dt_, kind=kind).ap()

    ident_bf = d("ident_bf", [128, 128], BF16)
    ident_f = d("ident_f", [128, 128], F32)
    ones_row = d("ones_row", [1, 512], BF16)
    ones_col = d("ones_col", [128, 1], BF16)
    ones_row_f = d("ones_row_f", [1, 128], F32)

    io1 = dict(ident_bf=ident_bf, ones_row_f=ones_row_f,
               x_own=d("x_own", [TOK, D], F32), x_halo=d("x_halo", [1024, D], F32),
               wqB=d("wqB", [16, 128, D], F32), wkB=d("wkB", [16, 128, 512], F32),
               wvB=d("wvB", [16, 128, 512], F32), woB=d("woB", [16, 128, D], F32),
               gain_row=d("gain_row", [1, 16], F32),
               rope_own_c=d("rope_own_c", [TOK, 64], F32),
               rope_own_s=d("rope_own_s", [TOK, 64], F32),
               rope_halo_c=d("rope_halo_c", [1024, 64], F32),
               rope_halo_s=d("rope_halo_s", [1024, 64], F32),
               maskadd=d("maskadd", [2, 8, 5, 128, 128], F32))
    io2 = dict(ident_bf=ident_bf,
               w1=d("w1", [128, 128, D], F32), wp=d("wp", [16, 128, FF0], F32))
    io3 = dict(ident_bf=ident_bf, ones_row_f=ones_row_f,
               wqA=d("wqA", [16, 128, 16 * 128], F32),
               wkA=d("wkA", [16, 128, 16 * 128], F32),
               wvB=d("wvB1", [16, 128, D], F32),
               bqP=d("bqP", [128, 16], F32), bkP=d("bkP", [128, 16], F32),
               bv_row=d("bv_row", [1, D], F32),
               ropeq_c=d("ropeq_c", [64, TOK], F32),
               ropeq_s=d("ropeq_s", [64, TOK], F32),
               ropek_c=d("ropek_c", [64, TOK], F32),
               ropek_s=d("ropek_s", [64, TOK], F32))
    io4 = dict(ident_f=ident_f, ones_row=ones_row, ones_col=ones_col,
               ones_row_f=ones_row_f,
               woB=d("woB1", [16, 128, D], F32),
               bo_row=d("bo_row", [1, D], F32),
               kill_slots=d("kill_slots", [1, 64 * 16], BF16),
               tri_add=d("tri_add", [128, 128], F32),
               tri01T=d("tri01T", [128, 128], BF16))
    io5 = dict(ident_bf=ident_bf, ident_f=ident_f, ones_row=ones_row,
               ones_col=ones_col,
               wg=d("wg", [64, 128, 16 * 128], I32),
               wu=d("wu", [64, 128, 16 * 128], I32),
               wd=d("wd", [64, 128, D], I32),
               gsc=d("gsc", [128, 64], F32), usc=d("usc", [128, 64], F32),
               dsc_row=d("dsc_row", [1, D], F32))

    x1i = d("x1i", [TOK, D], F32, "Internal")
    x2in = d("cc_x2_in", [TOK, D], F32, "Internal")
    x2g = d("cc_x2_g", [4 * TOK, D], F32, "Internal")
    qtoi = d("qto_i", [D, TOK], BF16, "Internal")
    ktoi = d("kto_i", [D, TOK], BF16, "Internal")
    voi = d("vo_i", [TOK, D], BF16, "Internal")
    ktgi = d("ktg_i", [4 * D, TOK], BF16, "Internal")
    vgi = d("vg_i", [4 * TOK, D], BF16, "Internal")
    x3i = d("x3i", [TOK, D], F32, "Internal")
    yout = d("y", [TOK, D], F32, "ExternalOutput")

    x2idx = d("x2idx", [128, 16], I32)
    x2idxs = es.enter_context(nc.sbuf_tensor("x2idxs", [128, 16], I32))
    x2idxs_b = Buf()

    def x2_load(e, qt, dst):
        return e.indirect_dma_start(
            dst, None, x2g[:, :],
            bass.IndirectOffsetOnAxis(ap=x2idxs[:, qt:qt + 1], axis=0))

    io1["x1"] = x1i
    io2.update(x1=x1i, x2=x2in)
    io3.update(x2=x2g, qt_o=qtoi, kt_o=ktoi, v_o=voi)
    io4.update(qto=qtoi, ktg=ktgi, vg=vgi, kto=ktoi, vo=voi, x2=x2g, x3=x3i)
    io5.update(x3=x3i, y=yout)

    def stage(builder, io):
        Ls = LaunchCtx(nc, es, shared=L0)
        builder(nc, es, L=Ls, io=io)
        P.barrier()

    cdep = P.emit("sync", lambda e: e.dma_start(out=x2idxs[:], in_=x2idx[:]),
                  writes=(x2idxs_b,), dma=True)
    io3.update(x2_load=x2_load, x2_dep=cdep, x2_eng="gpsimd")
    io4.update(x2_load=x2_load, x2_dep=cdep, x2_eng="gpsimd")

    stage(build_l0_attn, io1)
    stage(build_l0_ffn, io2)
    P.emit("gpsimd", lambda e: e.collective_compute(
        "AllGather", ALU.bypass, RG,
        [x2in[:, :]], [x2g[:, :]]), dma=True, coll=True)
    P.barrier()
    stage(build_l1_qkv, io3)
    P.emit("gpsimd", lambda e: e.collective_compute(
        "AllGather", ALU.bypass, RG,
        [ktoi[:, :]], [ktgi[:, :]]), dma=True, coll=True)
    P.emit("gpsimd", lambda e: e.collective_compute(
        "AllGather", ALU.bypass, RG,
        [voi[:, :]], [vgi[:, :]]), dma=True, coll=True)
    P.barrier()
    stage(build_l1_attn, io4)
    stage(build_l1_ffn, io5)

    P.finalize(nc, es)
    return nc


def _a_const_host(s_):
    t0 = 63 - 4 * s_
    j = t0 // 8
    r = min(j, 7 - j)
    return r * 2048 + (0 if j == r else 1024) + (t0 - 8 * j) * 128


def host_prep_fused(inputs, core):
    c = core % 4
    idx = np.empty((128, 16), np.int32)
    for s_ in range(16):
        idx[:, s_] = _a_const_host(s_) - 128 * c + np.arange(128)
    m = {"x2idx": idx}
    m.update(host_prep_l0_attn(inputs, core))
    m.update(host_prep_l0_ffn(inputs))
    q = host_prep_l1_qkv(inputs, core)
    m["wvB1"] = q.pop("wvB")
    m.update(q)
    a = host_prep_l1_attn(inputs, core)
    m["woB1"] = a.pop("woB")
    m.update(a)
    m.update(host_prep_l1_ffn(inputs))
    return m


def kernel_fused(**inputs):
    cores = list(range(8))
    nc = _get_nc("fused", build_fused)
    maps = [host_prep_fused(inputs, c) for c in cores]
    res = run_bass_kernel_spmd(nc, maps, core_ids=cores).results
    out = np.empty((B, S, D), np.float32)
    for c in cores:
        b, rows = interleaved_rows(c)
        out[b][rows] = res[c]["y"]
    return out


def kernel(**inputs):
    return kernel_fused(**inputs)



def build_half_a(nc, es):
    L0 = LaunchCtx(nc, es)
    L0.arena = es.enter_context(nc.sbuf_tensor("arena", [128, 207 * 1024],
                                               mybir.dt.uint8))
    P = L0.P

    def d(name, shape, dt_, kind="ExternalInput"):
        return nc.dram_tensor(name, shape, dt_, kind=kind).ap()

    ident_bf = d("ident_bf", [128, 128], BF16)
    ones_row_f = d("ones_row_f", [1, 128], F32)
    io1 = dict(ident_bf=ident_bf, ones_row_f=ones_row_f,
               x_own=d("x_own", [TOK, D], F32), x_halo=d("x_halo", [1024, D], F32),
               wqB=d("wqB", [16, 128, D], F32), wkB=d("wkB", [16, 128, 512], F32),
               wvB=d("wvB", [16, 128, 512], F32), woB=d("woB", [16, 128, D], F32),
               gain_row=d("gain_row", [1, 16], F32),
               rope_own_c=d("rope_own_c", [TOK, 64], F32),
               rope_own_s=d("rope_own_s", [TOK, 64], F32),
               rope_halo_c=d("rope_halo_c", [1024, 64], F32),
               rope_halo_s=d("rope_halo_s", [1024, 64], F32),
               maskadd=d("maskadd", [2, 8, 5, 128, 128], F32))
    io2 = dict(ident_bf=ident_bf,
               w1=d("w1", [128, 128, D], F32), wp=d("wp", [16, 128, FF0], F32))
    io3 = dict(ident_bf=ident_bf, ones_row_f=ones_row_f,
               wqA=d("wqA", [16, 128, 16 * 128], F32),
               wkA=d("wkA", [16, 128, 16 * 128], F32),
               wvB=d("wvB1", [16, 128, D], F32),
               bqP=d("bqP", [128, 16], F32), bkP=d("bkP", [128, 16], F32),
               bv_row=d("bv_row", [1, D], F32),
               ropeq_c=d("ropeq_c", [64, TOK], F32),
               ropeq_s=d("ropeq_s", [64, TOK], F32),
               ropek_c=d("ropek_c", [64, TOK], F32),
               ropek_s=d("ropek_s", [64, TOK], F32))
    x1i = d("x1i", [TOK, D], F32, "Internal")
    x2o = d("x2", [TOK, D], F32, "ExternalOutput")
    io1["x1"] = x1i
    io2.update(x1=x1i, x2=x2o)
    io3.update(x2=x2o,
               qt_o=d("qt_o", [D, TOK], BF16, "ExternalOutput"),
               kt_o=d("kt_o", [D, TOK], BF16, "ExternalOutput"),
               v_o=d("v_o", [TOK, D], BF16, "ExternalOutput"))

    def stage(builder, io):
        Ls = LaunchCtx(nc, es, shared=L0)
        builder(nc, es, L=Ls, io=io)
        P.barrier()

    stage(build_l0_attn, io1)
    stage(build_l0_ffn, io2)
    stage(build_l1_qkv, io3)
    P.finalize(nc, es)
    return nc


def build_half_b(nc, es):
    L0 = LaunchCtx(nc, es)
    L0.arena = es.enter_context(nc.sbuf_tensor("arena", [128, 207 * 1024],
                                               mybir.dt.uint8))
    P = L0.P

    def d(name, shape, dt_, kind="ExternalInput"):
        return nc.dram_tensor(name, shape, dt_, kind=kind).ap()

    io4 = dict(ident_f=d("ident_f", [128, 128], F32),
               ones_row=d("ones_row", [1, 512], BF16),
               ones_col=d("ones_col", [128, 1], BF16),
               ones_row_f=d("ones_row_f", [1, 128], F32),
               woB=d("woB1", [16, 128, D], F32),
               bo_row=d("bo_row", [1, D], F32),
               kill_slots=d("kill_slots", [1, 64 * 16], BF16),
               tri_add=d("tri_add", [128, 128], F32),
               tri01T=d("tri01T", [128, 128], BF16),
               qto=d("qto", [D, TOK], BF16), ktg=d("ktg", [4 * D, TOK], BF16),
               vg=d("vg", [4 * TOK, D], BF16), kto=d("kto", [D, TOK], BF16),
               vo=d("vo", [TOK, D], BF16),
               x2=d("x2", [TOK, D], F32))
    io5 = dict(ident_bf=d("ident_bf", [128, 128], BF16),
               ident_f=io4["ident_f"], ones_row=io4["ones_row"],
               ones_col=io4["ones_col"],
               wg=d("wg", [64, 128, 16 * 128], I32),
               wu=d("wu", [64, 128, 16 * 128], I32),
               wd=d("wd", [64, 128, D], I32),
               gsc=d("gsc", [128, 64], F32), usc=d("usc", [128, 64], F32),
               dsc_row=d("dsc_row", [1, D], F32))
    x3i = d("x3i", [TOK, D], F32, "Internal")
    io4["x3"] = x3i
    io5.update(x3=x3i, y=d("y", [TOK, D], F32, "ExternalOutput"))

    def stage(builder, io):
        Ls = LaunchCtx(nc, es, shared=L0)
        builder(nc, es, L=Ls, io=io)
        P.barrier()

    stage(build_l1_attn, io4)
    stage(build_l1_ffn, io5)
    P.finalize(nc, es)
    return nc


def kernel_2launch(**inputs):
    cores = list(range(8))
    nc = _get_nc("halfa", build_half_a)
    maps = []
    for c in cores:
        m = {}
        m.update(host_prep_l0_attn(inputs, c))
        m.update(host_prep_l0_ffn(inputs))
        b, crows = contiguous_rows(c)
        q = host_prep_l1_qkv(inputs, c, rows=crows)
        m["wvB1"] = q.pop("wvB")
        m.update(q)
        maps.append(m)
    ra = run_bass_kernel_spmd(nc, maps, core_ids=cores).results

    x2f = np.empty((B, S, D), np.float32)
    qtf = np.empty((B, D, S), bf16)
    ktf = np.empty((B, D, S), bf16)
    vf = np.empty((B, S, D), bf16)
    for c in cores:
        b, crows = contiguous_rows(c)
        x2f[b][crows] = ra[c]["x2"]
        qtf[b][:, crows] = ra[c]["qt_o"]
        ktf[b][:, crows] = ra[c]["kt_o"]
        vf[b][crows] = ra[c]["v_o"]

    nc = _get_nc("halfb", build_half_b)
    maps = []
    for c in cores:
        b, irows = interleaved_rows(c)
        m = host_prep_l1_attn(inputs, c)
        m["woB1"] = m.pop("woB")
        f = host_prep_l1_ffn(inputs)
        m.update(f)
        m["qto"] = np.ascontiguousarray(qtf[b][:, irows])
        m["kto"] = np.ascontiguousarray(ktf[b][:, irows])
        m["vo"] = np.ascontiguousarray(vf[b][irows])
        ktg = np.empty((4 * D, TOK), bf16)
        vg = np.empty((4 * TOK, D), bf16)
        for r in range(4):
            _, rrows = interleaved_rows(4 * b + r)
            ktg[r * D:(r + 1) * D, :] = ktf[b][:, rrows]
            vg[r * TOK:(r + 1) * TOK, :] = vf[b][rrows]
        m["ktg"] = ktg
        m["vg"] = vg
        m["x2"] = np.ascontiguousarray(x2f[b][irows])
        maps.append(m)
    rb = run_bass_kernel_spmd(nc, maps, core_ids=cores).results
    out = np.empty((B, S, D), np.float32)
    for c in cores:
        b, irows = interleaved_rows(c)
        out[b][irows] = rb[c]["y"]
    return out


def kernel(**inputs):
    return kernel_2launch(**inputs)
